# version 02
# baseline (speedup 1.0000x reference)
import numpy as np
import concourse.bass as bass
import concourse.mybir as mybir
from concourse.bass_utils import run_bass_kernel_spmd

F32 = mybir.dt.float32
AF = mybir.ActivationFunctionType
ALU = mybir.AluOpType
AX = mybir.AxisListType

D = 2048
DIN = 4096
NH = 64
HP = 64
NCONV = 4128
INDIM = 8288
INTER = 8192
EPS = 1e-5
Q = 128
CLAMP = 40.0


class Sem:
    __slots__ = ("h", "n")

    def __init__(self, nc, name):
        self.h = nc.alloc_semaphore(name)
        self.n = 0


class B:
    ENGINES = ("tensor", "vector", "scalar", "sync", "gpsimd")

    def __init__(self, nc):
        self.nc = nc
        self.cmds = {e: [] for e in self.ENGINES}
        self._i = 0
        self.ms = {e: Sem(nc, "ms_" + e) for e in self.ENGINES}

    def sem(self, name=None):
        self._i += 1
        return Sem(self.nc, name or f"s{self._i}")

    def em(self, eng, make, inc=False, sem=None, n=1):
        caps = []
        if inc:
            m = self.ms[eng]
            m.n += 1
            caps.append((m.h, 1))
        if sem is not None:
            sem.n += n
            caps.append((sem.h, n))

        def fn(e, make=make, caps=tuple(caps)):
            ins = make(e)
            for h, k in caps:
                ins = ins.then_inc(h, k)
        self.cmds[eng].append(fn)

    def dma(self, eng, out, in_, sem=None):
        self.em(eng, lambda e, o=out, i=in_: e.dma_start(out=o, in_=i), sem=sem, n=16)

    def wait(self, eng, sem, val=None):
        v = sem.n if val is None else val
        if v <= 0:
            return
        h = sem.h
        self.cmds[eng].append(lambda e, h=h, v=v: e.wait_ge(h, v))

    def waitm(self, eng, src_eng, val=None):
        self.wait(eng, self.ms[src_eng], val)


def build_program(n_tiles, halo_tiles, dbg=False):
    T = n_tiles * Q
    Thalf = T // 2
    nth = n_tiles // 2
    assert n_tiles % 2 == 0
    Town = T - halo_tiles * Q
    n_own = n_tiles - halo_tiles
    assert Town % 256 == 0

    def mov_tiles(total):
        r, t0 = [], 0
        while t0 < total:
            tw = min(512, total - t0)
            r.append((t0, tw))
            t0 += tw
        return r

    jth = mov_tiles(Thalf)
    jo = mov_tiles(Town)

    nc = bass.Bass("TRN2", target_bir_lowering=False, detect_race_conditions=False)
    b = B(nc)
    T_, V_, S_, SY, G_ = "tensor", "vector", "scalar", "sync", "gpsimd"

    x_in = nc.dram_tensor("x_in", [T, D], F32, kind="ExternalInput")
    w_in = nc.dram_tensor("w_in", [D, INDIM], F32, kind="ExternalInput")
    w_out = nc.dram_tensor("w_out", [DIN, D], F32, kind="ExternalInput")
    w_gu_t = nc.dram_tensor("w_gu_t", [D, 2 * INTER], F32, kind="ExternalInput")
    w_down_t = nc.dram_tensor("w_down_t", [INTER, D], F32, kind="ExternalInput")
    convw = nc.dram_tensor("convw", [128, 33 * 4], F32, kind="ExternalInput")
    convb = nc.dram_tensor("convb", [128, 33], F32, kind="ExternalInput")
    alog = nc.dram_tensor("alog", [NH, 1], F32, kind="ExternalInput")
    dtbias = nc.dram_tensor("dtbias", [NH, 1], F32, kind="ExternalInput")
    dskip = nc.dram_tensor("dskip", [128, NH], F32, kind="ExternalInput")
    normw = nc.dram_tensor("normw", [128, DIN], F32, kind="ExternalInput")
    hm_w = max(halo_tiles * Q, 1)
    halomask = nc.dram_tensor("halomask", [128, hm_w], F32, kind="ExternalInput")
    out_t = nc.dram_tensor("out", [Town, D], F32, kind="ExternalOutput")

    kd = "ExternalOutput" if dbg else "Internal"
    gsilu_s = nc.dram_tensor("gsilu_s", [DIN, T], F32, kind=kd)
    xbc_s = nc.dram_tensor("xbc_s", [NCONV, T], F32, kind=kd)
    yn_s = nc.dram_tensor("yn_s", [DIN, Town], F32, kind=kd)
    attn_s = nc.dram_tensor("attn_s", [Town, D], F32, kind=kd)
    x2_s = nc.dram_tensor("x2_s", [Town, D], F32, kind=kd)
    h_s = nc.dram_tensor("h_s", [INTER, Town], F32, kind=kd)

    ident_h = nc.inline_tensor(np.eye(128, dtype=np.float32), "c_ident")
    bdneg = np.zeros((2, 64, 4096), np.float32)
    for j in range(32):
        bdneg[0, j, j * 128:(j + 1) * 128] = -1.0
        bdneg[1, 32 + j, j * 128:(j + 1) * 128] = -1.0
    bdneg_h = [nc.inline_tensor(bdneg[0], "c_bdneg0"), nc.inline_tensor(bdneg[1], "c_bdneg1")]
    triu_h = nc.inline_tensor(np.triu(np.ones((128, 128), np.float32)), "c_triu")
    ones1_h = nc.inline_tensor(np.ones((1, 128), np.float32), "c_ones1")
    eye16_h = nc.inline_tensor(np.eye(16, dtype=np.float32), "c_eye16")

    A = nc.alloc_sbuf_tensor
    ident = A("ident", [128, 128], F32)
    triu = A("triu", [128, 128], F32)
    ones1 = A("ones1", [1, 128], F32)
    eye16 = A("eye16", [16, 16], F32)
    cwsb = A("cwsb", [128, 33 * 4], F32)
    cbsb = A("cbsb", [128, 33], F32)
    aneg = A("aneg", [128, 1], F32)
    dtb = A("dtb", [128, 1], F32)
    dsk = A("dsk", [128, NH], F32)
    nw = A("nw", [128, DIN], F32)
    hmask = A("hmask", [128, hm_w], F32)
    states = A("states", [16, NH * HP], F32)
    stats = A("stats", [128, 8], F32)
    rowst = A("rowst", [1, 64], F32)
    dax16 = A("dax16", [16, NH], F32)
    dAI = A("dAI", [16, 16], F32)
    bw16 = A("bw16", [128, 16], F32)
    sv3 = A("sv3", [128, 33 * 3], F32)
    cvt = A("cvt", [128, 2], F32)

    rem = nc.sbuf_bytes_remaining // 4
    AR = (rem - 512) // 4 * 4
    arena = A("arena", [128, AR], F32)
    ar = arena.ap()

    def rg(off, n):
        assert off + n <= AR, (off, n, AR)
        return ar[:, off:off + n]

    x1fm_n = nth * 16 * 128
    o = x1fm_n
    XT = o; o += 2 * D
    X1T = o; o += D
    SQB = o; o += D
    WBUF = o; o += 2 * 2048
    GBUF = o; o += 2 * 512
    XPRE = o; o += Thalf + 4
    XBCB = o; o += 2 * Thalf
    DTRAW = o; o += T
    AB_END = o
    assert AB_END <= AR, (AB_END, AR)

    x1fm = rg(0, x1fm_n)
    x1fm_r = x1fm.rearrange("p (i k t) -> p i k t", k=16, t=128)
    xt_r = [rg(XT, D), rg(XT + D, D)]
    x1t = rg(X1T, D)
    sqb = rg(SQB, D)
    wbuf_r = [rg(WBUF, 2048), rg(WBUF + 2048, 2048)]
    gbuf_r = [rg(GBUF, 512), rg(GBUF + 512, 512)]
    xpre = rg(XPRE, Thalf + 4)
    xbc_r = [rg(XBCB, Thalf), rg(XBCB + Thalf, Thalf)]
    dtraw = rg(DTRAW, T)

    o = 0
    BD0 = o; o += 4096
    BD1 = o; o += 4096
    BIGD = o; o += 4096
    XBCC = o; o += 33 * 128
    XTM = o; o += DIN
    YTM = o; o += DIN
    GFM = o; o += DIN
    CST = o; o += 5 * 128
    BCS = o; o += 3 * 128
    YNST = o; o += 4 * 128
    SSD_B = o
    if max(SSD_B, AB_END) + 5 * T + 128 <= AR:
        CUMO = max(SSD_B, AB_END)
    else:
        CUMO = SSD_B
        assert CUMO + 2 * T <= DTRAW, (CUMO + 2 * T, DTRAW)
        assert CUMO + 4 * T >= DTRAW + T or CUMO + 5 * T <= DTRAW
    CUMPRO = CUMO + T
    OCPO = CUMPRO + T
    CUMLOO = OCPO + T
    DTTO = CUMLOO + T
    DTAO = DTTO + T
    SSD_END = DTAO + 128
    assert SSD_END <= AR, (SSD_END, AR)

    bd_r = [rg(BD0, 4096), rg(BD1, 4096)]
    bigD = rg(BIGD, 4096)
    xbcc = rg(XBCC, 33 * 128)
    xtm = rg(XTM, DIN)
    ytm = rg(YTM, DIN)
    gfm = rg(GFM, DIN)
    cumT_sb = rg(CST, 64)
    ecumT = rg(CST + 64, 64)
    wexT = rg(CST + 128, 64)
    wexs = rg(CST + 192, 128)
    bcs = rg(BCS, 384)
    ynst = rg(YNST, 512)
    cum = rg(CUMO, T)
    cumpr = rg(CUMPRO, T)
    ocp = rg(OCPO, T)
    cumlo = rg(CUMLOO, T)
    dt_t = rg(DTTO, T)
    dtA = rg(DTAO, 128)

    wouth = rg(0, 32 * 1024)
    yst_r = [rg(32 * 1024, 4096), rg(32 * 1024 + 4096, 4096)]
    ast_r = [rg(32 * 1024 + 8192, 1024), rg(32 * 1024 + 9216, 1024)]
    assert 32 * 1024 + 10240 <= AR

    x3fm_n = n_own * 16 * 128
    x3fm = rg(0, x3fm_n)
    x3fm_r = x3fm.rearrange("p (i k t) -> p i k t", k=16, t=128)
    fo = x3fm_n
    f_attn = rg(fo, D)
    f_x = rg(fo + D, D)
    f_tmp = rg(fo + 2 * D, D)
    f_x2 = rg(fo + 3 * D, D)
    f_x3 = rg(fo + 4 * D, D)
    assert fo + 5 * D <= AR
    wgu_r = [rg(fo, 2048), rg(fo + 2048, 2048)]
    gs_b = rg(fo + 4096, Town)
    hb = rg(fo + 4096 + Town, Town)
    assert fo + 4096 + 2 * Town <= AR

    hq = rg(0, 64 * 256)
    wdb_r = [rg(64 * 256, 2048), rg(64 * 256 + 2048, 2048)]
    ho = 64 * 256 + 4096
    dtm_r = [rg(ho, D), rg(ho + D, D)]
    h_x2 = rg(ho + 2 * D, D)
    h_out = rg(ho + 3 * D, D)
    h_sq = rg(ho + 4 * D, D)
    assert ho + 5 * D <= AR

    pb = [nc.alloc_psum_tensor(f"pb{i}", [128, 512], F32) for i in range(8)]

    s_ld = b.sem("ld")
    b.dma(SY, ident[:, :], ident_h.ap(), s_ld)
    b.dma(SY, triu[:, :], triu_h.ap(), s_ld)
    b.dma(SY, ones1[:, :], ones1_h.ap(), s_ld)
    b.dma(SY, eye16[:, :], eye16_h.ap(), s_ld)
    b.dma(SY, cwsb[:, :], convw.ap()[:, :], s_ld)
    b.dma(SY, cbsb[:, :], convb.ap()[:, :], s_ld)
    b.dma(SY, aneg[0:64, 0:1], alog.ap(), s_ld)
    b.dma(SY, dtb[0:64, 0:1], dtbias.ap(), s_ld)
    b.dma(SY, dsk[:, :], dskip.ap(), s_ld)
    b.dma(SY, nw[:, :], normw.ap(), s_ld)
    b.dma(SY, hmask[:, :], halomask.ap(), s_ld)

    b.wait(S_, s_ld)
    b.em(S_, lambda e: e.activation(aneg[0:64, 0:1], aneg[0:64, 0:1], AF.Exp, scale=1.0))
    b.em(S_, lambda e: e.mul(aneg[0:64, 0:1], aneg[0:64, 0:1], -1.0), inc=True)
    b.wait(V_, s_ld)
    b.em(V_, lambda e: e.memset(cvt[:, 0:1], float(EPS)))
    b.em(V_, lambda e: e.memset(cvt[:, 1:2], 1.0))
    b.em(V_, lambda e: e.memset(xpre[:, 0:4], 0.0))
    b.em(V_, lambda e: e.memset(stats[:, :], 0.0), inc=True)
    a_init_v = b.ms[V_].n
    a_init_s = b.ms[S_].n

    s_ldx = [b.sem("ldx0"), b.sem("ldx1")]
    s_ldw = [b.sem("ldw0"), b.sem("ldw1")]
    s_gsp = [b.sem("gsp0"), b.sem("gsp1")]
    s_xsp = [b.sem("xsp0"), b.sem("xsp1")]
    NCC = (INDIM + 127) // 128

    bank_ev = [0] * 8
    bank_eng = [V_] * 8
    b.wait(T_, s_ld)
    b.waitm(T_, V_, a_init_v)

    xt_ev = [0, 0]
    gate_cnt = 0
    for half in range(2):
        tb = half * nth
        for li in range(nth):
            gi = tb + li
            sl = li % 2
            b.wait(SY, b.ms[V_], xt_ev[sl])
            b.dma(SY, xt_r[sl], x_in.ap()[gi * Q:(gi + 1) * Q, :], s_ldx[sl])
            b.wait(V_, s_ldx[sl])
            b.em(V_, lambda e, sl=sl: e.reduce_sum(stats[:, 0:1], xt_r[sl], axis=AX.X), inc=True)
            b.wait(S_, s_ldx[sl])
            b.waitm(S_, V_)
            b.em(S_, lambda e, sl=sl: e.activation(sqb, xt_r[sl], AF.Square, accum_out=stats[:, 1:2]),
                 inc=True)
            b.waitm(V_, S_)
            b.em(V_, lambda e: e.tensor_scalar(stats[:, 2:3], stats[:, 0:1], 1.0 / D, None, ALU.mult))
            b.em(V_, lambda e: e.tensor_tensor(stats[:, 3:4], stats[:, 2:3], stats[:, 2:3], ALU.mult))
            b.em(V_, lambda e: e.scalar_tensor_tensor(stats[:, 4:5], stats[:, 1:2], 1.0 / D,
                                                      stats[:, 3:4], ALU.mult, ALU.subtract), inc=True)
            b.waitm(S_, V_)
            b.em(S_, lambda e: e.activation(stats[:, 5:6], stats[:, 4:5], AF.Ln, bias=cvt[:, 0:1], scale=1.0))
            b.em(S_, lambda e: e.activation(stats[:, 6:7], stats[:, 5:6], AF.Exp, scale=-0.5), inc=True)
            b.waitm(V_, S_)
            b.em(V_, lambda e, sl=sl: e.tensor_scalar(x1t, xt_r[sl], stats[:, 2:3], stats[:, 6:7],
                                                      ALU.subtract, ALU.mult), inc=True)
            xt_ev[sl] = b.ms[V_].n
            b.waitm(T_, V_)
            for g in range(4):
                bank = pb[g]
                b.wait(T_, b.ms[bank_eng[g]], bank_ev[g])
                for u in range(4):
                    k = g * 4 + u
                    b.em(T_, lambda e, k=k, bank=bank, u=u: e.transpose(
                        bank[:, u * 128:(u + 1) * 128], x1t[:, k * 128:(k + 1) * 128], ident[:, :]),
                        inc=(u == 3))
                b.waitm(V_, T_)
                b.em(V_, lambda e, li=li, g=g, bank=bank: e.tensor_copy(
                    x1fm[:, (li * 16 + g * 4) * 128:(li * 16 + g * 4 + 4) * 128], bank[:, :]), inc=True)
                bank_ev[g] = b.ms[V_].n
                bank_eng[g] = V_

        b.waitm(T_, V_)
        for c in range(NCC):
            cw = min(128, INDIM - c * 128)
            sl = c % 2
            if c == 0:
                if half == 1:
                    b.waitm(SY, T_)
                for k in range(16):
                    b.dma(SY, wbuf_r[0][:, k * 128:k * 128 + cw],
                          w_in.ap()[k * 128:(k + 1) * 128, 0:cw], s_ldw[0])
            if c + 1 < NCC:
                cw2 = min(128, INDIM - (c + 1) * 128)
                if c >= 1:
                    b.waitm(SY, T_)
                for k in range(16):
                    b.dma(SY, wbuf_r[(c + 1) % 2][:, k * 128:k * 128 + cw2],
                          w_in.ap()[k * 128:(k + 1) * 128, (c + 1) * 128:(c + 1) * 128 + cw2],
                          s_ldw[(c + 1) % 2])
            b.wait(T_, s_ldw[sl])
            for j, (t0, tw) in enumerate(jth):
                bank = pb[j]
                b.wait(T_, b.ms[bank_eng[j]], bank_ev[j])
                ntl = (tw + 127) // 128
                i0 = t0 // 128
                if c < 64:
                    for k in range(16):
                        b.em(T_, lambda e, sl=sl, k=k, cw=cw, bank=bank, i0=i0, ntl=ntl, tw=tw:
                             e.matmul(bank[0:cw, 0:tw], wbuf_r[sl][:, k * 128:k * 128 + cw],
                                      x1fm_r[:, i0:i0 + ntl, k:k + 1, :],
                                      start=(k == 0), stop=(k == 15)),
                             inc=(k == 15))
                else:
                    bank2 = pb[3 + j]
                    b.wait(T_, b.ms[bank_eng[3 + j]], bank_ev[3 + j])
                    for k in range(16):
                        b.em(T_, lambda e, sl=sl, k=k, bank=bank, i0=i0, ntl=ntl, tw=tw:
                             e.matmul(bank[0:32, 0:tw], wbuf_r[sl][:, k * 128:k * 128 + 32],
                                      x1fm_r[:, i0:i0 + ntl, k:k + 1, :],
                                      start=(k == 0), stop=(k == 15)))
                        b.em(T_, lambda e, sl=sl, k=k, bank2=bank2, i0=i0, ntl=ntl, tw=tw:
                             e.matmul(bank2[0:64, 0:tw], wbuf_r[sl][:, k * 128 + 32:k * 128 + 96],
                                      x1fm_r[:, i0:i0 + ntl, k:k + 1, :],
                                      start=(k == 0), stop=(k == 15)),
                             inc=(k == 15))
                mmv = b.ms[T_].n
                if c < 32:
                    gsl = gate_cnt % 2
                    b.waitm(S_, T_, mmv)
                    b.wait(S_, s_gsp[gsl])
                    b.em(S_, lambda e, bank=bank, tw=tw, gsl=gsl:
                         e.activation(gbuf_r[gsl][:, 0:tw], bank[0:128, 0:tw], AF.Silu), inc=True)
                    bank_ev[j] = b.ms[S_].n
                    bank_eng[j] = S_
                    b.waitm(SY, S_)
                    b.dma(SY, gsilu_s.ap()[c * 128:(c + 1) * 128,
                                           half * Thalf + t0:half * Thalf + t0 + tw],
                          gbuf_r[gsl][:, 0:tw], s_gsp[gsl])
                    gate_cnt += 1
                elif c < 64:
                    b.waitm(V_, T_, mmv)
                    b.em(V_, lambda e, bank=bank, t0=t0, tw=tw: e.tensor_copy(
                        xpre[:, 4 + t0:4 + t0 + tw], bank[0:128, 0:tw]), inc=True)
                    bank_ev[j] = b.ms[V_].n
                    bank_eng[j] = V_
                else:
                    b.waitm(V_, T_, mmv)
                    b.em(V_, lambda e, bank=bank, t0=t0, tw=tw: e.tensor_copy(
                        xpre[0:32, 4 + t0:4 + t0 + tw], bank[0:32, 0:tw]))
                    b.em(V_, lambda e, j=j, t0=t0, tw=tw, half=half: e.tensor_copy(
                        dtraw[0:64, half * Thalf + t0:half * Thalf + t0 + tw], pb[3 + j][0:64, 0:tw]),
                        inc=True)
                    bank_ev[j] = b.ms[V_].n
                    bank_eng[j] = V_
                    bank_ev[3 + j] = b.ms[V_].n
                    bank_eng[3 + j] = V_
            if c >= 32:
                cc = c - 32
                nch = 128 if c < 64 else 32
                xb = xbc_r[cc % 2]
                b.wait(V_, s_xsp[cc % 2])
                if half == 1:
                    b.em(V_, lambda e, cc=cc, nch=nch: e.tensor_copy(
                        xpre[0:nch, 1:4], sv3[0:nch, cc * 3:cc * 3 + 3]))
                if dbg == 2:
                    b.em(V_, lambda e, nch=nch, xb=xb: e.tensor_copy(
                        xb[0:nch, 0:Thalf], xpre[0:nch, 4:4 + Thalf]))
                b.em(V_, lambda e, cc=cc, nch=nch, xb=xb: (None if dbg == 2 else e.tensor_scalar(
                    xb[0:nch, 0:Thalf], xpre[0:nch, 1:1 + Thalf], cwsb[0:nch, cc * 4:cc * 4 + 1],
                    cbsb[0:nch, cc:cc + 1], ALU.mult, ALU.add)) or e.nop())
                for k in range(1, 4):
                    if dbg == 2:
                        break
                    b.em(V_, lambda e, cc=cc, nch=nch, xb=xb, k=k: e.scalar_tensor_tensor(
                        xb[0:nch, 0:Thalf], xpre[0:nch, 1 + k:1 + k + Thalf],
                        cwsb[0:nch, cc * 4 + k:cc * 4 + k + 1], xb[0:nch, 0:Thalf],
                        ALU.mult, ALU.add))
                if half == 0:
                    b.em(V_, lambda e, cc=cc, nch=nch: e.tensor_copy(
                        sv3[0:nch, cc * 3:cc * 3 + 3], xpre[0:nch, 1 + Thalf:4 + Thalf]))
                    if halo_tiles > 0:
                        b.em(V_, lambda e, nch=nch, xb=xb: e.tensor_tensor(
                            xb[0:nch, 0:halo_tiles * Q], xb[0:nch, 0:halo_tiles * Q],
                            hmask[0:nch, :], ALU.mult))
                b.em(V_, lambda e, nch=nch, xb=xb: e.tensor_copy(xb[0:nch, 0:1], xb[0:nch, 0:1]),
                     inc=True)
                b.waitm(S_, V_)
                b.em(S_, lambda e, nch=nch, xb=xb: (e.nop() if dbg in (2, 3) else e.activation(
                    xb[0:nch, 0:Thalf], xb[0:nch, 0:Thalf], AF.Silu)), inc=True)
                b.waitm(SY, S_)
                b.dma(SY, xbc_s.ap()[cc * 128:cc * 128 + nch, half * Thalf:(half + 1) * Thalf],
                      xb[0:nch, 0:Thalf], s_xsp[cc % 2])

    b.waitm(S_, V_)
    b.em(S_, lambda e: e.activation(dt_t[0:64, :], dtraw[0:64, :], AF.Exp, bias=dtb[0:64, 0:1],
                                    scale=1.0))
    b.em(S_, lambda e: e.activation(dt_t[0:64, :], dt_t[0:64, :], AF.Ln, bias=cvt[0:64, 1:2], scale=1.0))
    b.em(S_, lambda e: e.activation(dtraw[0:64, :], dt_t[0:64, :], AF.Ln, scale=1.0), inc=True)
    b.waitm(V_, S_)
    b.em(V_, lambda e: e.memset(sqb[0:64, 0:Q], 0.0))
    for ci in range(n_tiles):
        b.em(V_, lambda e, ci=ci: e.tensor_scalar(
            dtA[0:64, 0:Q], dt_t[0:64, ci * Q:(ci + 1) * Q], aneg[0:64, 0:1], None, ALU.mult))
        b.em(V_, lambda e, ci=ci: e.tensor_tensor_scan(
            cum[0:64, ci * Q:(ci + 1) * Q], dtA[0:64, 0:Q], sqb[0:64, 0:Q],
            0.0, ALU.add, ALU.add))
    b.em(V_, lambda e: e.tensor_tensor(cumpr[0:64, :], cum[0:64, :], dtraw[0:64, :], ALU.subtract))
    b.em(V_, lambda e: e.memset(ocp[0:64, :], 1.0))
    b.em(V_, lambda e: e.memset(cum[64:128, :], 0.0))
    b.em(V_, lambda e: e.memset(cumpr[64:128, :], 0.0))
    b.em(V_, lambda e: e.memset(wexs[64:128, :], 0.0))
    b.em(V_, lambda e: e.memset(states[:, :], 0.0))
    b.em(V_, lambda e: e.memset(bd_r[0][32:64, :], 0.0))
    b.em(V_, lambda e: e.memset(bd_r[1][0:32, :], 0.0))
    s_bdl = b.sem("bdl")
    b.waitm(SY, V_)
    for hf in range(2):
        b.dma(SY, bd_r[hf][64:128, :], bdneg_h[hf].ap(), s_bdl)
    b.wait(G_, s_bdl)
    b.wait(T_, s_bdl)
    b.em(V_, lambda e: e.tensor_copy(cumlo[0:64, :], cum[0:64, :]), inc=True)
    s_Cd = b.sem("Cd")
    b.waitm(SY, V_)
    b.dma(SY, ocp[64:128, :], cumpr[0:64, :], s_Cd)

    s_xc = b.sem("xc")
    s_gf = b.sem("gf")
    s_rs = b.sem("rs")
    s_ysp = [b.sem(f"ysp{i}") for i in range(4)]
    b.wait(G_, s_Cd)
    b.wait(T_, s_Cd)
    b.waitm(G_, V_)

    gfm_free_v = 0
    gfm_free_s = 0
    xbcc_free = 0
    yn_cnt = 0

    for ci in range(n_tiles):
        own = ci >= halo_tiles
        oci = ci - halo_tiles
        tsl = slice(ci * Q, (ci + 1) * Q)
        b.waitm(SY, T_, xbcc_free)
        if ci == 0:
            b.wait(SY, s_xsp[0]); b.wait(SY, s_xsp[1])
        for a in range(32):
            b.dma(SY, xbcc[:, a * 128:(a + 1) * 128], xbc_s.ap()[a * 128:(a + 1) * 128, tsl], s_xc)
        b.dma(SY, bcs[0:16, 0:128], xbc_s.ap()[4096:4112, tsl], s_xc)
        b.dma(SY, bcs[0:16, 128:256], xbc_s.ap()[4112:4128, tsl], s_xc)
        if own:
            b.waitm(SY, V_, gfm_free_v)
            b.waitm(SY, S_, gfm_free_s)
            if oci == 0:
                b.wait(SY, s_gsp[0]); b.wait(SY, s_gsp[1])
            for a in range(32):
                b.dma(SY, gfm[:, a * 128:(a + 1) * 128], gsilu_s.ap()[a * 128:(a + 1) * 128, tsl], s_gf)

        b.waitm(G_, T_)
        for hf in range(2):
            h0 = hf * 32
            srcap = cumlo[h0:h0 + 32, tsl].unsqueeze(1).to_broadcast([32, 32, 128])
            dstap = bd_r[hf][h0:h0 + 32, :].rearrange("p (h t) -> p h t", h=32)
            b.em(G_, lambda e, srcap=srcap, dstap=dstap: e.affine_select(
                dstap, srcap, pattern=[[1, 32], [0, 128]], compare_op=ALU.is_equal,
                fill=0.0, base=0, channel_multiplier=-1), inc=(hf == 1))
        bd_ready = b.ms[G_].n

        b.wait(T_, b.ms[bank_eng[2]], bank_ev[2])
        b.em(T_, lambda e, tsl=tsl: e.transpose(pb[2][:, 0:128], cum[:, tsl], ident[:, :]), inc=True)
        b.waitm(V_, T_)
        b.em(V_, lambda e: e.tensor_copy(cumT_sb[:, :], pb[2][:, 0:64]), inc=True)
        bank_ev[2] = b.ms[V_].n; bank_eng[2] = V_
        b.waitm(S_, V_)
        b.em(S_, lambda e: e.activation(ecumT[:, :], cumT_sb[:, :], AF.Exp, scale=1.0), inc=True)
        b.em(V_, lambda e, ci=ci: e.tensor_scalar(
            wexs[0:64, :], cumpr[0:64, ci * Q:(ci + 1) * Q],
            cum[0:64, ci * Q + Q - 1:ci * Q + Q], -1.0, ALU.subtract, ALU.mult), inc=True)
        b.waitm(S_, V_)
        b.em(S_, lambda e: e.activation(wexs[0:64, :], wexs[0:64, :], AF.Exp, scale=1.0), inc=True)
        b.waitm(T_, S_)
        b.wait(T_, b.ms[bank_eng[3]], bank_ev[3])
        b.em(T_, lambda e: e.transpose(pb[3][:, 0:128], wexs[:, :], ident[:, :]), inc=True)
        b.waitm(V_, T_)
        b.em(V_, lambda e: e.tensor_copy(wexT[:, :], pb[3][:, 0:64]), inc=True)
        bank_ev[3] = b.ms[V_].n; bank_eng[3] = V_
        b.waitm(SY, V_, bank_ev[2])
        b.dma(SY, rowst[0:1, 0:64], cumT_sb[127:128, :], s_rs)
        b.wait(S_, s_rs)
        b.em(S_, lambda e: e.activation(rowst[0:1, 0:64], rowst[0:1, 0:64], AF.Exp, scale=1.0), inc=True)
        b.waitm(T_, S_)
        b.wait(T_, b.ms[bank_eng[3]], bank_ev[3])
        b.em(T_, lambda e: e.matmul(pb[3][0:16, 0:64], ones1[0:1, 0:16], rowst[0:1, 0:64],
                                    start=True, stop=True), inc=True)
        b.waitm(V_, T_)
        b.em(V_, lambda e: e.tensor_copy(dax16[:, :], pb[3][0:16, 0:64]), inc=True)
        bank_ev[3] = b.ms[V_].n; bank_eng[3] = V_

        b.wait(T_, s_xc)
        for g in range(8):
            bk = 2 + g % 2
            bank = pb[bk]
            b.wait(T_, b.ms[bank_eng[bk]], bank_ev[bk])
            for u in range(4):
                a = g * 4 + u
                b.em(T_, lambda e, a=a, bank=bank, u=u: e.transpose(
                    bank[:, u * 128:(u + 1) * 128], xbcc[:, a * 128:(a + 1) * 128], ident[:, :]),
                    inc=(u == 3))
            b.waitm(V_, T_)
            b.em(V_, lambda e, g=g, bank=bank: e.tensor_copy(
                xtm[:, g * 512:(g + 1) * 512], bank[:, :]), inc=True)
            bank_ev[bk] = b.ms[V_].n; bank_eng[bk] = V_
        b.wait(T_, b.ms[bank_eng[2]], bank_ev[2])
        b.em(T_, lambda e: e.transpose(pb[2][:, 0:16], bcs[0:16, 0:128], ident[0:16, 0:16]), inc=True)
        xbcc_free = b.ms[T_].n
        b.waitm(V_, T_)
        b.em(V_, lambda e: e.tensor_copy(bcs[:, 256:272], pb[2][:, 0:16]), inc=True)
        bank_ev[2] = b.ms[V_].n; bank_eng[2] = V_

        b.wait(T_, b.ms[bank_eng[4]], bank_ev[4])
        b.em(T_, lambda e: e.matmul(pb[4][:, 0:128], bcs[0:16, 0:128], bcs[0:16, 128:256],
                                    start=True, stop=True), inc=True)
        b.waitm(V_, T_)
        b.em(V_, lambda e: e.tensor_tensor(wexs[:, :], pb[4][:, 0:128], triu[:, :], ALU.mult), inc=True)
        bank_ev[4] = b.ms[V_].n; bank_eng[4] = V_
        mcb_v = b.ms[V_].n

        for hf in range(2):
            h0 = hf * 32
            b.wait(T_, b.ms[G_], bd_ready)
            for jj in range(8):
                bk = jj % 2
                bank = pb[bk]
                b.wait(T_, b.ms[bank_eng[bk]], bank_ev[bk])
                b.em(T_, lambda e, hf=hf, jj=jj, bank=bank, tsl=tsl: e.matmul(
                    bank[:, 0:512], ocp[:, tsl], bd_r[hf][:, jj * 512:(jj + 1) * 512],
                    start=True, stop=True), inc=True)
                b.waitm(V_, T_)
                b.em(V_, lambda e, jj=jj, bank=bank: e.tensor_scalar(
                    bigD[:, jj * 512:(jj + 1) * 512], bank[:, 0:512], CLAMP, None, ALU.min), inc=True)
                bank_ev[bk] = b.ms[V_].n; bank_eng[bk] = V_
            b.waitm(S_, V_)
            b.em(S_, lambda e: e.activation(bigD[:, :], bigD[:, :], AF.Exp, scale=1.0), inc=True)
            b.waitm(V_, S_)
            b.em(V_, lambda e: e.tensor_tensor(
                bigD.rearrange("p (h t) -> p h t", h=32), bigD.rearrange("p (h t) -> p h t", h=32),
                wexs.unsqueeze(1).to_broadcast([128, 32, 128]), ALU.mult), inc=True)
            s_ready = b.ms[V_].n
            b.waitm(T_, V_, s_ready)
            for hh in range(32):
                h = h0 + hh
                if own:
                    bi = 4 + h % 2
                    bs = 6 + h % 2
                    b.wait(T_, b.ms[bank_eng[bi]], bank_ev[bi])
                    b.em(T_, lambda e, hh=hh, h=h, bi=bi: e.matmul(
                        pb[bi][:, 0:64], bigD[:, hh * 128:(hh + 1) * 128], xtm[:, h * 64:(h + 1) * 64],
                        start=True, stop=True), inc=True)
                    b.wait(T_, b.ms[bank_eng[bs]], bank_ev[bs])
                    b.em(T_, lambda e, h=h, bs=bs: e.matmul(
                        pb[bs][:, 0:64], bcs[0:16, 128:256], states[0:16, h * 64:(h + 1) * 64],
                        start=True, stop=True), inc=True)
                    ymm = b.ms[T_].n
                    b.waitm(V_, T_, ymm)
                    b.em(V_, lambda e, h=h, bi=bi: e.scalar_tensor_tensor(
                        ytm[:, h * 64:(h + 1) * 64], xtm[:, h * 64:(h + 1) * 64], dsk[:, h:h + 1],
                        pb[bi][:, 0:64], ALU.mult, ALU.add))
                    b.em(V_, lambda e, h=h, bs=bs: e.scalar_tensor_tensor(
                        ytm[:, h * 64:(h + 1) * 64], pb[bs][:, 0:64], ecumT[:, h:h + 1],
                        ytm[:, h * 64:(h + 1) * 64], ALU.mult, ALU.add), inc=True)
                    bank_ev[bi] = b.ms[V_].n; bank_eng[bi] = V_
                    bank_ev[bs] = b.ms[V_].n; bank_eng[bs] = V_
                b.em(V_, lambda e, h=h: e.tensor_scalar(bw16[:, :], bcs[:, 256:272],
                                                        wexT[:, h:h + 1], None, ALU.mult))
                b.em(V_, lambda e, h=h: e.tensor_scalar(dAI[:, :], eye16[:, :],
                                                        dax16[:, h:h + 1], None, ALU.mult), inc=True)
                bwv = b.ms[V_].n
                bu = 2 + h % 2
                b.waitm(T_, V_, bwv)
                b.wait(T_, b.ms[bank_eng[bu]], bank_ev[bu])
                b.em(T_, lambda e, h=h, bu=bu: e.matmul(
                    pb[bu][0:16, 0:64], bw16[:, :], xtm[:, h * 64:(h + 1) * 64],
                    start=True, stop=False))
                b.em(T_, lambda e, h=h, bu=bu: e.matmul(
                    pb[bu][0:16, 0:64], dAI[:, :], states[0:16, h * 64:(h + 1) * 64],
                    start=False, stop=True), inc=True)
                b.waitm(V_, T_)
                b.em(V_, lambda e, h=h, bu=bu: e.tensor_copy(
                    states[0:16, h * 64:(h + 1) * 64], pb[bu][0:16, 0:64]), inc=True)
                bank_ev[bu] = b.ms[V_].n; bank_eng[bu] = V_

        if own:
            b.wait(T_, s_gf)
            for g in range(32):
                bk = 2 + g % 2
                bank = pb[bk]
                b.wait(T_, b.ms[bank_eng[bk]], bank_ev[bk])
                b.em(T_, lambda e, g=g, bank=bank: e.transpose(
                    bank[:, 0:128], gfm[:, g * 128:(g + 1) * 128], ident[:, :]), inc=True)
                b.waitm(V_, T_)
                b.em(V_, lambda e, g=g, bank=bank: e.tensor_tensor(
                    ytm[:, g * 128:(g + 1) * 128], bank[:, 0:128],
                    ytm[:, g * 128:(g + 1) * 128], ALU.mult), inc=True)
                bank_ev[bk] = b.ms[V_].n; bank_eng[bk] = V_
            gfm_free_v = b.ms[V_].n
            b.waitm(S_, V_)
            b.em(S_, lambda e: e.activation(gfm[:, :], ytm[:, :], AF.Square, accum_out=stats[:, 1:2]))
            b.em(S_, lambda e: e.activation(stats[:, 5:6], stats[:, 1:2], AF.Ln, bias=cvt[:, 0:1],
                                            scale=1.0 / DIN))
            b.em(S_, lambda e: e.activation(stats[:, 6:7], stats[:, 5:6], AF.Exp, scale=-0.5), inc=True)
            gfm_free_s = b.ms[S_].n
            b.waitm(V_, S_)
            b.em(V_, lambda e: e.scalar_tensor_tensor(ytm[:, :], ytm[:, :], stats[:, 6:7], nw[:, :],
                                                      ALU.mult, ALU.mult), inc=True)
            ynv = b.ms[V_].n
            b.waitm(T_, V_, ynv)
            for g in range(32):
                bk = 2 + g % 2
                bank = pb[bk]
                ysl = yn_cnt % 4
                b.wait(T_, b.ms[bank_eng[bk]], bank_ev[bk])
                b.em(T_, lambda e, g=g, bank=bank: e.transpose(
                    bank[:, 0:128], ytm[:, g * 128:(g + 1) * 128], ident[:, :]), inc=True)
                b.waitm(V_, T_)
                b.wait(V_, s_ysp[ysl])
                b.em(V_, lambda e, ysl=ysl, bank=bank: e.tensor_copy(
                    ynst[:, ysl * 128:(ysl + 1) * 128], bank[:, 0:128]), inc=True)
                bank_ev[bk] = b.ms[V_].n; bank_eng[bk] = V_
                b.waitm(SY, V_)
                b.dma(SY, yn_s.ap()[g * 128:(g + 1) * 128, oci * Q:(oci + 1) * Q],
                      ynst[:, ysl * 128:(ysl + 1) * 128], s_ysp[ysl])
                yn_cnt += 1

    s_wo = b.sem("wo")
    s_yl = [b.sem("yl0"), b.sem("yl1")]
    s_asp = [b.sem("asp0"), b.sem("asp1")]
    yst_ev = [0, 0]
    for i in range(4):
        b.wait(SY, s_ysp[i])
    ecnt = 0
    for oh in range(2):
        if oh == 1:
            b.waitm(SY, T_)
        for k in range(32):
            b.dma(SY, wouth[:, k * 1024:(k + 1) * 1024],
                  w_out.ap()[k * 128:(k + 1) * 128, oh * 1024:(oh + 1) * 1024], s_wo)
        for i in range(n_own):
            sl = i % 2
            b.wait(SY, b.ms[T_], yst_ev[sl])
            for k in range(32):
                b.dma(SY, yst_r[sl][:, k * 128:(k + 1) * 128],
                      yn_s.ap()[k * 128:(k + 1) * 128, i * Q:(i + 1) * Q], s_yl[sl])
            b.wait(T_, s_yl[sl])
            if i == 0:
                b.wait(T_, s_wo)
            for bk in range(2):
                b.wait(T_, b.ms[bank_eng[bk]], bank_ev[bk])
            for k in range(32):
                for bk in range(2):
                    b.em(T_, lambda e, sl=sl, k=k, bk=bk: e.matmul(
                        pb[bk][:, 0:512], yst_r[sl][:, k * 128:(k + 1) * 128],
                        wouth[:, k * 1024 + bk * 512:k * 1024 + bk * 512 + 512],
                        start=(k == 0), stop=(k == 31)), inc=(k == 31 and bk == 1))
            yst_ev[sl] = b.ms[T_].n
            asl = ecnt % 2
            b.waitm(V_, T_)
            b.wait(V_, s_asp[asl])
            for bk in range(2):
                b.em(V_, lambda e, asl=asl, bk=bk: e.tensor_copy(
                    ast_r[asl][:, bk * 512:(bk + 1) * 512], pb[bk][:, 0:512]), inc=(bk == 1))
            for bk in range(2):
                bank_ev[bk] = b.ms[V_].n; bank_eng[bk] = V_
            b.waitm(SY, V_)
            b.dma(SY, attn_s.ap()[i * Q:(i + 1) * Q, oh * 1024:(oh + 1) * 1024],
                  ast_r[asl][:, 0:1024], s_asp[asl])
            ecnt += 1

    s_fa = b.sem("fa")
    s_fx = b.sem("fx")
    s_x2p = b.sem("x2p")

    def ln_tile(src, dst, scr):
        b.em(V_, lambda e: e.reduce_sum(stats[:, 0:1], src, axis=AX.X), inc=True)
        b.waitm(S_, V_)
        b.em(S_, lambda e: e.activation(scr, src, AF.Square, accum_out=stats[:, 1:2]), inc=True)
        b.waitm(V_, S_)
        b.em(V_, lambda e: e.tensor_scalar(stats[:, 2:3], stats[:, 0:1], 1.0 / D, None, ALU.mult))
        b.em(V_, lambda e: e.tensor_tensor(stats[:, 3:4], stats[:, 2:3], stats[:, 2:3], ALU.mult))
        b.em(V_, lambda e: e.scalar_tensor_tensor(stats[:, 4:5], stats[:, 1:2], 1.0 / D,
                                                  stats[:, 3:4], ALU.mult, ALU.subtract), inc=True)
        b.waitm(S_, V_)
        b.em(S_, lambda e: e.activation(stats[:, 5:6], stats[:, 4:5], AF.Ln, bias=cvt[:, 0:1], scale=1.0))
        b.em(S_, lambda e: e.activation(stats[:, 6:7], stats[:, 5:6], AF.Exp, scale=-0.5), inc=True)
        b.waitm(V_, S_)
        b.em(V_, lambda e: e.tensor_scalar(dst, src, stats[:, 2:3], stats[:, 6:7],
                                           ALU.subtract, ALU.mult), inc=True)

    b.wait(SY, s_asp[0]); b.wait(SY, s_asp[1])
    fa_ev = 0
    for i in range(n_own):
        b.waitm(SY, V_, fa_ev)
        b.dma(SY, f_attn, attn_s.ap()[i * Q:(i + 1) * Q, :], s_fa)
        b.dma(SY, f_x, x_in.ap()[(halo_tiles + i) * Q:(halo_tiles + i + 1) * Q, :], s_fx)
        b.wait(V_, s_fa)
        b.wait(V_, s_fx)
        b.wait(S_, s_fa)
        ln_tile(f_attn, f_attn, f_tmp)
        if i >= 1:
            b.wait(V_, s_x2p, 16 * i)
        b.em(V_, lambda e: e.tensor_tensor(f_x2, f_x, f_attn, ALU.add), inc=True)
        fa_ev = b.ms[V_].n
        b.waitm(SY, V_)
        b.dma(SY, x2_s.ap()[i * Q:(i + 1) * Q, :], f_x2, s_x2p)
        ln_tile(f_x2, f_x3, f_tmp)
        b.waitm(T_, V_)
        for g in range(4):
            bk = g % 2
            bank = pb[bk]
            b.wait(T_, b.ms[bank_eng[bk]], bank_ev[bk])
            for u in range(4):
                k = g * 4 + u
                b.em(T_, lambda e, k=k, bank=bank, u=u: e.transpose(
                    bank[:, u * 128:(u + 1) * 128], f_x3[:, k * 128:(k + 1) * 128], ident[:, :]),
                    inc=(u == 3))
            b.waitm(V_, T_)
            b.em(V_, lambda e, i=i, g=g, bank=bank: e.tensor_copy(
                x3fm[:, (i * 16 + g * 4) * 128:(i * 16 + g * 4 + 4) * 128], bank[:, :]), inc=True)
            bank_ev[bk] = b.ms[V_].n; bank_eng[bk] = V_

    s_wg = [b.sem("wg0"), b.sem("wg1")]
    s_hsp = b.sem("hsp")
    b.waitm(T_, V_)
    njo = len(jo)
    wg_ev = [0, 0]
    for c in range(NH):
        for part in range(2):
            slot = part
            col0 = part * INTER + c * 128
            b.wait(SY, b.ms[T_], wg_ev[slot])
            for k in range(16):
                b.dma(SY, wgu_r[slot][:, k * 128:(k + 1) * 128],
                      w_gu_t.ap()[k * 128:(k + 1) * 128, col0:col0 + 128], s_wg[slot])
            b.wait(T_, s_wg[slot])
            for j, (t0, tw) in enumerate(jo):
                bk = part * 4 + j
                bank = pb[bk]
                b.wait(T_, b.ms[bank_eng[bk]], bank_ev[bk])
                ntl = (tw + 127) // 128
                i0 = t0 // 128
                for k in range(16):
                    b.em(T_, lambda e, slot=slot, k=k, bank=bank, i0=i0, ntl=ntl, tw=tw: e.matmul(
                        bank[:, 0:tw], wgu_r[slot][:, k * 128:(k + 1) * 128],
                        x3fm_r[:, i0:i0 + ntl, k:k + 1, :], start=(k == 0), stop=(k == 15)),
                        inc=(k == 15))
            wg_ev[slot] = b.ms[T_].n
            if part == 0:
                b.waitm(S_, T_)
                b.waitm(S_, V_)
                for j, (t0, tw) in enumerate(jo):
                    b.em(S_, lambda e, j=j, t0=t0, tw=tw: e.activation(
                        gs_b[:, t0:t0 + tw], pb[j][:, 0:tw], AF.Silu), inc=(j == njo - 1))
                for j in range(njo):
                    bank_ev[j] = b.ms[S_].n; bank_eng[j] = S_
            else:
                b.waitm(V_, S_)
                b.waitm(V_, T_)
                if c >= 1:
                    b.wait(V_, s_hsp, 16 * c)
                for j, (t0, tw) in enumerate(jo):
                    b.em(V_, lambda e, j=j, t0=t0, tw=tw: e.tensor_tensor(
                        hb[:, t0:t0 + tw], pb[4 + j][:, 0:tw], gs_b[:, t0:t0 + tw], ALU.mult),
                        inc=(j == njo - 1))
                for j in range(njo):
                    bank_ev[4 + j] = b.ms[V_].n; bank_eng[4 + j] = V_
                b.waitm(SY, V_)
                b.dma(SY, h_s.ap()[c * 128:(c + 1) * 128, :], hb[:, 0:Town], s_hsp)

    s_hq = b.sem("hq")
    s_wd = [b.sem("wd0"), b.sem("wd1")]
    s_x2l = b.sem("x2l")
    s_out = b.sem("outd")
    b.wait(SY, s_hsp)
    b.wait(SY, s_x2p)
    n8 = Town // 256
    wd_ev = [0, 0]
    for q in range(n8):
        b.waitm(SY, T_)
        for k in range(64):
            b.dma(SY, hq[:, k * 256:(k + 1) * 256],
                  h_s.ap()[k * 128:(k + 1) * 128, q * 256:(q + 1) * 256], s_hq)
        b.wait(T_, s_hq)
        for k in range(64):
            slot = k % 2
            b.wait(SY, b.ms[T_], wd_ev[slot])
            b.dma(SY, wdb_r[slot][:, :], w_down_t.ap()[k * 128:(k + 1) * 128, :], s_wd[slot])
            b.wait(T_, s_wd[slot])
            if k == 0:
                for bk in range(8):
                    b.wait(T_, b.ms[bank_eng[bk]], bank_ev[bk])
            for tt in range(2):
                for ob in range(4):
                    bk = tt * 4 + ob
                    b.em(T_, lambda e, k=k, tt=tt, ob=ob, bk=bk, slot=slot: e.matmul(
                        pb[bk][:, 0:512], hq[:, k * 256 + tt * 128:k * 256 + tt * 128 + 128],
                        wdb_r[slot][:, ob * 512:(ob + 1) * 512],
                        start=(k == 0), stop=(k == 63)), inc=(tt == 1 and ob == 3))
            wd_ev[slot] = b.ms[T_].n
        dn_mm = b.ms[T_].n
        for tt in range(2):
            gi = q * 2 + tt
            dsl = tt % 2
            b.waitm(V_, T_, dn_mm)
            for ob in range(4):
                b.em(V_, lambda e, dsl=dsl, tt=tt, ob=ob: e.tensor_copy(
                    dtm_r[dsl][:, ob * 512:(ob + 1) * 512], pb[tt * 4 + ob][:, 0:512]),
                    inc=(ob == 3))
            for ob in range(4):
                bank_ev[tt * 4 + ob] = b.ms[V_].n; bank_eng[tt * 4 + ob] = V_
            b.waitm(SY, V_)
            b.dma(SY, h_x2, x2_s.ap()[gi * Q:(gi + 1) * Q, :], s_x2l)
            b.wait(V_, s_x2l)
            b.wait(S_, s_x2l)
            ln_tile(dtm_r[dsl], h_out, h_sq)
            b.em(V_, lambda e: e.tensor_tensor(h_out, h_out, h_x2, ALU.add), inc=True)
            b.waitm(SY, V_)
            b.dma(SY, out_t.ap()[gi * Q:(gi + 1) * Q, :], h_out, s_out)
            b.wait(SY, s_out)

    b.wait(SY, s_out)

    with nc.Block() as block:
        @block.tensor
        def _(e):
            for fn in b.cmds["tensor"]:
                fn(e)

        @block.vector
        def _(e):
            for fn in b.cmds["vector"]:
                fn(e)

        @block.scalar
        def _(e):
            for fn in b.cmds["scalar"]:
                fn(e)

        @block.sync
        def _(e):
            for fn in b.cmds["sync"]:
                fn(e)

        @block.gpsimd
        def _(e):
            for fn in b.cmds["gpsimd"]:
                fn(e)

    nc.finalize()
    return nc


N_CORES = 8
N_TILES = 18
HALO_TILES = 2


def make_inmaps(inputs, n_tiles=N_TILES, halo_tiles=HALO_TILES, n_cores=N_CORES):
    T = n_tiles * Q
    Town = T - halo_tiles * Q
    halo = halo_tiles * Q
    x = np.ascontiguousarray(np.asarray(inputs["L0_attn_hidden_states"], dtype=np.float32)
                             ).reshape(-1, D)
    total = x.shape[0]
    assert total == n_cores * Town, (total, n_cores, Town)
    seq = total // 2
    w_in_a = np.ascontiguousarray(np.asarray(inputs["L0_attn_W_in"], np.float32))
    w_out_a = np.ascontiguousarray(np.asarray(inputs["L0_attn_W_out"], np.float32))
    w_gu_t = np.ascontiguousarray(np.asarray(inputs["L0_ffn_w_gate_up"], np.float32).T)
    w_down_t = np.ascontiguousarray(np.asarray(inputs["L0_ffn_w_down"], np.float32).T)
    cw_raw = np.asarray(inputs["L0_attn_conv_w"], np.float32)
    cb_raw = np.asarray(inputs["L0_attn_conv_b"], np.float32)
    convw = np.zeros((128, 33 * 4), np.float32)
    convb = np.zeros((128, 33), np.float32)
    for a in range(33):
        nch = min(128, NCONV - a * 128)
        convw[0:nch, a * 4:(a + 1) * 4] = cw_raw[a * 128:a * 128 + nch, :]
        convb[0:nch, a] = cb_raw[a * 128:a * 128 + nch]
    alog_a = np.asarray(inputs["L0_attn_A_log"], np.float32).reshape(NH, 1).copy()
    dtbias_a = np.asarray(inputs["L0_attn_dt_bias"], np.float32).reshape(NH, 1).copy()
    dskip_a = np.ascontiguousarray(np.broadcast_to(
        np.asarray(inputs["L0_attn_D_skip"], np.float32)[None, :], (128, NH)))
    normw_a = np.ascontiguousarray(np.broadcast_to(
        np.asarray(inputs["L0_attn_norm_w"], np.float32)[None, :], (128, DIN)))

    in_maps = []
    for core in range(n_cores):
        r0 = core * Town
        start = r0 - halo
        seq_start = (r0 // seq) * seq
        if start < seq_start:
            pad = seq_start - start
            xs = np.concatenate([np.zeros((pad, D), np.float32), x[seq_start:r0 + Town]], axis=0)
            hm = np.zeros((128, max(halo, 1)), np.float32)
            if halo > pad:
                hm[:, pad:] = 1.0
        else:
            xs = np.ascontiguousarray(x[start:r0 + Town])
            hm = np.ones((128, max(halo, 1)), np.float32)
        in_maps.append({
            "x_in": np.ascontiguousarray(xs), "w_in": w_in_a, "w_out": w_out_a,
            "w_gu_t": w_gu_t, "w_down_t": w_down_t, "convw": convw, "convb": convb,
            "alog": alog_a, "dtbias": dtbias_a, "dskip": dskip_a, "normw": normw_a,
            "halomask": np.ascontiguousarray(hm),
        })
    return in_maps


def kernel(**inputs):
    in_maps = make_inmaps(inputs)
    nc = build_program(N_TILES, HALO_TILES)
    res = run_bass_kernel_spmd(nc, in_maps, core_ids=list(range(N_CORES)))
    out = np.concatenate([np.asarray(r["out"]) for r in res.results], axis=0)
    return out.reshape(2, 8192, D).astype(np.float32)


# version 04
# speedup vs baseline: 1.1058x; 1.1058x over previous
import numpy as np
import concourse.bass as bass
import concourse.mybir as mybir
from concourse.bass_utils import run_bass_kernel_spmd

F32 = mybir.dt.float32
BF16 = mybir.dt.bfloat16
AF = mybir.ActivationFunctionType
ALU = mybir.AluOpType
AX = mybir.AxisListType

D = 2048
DIN = 4096
NH = 64
HP = 64
NCONV = 4128
INDIM = 8288
INTER = 8192
EPS = 1e-5
Q = 128
CLAMP = 40.0


class Sem:
    __slots__ = ("h", "n")

    def __init__(self, nc, name):
        self.h = nc.alloc_semaphore(name)
        self.n = 0


class B:
    ENGINES = ("tensor", "vector", "scalar", "sync", "gpsimd")

    def __init__(self, nc):
        self.nc = nc
        self.cmds = {e: [] for e in self.ENGINES}
        self._i = 0
        self.ms = {e: Sem(nc, "ms_" + e) for e in self.ENGINES}

    def sem(self, name=None):
        self._i += 1
        return Sem(self.nc, name or f"s{self._i}")

    def em(self, eng, make, inc=False, sem=None, n=1):
        caps = []
        if inc:
            m = self.ms[eng]
            m.n += 1
            caps.append((m.h, 1))
        if sem is not None:
            sem.n += n
            caps.append((sem.h, n))

        def fn(e, make=make, caps=tuple(caps)):
            ins = make(e)
            for h, k in caps:
                ins = ins.then_inc(h, k)
        self.cmds[eng].append(fn)

    def dma(self, eng, out, in_, sem=None):
        self.em(eng, lambda e, o=out, i=in_: e.dma_start(out=o, in_=i), sem=sem, n=16)

    def wait(self, eng, sem, val=None):
        v = sem.n if val is None else val
        if v <= 0:
            return
        h = sem.h
        self.cmds[eng].append(lambda e, h=h, v=v: e.wait_ge(h, v))

    def waitm(self, eng, src_eng, val=None):
        self.wait(eng, self.ms[src_eng], val)


def build_program(n_tiles, halo_tiles, dbg=False):
    T = n_tiles * Q
    Thalf = T // 2
    nth = n_tiles // 2
    assert n_tiles % 2 == 0
    Town = T - halo_tiles * Q
    n_own = n_tiles - halo_tiles
    assert Town % 256 == 0

    def mov_tiles(total):
        r, t0 = [], 0
        while t0 < total:
            tw = min(512, total - t0)
            r.append((t0, tw))
            t0 += tw
        return r

    jth = mov_tiles(Thalf)
    jo = mov_tiles(Town)

    nc = bass.Bass("TRN2", target_bir_lowering=False, detect_race_conditions=False)
    b = B(nc)
    T_, V_, S_, SY, G_ = "tensor", "vector", "scalar", "sync", "gpsimd"

    x_in = nc.dram_tensor("x_in", [T, D], F32, kind="ExternalInput")
    w_in = nc.dram_tensor("w_in", [D, INDIM], F32, kind="ExternalInput")
    w_out = nc.dram_tensor("w_out", [DIN, D], F32, kind="ExternalInput")
    w_gu_t = nc.dram_tensor("w_gu_t", [D, 2 * INTER], F32, kind="ExternalInput")
    w_down_t = nc.dram_tensor("w_down_t", [INTER, D], F32, kind="ExternalInput")
    convw = nc.dram_tensor("convw", [128, 33 * 4], F32, kind="ExternalInput")
    convb = nc.dram_tensor("convb", [128, 33], F32, kind="ExternalInput")
    alog = nc.dram_tensor("alog", [NH, 1], F32, kind="ExternalInput")
    dtbias = nc.dram_tensor("dtbias", [NH, 1], F32, kind="ExternalInput")
    dskip = nc.dram_tensor("dskip", [128, NH], F32, kind="ExternalInput")
    normw = nc.dram_tensor("normw", [128, DIN], F32, kind="ExternalInput")
    hm_w = max(halo_tiles * Q, 1)
    halomask = nc.dram_tensor("halomask", [128, hm_w], F32, kind="ExternalInput")
    out_t = nc.dram_tensor("out", [Town, D], F32, kind="ExternalOutput")

    kd = "ExternalOutput" if dbg else "Internal"
    gsilu_s = nc.dram_tensor("gsilu_s", [DIN, T], F32, kind=kd)
    xbc_s = nc.dram_tensor("xbc_s", [NCONV, T], F32, kind=kd)
    yn_sh = nc.dram_tensor("yn_sh", [DIN, Town], BF16, kind=kd)
    yn_sl = nc.dram_tensor("yn_sl", [DIN, Town], BF16, kind=kd)
    attn_s = nc.dram_tensor("attn_s", [Town, D], F32, kind=kd)
    x2_s = nc.dram_tensor("x2_s", [Town, D], F32, kind=kd)
    h_sh = nc.dram_tensor("h_sh", [INTER, Town], BF16, kind=kd)
    h_sl = nc.dram_tensor("h_sl", [INTER, Town], BF16, kind=kd)
    wd_sh = nc.dram_tensor("wd_sh", [INTER, D], BF16)
    wd_sl = nc.dram_tensor("wd_sl", [INTER, D], BF16)

    ident_h = nc.inline_tensor(np.eye(128, dtype=np.float32), "c_ident")
    bdneg = np.zeros((2, 64, 4096), np.float32)
    for j in range(32):
        bdneg[0, j, j * 128:(j + 1) * 128] = -1.0
        bdneg[1, 32 + j, j * 128:(j + 1) * 128] = -1.0
    bdneg_h = [nc.inline_tensor(bdneg[0], "c_bdneg0"), nc.inline_tensor(bdneg[1], "c_bdneg1")]
    triu_h = nc.inline_tensor(np.triu(np.ones((128, 128), np.float32)), "c_triu")
    ones1_h = nc.inline_tensor(np.ones((1, 128), np.float32), "c_ones1")
    eye16_h = nc.inline_tensor(np.eye(16, dtype=np.float32), "c_eye16")

    A = nc.alloc_sbuf_tensor
    ident = A("ident", [128, 128], F32)
    triu = A("triu", [128, 128], F32)
    ones1 = A("ones1", [1, 128], F32)
    eye16 = A("eye16", [16, 16], F32)
    cwsb = A("cwsb", [128, 33 * 4], F32)
    cbsb = A("cbsb", [128, 33], F32)
    aneg = A("aneg", [128, 1], F32)
    dtb = A("dtb", [128, 1], F32)
    dsk = A("dsk", [128, NH], F32)
    hmask = A("hmask", [128, hm_w], F32)
    states = A("states", [16, NH * HP], F32)
    stats = A("stats", [128, 8], F32)
    rowst = A("rowst", [1, 64], F32)
    dax16 = A("dax16", [16, NH], F32)
    dAI = A("dAI", [16, 16], F32)
    bw16 = A("bw16", [128, 16], F32)
    sv3 = A("sv3", [128, 33 * 3], F32)
    cvt = A("cvt", [128, 2], F32)

    rem = nc.sbuf_bytes_remaining // 4
    AR = (rem - 512) // 4 * 4
    arena = A("arena", [128, AR], F32)
    ar = arena.ap()

    def rg(off, n):
        assert off + n <= AR, (off, n, AR)
        return ar[:, off:off + n]

    def rgb(off, nb):
        assert nb % 2 == 0 and off + nb // 2 <= AR, (off, nb, AR)
        return ar[:, off:off + nb // 2].bitcast(BF16)

    x1fm_n = nth * 16 * 128
    o = x1fm_n
    XT = o; o += 2 * D
    X1T = o; o += D
    SQB = o; o += D
    WBUF = o; o += 2 * 2048
    WHI = o; o += 2 * 1024
    WLO = o; o += 2 * 1024
    GBUF = o; o += 2 * 512
    XPRE = o; o += Thalf + 4
    XBCB = o; o += 2 * Thalf
    DTRAW = o; o += T
    WDPP = o; o += 4096
    AB_END = o
    assert AB_END <= AR, (AB_END, AR)

    x1hi = rgb(0, x1fm_n)
    x1lo = rgb(x1fm_n // 2, x1fm_n)
    x1hi_r = x1hi.rearrange("p (i k t) -> p i k t", k=16, t=128)
    x1lo_r = x1lo.rearrange("p (i k t) -> p i k t", k=16, t=128)
    xt_r = [rg(XT, D), rg(XT + D, D)]
    x1t = rg(X1T, D)
    sqb = rg(SQB, D)
    wbuf_r = [rg(WBUF, 2048), rg(WBUF + 2048, 2048)]
    whi_r = [rgb(WHI, 2048), rgb(WHI + 1024, 2048)]
    wlo_r = [rgb(WLO, 2048), rgb(WLO + 1024, 2048)]
    gbuf_r = [rg(GBUF, 512), rg(GBUF + 512, 512)]
    xpre = rg(XPRE, Thalf + 4)
    xbc_r = [rg(XBCB, Thalf), rg(XBCB + Thalf, Thalf)]
    dtraw = rg(DTRAW, T)
    wpp_st = rg(WDPP, 2048)
    wpp_hi = rgb(WDPP + 2048, 2048)
    wpp_lo = rgb(WDPP + 3072, 2048)

    o = 0
    BD0 = o; o += 4096
    BD1 = o; o += 4096
    BIGD = o; o += 4096
    XBCC = o; o += 33 * 128
    XTM = o; o += DIN
    YTM = o; o += DIN
    GFM = o; o += DIN
    CST = o; o += 5 * 128
    BCS = o; o += 3 * 128
    YNST = o; o += 4 * 128
    SSD_B = o
    if max(SSD_B, AB_END) + 5 * T + 128 + DIN <= AR:
        CUMO = max(SSD_B, AB_END)
    else:
        CUMO = SSD_B
        assert CUMO + 3 * T <= DTRAW, (CUMO + 3 * T, DTRAW)
    CUMPRO = CUMO + T
    DTTO = CUMPRO + T
    OCPO = DTTO + T
    CUMLOO = OCPO + T
    DTAO = CUMLOO + T
    NWO = DTAO + 128
    SSD_END = NWO + DIN
    assert SSD_END <= AR, (SSD_END, AR)

    bd_r = [rg(BD0, 4096), rg(BD1, 4096)]
    bigD = rg(BIGD, 4096)
    xbcc = rg(XBCC, 33 * 128)
    xtm = rg(XTM, DIN)
    ytm = rg(YTM, DIN)
    gfm = rg(GFM, DIN)
    cumT_sb = rg(CST, 64)
    ecumT = rg(CST + 64, 64)
    wexT = rg(CST + 128, 64)
    wexs = rg(CST + 192, 128)
    bcs = rg(BCS, 384)
    ynst = rg(YNST, 512)
    cum = rg(CUMO, T)
    cumpr = rg(CUMPRO, T)
    ocp = rg(OCPO, T)
    cumlo = rg(CUMLOO, T)
    dt_t = rg(DTTO, T)
    dtA = rg(DTAO, 128)
    nw = rg(NWO, DIN)

    WOH = 0
    wouth_hi = rgb(WOH, 32 * 512)
    wouth_lo = rgb(WOH + 8192, 32 * 512)
    WST = WOH + 16384
    wst_r = [rg(WST, 512), rg(WST + 512, 512)]
    YST = WST + 1024
    yst_hi = [rgb(YST + s * 4096, 4096) for s in range(3)]
    yst_lo = [rgb(YST + s * 4096 + 2048, 4096) for s in range(3)]
    AST = YST + 3 * 4096
    ast_r = [rg(AST, 512), rg(AST + 512, 512)]
    assert AST + 1024 <= AR, (AST + 1024, AR)

    x3fm_n = n_own * 16 * 128
    x3hi = rgb(0, x3fm_n)
    x3lo = rgb(x3fm_n // 2, x3fm_n)
    x3hi_r = x3hi.rearrange("p (i k t) -> p i k t", k=16, t=128)
    x3lo_r = x3lo.rearrange("p (i k t) -> p i k t", k=16, t=128)
    fo = x3fm_n
    f_attn = rg(fo, D)
    f_x = rg(fo + D, D)
    f_tmp = rg(fo + 2 * D, D)
    f_x2 = rg(fo + 3 * D, D)
    f_x3 = rg(fo + 4 * D, D)
    assert fo + 5 * D <= AR
    wgu_r = [rg(fo, 2048), rg(fo + 2048, 2048)]
    wguhi_r = [rgb(fo + 4096, 2048), rgb(fo + 5120, 2048)]
    wgulo_r = [rgb(fo + 6144, 2048), rgb(fo + 7168, 2048)]
    gs_b = rg(fo + 8192, Town)
    hb = rg(fo + 8192 + Town, Town)
    hb_hi = rgb(fo + 8192 + 2 * Town, Town)
    hb_lo = rgb(fo + 8192 + 2 * Town + Town // 2, Town)
    assert fo + 8192 + 3 * Town <= AR, (fo + 8192 + 3 * Town, AR)

    hq_hi = rgb(0, 64 * 256)
    hq_lo = rgb(64 * 128, 64 * 256)
    HWD = 64 * 256
    wdhi_r = [rgb(HWD, 2048), rgb(HWD + 1024, 2048)]
    wdlo_r = [rgb(HWD + 2048, 2048), rgb(HWD + 3072, 2048)]
    ho = HWD + 4096
    dtm_r = [rg(ho, D), rg(ho + D, D)]
    h_x2 = rg(ho + 2 * D, D)
    h_out = rg(ho + 3 * D, D)
    h_sq = rg(ho + 4 * D, D)
    WCV = ho + 5 * D
    wcv_r = [rg(WCV, 2048), rg(WCV + 2048, 2048)]
    assert WCV + 4096 <= AR, (WCV + 4096, AR)

    pb = [nc.alloc_psum_tensor(f"pb{i}", [128, 512], F32) for i in range(8)]

    s_ld = b.sem("ld")
    b.dma(SY, ident[:, :], ident_h.ap(), s_ld)
    b.dma(SY, triu[:, :], triu_h.ap(), s_ld)
    b.dma(SY, ones1[:, :], ones1_h.ap(), s_ld)
    b.dma(SY, eye16[:, :], eye16_h.ap(), s_ld)
    b.dma(SY, cwsb[:, :], convw.ap()[:, :], s_ld)
    b.dma(SY, cbsb[:, :], convb.ap()[:, :], s_ld)
    b.dma(SY, aneg[0:64, 0:1], alog.ap(), s_ld)
    b.dma(SY, dtb[0:64, 0:1], dtbias.ap(), s_ld)
    b.dma(SY, dsk[:, :], dskip.ap(), s_ld)
    b.dma(SY, hmask[:, :], halomask.ap(), s_ld)

    b.wait(S_, s_ld)
    b.em(S_, lambda e: e.activation(aneg[0:64, 0:1], aneg[0:64, 0:1], AF.Exp, scale=1.0))
    b.em(S_, lambda e: e.mul(aneg[0:64, 0:1], aneg[0:64, 0:1], -1.0), inc=True)
    b.wait(V_, s_ld)
    b.em(V_, lambda e: e.memset(cvt[:, 0:1], float(EPS)))
    b.em(V_, lambda e: e.memset(cvt[:, 1:2], 1.0))
    b.em(V_, lambda e: e.memset(xpre[:, 0:4], 0.0))
    b.em(V_, lambda e: e.memset(wbuf_r[0][:, :], 0.0))
    b.em(V_, lambda e: e.memset(wbuf_r[1][:, :], 0.0))
    b.em(V_, lambda e: e.memset(stats[:, :], 0.0), inc=True)
    a_init_v = b.ms[V_].n
    a_init_s = b.ms[S_].n

    s_ldx = [b.sem("ldx0"), b.sem("ldx1")]
    s_ldw = [b.sem("ldw0"), b.sem("ldw1")]
    s_gsp = [b.sem("gsp0"), b.sem("gsp1")]
    s_xsp = [b.sem("xsp0"), b.sem("xsp1")]
    NCC = (INDIM + 127) // 128

    bank_ev = [0] * 8
    bank_eng = [V_] * 8
    b.wait(T_, s_ld)
    b.waitm(T_, V_, a_init_v)

    xt_ev = [0, 0]
    gate_cnt = 0
    wconv_v = [0, 0]
    s_wpl = b.sem("wpl")
    s_wpc = b.sem("wpc")
    wpp_k = [0]

    def wd_prepass_step():
        k = wpp_k[0]
        if k >= 64:
            return
        wpp_k[0] += 1
        if k > 0:
            b.wait(G_, s_wpc)
        b.dma(G_, wpp_st, w_down_t.ap()[k * 128:(k + 1) * 128, :], s_wpl)
        b.wait(S_, s_wpl)
        b.em(S_, lambda e: e.copy(wpp_hi[:, :], wpp_st), inc=True)
        b.waitm(V_, S_)
        b.em(V_, lambda e: e.tensor_tensor(wpp_lo[:, :], wpp_st, wpp_hi[:, :], ALU.subtract),
             inc=True)
        b.waitm(S_, V_)
        b.em(S_, lambda e, k=k: e.dma_start(out=wd_sh.ap()[k * 128:(k + 1) * 128, :],
                                            in_=wpp_hi[:, :]), sem=s_wpc, n=16)
        b.em(S_, lambda e, k=k: e.dma_start(out=wd_sl.ap()[k * 128:(k + 1) * 128, :],
                                            in_=wpp_lo[:, :]), sem=s_wpc, n=16)
    for half in range(2):
        tb = half * nth
        for li in range(nth):
            gi = tb + li
            sl = li % 2
            b.wait(SY, b.ms[V_], xt_ev[sl])
            b.dma(SY, xt_r[sl], x_in.ap()[gi * Q:(gi + 1) * Q, :], s_ldx[sl])
            b.wait(V_, s_ldx[sl])
            b.em(V_, lambda e, sl=sl: e.reduce_sum(stats[:, 0:1], xt_r[sl], axis=AX.X), inc=True)
            b.wait(S_, s_ldx[sl])
            b.waitm(S_, V_)
            b.em(S_, lambda e, sl=sl: e.activation(sqb, xt_r[sl], AF.Square, accum_out=stats[:, 1:2]),
                 inc=True)
            b.waitm(V_, S_)
            b.em(V_, lambda e: e.tensor_scalar(stats[:, 2:3], stats[:, 0:1], 1.0 / D, None, ALU.mult))
            b.em(V_, lambda e: e.tensor_tensor(stats[:, 3:4], stats[:, 2:3], stats[:, 2:3], ALU.mult))
            b.em(V_, lambda e: e.scalar_tensor_tensor(stats[:, 4:5], stats[:, 1:2], 1.0 / D,
                                                      stats[:, 3:4], ALU.mult, ALU.subtract), inc=True)
            b.waitm(S_, V_)
            b.em(S_, lambda e: e.activation(stats[:, 5:6], stats[:, 4:5], AF.Ln, bias=cvt[:, 0:1], scale=1.0))
            b.em(S_, lambda e: e.activation(stats[:, 6:7], stats[:, 5:6], AF.Exp, scale=-0.5), inc=True)
            b.waitm(V_, S_)
            b.em(V_, lambda e, sl=sl: e.tensor_scalar(x1t, xt_r[sl], stats[:, 2:3], stats[:, 6:7],
                                                      ALU.subtract, ALU.mult), inc=True)
            xt_ev[sl] = b.ms[V_].n
            for _ in range(4):
                wd_prepass_step()
            b.waitm(T_, V_)
            for g in range(4):
                bank = pb[g]
                b.wait(T_, b.ms[bank_eng[g]], bank_ev[g])
                for u in range(4):
                    k = g * 4 + u
                    b.em(T_, lambda e, k=k, bank=bank, u=u: e.transpose(
                        bank[:, u * 128:(u + 1) * 128], x1t[:, k * 128:(k + 1) * 128], ident[:, :]),
                        inc=(u == 3))
                b.waitm(S_, T_)
                b.em(S_, lambda e, li=li, g=g, bank=bank: e.copy(
                    x1hi[:, (li * 16 + g * 4) * 128:(li * 16 + g * 4 + 4) * 128], bank[:, :]), inc=True)
                b.waitm(V_, S_)
                b.em(V_, lambda e, li=li, g=g, bank=bank: e.tensor_tensor(
                    x1lo[:, (li * 16 + g * 4) * 128:(li * 16 + g * 4 + 4) * 128], bank[:, :],
                    x1hi[:, (li * 16 + g * 4) * 128:(li * 16 + g * 4 + 4) * 128], ALU.subtract), inc=True)
                bank_ev[g] = b.ms[V_].n
                bank_eng[g] = V_

        while wpp_k[0] < 64 and half == 1:
            wd_prepass_step()
        b.waitm(T_, V_)
        for c in range(NCC):
            cw = min(128, INDIM - c * 128)
            sl = c % 2
            if c == 0:
                if half == 1:
                    b.waitm(SY, T_)
                for k in range(16):
                    b.dma(SY, wbuf_r[0][:, k * 128:k * 128 + cw],
                          w_in.ap()[k * 128:(k + 1) * 128, 0:cw], s_ldw[0])
                b.wait(S_, s_ldw[0])
                b.em(S_, lambda e, cw=cw: e.copy(whi_r[0][:, 0:16 * 128], wbuf_r[0][:, :]), inc=True)
                b.waitm(V_, S_)
                b.em(V_, lambda e, cw=cw: e.tensor_tensor(
                    wlo_r[0][:, 0:16 * 128], wbuf_r[0][:, :], whi_r[0][:, 0:16 * 128],
                    ALU.subtract), inc=True)
                wconv_v[0] = b.ms[V_].n
            if c + 1 < NCC:
                cw2 = min(128, INDIM - (c + 1) * 128)
                if c >= 1:
                    b.waitm(SY, T_)
                sl2 = (c + 1) % 2
                for k in range(16):
                    b.dma(SY, wbuf_r[sl2][:, k * 128:k * 128 + cw2],
                          w_in.ap()[k * 128:(k + 1) * 128, (c + 1) * 128:(c + 1) * 128 + cw2],
                          s_ldw[sl2])
                b.wait(S_, s_ldw[sl2])
                if c >= 1:
                    b.waitm(S_, T_)
                b.em(S_, lambda e, sl2=sl2: e.copy(whi_r[sl2][:, 0:16 * 128], wbuf_r[sl2][:, :]),
                     inc=True)
                b.waitm(V_, S_)
                b.em(V_, lambda e, sl2=sl2: e.tensor_tensor(
                    wlo_r[sl2][:, 0:16 * 128], wbuf_r[sl2][:, :], whi_r[sl2][:, 0:16 * 128],
                    ALU.subtract), inc=True)
                wconv_v[sl2] = b.ms[V_].n
            b.waitm(T_, V_, wconv_v[sl])
            for j, (t0, tw) in enumerate(jth):
                bank = pb[j]
                b.wait(T_, b.ms[bank_eng[j]], bank_ev[j])
                ntl = (tw + 127) // 128
                i0 = t0 // 128
                if c < 64:
                    for k in range(16):
                        for wop, xop, st, sp in (
                            (whi_r[sl], x1hi_r, k == 0, False),
                            (whi_r[sl], x1lo_r, False, False),
                            (wlo_r[sl], x1hi_r, False, k == 15),
                        ):
                            b.em(T_, lambda e, k=k, cw=cw, bank=bank, i0=i0, ntl=ntl, tw=tw,
                                 wop=wop, xop=xop, st=st, sp=sp:
                                 e.matmul(bank[0:cw, 0:tw], wop[:, k * 128:k * 128 + cw],
                                          xop[:, i0:i0 + ntl, k:k + 1, :],
                                          start=st, stop=sp),
                                 inc=sp)
                else:
                    bank2 = pb[3 + j]
                    b.wait(T_, b.ms[bank_eng[3 + j]], bank_ev[3 + j])
                    for k in range(16):
                        for wop, xop, st, sp in (
                            (whi_r[sl], x1hi_r, k == 0, False),
                            (whi_r[sl], x1lo_r, False, False),
                            (wlo_r[sl], x1hi_r, False, k == 15),
                        ):
                            b.em(T_, lambda e, k=k, bank=bank, i0=i0, ntl=ntl, tw=tw,
                                 wop=wop, xop=xop, st=st, sp=sp:
                                 e.matmul(bank[0:32, 0:tw], wop[:, k * 128:k * 128 + 32],
                                          xop[:, i0:i0 + ntl, k:k + 1, :],
                                          start=st, stop=sp))
                            b.em(T_, lambda e, k=k, bank2=bank2, i0=i0, ntl=ntl, tw=tw,
                                 wop=wop, xop=xop, st=st, sp=sp:
                                 e.matmul(bank2[0:64, 0:tw], wop[:, k * 128 + 32:k * 128 + 96],
                                          xop[:, i0:i0 + ntl, k:k + 1, :],
                                          start=st, stop=sp),
                                 inc=sp)
                mmv = b.ms[T_].n
                if c < 32:
                    gsl = gate_cnt % 2
                    b.waitm(S_, T_, mmv)
                    b.wait(S_, s_gsp[gsl])
                    b.em(S_, lambda e, bank=bank, tw=tw, gsl=gsl:
                         e.activation(gbuf_r[gsl][:, 0:tw], bank[0:128, 0:tw], AF.Silu), inc=True)
                    bank_ev[j] = b.ms[S_].n
                    bank_eng[j] = S_
                    b.waitm(SY, S_)
                    b.dma(SY, gsilu_s.ap()[c * 128:(c + 1) * 128,
                                           half * Thalf + t0:half * Thalf + t0 + tw],
                          gbuf_r[gsl][:, 0:tw], s_gsp[gsl])
                    gate_cnt += 1
                elif c < 64:
                    b.waitm(V_, T_, mmv)
                    b.em(V_, lambda e, bank=bank, t0=t0, tw=tw: e.tensor_copy(
                        xpre[:, 4 + t0:4 + t0 + tw], bank[0:128, 0:tw]), inc=True)
                    bank_ev[j] = b.ms[V_].n
                    bank_eng[j] = V_
                else:
                    b.waitm(V_, T_, mmv)
                    b.em(V_, lambda e, bank=bank, t0=t0, tw=tw: e.tensor_copy(
                        xpre[0:32, 4 + t0:4 + t0 + tw], bank[0:32, 0:tw]))
                    b.em(V_, lambda e, j=j, t0=t0, tw=tw, half=half: e.tensor_copy(
                        dtraw[0:64, half * Thalf + t0:half * Thalf + t0 + tw], pb[3 + j][0:64, 0:tw]),
                        inc=True)
                    bank_ev[j] = b.ms[V_].n
                    bank_eng[j] = V_
                    bank_ev[3 + j] = b.ms[V_].n
                    bank_eng[3 + j] = V_
            if c >= 32:
                cc = c - 32
                nch = 128 if c < 64 else 32
                xb = xbc_r[cc % 2]
                b.wait(V_, s_xsp[cc % 2])
                if half == 1:
                    b.em(V_, lambda e, cc=cc, nch=nch: e.tensor_copy(
                        xpre[0:nch, 1:4], sv3[0:nch, cc * 3:cc * 3 + 3]))
                if dbg == 2:
                    b.em(V_, lambda e, nch=nch, xb=xb: e.tensor_copy(
                        xb[0:nch, 0:Thalf], xpre[0:nch, 4:4 + Thalf]))
                b.em(V_, lambda e, cc=cc, nch=nch, xb=xb: (None if dbg == 2 else e.tensor_scalar(
                    xb[0:nch, 0:Thalf], xpre[0:nch, 1:1 + Thalf], cwsb[0:nch, cc * 4:cc * 4 + 1],
                    cbsb[0:nch, cc:cc + 1], ALU.mult, ALU.add)) or e.nop())
                for k in range(1, 4):
                    if dbg == 2:
                        break
                    b.em(V_, lambda e, cc=cc, nch=nch, xb=xb, k=k: e.scalar_tensor_tensor(
                        xb[0:nch, 0:Thalf], xpre[0:nch, 1 + k:1 + k + Thalf],
                        cwsb[0:nch, cc * 4 + k:cc * 4 + k + 1], xb[0:nch, 0:Thalf],
                        ALU.mult, ALU.add))
                if half == 0:
                    b.em(V_, lambda e, cc=cc, nch=nch: e.tensor_copy(
                        sv3[0:nch, cc * 3:cc * 3 + 3], xpre[0:nch, 1 + Thalf:4 + Thalf]))
                    if halo_tiles > 0:
                        b.em(V_, lambda e, nch=nch, xb=xb: e.tensor_tensor(
                            xb[0:nch, 0:halo_tiles * Q], xb[0:nch, 0:halo_tiles * Q],
                            hmask[0:nch, :], ALU.mult))
                b.em(V_, lambda e, nch=nch, xb=xb: e.tensor_copy(xb[0:nch, 0:1], xb[0:nch, 0:1]),
                     inc=True)
                b.waitm(S_, V_)
                b.em(S_, lambda e, nch=nch, xb=xb: (e.nop() if dbg in (2, 3) else e.activation(
                    xb[0:nch, 0:Thalf], xb[0:nch, 0:Thalf], AF.Silu)), inc=True)
                b.waitm(SY, S_)
                b.dma(SY, xbc_s.ap()[cc * 128:cc * 128 + nch, half * Thalf:(half + 1) * Thalf],
                      xb[0:nch, 0:Thalf], s_xsp[cc % 2])

    b.waitm(S_, V_)
    b.em(S_, lambda e: e.activation(dt_t[0:64, :], dtraw[0:64, :], AF.Exp, bias=dtb[0:64, 0:1],
                                    scale=1.0))
    b.em(S_, lambda e: e.activation(dt_t[0:64, :], dt_t[0:64, :], AF.Ln, bias=cvt[0:64, 1:2], scale=1.0))
    b.em(S_, lambda e: e.activation(dtraw[0:64, :], dt_t[0:64, :], AF.Ln, scale=1.0), inc=True)
    b.waitm(V_, S_)
    b.em(V_, lambda e: e.memset(sqb[0:64, 0:Q], 0.0))
    for ci in range(n_tiles):
        b.em(V_, lambda e, ci=ci: e.tensor_scalar(
            dtA[0:64, 0:Q], dt_t[0:64, ci * Q:(ci + 1) * Q], aneg[0:64, 0:1], None, ALU.mult))
        b.em(V_, lambda e, ci=ci: e.tensor_tensor_scan(
            cum[0:64, ci * Q:(ci + 1) * Q], dtA[0:64, 0:Q], sqb[0:64, 0:Q],
            0.0, ALU.add, ALU.add))
    b.em(V_, lambda e: e.tensor_tensor(cumpr[0:64, :], cum[0:64, :], dtraw[0:64, :], ALU.subtract))
    b.em(V_, lambda e: e.memset(ocp[0:64, :], 1.0))
    b.em(V_, lambda e: e.memset(cum[64:128, :], 0.0))
    b.em(V_, lambda e: e.memset(cumpr[64:128, :], 0.0))
    b.em(V_, lambda e: e.memset(wexs[64:128, :], 0.0))
    b.em(V_, lambda e: e.memset(states[:, :], 0.0))
    b.em(V_, lambda e: e.memset(bd_r[0][32:64, :], 0.0))
    b.em(V_, lambda e: e.memset(bd_r[1][0:32, :], 0.0))
    s_bdl = b.sem("bdl")
    b.waitm(SY, V_)
    for hf in range(2):
        b.dma(SY, bd_r[hf][64:128, :], bdneg_h[hf].ap(), s_bdl)
    b.wait(G_, s_bdl)
    b.wait(T_, s_bdl)
    b.em(V_, lambda e: e.tensor_copy(cumlo[0:64, :], cum[0:64, :]), inc=True)
    s_Cd = b.sem("Cd")
    b.waitm(SY, V_)
    b.dma(SY, ocp[64:128, :], cumpr[0:64, :], s_Cd)

    s_xc = b.sem("xc")
    s_gf = b.sem("gf")
    s_rs = b.sem("rs")
    s_ysp = [b.sem(f"ysp{i}") for i in range(4)]
    s_nw = b.sem("nw")
    b.waitm(SY, V_)
    b.dma(SY, nw[:, :], normw.ap(), s_nw)
    b.wait(G_, s_Cd)
    b.wait(T_, s_Cd)
    b.waitm(G_, V_)

    gfm_free_v = 0
    gfm_free_s = 0
    xbcc_free = 0
    yn_cnt = 0

    for ci in range(n_tiles):
        own = ci >= halo_tiles
        oci = ci - halo_tiles
        tsl = slice(ci * Q, (ci + 1) * Q)
        b.waitm(SY, T_, xbcc_free)
        if ci == 0:
            b.wait(SY, s_xsp[0]); b.wait(SY, s_xsp[1])
        for g4 in range(8):
            b.dma(SY, xbcc[:, g4 * 512:(g4 + 1) * 512].rearrange("p (a t) -> p a t", a=4),
                  xbc_s.ap()[0:4096, :].rearrange("(a p) t -> p a t", p=128)[:, g4 * 4:(g4 + 1) * 4, tsl],
                  s_xc)
        b.dma(SY, bcs[0:16, 0:128], xbc_s.ap()[4096:4112, tsl], s_xc)
        b.dma(SY, bcs[0:16, 128:256], xbc_s.ap()[4112:4128, tsl], s_xc)
        if own:
            b.waitm(SY, V_, gfm_free_v)
            b.waitm(SY, S_, gfm_free_s)
            if oci == 0:
                b.wait(SY, s_gsp[0]); b.wait(SY, s_gsp[1])
            for g4 in range(8):
                b.dma(SY, gfm[:, g4 * 512:(g4 + 1) * 512].rearrange("p (a t) -> p a t", a=4),
                      gsilu_s.ap().rearrange("(a p) t -> p a t", p=128)[:, g4 * 4:(g4 + 1) * 4, tsl],
                      s_gf)

        b.waitm(G_, T_)
        for hf in range(2):
            h0 = hf * 32
            srcap = cumlo[h0:h0 + 32, tsl].unsqueeze(1).to_broadcast([32, 32, 128])
            dstap = bd_r[hf][h0:h0 + 32, :].rearrange("p (h t) -> p h t", h=32)
            b.em(G_, lambda e, srcap=srcap, dstap=dstap: e.affine_select(
                dstap, srcap, pattern=[[1, 32], [0, 128]], compare_op=ALU.is_equal,
                fill=0.0, base=0, channel_multiplier=-1), inc=(hf == 1))
        bd_ready = b.ms[G_].n

        b.wait(T_, b.ms[bank_eng[2]], bank_ev[2])
        b.em(T_, lambda e, tsl=tsl: e.transpose(pb[2][:, 0:128], cum[:, tsl], ident[:, :]), inc=True)
        b.waitm(V_, T_)
        b.em(V_, lambda e: e.tensor_copy(cumT_sb[:, :], pb[2][:, 0:64]), inc=True)
        bank_ev[2] = b.ms[V_].n; bank_eng[2] = V_
        b.waitm(S_, V_)
        b.em(S_, lambda e: e.activation(ecumT[:, :], cumT_sb[:, :], AF.Exp, scale=1.0), inc=True)
        b.em(V_, lambda e, ci=ci: e.tensor_scalar(
            wexs[0:64, :], cumpr[0:64, ci * Q:(ci + 1) * Q],
            cum[0:64, ci * Q + Q - 1:ci * Q + Q], -1.0, ALU.subtract, ALU.mult), inc=True)
        b.waitm(S_, V_)
        b.em(S_, lambda e: e.activation(wexs[0:64, :], wexs[0:64, :], AF.Exp, scale=1.0), inc=True)
        b.waitm(T_, S_)
        b.wait(T_, b.ms[bank_eng[3]], bank_ev[3])
        b.em(T_, lambda e: e.transpose(pb[3][:, 0:128], wexs[:, :], ident[:, :]), inc=True)
        b.waitm(V_, T_)
        b.em(V_, lambda e: e.tensor_copy(wexT[:, :], pb[3][:, 0:64]), inc=True)
        bank_ev[3] = b.ms[V_].n; bank_eng[3] = V_
        b.em(S_, lambda e: e.dma_start(out=rowst[0:1, 0:64], in_=ecumT[127:128, :]), sem=s_rs, n=16)
        b.wait(T_, s_rs)
        b.wait(T_, b.ms[bank_eng[3]], bank_ev[3])
        b.em(T_, lambda e: e.matmul(pb[3][0:16, 0:64], ones1[0:1, 0:16], rowst[0:1, 0:64],
                                    start=True, stop=True), inc=True)
        b.waitm(V_, T_)
        b.em(V_, lambda e: e.tensor_copy(dax16[:, :], pb[3][0:16, 0:64]), inc=True)
        bank_ev[3] = b.ms[V_].n; bank_eng[3] = V_

        b.wait(T_, s_xc)
        for g in range(8):
            bk = 2 + g % 2
            bank = pb[bk]
            b.wait(T_, b.ms[bank_eng[bk]], bank_ev[bk])
            for u in range(4):
                a = g * 4 + u
                b.em(T_, lambda e, a=a, bank=bank, u=u: e.transpose(
                    bank[:, u * 128:(u + 1) * 128], xbcc[:, a * 128:(a + 1) * 128], ident[:, :]),
                    inc=(u == 3))
            b.waitm(V_, T_)
            b.em(V_, lambda e, g=g, bank=bank: e.tensor_copy(
                xtm[:, g * 512:(g + 1) * 512], bank[:, :]), inc=True)
            bank_ev[bk] = b.ms[V_].n; bank_eng[bk] = V_
        b.wait(T_, b.ms[bank_eng[2]], bank_ev[2])
        b.em(T_, lambda e: e.transpose(pb[2][:, 0:16], bcs[0:16, 0:128], ident[0:16, 0:16]), inc=True)
        xbcc_free = b.ms[T_].n
        b.waitm(V_, T_)
        b.em(V_, lambda e: e.tensor_copy(bcs[:, 256:272], pb[2][:, 0:16]), inc=True)
        bank_ev[2] = b.ms[V_].n; bank_eng[2] = V_

        b.wait(T_, b.ms[bank_eng[4]], bank_ev[4])
        b.em(T_, lambda e: e.matmul(pb[4][:, 0:128], bcs[0:16, 0:128], bcs[0:16, 128:256],
                                    start=True, stop=True), inc=True)
        b.waitm(V_, T_)
        b.em(V_, lambda e: e.tensor_tensor(wexs[:, :], pb[4][:, 0:128], triu[:, :], ALU.mult), inc=True)
        bank_ev[4] = b.ms[V_].n; bank_eng[4] = V_
        mcb_v = b.ms[V_].n

        for hf in range(2):
            h0 = hf * 32
            b.wait(T_, b.ms[G_], bd_ready)
            for jj in range(8):
                bk = jj % 2
                bank = pb[bk]
                b.wait(T_, b.ms[bank_eng[bk]], bank_ev[bk])
                b.em(T_, lambda e, hf=hf, jj=jj, bank=bank, tsl=tsl: e.matmul(
                    bank[:, 0:512], ocp[:, tsl], bd_r[hf][:, jj * 512:(jj + 1) * 512],
                    start=True, stop=True), inc=True)
                b.waitm(V_, T_)
                b.em(V_, lambda e, jj=jj, bank=bank: e.tensor_scalar(
                    bigD[:, jj * 512:(jj + 1) * 512], bank[:, 0:512], CLAMP, None, ALU.min), inc=True)
                bank_ev[bk] = b.ms[V_].n; bank_eng[bk] = V_
            b.waitm(S_, V_)
            b.em(S_, lambda e: e.activation(bigD[:, :], bigD[:, :], AF.Exp, scale=1.0), inc=True)
            b.waitm(V_, S_)
            b.em(V_, lambda e: e.tensor_tensor(
                bigD.rearrange("p (h t) -> p h t", h=32), bigD.rearrange("p (h t) -> p h t", h=32),
                wexs.unsqueeze(1).to_broadcast([128, 32, 128]), ALU.mult), inc=True)
            s_ready = b.ms[V_].n
            b.waitm(T_, V_, s_ready)
            for hh in range(32):
                h = h0 + hh
                if own:
                    bi = 4 + h % 2
                    bs = 6 + h % 2
                    b.wait(T_, b.ms[bank_eng[bi]], bank_ev[bi])
                    b.em(T_, lambda e, hh=hh, h=h, bi=bi: e.matmul(
                        pb[bi][:, 0:64], bigD[:, hh * 128:(hh + 1) * 128], xtm[:, h * 64:(h + 1) * 64],
                        start=True, stop=True), inc=True)
                    b.wait(T_, b.ms[bank_eng[bs]], bank_ev[bs])
                    b.em(T_, lambda e, h=h, bs=bs: e.matmul(
                        pb[bs][:, 0:64], bcs[0:16, 128:256], states[0:16, h * 64:(h + 1) * 64],
                        start=True, stop=True), inc=True)
                    ymm = b.ms[T_].n
                    b.waitm(V_, T_, ymm)
                    b.em(V_, lambda e, h=h, bi=bi: e.scalar_tensor_tensor(
                        ytm[:, h * 64:(h + 1) * 64], xtm[:, h * 64:(h + 1) * 64], dsk[:, h:h + 1],
                        pb[bi][:, 0:64], ALU.mult, ALU.add))
                    b.em(V_, lambda e, h=h, bs=bs: e.scalar_tensor_tensor(
                        ytm[:, h * 64:(h + 1) * 64], pb[bs][:, 0:64], ecumT[:, h:h + 1],
                        ytm[:, h * 64:(h + 1) * 64], ALU.mult, ALU.add), inc=True)
                    bank_ev[bi] = b.ms[V_].n; bank_eng[bi] = V_
                    bank_ev[bs] = b.ms[V_].n; bank_eng[bs] = V_
                b.em(V_, lambda e, h=h: e.tensor_scalar(bw16[:, :], bcs[:, 256:272],
                                                        wexT[:, h:h + 1], None, ALU.mult))
                b.em(V_, lambda e, h=h: e.tensor_scalar(dAI[:, :], eye16[:, :],
                                                        dax16[:, h:h + 1], None, ALU.mult), inc=True)
                bwv = b.ms[V_].n
                bu = 2 + h % 2
                b.waitm(T_, V_, bwv)
                b.wait(T_, b.ms[bank_eng[bu]], bank_ev[bu])
                b.em(T_, lambda e, h=h, bu=bu: e.matmul(
                    pb[bu][0:16, 0:64], bw16[:, :], xtm[:, h * 64:(h + 1) * 64],
                    start=True, stop=False))
                b.em(T_, lambda e, h=h, bu=bu: e.matmul(
                    pb[bu][0:16, 0:64], dAI[:, :], states[0:16, h * 64:(h + 1) * 64],
                    start=False, stop=True), inc=True)
                b.waitm(V_, T_)
                b.em(V_, lambda e, h=h, bu=bu: e.tensor_copy(
                    states[0:16, h * 64:(h + 1) * 64], pb[bu][0:16, 0:64]), inc=True)
                bank_ev[bu] = b.ms[V_].n; bank_eng[bu] = V_

        if own:
            b.wait(T_, s_gf)
            for g in range(32):
                bk = 2 + g % 2
                bank = pb[bk]
                b.wait(T_, b.ms[bank_eng[bk]], bank_ev[bk])
                b.em(T_, lambda e, g=g, bank=bank: e.transpose(
                    bank[:, 0:128], gfm[:, g * 128:(g + 1) * 128], ident[:, :]), inc=True)
                b.waitm(V_, T_)
                b.em(V_, lambda e, g=g, bank=bank: e.tensor_tensor(
                    ytm[:, g * 128:(g + 1) * 128], bank[:, 0:128],
                    ytm[:, g * 128:(g + 1) * 128], ALU.mult), inc=True)
                bank_ev[bk] = b.ms[V_].n; bank_eng[bk] = V_
            gfm_free_v = b.ms[V_].n
            b.waitm(S_, V_)
            b.em(S_, lambda e: e.activation(gfm[:, :], ytm[:, :], AF.Square, accum_out=stats[:, 1:2]))
            b.em(S_, lambda e: e.activation(stats[:, 5:6], stats[:, 1:2], AF.Ln, bias=cvt[:, 0:1],
                                            scale=1.0 / DIN))
            b.em(S_, lambda e: e.activation(stats[:, 6:7], stats[:, 5:6], AF.Exp, scale=-0.5), inc=True)
            gfm_free_s = b.ms[S_].n
            b.waitm(V_, S_)
            b.wait(V_, s_nw)
            b.em(V_, lambda e: e.scalar_tensor_tensor(ytm[:, :], ytm[:, :], stats[:, 6:7], nw[:, :],
                                                      ALU.mult, ALU.mult), inc=True)
            ynv = b.ms[V_].n
            b.waitm(T_, V_, ynv)
            for g in range(32):
                bk = 2 + g % 2
                bank = pb[bk]
                ysl = yn_cnt % 4
                yh = rgb(YNST + ysl * 128, 256)
                yl = rgb(YNST + ysl * 128 + 64, 256)
                b.wait(T_, b.ms[bank_eng[bk]], bank_ev[bk])
                b.em(T_, lambda e, g=g, bank=bank: e.transpose(
                    bank[:, 0:128], ytm[:, g * 128:(g + 1) * 128], ident[:, :]), inc=True)
                b.waitm(S_, T_)
                b.wait(S_, s_ysp[ysl])
                b.em(S_, lambda e, yh=yh, bank=bank: e.copy(yh[:, 0:128], bank[:, 0:128]), inc=True)
                b.waitm(V_, S_)
                b.em(V_, lambda e, yh=yh, yl=yl, bank=bank: e.tensor_tensor(
                    yl[:, 0:128], bank[:, 0:128], yh[:, 0:128], ALU.subtract), inc=True)
                bank_ev[bk] = b.ms[V_].n; bank_eng[bk] = V_
                b.waitm(SY, V_)
                b.dma(SY, yn_sh.ap()[g * 128:(g + 1) * 128, oci * Q:(oci + 1) * Q],
                      yh[:, 0:128], s_ysp[ysl])
                b.dma(SY, yn_sl.ap()[g * 128:(g + 1) * 128, oci * Q:(oci + 1) * Q],
                      yl[:, 0:128], s_ysp[ysl])
                yn_cnt += 1

    s_wo = [b.sem("wo0"), b.sem("wo1")]
    s_yl = [b.sem(f"yl{s}") for s in range(3)]
    s_asp = [b.sem("asp0"), b.sem("asp1")]
    yst_ev = [0, 0, 0]
    for i in range(4):
        b.wait(SY, s_ysp[i])
    ecnt = 0
    for oq in range(4):
        if oq >= 1:
            b.waitm(SY, T_)
        wst_cv = [0, 0]
        for k in range(32):
            wsl = k % 2
            b.wait(SY, b.ms[V_], wst_cv[wsl])
            b.dma(SY, wst_r[wsl],
                  w_out.ap()[k * 128:(k + 1) * 128, oq * 512:(oq + 1) * 512], s_wo[wsl])
            b.wait(S_, s_wo[wsl])
            if oq >= 1:
                b.waitm(S_, T_)
            b.em(S_, lambda e, k=k, wsl=wsl: e.copy(
                wouth_hi[:, k * 512:(k + 1) * 512], wst_r[wsl]), inc=True)
            b.waitm(V_, S_)
            b.em(V_, lambda e, k=k, wsl=wsl: e.tensor_tensor(
                wouth_lo[:, k * 512:(k + 1) * 512], wst_r[wsl],
                wouth_hi[:, k * 512:(k + 1) * 512], ALU.subtract), inc=True)
            wst_cv[wsl] = b.ms[V_].n
        wo_conv = b.ms[V_].n
        for i in range(n_own):
            sl = i % 3
            if oq > 0 or i >= 3:
                b.wait(SY, b.ms[T_], yst_ev[sl])
            for gsl4 in range(8):
                b.dma(SY, yst_hi[sl].rearrange("p (a t) -> p a t", a=32)[:, gsl4 * 4:(gsl4 + 1) * 4],
                      yn_sh.ap().rearrange("(a p) t -> p a t", p=128)[:, gsl4 * 4:(gsl4 + 1) * 4,
                                                                     i * Q:(i + 1) * Q], s_yl[sl])
                b.dma(SY, yst_lo[sl].rearrange("p (a t) -> p a t", a=32)[:, gsl4 * 4:(gsl4 + 1) * 4],
                      yn_sl.ap().rearrange("(a p) t -> p a t", p=128)[:, gsl4 * 4:(gsl4 + 1) * 4,
                                                                     i * Q:(i + 1) * Q], s_yl[sl])
            b.wait(T_, s_yl[sl])
            if i == 0:
                b.waitm(T_, V_, wo_conv)
            bk = ecnt % 2
            b.wait(T_, b.ms[bank_eng[bk]], bank_ev[bk])
            for k in range(32):
                for yop, wop, st, sp in (
                    (yst_hi[sl], wouth_hi, k == 0, False),
                    (yst_hi[sl], wouth_lo, False, False),
                    (yst_lo[sl], wouth_hi, False, k == 31),
                ):
                    b.em(T_, lambda e, k=k, bk=bk, yop=yop, wop=wop, st=st, sp=sp: e.matmul(
                        pb[bk][:, 0:512], yop[:, k * 128:(k + 1) * 128],
                        wop[:, k * 512:(k + 1) * 512],
                        start=st, stop=sp), inc=sp)
            yst_ev[sl] = b.ms[T_].n
            asl = ecnt % 2
            b.waitm(V_, T_)
            b.wait(V_, s_asp[asl])
            b.em(V_, lambda e, asl=asl, bk=bk: e.tensor_copy(
                ast_r[asl][:, 0:512], pb[bk][:, 0:512]), inc=True)
            bank_ev[bk] = b.ms[V_].n; bank_eng[bk] = V_
            b.waitm(SY, V_)
            b.dma(SY, attn_s.ap()[i * Q:(i + 1) * Q, oq * 512:(oq + 1) * 512],
                  ast_r[asl][:, 0:512], s_asp[asl])
            ecnt += 1

    s_fa = b.sem("fa")
    s_fx = b.sem("fx")
    s_x2p = b.sem("x2p")

    def ln_tile(src, dst, scr):
        b.em(V_, lambda e: e.reduce_sum(stats[:, 0:1], src, axis=AX.X), inc=True)
        b.waitm(S_, V_)
        b.em(S_, lambda e: e.activation(scr, src, AF.Square, accum_out=stats[:, 1:2]), inc=True)
        b.waitm(V_, S_)
        b.em(V_, lambda e: e.tensor_scalar(stats[:, 2:3], stats[:, 0:1], 1.0 / D, None, ALU.mult))
        b.em(V_, lambda e: e.tensor_tensor(stats[:, 3:4], stats[:, 2:3], stats[:, 2:3], ALU.mult))
        b.em(V_, lambda e: e.scalar_tensor_tensor(stats[:, 4:5], stats[:, 1:2], 1.0 / D,
                                                  stats[:, 3:4], ALU.mult, ALU.subtract), inc=True)
        b.waitm(S_, V_)
        b.em(S_, lambda e: e.activation(stats[:, 5:6], stats[:, 4:5], AF.Ln, bias=cvt[:, 0:1], scale=1.0))
        b.em(S_, lambda e: e.activation(stats[:, 6:7], stats[:, 5:6], AF.Exp, scale=-0.5), inc=True)
        b.waitm(V_, S_)
        b.em(V_, lambda e: e.tensor_scalar(dst, src, stats[:, 2:3], stats[:, 6:7],
                                           ALU.subtract, ALU.mult), inc=True)

    b.wait(SY, s_asp[0]); b.wait(SY, s_asp[1])
    fa_ev = 0
    for i in range(n_own):
        b.waitm(SY, V_, fa_ev)
        b.dma(SY, f_attn, attn_s.ap()[i * Q:(i + 1) * Q, :], s_fa)
        b.dma(SY, f_x, x_in.ap()[(halo_tiles + i) * Q:(halo_tiles + i + 1) * Q, :], s_fx)
        b.wait(V_, s_fa)
        b.wait(V_, s_fx)
        b.wait(S_, s_fa)
        ln_tile(f_attn, f_attn, f_tmp)
        if i >= 1:
            b.wait(V_, s_x2p, 16 * i)
        b.em(V_, lambda e: e.tensor_tensor(f_x2, f_x, f_attn, ALU.add), inc=True)
        fa_ev = b.ms[V_].n
        b.waitm(SY, V_)
        b.dma(SY, x2_s.ap()[i * Q:(i + 1) * Q, :], f_x2, s_x2p)
        ln_tile(f_x2, f_x3, f_tmp)
        b.waitm(T_, V_)
        for g in range(4):
            bk = g % 2
            bank = pb[bk]
            b.wait(T_, b.ms[bank_eng[bk]], bank_ev[bk])
            for u in range(4):
                k = g * 4 + u
                b.em(T_, lambda e, k=k, bank=bank, u=u: e.transpose(
                    bank[:, u * 128:(u + 1) * 128], f_x3[:, k * 128:(k + 1) * 128], ident[:, :]),
                    inc=(u == 3))
            b.waitm(S_, T_)
            b.em(S_, lambda e, i=i, g=g, bank=bank: e.copy(
                x3hi[:, (i * 16 + g * 4) * 128:(i * 16 + g * 4 + 4) * 128], bank[:, :]), inc=True)
            b.waitm(V_, S_)
            b.em(V_, lambda e, i=i, g=g, bank=bank: e.tensor_tensor(
                x3lo[:, (i * 16 + g * 4) * 128:(i * 16 + g * 4 + 4) * 128], bank[:, :],
                x3hi[:, (i * 16 + g * 4) * 128:(i * 16 + g * 4 + 4) * 128], ALU.subtract), inc=True)
            bank_ev[bk] = b.ms[V_].n; bank_eng[bk] = V_

    s_wg = [b.sem("wg0"), b.sem("wg1")]
    s_hsp = b.sem("hsp")
    b.waitm(T_, V_)
    njo = len(jo)
    wg_ev = [0, 0]
    wg_cv = [0, 0]
    for c in range(NH):
        for part in range(2):
            slot = part
            col0 = part * INTER + c * 128
            b.wait(SY, b.ms[V_], wg_cv[slot])
            for k in range(16):
                b.dma(SY, wgu_r[slot][:, k * 128:(k + 1) * 128],
                      w_gu_t.ap()[k * 128:(k + 1) * 128, col0:col0 + 128], s_wg[slot])
            b.wait(S_, s_wg[slot])
            b.wait(S_, b.ms[T_], wg_ev[slot])
            b.em(S_, lambda e, slot=slot: e.copy(wguhi_r[slot][:, :], wgu_r[slot][:, :]), inc=True)
            b.waitm(V_, S_)
            b.em(V_, lambda e, slot=slot: e.tensor_tensor(
                wgulo_r[slot][:, :], wgu_r[slot][:, :], wguhi_r[slot][:, :], ALU.subtract), inc=True)
            wg_cv[slot] = b.ms[V_].n
            b.waitm(T_, V_, wg_cv[slot])
            for j, (t0, tw) in enumerate(jo):
                bk = part * 4 + j
                bank = pb[bk]
                b.wait(T_, b.ms[bank_eng[bk]], bank_ev[bk])
                ntl = (tw + 127) // 128
                i0 = t0 // 128
                for k in range(16):
                    for wop, xop, st, sp in (
                        (wguhi_r[slot], x3hi_r, k == 0, False),
                        (wguhi_r[slot], x3lo_r, False, False),
                        (wgulo_r[slot], x3hi_r, False, k == 15),
                    ):
                        b.em(T_, lambda e, k=k, bank=bank, i0=i0, ntl=ntl, tw=tw,
                             wop=wop, xop=xop, st=st, sp=sp: e.matmul(
                            bank[:, 0:tw], wop[:, k * 128:(k + 1) * 128],
                            xop[:, i0:i0 + ntl, k:k + 1, :], start=st, stop=sp),
                            inc=sp)
            wg_ev[slot] = b.ms[T_].n
            if part == 0:
                b.waitm(S_, T_)
                b.waitm(S_, V_)
                for j, (t0, tw) in enumerate(jo):
                    b.em(S_, lambda e, j=j, t0=t0, tw=tw: e.activation(
                        gs_b[:, t0:t0 + tw], pb[j][:, 0:tw], AF.Silu), inc=(j == njo - 1))
                for j in range(njo):
                    bank_ev[j] = b.ms[S_].n; bank_eng[j] = S_
            else:
                b.waitm(V_, S_)
                b.waitm(V_, T_)
                if c >= 1:
                    b.wait(V_, s_hsp, 32 * c)
                for j, (t0, tw) in enumerate(jo):
                    b.em(V_, lambda e, j=j, t0=t0, tw=tw: e.tensor_tensor(
                        hb[:, t0:t0 + tw], pb[4 + j][:, 0:tw], gs_b[:, t0:t0 + tw], ALU.mult),
                        inc=(j == njo - 1))
                for j in range(njo):
                    bank_ev[4 + j] = b.ms[V_].n; bank_eng[4 + j] = V_
                b.waitm(S_, V_)
                b.em(S_, lambda e: e.copy(hb_hi[:, :], hb[:, 0:Town]), inc=True)
                b.waitm(V_, S_)
                b.em(V_, lambda e: e.tensor_tensor(hb_lo[:, :], hb[:, 0:Town], hb_hi[:, :],
                                                   ALU.subtract), inc=True)
                b.waitm(SY, V_)
                b.dma(SY, h_sh.ap()[c * 128:(c + 1) * 128, :], hb_hi[:, :], s_hsp)
                b.dma(SY, h_sl.ap()[c * 128:(c + 1) * 128, :], hb_lo[:, :], s_hsp)

    s_hq = b.sem("hq")
    s_wd = [b.sem("wd0"), b.sem("wd1")]
    s_x2l = b.sem("x2l")
    s_out = b.sem("outd")
    b.wait(SY, s_hsp)
    b.wait(SY, s_x2p)
    b.wait(SY, s_wpc)
    n8 = Town // 256
    wd_ev = [0, 0]
    for q in range(n8):
        b.waitm(SY, T_)
        for g4 in range(16):
            b.dma(SY, hq_hi.rearrange("p (a t) -> p a t", a=64)[:, g4 * 4:(g4 + 1) * 4],
                  h_sh.ap().rearrange("(a p) t -> p a t", p=128)[:, g4 * 4:(g4 + 1) * 4,
                                                                q * 256:(q + 1) * 256], s_hq)
            b.dma(SY, hq_lo.rearrange("p (a t) -> p a t", a=64)[:, g4 * 4:(g4 + 1) * 4],
                  h_sl.ap().rearrange("(a p) t -> p a t", p=128)[:, g4 * 4:(g4 + 1) * 4,
                                                                q * 256:(q + 1) * 256], s_hq)
        b.wait(T_, s_hq)
        for k in range(64):
            slot = k % 2
            b.wait(SY, b.ms[T_], wd_ev[slot])
            b.dma(SY, wdhi_r[slot][:, :], wd_sh.ap()[k * 128:(k + 1) * 128, :], s_wd[slot])
            b.dma(SY, wdlo_r[slot][:, :], wd_sl.ap()[k * 128:(k + 1) * 128, :], s_wd[slot])
            b.wait(T_, s_wd[slot])
            if k == 0:
                for bk in range(8):
                    b.wait(T_, b.ms[bank_eng[bk]], bank_ev[bk])
            for tt in range(2):
                for ob in range(4):
                    bk = tt * 4 + ob
                    for pi, (hop, wop, st, sp) in enumerate((
                        (hq_hi, wdhi_r[slot], k == 0, False),
                        (hq_hi, wdlo_r[slot], False, False),
                        (hq_lo, wdhi_r[slot], False, k == 63),
                    )):
                        b.em(T_, lambda e, k=k, tt=tt, ob=ob, bk=bk, hop=hop, wop=wop,
                             st=st, sp=sp: e.matmul(
                            pb[bk][:, 0:512], hop[:, k * 256 + tt * 128:k * 256 + tt * 128 + 128],
                            wop[:, ob * 512:(ob + 1) * 512],
                            start=st, stop=sp), inc=(pi == 2 and tt == 1 and ob == 3))
            wd_ev[slot] = b.ms[T_].n
        dn_mm = b.ms[T_].n
        for tt in range(2):
            gi = q * 2 + tt
            dsl = tt % 2
            b.waitm(V_, T_, dn_mm)
            for ob in range(4):
                b.em(V_, lambda e, dsl=dsl, tt=tt, ob=ob: e.tensor_copy(
                    dtm_r[dsl][:, ob * 512:(ob + 1) * 512], pb[tt * 4 + ob][:, 0:512]),
                    inc=(ob == 3))
            for ob in range(4):
                bank_ev[tt * 4 + ob] = b.ms[V_].n; bank_eng[tt * 4 + ob] = V_
            b.waitm(SY, V_)
            b.dma(SY, h_x2, x2_s.ap()[gi * Q:(gi + 1) * Q, :], s_x2l)
            b.wait(V_, s_x2l)
            b.wait(S_, s_x2l)
            ln_tile(dtm_r[dsl], h_out, h_sq)
            b.em(V_, lambda e: e.tensor_tensor(h_out, h_out, h_x2, ALU.add), inc=True)
            b.waitm(SY, V_)
            b.dma(SY, out_t.ap()[gi * Q:(gi + 1) * Q, :], h_out, s_out)
            b.wait(SY, s_out)

    b.wait(SY, s_out)

    with nc.Block() as block:
        @block.tensor
        def _(e):
            for fn in b.cmds["tensor"]:
                fn(e)

        @block.vector
        def _(e):
            for fn in b.cmds["vector"]:
                fn(e)

        @block.scalar
        def _(e):
            for fn in b.cmds["scalar"]:
                fn(e)

        @block.sync
        def _(e):
            for fn in b.cmds["sync"]:
                fn(e)

        @block.gpsimd
        def _(e):
            for fn in b.cmds["gpsimd"]:
                fn(e)

    nc.finalize()
    return nc


N_CORES = 8
N_TILES = 18
HALO_TILES = 2


def make_inmaps(inputs, n_tiles=N_TILES, halo_tiles=HALO_TILES, n_cores=N_CORES):
    T = n_tiles * Q
    Town = T - halo_tiles * Q
    halo = halo_tiles * Q
    x = np.ascontiguousarray(np.asarray(inputs["L0_attn_hidden_states"], dtype=np.float32)
                             ).reshape(-1, D)
    total = x.shape[0]
    assert total == n_cores * Town, (total, n_cores, Town)
    seq = total // 2
    w_in_a = np.ascontiguousarray(np.asarray(inputs["L0_attn_W_in"], np.float32))
    w_out_a = np.ascontiguousarray(np.asarray(inputs["L0_attn_W_out"], np.float32))
    w_gu_t = np.ascontiguousarray(np.asarray(inputs["L0_ffn_w_gate_up"], np.float32).T)
    w_down_t = np.ascontiguousarray(np.asarray(inputs["L0_ffn_w_down"], np.float32).T)
    cw_raw = np.asarray(inputs["L0_attn_conv_w"], np.float32)
    cb_raw = np.asarray(inputs["L0_attn_conv_b"], np.float32)
    convw = np.zeros((128, 33 * 4), np.float32)
    convb = np.zeros((128, 33), np.float32)
    for a in range(33):
        nch = min(128, NCONV - a * 128)
        convw[0:nch, a * 4:(a + 1) * 4] = cw_raw[a * 128:a * 128 + nch, :]
        convb[0:nch, a] = cb_raw[a * 128:a * 128 + nch]
    alog_a = np.asarray(inputs["L0_attn_A_log"], np.float32).reshape(NH, 1).copy()
    dtbias_a = np.asarray(inputs["L0_attn_dt_bias"], np.float32).reshape(NH, 1).copy()
    dskip_a = np.ascontiguousarray(np.broadcast_to(
        np.asarray(inputs["L0_attn_D_skip"], np.float32)[None, :], (128, NH)))
    normw_a = np.ascontiguousarray(np.broadcast_to(
        np.asarray(inputs["L0_attn_norm_w"], np.float32)[None, :], (128, DIN)))

    in_maps = []
    for core in range(n_cores):
        r0 = core * Town
        start = r0 - halo
        seq_start = (r0 // seq) * seq
        if start < seq_start:
            pad = seq_start - start
            xs = np.concatenate([np.zeros((pad, D), np.float32), x[seq_start:r0 + Town]], axis=0)
            hm = np.zeros((128, max(halo, 1)), np.float32)
            if halo > pad:
                hm[:, pad:] = 1.0
        else:
            xs = np.ascontiguousarray(x[start:r0 + Town])
            hm = np.ones((128, max(halo, 1)), np.float32)
        in_maps.append({
            "x_in": np.ascontiguousarray(xs), "w_in": w_in_a, "w_out": w_out_a,
            "w_gu_t": w_gu_t, "w_down_t": w_down_t, "convw": convw, "convb": convb,
            "alog": alog_a, "dtbias": dtbias_a, "dskip": dskip_a, "normw": normw_a,
            "halomask": np.ascontiguousarray(hm),
        })
    return in_maps


def kernel(**inputs):
    in_maps = make_inmaps(inputs)
    nc = build_program(N_TILES, HALO_TILES)
    res = run_bass_kernel_spmd(nc, in_maps, core_ids=list(range(N_CORES)))
    out = np.concatenate([np.asarray(r["out"]) for r in res.results], axis=0)
    return out.reshape(2, 8192, D).astype(np.float32)


# version 05
# speedup vs baseline: 1.1274x; 1.0195x over previous
import numpy as np
import concourse.bass as bass
import concourse.mybir as mybir
from concourse.bass_utils import run_bass_kernel_spmd

F32 = mybir.dt.float32
BF16 = mybir.dt.bfloat16
AF = mybir.ActivationFunctionType
ALU = mybir.AluOpType
AX = mybir.AxisListType

D = 2048
DIN = 4096
NH = 64
HP = 64
NCONV = 4128
INDIM = 8288
INTER = 8192
EPS = 1e-5
Q = 128
CLAMP = 40.0


class Sem:
    __slots__ = ("h", "n")

    def __init__(self, nc, name):
        self.h = nc.alloc_semaphore(name)
        self.n = 0


class B:
    ENGINES = ("tensor", "vector", "scalar", "sync", "gpsimd")

    def __init__(self, nc):
        self.nc = nc
        self.cmds = {e: [] for e in self.ENGINES}
        self._i = 0
        self.ms = {e: Sem(nc, "ms_" + e) for e in self.ENGINES}

    def sem(self, name=None):
        self._i += 1
        return Sem(self.nc, name or f"s{self._i}")

    def em(self, eng, make, inc=False, sem=None, n=1):
        caps = []
        if inc:
            m = self.ms[eng]
            m.n += 1
            caps.append((m.h, 1))
        if sem is not None:
            sem.n += n
            caps.append((sem.h, n))

        def fn(e, make=make, caps=tuple(caps)):
            ins = make(e)
            for h, k in caps:
                ins = ins.then_inc(h, k)
        self.cmds[eng].append(fn)

    def dma(self, eng, out, in_, sem=None):
        self.em(eng, lambda e, o=out, i=in_: e.dma_start(out=o, in_=i), sem=sem, n=16)

    def wait(self, eng, sem, val=None):
        v = sem.n if val is None else val
        if v <= 0:
            return
        h = sem.h
        self.cmds[eng].append(lambda e, h=h, v=v: e.wait_ge(h, v))

    def waitm(self, eng, src_eng, val=None):
        self.wait(eng, self.ms[src_eng], val)


def build_program(n_tiles, halo_tiles, dbg=False):
    T = n_tiles * Q
    Thalf = T // 2
    nth = n_tiles // 2
    assert n_tiles % 2 == 0
    Town = T - halo_tiles * Q
    n_own = n_tiles - halo_tiles
    assert Town % 256 == 0

    def mov_tiles(total):
        r, t0 = [], 0
        while t0 < total:
            tw = min(512, total - t0)
            r.append((t0, tw))
            t0 += tw
        return r

    jth = mov_tiles(Thalf)
    jo = mov_tiles(Town)

    nc = bass.Bass("TRN2", target_bir_lowering=False, detect_race_conditions=False)
    b = B(nc)
    T_, V_, S_, SY, G_ = "tensor", "vector", "scalar", "sync", "gpsimd"

    x_in = nc.dram_tensor("x_in", [T, D], F32, kind="ExternalInput")
    w_in = nc.dram_tensor("w_in", [D, INDIM], F32, kind="ExternalInput")
    w_out = nc.dram_tensor("w_out", [DIN, D], F32, kind="ExternalInput")
    w_gu_t = nc.dram_tensor("w_gu_t", [D, 2 * INTER], F32, kind="ExternalInput")
    w_down_t = nc.dram_tensor("w_down_t", [INTER, D], F32, kind="ExternalInput")
    convw = nc.dram_tensor("convw", [128, 33 * 4], F32, kind="ExternalInput")
    convb = nc.dram_tensor("convb", [128, 33], F32, kind="ExternalInput")
    alog = nc.dram_tensor("alog", [NH, 1], F32, kind="ExternalInput")
    dtbias = nc.dram_tensor("dtbias", [NH, 1], F32, kind="ExternalInput")
    dskip = nc.dram_tensor("dskip", [128, NH], F32, kind="ExternalInput")
    normw = nc.dram_tensor("normw", [128, DIN], F32, kind="ExternalInput")
    hm_w = max(halo_tiles * Q, 1)
    halomask = nc.dram_tensor("halomask", [128, hm_w], F32, kind="ExternalInput")
    out_t = nc.dram_tensor("out", [Town, D], F32, kind="ExternalOutput")

    kd = "ExternalOutput" if dbg else "Internal"
    gsilu_s = nc.dram_tensor("gsilu_s", [DIN, T], F32, kind=kd)
    xbc_s = nc.dram_tensor("xbc_s", [NCONV, T], F32, kind=kd)
    yn_sh = nc.dram_tensor("yn_sh", [DIN, Town], BF16, kind=kd)
    yn_sl = nc.dram_tensor("yn_sl", [DIN, Town], BF16, kind=kd)
    attn_s = nc.dram_tensor("attn_s", [Town, D], F32, kind=kd)
    x2_s = nc.dram_tensor("x2_s", [Town, D], F32, kind=kd)
    h_sh = nc.dram_tensor("h_sh", [INTER, Town], BF16, kind=kd)
    h_sl = nc.dram_tensor("h_sl", [INTER, Town], BF16, kind=kd)
    wd_sh = nc.dram_tensor("wd_sh", [INTER, D], BF16)
    wd_sl = nc.dram_tensor("wd_sl", [INTER, D], BF16)

    ident_h = nc.inline_tensor(np.eye(128, dtype=np.float32), "c_ident")
    bdneg = np.zeros((2, 64, 4096), np.float32)
    for j in range(32):
        bdneg[0, j, j * 128:(j + 1) * 128] = -1.0
        bdneg[1, 32 + j, j * 128:(j + 1) * 128] = -1.0
    bdneg_h = [nc.inline_tensor(bdneg[0], "c_bdneg0"), nc.inline_tensor(bdneg[1], "c_bdneg1")]
    triu_h = nc.inline_tensor(np.triu(np.ones((128, 128), np.float32)), "c_triu")
    ones1_h = nc.inline_tensor(np.ones((1, 128), np.float32), "c_ones1")
    eye16_h = nc.inline_tensor(np.eye(16, dtype=np.float32), "c_eye16")

    A = nc.alloc_sbuf_tensor
    ident = A("ident", [128, 128], F32)
    triu = A("triu", [128, 128], F32)
    ones1 = A("ones1", [1, 128], F32)
    eye16 = A("eye16", [16, 16], F32)
    cwsb = A("cwsb", [128, 33 * 4], F32)
    cbsb = A("cbsb", [128, 33], F32)
    aneg = A("aneg", [128, 1], F32)
    dtb = A("dtb", [128, 1], F32)
    dsk = A("dsk", [128, NH], F32)
    hmask = A("hmask", [128, hm_w], F32)
    states = A("states", [16, NH * HP], F32)
    stats = A("stats", [128, 8], F32)
    rowst = A("rowst", [1, 64], F32)
    dax16 = A("dax16", [16, NH], F32)
    dAI = A("dAI", [16, 16], F32)
    bw16 = A("bw16", [128, 16], F32)
    sv3 = A("sv3", [128, 33 * 3], F32)
    cvt = A("cvt", [128, 2], F32)

    rem = nc.sbuf_bytes_remaining // 4
    AR = (rem - 512) // 4 * 4
    arena = A("arena", [128, AR], F32)
    ar = arena.ap()

    def rg(off, n):
        assert off + n <= AR, (off, n, AR)
        return ar[:, off:off + n]

    def rgb(off, nb):
        assert nb % 2 == 0 and off + nb // 2 <= AR, (off, nb, AR)
        return ar[:, off:off + nb // 2].bitcast(BF16)

    x1fm_n = nth * 16 * 128
    o = x1fm_n
    XT = o; o += 2 * D
    X1T = o; o += D
    SQB = o; o += D
    WBUF = o; o += 2 * 2048
    WHI = o; o += 2 * 1024
    WLO = o; o += 2 * 1024
    GBUF = o; o += 2 * 512
    XPRE = o; o += Thalf + 4
    XBCB = o; o += 2 * Thalf
    DTRAW = o; o += T
    WDPP = o; o += 4096
    AB_END = o
    assert AB_END <= AR, (AB_END, AR)

    x1hi = rgb(0, x1fm_n)
    x1lo = rgb(x1fm_n // 2, x1fm_n)
    x1hi_r = x1hi.rearrange("p (i k t) -> p i k t", k=16, t=128)
    x1lo_r = x1lo.rearrange("p (i k t) -> p i k t", k=16, t=128)
    xt_r = [rg(XT, D), rg(XT + D, D)]
    x1t = rg(X1T, D)
    sqb = rg(SQB, D)
    wbuf_r = [rg(WBUF, 2048), rg(WBUF + 2048, 2048)]
    whi_r = [rgb(WHI, 2048), rgb(WHI + 1024, 2048)]
    wlo_r = [rgb(WLO, 2048), rgb(WLO + 1024, 2048)]
    gbuf_r = [rg(GBUF, 512), rg(GBUF + 512, 512)]
    xpre = rg(XPRE, Thalf + 4)
    xbc_r = [rg(XBCB, Thalf), rg(XBCB + Thalf, Thalf)]
    dtraw = rg(DTRAW, T)
    wpp_st = rg(WDPP, 2048)
    wpp_hi = rgb(WDPP + 2048, 2048)
    wpp_lo = rgb(WDPP + 3072, 2048)

    o = 0
    BD0 = o; o += 4096
    BD1 = o; o += 4096
    BIGD = o; o += 4096
    XBCC = o; o += 33 * 128
    XTM = o; o += DIN
    YTM = o; o += DIN
    GFM = o; o += DIN
    CST = o; o += 5 * 128
    BCS = o; o += 3 * 128
    YNST = o; o += 4 * 128
    SSD_B = o
    if max(SSD_B, AB_END) + 5 * T + 128 + DIN <= AR:
        CUMO = max(SSD_B, AB_END)
    else:
        CUMO = SSD_B
        assert CUMO + 3 * T <= DTRAW, (CUMO + 3 * T, DTRAW)
    CUMPRO = CUMO + T
    DTTO = CUMPRO + T
    OCPO = DTTO + T
    CUMLOO = OCPO + T
    DTAO = CUMLOO + T
    NWO = DTAO + 128
    SSD_END = NWO + DIN
    assert SSD_END <= AR, (SSD_END, AR)

    bd_r = [rg(BD0, 4096), rg(BD1, 4096)]
    bigD = rg(BIGD, 4096)
    xbcc = rg(XBCC, 33 * 128)
    xtm = rg(XTM, DIN)
    ytm = rg(YTM, DIN)
    gfm = rg(GFM, DIN)
    cumT_sb = rg(CST, 64)
    ecumT = rg(CST + 64, 64)
    wexT = rg(CST + 128, 64)
    wexs = rg(CST + 192, 128)
    bcs = rg(BCS, 384)
    ynst = rg(YNST, 512)
    cum = rg(CUMO, T)
    cumpr = rg(CUMPRO, T)
    ocp = rg(OCPO, T)
    cumlo = rg(CUMLOO, T)
    dt_t = rg(DTTO, T)
    dtA = rg(DTAO, 128)
    nw = rg(NWO, DIN)
    BW8 = NWO + DIN
    bw8_r = [rg(BW8, 128), rg(BW8 + 128, 128)]
    dai8_r = [rg(BW8 + 256, 128), rg(BW8 + 384, 128)]
    assert BW8 + 512 <= AR, (BW8 + 512, AR)

    WOH = 0
    wouth_hi = rgb(WOH, 32 * 512)
    wouth_lo = rgb(WOH + 8192, 32 * 512)
    WST = WOH + 16384
    wst_r = [rg(WST, 512), rg(WST + 512, 512)]
    YST = WST + 1024
    yst_hi = [rgb(YST + s * 4096, 4096) for s in range(3)]
    yst_lo = [rgb(YST + s * 4096 + 2048, 4096) for s in range(3)]
    AST = YST + 3 * 4096
    ast_r = [rg(AST, 512), rg(AST + 512, 512)]
    assert AST + 1024 <= AR, (AST + 1024, AR)

    x3fm_n = n_own * 16 * 128
    x3hi = rgb(0, x3fm_n)
    x3lo = rgb(x3fm_n // 2, x3fm_n)
    x3hi_r = x3hi.rearrange("p (i k t) -> p i k t", k=16, t=128)
    x3lo_r = x3lo.rearrange("p (i k t) -> p i k t", k=16, t=128)
    fo = x3fm_n
    f_attn = rg(fo, D)
    f_x = rg(fo + D, D)
    f_tmp = rg(fo + 2 * D, D)
    f_x2 = rg(fo + 3 * D, D)
    f_x3 = rg(fo + 4 * D, D)
    assert fo + 5 * D <= AR
    wgu_r = [rg(fo, 2048), rg(fo + 2048, 2048)]
    wguhi_r = [rgb(fo + 4096, 2048), rgb(fo + 5120, 2048)]
    wgulo_r = [rgb(fo + 6144, 2048), rgb(fo + 7168, 2048)]
    gs_b = rg(fo + 8192, Town)
    hb = rg(fo + 8192 + Town, Town)
    hb_hi = rgb(fo + 8192 + 2 * Town, Town)
    hb_lo = rgb(fo + 8192 + 2 * Town + Town // 2, Town)
    assert fo + 8192 + 3 * Town <= AR, (fo + 8192 + 3 * Town, AR)

    hq_hi = rgb(0, 64 * 256)
    hq_lo = rgb(64 * 128, 64 * 256)
    HWD = 64 * 256
    wdhi_r = [rgb(HWD, 2048), rgb(HWD + 1024, 2048)]
    wdlo_r = [rgb(HWD + 2048, 2048), rgb(HWD + 3072, 2048)]
    ho = HWD + 4096
    dtm_r = [rg(ho, D), rg(ho + D, D)]
    h_x2 = rg(ho + 2 * D, D)
    h_out = rg(ho + 3 * D, D)
    h_sq = rg(ho + 4 * D, D)
    WCV = ho + 5 * D
    wcv_r = [rg(WCV, 2048), rg(WCV + 2048, 2048)]
    assert WCV + 4096 <= AR, (WCV + 4096, AR)

    pb = [nc.alloc_psum_tensor(f"pb{i}", [128, 512], F32) for i in range(8)]

    s_ld = b.sem("ld")
    b.dma(SY, ident[:, :], ident_h.ap(), s_ld)
    b.dma(SY, triu[:, :], triu_h.ap(), s_ld)
    b.dma(SY, ones1[:, :], ones1_h.ap(), s_ld)
    b.dma(SY, eye16[:, :], eye16_h.ap(), s_ld)
    b.dma(SY, cwsb[:, :], convw.ap()[:, :], s_ld)
    b.dma(SY, cbsb[:, :], convb.ap()[:, :], s_ld)
    b.dma(SY, aneg[0:64, 0:1], alog.ap(), s_ld)
    b.dma(SY, dtb[0:64, 0:1], dtbias.ap(), s_ld)
    b.dma(SY, dsk[:, :], dskip.ap(), s_ld)
    b.dma(SY, hmask[:, :], halomask.ap(), s_ld)

    b.wait(S_, s_ld)
    b.em(S_, lambda e: e.activation(aneg[0:64, 0:1], aneg[0:64, 0:1], AF.Exp, scale=1.0))
    b.em(S_, lambda e: e.mul(aneg[0:64, 0:1], aneg[0:64, 0:1], -1.0), inc=True)
    b.wait(V_, s_ld)
    b.em(V_, lambda e: e.memset(cvt[:, 0:1], float(EPS)))
    b.em(V_, lambda e: e.memset(cvt[:, 1:2], 1.0))
    b.em(V_, lambda e: e.memset(xpre[:, 0:4], 0.0))
    b.em(V_, lambda e: e.memset(wbuf_r[0][:, :], 0.0))
    b.em(V_, lambda e: e.memset(wbuf_r[1][:, :], 0.0))
    b.em(V_, lambda e: e.memset(stats[:, :], 0.0), inc=True)
    a_init_v = b.ms[V_].n
    a_init_s = b.ms[S_].n

    s_ldx = [b.sem("ldx0"), b.sem("ldx1")]
    s_ldw = [b.sem("ldw0"), b.sem("ldw1")]
    s_gsp = [b.sem("gsp0"), b.sem("gsp1")]
    s_xsp = [b.sem("xsp0"), b.sem("xsp1")]
    NCC = (INDIM + 127) // 128

    bank_ev = [0] * 8
    bank_eng = [V_] * 8
    b.wait(T_, s_ld)
    b.waitm(T_, V_, a_init_v)

    xt_ev = [0, 0]
    gate_cnt = 0
    wconv_v = [0, 0]
    s_wpl = b.sem("wpl")
    s_wpc = b.sem("wpc")
    wpp_k = [0]

    def wd_prepass_step():
        k = wpp_k[0]
        if k >= 64:
            return
        wpp_k[0] += 1
        if k > 0:
            b.wait(G_, s_wpc)
        b.dma(G_, wpp_st, w_down_t.ap()[k * 128:(k + 1) * 128, :], s_wpl)
        b.wait(S_, s_wpl)
        b.em(S_, lambda e: e.copy(wpp_hi[:, :], wpp_st), inc=True)
        b.waitm(V_, S_)
        b.em(V_, lambda e: e.tensor_tensor(wpp_lo[:, :], wpp_st, wpp_hi[:, :], ALU.subtract),
             inc=True)
        b.waitm(S_, V_)
        b.em(S_, lambda e, k=k: e.dma_start(out=wd_sh.ap()[k * 128:(k + 1) * 128, :],
                                            in_=wpp_hi[:, :]), sem=s_wpc, n=16)
        b.em(S_, lambda e, k=k: e.dma_start(out=wd_sl.ap()[k * 128:(k + 1) * 128, :],
                                            in_=wpp_lo[:, :]), sem=s_wpc, n=16)
    for half in range(2):
        tb = half * nth
        for li in range(nth):
            gi = tb + li
            sl = li % 2
            b.wait(SY, b.ms[V_], xt_ev[sl])
            b.dma(SY, xt_r[sl], x_in.ap()[gi * Q:(gi + 1) * Q, :], s_ldx[sl])
            b.wait(V_, s_ldx[sl])
            b.em(V_, lambda e, sl=sl: e.reduce_sum(stats[:, 0:1], xt_r[sl], axis=AX.X), inc=True)
            b.wait(S_, s_ldx[sl])
            b.waitm(S_, V_)
            b.em(S_, lambda e, sl=sl: e.activation(sqb, xt_r[sl], AF.Square, accum_out=stats[:, 1:2]),
                 inc=True)
            b.waitm(V_, S_)
            b.em(V_, lambda e: e.tensor_scalar(stats[:, 2:3], stats[:, 0:1], 1.0 / D, None, ALU.mult))
            b.em(V_, lambda e: e.tensor_tensor(stats[:, 3:4], stats[:, 2:3], stats[:, 2:3], ALU.mult))
            b.em(V_, lambda e: e.scalar_tensor_tensor(stats[:, 4:5], stats[:, 1:2], 1.0 / D,
                                                      stats[:, 3:4], ALU.mult, ALU.subtract), inc=True)
            b.waitm(S_, V_)
            b.em(S_, lambda e: e.activation(stats[:, 5:6], stats[:, 4:5], AF.Ln, bias=cvt[:, 0:1], scale=1.0))
            b.em(S_, lambda e: e.activation(stats[:, 6:7], stats[:, 5:6], AF.Exp, scale=-0.5), inc=True)
            b.waitm(V_, S_)
            b.em(V_, lambda e, sl=sl: e.tensor_scalar(x1t, xt_r[sl], stats[:, 2:3], stats[:, 6:7],
                                                      ALU.subtract, ALU.mult), inc=True)
            xt_ev[sl] = b.ms[V_].n
            b.waitm(T_, V_)
            for g in range(4):
                bank = pb[g]
                b.wait(T_, b.ms[bank_eng[g]], bank_ev[g])
                for u in range(4):
                    k = g * 4 + u
                    b.em(T_, lambda e, k=k, bank=bank, u=u: e.transpose(
                        bank[:, u * 128:(u + 1) * 128], x1t[:, k * 128:(k + 1) * 128], ident[:, :]),
                        inc=(u == 3))
                b.waitm(S_, T_)
                b.em(S_, lambda e, li=li, g=g, bank=bank: e.copy(
                    x1hi[:, (li * 16 + g * 4) * 128:(li * 16 + g * 4 + 4) * 128], bank[:, :]), inc=True)
                b.waitm(V_, S_)
                b.em(V_, lambda e, li=li, g=g, bank=bank: e.tensor_tensor(
                    x1lo[:, (li * 16 + g * 4) * 128:(li * 16 + g * 4 + 4) * 128], bank[:, :],
                    x1hi[:, (li * 16 + g * 4) * 128:(li * 16 + g * 4 + 4) * 128], ALU.subtract), inc=True)
                bank_ev[g] = b.ms[V_].n
                bank_eng[g] = V_

        while wpp_k[0] < 64 and half == 1:
            wd_prepass_step()
        b.waitm(T_, V_)
        for c in range(NCC):
            cw = min(128, INDIM - c * 128)
            sl = c % 2
            if c == 0:
                if half == 1:
                    b.waitm(SY, T_)
                for k in range(16):
                    b.dma(SY, wbuf_r[0][:, k * 128:k * 128 + cw],
                          w_in.ap()[k * 128:(k + 1) * 128, 0:cw], s_ldw[0])
                b.wait(S_, s_ldw[0])
                b.em(S_, lambda e, cw=cw: e.copy(whi_r[0][:, 0:16 * 128], wbuf_r[0][:, :]), inc=True)
                b.waitm(V_, S_)
                b.em(V_, lambda e, cw=cw: e.tensor_tensor(
                    wlo_r[0][:, 0:16 * 128], wbuf_r[0][:, :], whi_r[0][:, 0:16 * 128],
                    ALU.subtract), inc=True)
                wconv_v[0] = b.ms[V_].n
            if c + 1 < NCC:
                cw2 = min(128, INDIM - (c + 1) * 128)
                if c >= 1:
                    b.waitm(SY, T_)
                sl2 = (c + 1) % 2
                for k in range(16):
                    b.dma(SY, wbuf_r[sl2][:, k * 128:k * 128 + cw2],
                          w_in.ap()[k * 128:(k + 1) * 128, (c + 1) * 128:(c + 1) * 128 + cw2],
                          s_ldw[sl2])
                b.wait(S_, s_ldw[sl2])
                if c >= 1:
                    b.waitm(S_, T_)
                b.em(S_, lambda e, sl2=sl2: e.copy(whi_r[sl2][:, 0:16 * 128], wbuf_r[sl2][:, :]),
                     inc=True)
                b.waitm(V_, S_)
                b.em(V_, lambda e, sl2=sl2: e.tensor_tensor(
                    wlo_r[sl2][:, 0:16 * 128], wbuf_r[sl2][:, :], whi_r[sl2][:, 0:16 * 128],
                    ALU.subtract), inc=True)
                wconv_v[sl2] = b.ms[V_].n
            wd_prepass_step()
            b.waitm(T_, V_, wconv_v[sl])
            for j, (t0, tw) in enumerate(jth):
                bank = pb[j]
                b.wait(T_, b.ms[bank_eng[j]], bank_ev[j])
                ntl = (tw + 127) // 128
                i0 = t0 // 128
                if c < 64:
                    for k in range(16):
                        for wop, xop, st, sp in (
                            (whi_r[sl], x1hi_r, k == 0, False),
                            (whi_r[sl], x1lo_r, False, False),
                            (wlo_r[sl], x1hi_r, False, k == 15),
                        ):
                            b.em(T_, lambda e, k=k, cw=cw, bank=bank, i0=i0, ntl=ntl, tw=tw,
                                 wop=wop, xop=xop, st=st, sp=sp:
                                 e.matmul(bank[0:cw, 0:tw], wop[:, k * 128:k * 128 + cw],
                                          xop[:, i0:i0 + ntl, k:k + 1, :],
                                          start=st, stop=sp),
                                 inc=sp)
                else:
                    bank2 = pb[3 + j]
                    b.wait(T_, b.ms[bank_eng[3 + j]], bank_ev[3 + j])
                    for k in range(16):
                        for wop, xop, st, sp in (
                            (whi_r[sl], x1hi_r, k == 0, False),
                            (whi_r[sl], x1lo_r, False, False),
                            (wlo_r[sl], x1hi_r, False, k == 15),
                        ):
                            b.em(T_, lambda e, k=k, bank=bank, i0=i0, ntl=ntl, tw=tw,
                                 wop=wop, xop=xop, st=st, sp=sp:
                                 e.matmul(bank[0:32, 0:tw], wop[:, k * 128:k * 128 + 32],
                                          xop[:, i0:i0 + ntl, k:k + 1, :],
                                          start=st, stop=sp))
                            b.em(T_, lambda e, k=k, bank2=bank2, i0=i0, ntl=ntl, tw=tw,
                                 wop=wop, xop=xop, st=st, sp=sp:
                                 e.matmul(bank2[0:64, 0:tw], wop[:, k * 128 + 32:k * 128 + 96],
                                          xop[:, i0:i0 + ntl, k:k + 1, :],
                                          start=st, stop=sp),
                                 inc=sp)
                mmv = b.ms[T_].n
                if c < 32:
                    gsl = gate_cnt % 2
                    b.waitm(S_, T_, mmv)
                    b.wait(S_, s_gsp[gsl])
                    b.em(S_, lambda e, bank=bank, tw=tw, gsl=gsl:
                         e.activation(gbuf_r[gsl][:, 0:tw], bank[0:128, 0:tw], AF.Silu), inc=True)
                    bank_ev[j] = b.ms[S_].n
                    bank_eng[j] = S_
                    b.waitm(SY, S_)
                    b.dma(SY, gsilu_s.ap()[c * 128:(c + 1) * 128,
                                           half * Thalf + t0:half * Thalf + t0 + tw],
                          gbuf_r[gsl][:, 0:tw], s_gsp[gsl])
                    gate_cnt += 1
                elif c < 64:
                    b.waitm(V_, T_, mmv)
                    b.em(V_, lambda e, bank=bank, t0=t0, tw=tw: e.tensor_copy(
                        xpre[:, 4 + t0:4 + t0 + tw], bank[0:128, 0:tw]), inc=True)
                    bank_ev[j] = b.ms[V_].n
                    bank_eng[j] = V_
                else:
                    b.waitm(V_, T_, mmv)
                    b.em(V_, lambda e, bank=bank, t0=t0, tw=tw: e.tensor_copy(
                        xpre[0:32, 4 + t0:4 + t0 + tw], bank[0:32, 0:tw]))
                    b.em(V_, lambda e, j=j, t0=t0, tw=tw, half=half: e.tensor_copy(
                        dtraw[0:64, half * Thalf + t0:half * Thalf + t0 + tw], pb[3 + j][0:64, 0:tw]),
                        inc=True)
                    bank_ev[j] = b.ms[V_].n
                    bank_eng[j] = V_
                    bank_ev[3 + j] = b.ms[V_].n
                    bank_eng[3 + j] = V_
            if c >= 32:
                cc = c - 32
                nch = 128 if c < 64 else 32
                xb = xbc_r[cc % 2]
                b.wait(V_, s_xsp[cc % 2])
                if half == 1:
                    b.em(V_, lambda e, cc=cc, nch=nch: e.tensor_copy(
                        xpre[0:nch, 1:4], sv3[0:nch, cc * 3:cc * 3 + 3]))
                if dbg == 2:
                    b.em(V_, lambda e, nch=nch, xb=xb: e.tensor_copy(
                        xb[0:nch, 0:Thalf], xpre[0:nch, 4:4 + Thalf]))
                b.em(V_, lambda e, cc=cc, nch=nch, xb=xb: (None if dbg == 2 else e.tensor_scalar(
                    xb[0:nch, 0:Thalf], xpre[0:nch, 1:1 + Thalf], cwsb[0:nch, cc * 4:cc * 4 + 1],
                    cbsb[0:nch, cc:cc + 1], ALU.mult, ALU.add)) or e.nop())
                for k in range(1, 4):
                    if dbg == 2:
                        break
                    b.em(V_, lambda e, cc=cc, nch=nch, xb=xb, k=k: e.scalar_tensor_tensor(
                        xb[0:nch, 0:Thalf], xpre[0:nch, 1 + k:1 + k + Thalf],
                        cwsb[0:nch, cc * 4 + k:cc * 4 + k + 1], xb[0:nch, 0:Thalf],
                        ALU.mult, ALU.add))
                if half == 0:
                    b.em(V_, lambda e, cc=cc, nch=nch: e.tensor_copy(
                        sv3[0:nch, cc * 3:cc * 3 + 3], xpre[0:nch, 1 + Thalf:4 + Thalf]))
                    if halo_tiles > 0:
                        b.em(V_, lambda e, nch=nch, xb=xb: e.tensor_tensor(
                            xb[0:nch, 0:halo_tiles * Q], xb[0:nch, 0:halo_tiles * Q],
                            hmask[0:nch, :], ALU.mult))
                b.em(V_, lambda e, nch=nch, xb=xb: e.tensor_copy(xb[0:nch, 0:1], xb[0:nch, 0:1]),
                     inc=True)
                b.waitm(S_, V_)
                b.em(S_, lambda e, nch=nch, xb=xb: (e.nop() if dbg in (2, 3) else e.activation(
                    xb[0:nch, 0:Thalf], xb[0:nch, 0:Thalf], AF.Silu)), inc=True)
                b.waitm(SY, S_)
                b.dma(SY, xbc_s.ap()[cc * 128:cc * 128 + nch, half * Thalf:(half + 1) * Thalf],
                      xb[0:nch, 0:Thalf], s_xsp[cc % 2])

    b.waitm(S_, V_)
    b.em(S_, lambda e: e.activation(dt_t[0:64, :], dtraw[0:64, :], AF.Exp, bias=dtb[0:64, 0:1],
                                    scale=1.0))
    b.em(S_, lambda e: e.activation(dt_t[0:64, :], dt_t[0:64, :], AF.Ln, bias=cvt[0:64, 1:2], scale=1.0))
    b.em(S_, lambda e: e.activation(dtraw[0:64, :], dt_t[0:64, :], AF.Ln, scale=1.0), inc=True)
    b.waitm(V_, S_)
    b.em(V_, lambda e: e.memset(sqb[0:64, 0:Q], 0.0))
    for ci in range(n_tiles):
        b.em(V_, lambda e, ci=ci: e.tensor_scalar(
            dtA[0:64, 0:Q], dt_t[0:64, ci * Q:(ci + 1) * Q], aneg[0:64, 0:1], None, ALU.mult))
        b.em(V_, lambda e, ci=ci: e.tensor_tensor_scan(
            cum[0:64, ci * Q:(ci + 1) * Q], dtA[0:64, 0:Q], sqb[0:64, 0:Q],
            0.0, ALU.add, ALU.add))
    b.em(V_, lambda e: e.tensor_tensor(cumpr[0:64, :], cum[0:64, :], dtraw[0:64, :], ALU.subtract))
    b.em(V_, lambda e: e.memset(ocp[0:64, :], 1.0))
    b.em(V_, lambda e: e.memset(cum[64:128, :], 0.0))
    b.em(V_, lambda e: e.memset(cumpr[64:128, :], 0.0))
    b.em(V_, lambda e: e.memset(wexs[64:128, :], 0.0))
    b.em(V_, lambda e: e.memset(states[:, :], 0.0))
    b.em(V_, lambda e: e.memset(bd_r[0][32:64, :], 0.0))
    b.em(V_, lambda e: e.memset(bd_r[1][0:32, :], 0.0))
    s_bdl = b.sem("bdl")
    b.waitm(SY, V_)
    for hf in range(2):
        b.dma(SY, bd_r[hf][64:128, :], bdneg_h[hf].ap(), s_bdl)
    b.wait(G_, s_bdl)
    b.wait(T_, s_bdl)
    b.em(V_, lambda e: e.tensor_copy(cumlo[0:64, :], cum[0:64, :]), inc=True)
    s_Cd = b.sem("Cd")
    b.waitm(SY, V_)
    b.dma(SY, ocp[64:128, :], cumpr[0:64, :], s_Cd)

    s_xc = b.sem("xc")
    s_gf = b.sem("gf")
    s_rs = b.sem("rs")
    s_ysp = [b.sem(f"ysp{i}") for i in range(4)]
    s_nw = b.sem("nw")
    b.waitm(SY, V_)
    b.dma(SY, nw[:, :], normw.ap(), s_nw)
    b.wait(G_, s_Cd)
    b.wait(T_, s_Cd)
    b.waitm(G_, V_)

    gfm_free_v = 0
    gfm_free_s = 0
    xbcc_free = 0
    yn_cnt = 0

    for ci in range(n_tiles):
        own = ci >= halo_tiles
        oci = ci - halo_tiles
        tsl = slice(ci * Q, (ci + 1) * Q)
        b.waitm(SY, T_, xbcc_free)
        if ci == 0:
            b.wait(SY, s_xsp[0]); b.wait(SY, s_xsp[1])
        for g4 in range(8):
            b.dma(SY, xbcc[:, g4 * 512:(g4 + 1) * 512].rearrange("p (a t) -> p a t", a=4),
                  xbc_s.ap()[0:4096, :].rearrange("(a p) t -> p a t", p=128)[:, g4 * 4:(g4 + 1) * 4, tsl],
                  s_xc)
        b.dma(SY, bcs[0:16, 0:128], xbc_s.ap()[4096:4112, tsl], s_xc)
        b.dma(SY, bcs[0:16, 128:256], xbc_s.ap()[4112:4128, tsl], s_xc)
        if own:
            b.waitm(SY, V_, gfm_free_v)
            b.waitm(SY, S_, gfm_free_s)
            if oci == 0:
                b.wait(SY, s_gsp[0]); b.wait(SY, s_gsp[1])
            for g4 in range(8):
                b.dma(SY, gfm[:, g4 * 512:(g4 + 1) * 512].rearrange("p (a t) -> p a t", a=4),
                      gsilu_s.ap().rearrange("(a p) t -> p a t", p=128)[:, g4 * 4:(g4 + 1) * 4, tsl],
                      s_gf)

        b.waitm(G_, T_)
        for hf in range(2):
            h0 = hf * 32
            srcap = cumlo[h0:h0 + 32, tsl].unsqueeze(1).to_broadcast([32, 32, 128])
            dstap = bd_r[hf][h0:h0 + 32, :].rearrange("p (h t) -> p h t", h=32)
            b.em(G_, lambda e, srcap=srcap, dstap=dstap: e.affine_select(
                dstap, srcap, pattern=[[1, 32], [0, 128]], compare_op=ALU.is_equal,
                fill=0.0, base=0, channel_multiplier=-1), inc=(hf == 1))
        bd_ready = b.ms[G_].n

        b.wait(T_, b.ms[bank_eng[2]], bank_ev[2])
        b.em(T_, lambda e, tsl=tsl: e.transpose(pb[2][:, 0:128], cum[:, tsl], ident[:, :]), inc=True)
        b.waitm(V_, T_)
        b.em(V_, lambda e: e.tensor_copy(cumT_sb[:, :], pb[2][:, 0:64]), inc=True)
        bank_ev[2] = b.ms[V_].n; bank_eng[2] = V_
        b.waitm(S_, V_)
        b.em(S_, lambda e: e.activation(ecumT[:, :], cumT_sb[:, :], AF.Exp, scale=1.0), inc=True)
        b.em(V_, lambda e, ci=ci: e.tensor_scalar(
            wexs[0:64, :], cumpr[0:64, ci * Q:(ci + 1) * Q],
            cum[0:64, ci * Q + Q - 1:ci * Q + Q], -1.0, ALU.subtract, ALU.mult), inc=True)
        b.waitm(S_, V_)
        b.em(S_, lambda e: e.activation(wexs[0:64, :], wexs[0:64, :], AF.Exp, scale=1.0), inc=True)
        b.waitm(T_, S_)
        b.wait(T_, b.ms[bank_eng[3]], bank_ev[3])
        b.em(T_, lambda e: e.transpose(pb[3][:, 0:128], wexs[:, :], ident[:, :]), inc=True)
        b.waitm(V_, T_)
        b.em(V_, lambda e: e.tensor_copy(wexT[:, :], pb[3][:, 0:64]), inc=True)
        bank_ev[3] = b.ms[V_].n; bank_eng[3] = V_
        b.em(S_, lambda e: e.dma_start(out=rowst[0:1, 0:64], in_=ecumT[127:128, :]), sem=s_rs, n=16)
        b.wait(T_, s_rs)
        b.wait(T_, b.ms[bank_eng[3]], bank_ev[3])
        b.em(T_, lambda e: e.matmul(pb[3][0:16, 0:64], ones1[0:1, 0:16], rowst[0:1, 0:64],
                                    start=True, stop=True), inc=True)
        b.waitm(V_, T_)
        b.em(V_, lambda e: e.tensor_copy(dax16[:, :], pb[3][0:16, 0:64]), inc=True)
        bank_ev[3] = b.ms[V_].n; bank_eng[3] = V_

        b.wait(T_, s_xc)
        for g in range(8):
            bk = 2 + g % 2
            bank = pb[bk]
            b.wait(T_, b.ms[bank_eng[bk]], bank_ev[bk])
            for u in range(4):
                a = g * 4 + u
                b.em(T_, lambda e, a=a, bank=bank, u=u: e.transpose(
                    bank[:, u * 128:(u + 1) * 128], xbcc[:, a * 128:(a + 1) * 128], ident[:, :]),
                    inc=(u == 3))
            b.waitm(V_, T_)
            b.em(V_, lambda e, g=g, bank=bank: e.tensor_copy(
                xtm[:, g * 512:(g + 1) * 512], bank[:, :]), inc=True)
            bank_ev[bk] = b.ms[V_].n; bank_eng[bk] = V_
        b.wait(T_, b.ms[bank_eng[2]], bank_ev[2])
        b.em(T_, lambda e: e.transpose(pb[2][:, 0:16], bcs[0:16, 0:128], ident[0:16, 0:16]), inc=True)
        xbcc_free = b.ms[T_].n
        b.waitm(V_, T_)
        b.em(V_, lambda e: e.tensor_copy(bcs[:, 256:272], pb[2][:, 0:16]), inc=True)
        bank_ev[2] = b.ms[V_].n; bank_eng[2] = V_

        b.wait(T_, b.ms[bank_eng[4]], bank_ev[4])
        b.em(T_, lambda e: e.matmul(pb[4][:, 0:128], bcs[0:16, 0:128], bcs[0:16, 128:256],
                                    start=True, stop=True), inc=True)
        b.waitm(V_, T_)
        b.em(V_, lambda e: e.tensor_tensor(wexs[:, :], pb[4][:, 0:128], triu[:, :], ALU.mult), inc=True)
        bank_ev[4] = b.ms[V_].n; bank_eng[4] = V_
        mcb_v = b.ms[V_].n

        for hf in range(2):
            h0 = hf * 32
            b.wait(T_, b.ms[G_], bd_ready)
            for jj in range(8):
                bk = jj % 2
                bank = pb[bk]
                b.wait(T_, b.ms[bank_eng[bk]], bank_ev[bk])
                b.em(T_, lambda e, hf=hf, jj=jj, bank=bank, tsl=tsl: e.matmul(
                    bank[:, 0:512], ocp[:, tsl], bd_r[hf][:, jj * 512:(jj + 1) * 512],
                    start=True, stop=True), inc=True)
                b.waitm(V_, T_)
                b.em(V_, lambda e, jj=jj, bank=bank: e.tensor_scalar(
                    bigD[:, jj * 512:(jj + 1) * 512], bank[:, 0:512], CLAMP, None, ALU.min), inc=True)
                bank_ev[bk] = b.ms[V_].n; bank_eng[bk] = V_
            b.waitm(S_, V_)
            b.em(S_, lambda e: e.activation(bigD[:, :], bigD[:, :], AF.Exp, scale=1.0), inc=True)
            b.waitm(V_, S_)
            b.em(V_, lambda e: e.tensor_tensor(
                bigD.rearrange("p (h t) -> p h t", h=32), bigD.rearrange("p (h t) -> p h t", h=32),
                wexs.unsqueeze(1).to_broadcast([128, 32, 128]), ALU.mult), inc=True)
            s_ready = b.ms[V_].n
            b.waitm(T_, V_, s_ready)
            for g8 in range(4):
                hg = [h0 + g8 * 8 + j for j in range(8)]
                if own:
                    bi = 4 + g8 % 2
                    bs = 6 + g8 % 2
                    b.wait(T_, b.ms[bank_eng[bi]], bank_ev[bi])
                    b.wait(T_, b.ms[bank_eng[bs]], bank_ev[bs])
                    for j, h in enumerate(hg):
                        hh = h - h0
                        b.em(T_, lambda e, hh=hh, h=h, bi=bi, j=j: e.matmul(
                            pb[bi][:, j * 64:(j + 1) * 64], bigD[:, hh * 128:(hh + 1) * 128],
                            xtm[:, h * 64:(h + 1) * 64], start=True, stop=True))
                        b.em(T_, lambda e, h=h, bs=bs, j=j: e.matmul(
                            pb[bs][:, j * 64:(j + 1) * 64], bcs[0:16, 128:256],
                            states[0:16, h * 64:(h + 1) * 64], start=True, stop=True),
                            inc=(j == 7))
                    ymm = b.ms[T_].n
                    b.waitm(V_, T_, ymm)
                    for j, h in enumerate(hg):
                        b.em(V_, lambda e, h=h, bi=bi, j=j: e.scalar_tensor_tensor(
                            ytm[:, h * 64:(h + 1) * 64], xtm[:, h * 64:(h + 1) * 64],
                            dsk[:, h:h + 1], pb[bi][:, j * 64:(j + 1) * 64], ALU.mult, ALU.add))
                        b.em(V_, lambda e, h=h, bs=bs, j=j: e.scalar_tensor_tensor(
                            ytm[:, h * 64:(h + 1) * 64], pb[bs][:, j * 64:(j + 1) * 64],
                            ecumT[:, h:h + 1], ytm[:, h * 64:(h + 1) * 64], ALU.mult, ALU.add),
                            inc=(j == 7))
                    bank_ev[bi] = b.ms[V_].n; bank_eng[bi] = V_
                    bank_ev[bs] = b.ms[V_].n; bank_eng[bs] = V_
                wsl8 = g8 % 2
                b.waitm(V_, T_)
                for j, h in enumerate(hg):
                    b.em(V_, lambda e, h=h, j=j, wsl8=wsl8: e.tensor_scalar(
                        bw8_r[wsl8][:, j * 16:(j + 1) * 16], bcs[:, 256:272],
                        wexT[:, h:h + 1], None, ALU.mult))
                    b.em(V_, lambda e, h=h, j=j, wsl8=wsl8: e.tensor_scalar(
                        dai8_r[wsl8][0:16, j * 16:(j + 1) * 16], eye16[:, :],
                        dax16[:, h:h + 1], None, ALU.mult), inc=(j == 7))
                bwv = b.ms[V_].n
                bu = 2 + g8 % 2
                b.waitm(T_, V_, bwv)
                b.wait(T_, b.ms[bank_eng[bu]], bank_ev[bu])
                for j, h in enumerate(hg):
                    b.em(T_, lambda e, h=h, bu=bu, j=j, wsl8=wsl8: e.matmul(
                        pb[bu][0:16, j * 64:(j + 1) * 64], bw8_r[wsl8][:, j * 16:(j + 1) * 16],
                        xtm[:, h * 64:(h + 1) * 64], start=True, stop=False))
                    b.em(T_, lambda e, h=h, bu=bu, j=j, wsl8=wsl8: e.matmul(
                        pb[bu][0:16, j * 64:(j + 1) * 64], dai8_r[wsl8][0:16, j * 16:(j + 1) * 16],
                        states[0:16, h * 64:(h + 1) * 64], start=False, stop=True),
                        inc=(j == 7))
                b.waitm(V_, T_)
                b.em(V_, lambda e, h0=h0, g8=g8, bu=bu: e.tensor_copy(
                    states[0:16, (h0 + g8 * 8) * 64:(h0 + g8 * 8 + 8) * 64],
                    pb[bu][0:16, 0:512]), inc=True)
                bank_ev[bu] = b.ms[V_].n; bank_eng[bu] = V_

        if own:
            b.wait(T_, s_gf)
            for g in range(32):
                bk = 2 + g % 2
                bank = pb[bk]
                b.wait(T_, b.ms[bank_eng[bk]], bank_ev[bk])
                b.em(T_, lambda e, g=g, bank=bank: e.transpose(
                    bank[:, 0:128], gfm[:, g * 128:(g + 1) * 128], ident[:, :]), inc=True)
                b.waitm(V_, T_)
                b.em(V_, lambda e, g=g, bank=bank: e.tensor_tensor(
                    ytm[:, g * 128:(g + 1) * 128], bank[:, 0:128],
                    ytm[:, g * 128:(g + 1) * 128], ALU.mult), inc=True)
                bank_ev[bk] = b.ms[V_].n; bank_eng[bk] = V_
            gfm_free_v = b.ms[V_].n
            b.waitm(S_, V_)
            b.em(S_, lambda e: e.activation(gfm[:, :], ytm[:, :], AF.Square, accum_out=stats[:, 1:2]))
            b.em(S_, lambda e: e.activation(stats[:, 5:6], stats[:, 1:2], AF.Ln, bias=cvt[:, 0:1],
                                            scale=1.0 / DIN))
            b.em(S_, lambda e: e.activation(stats[:, 6:7], stats[:, 5:6], AF.Exp, scale=-0.5), inc=True)
            gfm_free_s = b.ms[S_].n
            b.waitm(V_, S_)
            b.wait(V_, s_nw)
            b.em(V_, lambda e: e.scalar_tensor_tensor(ytm[:, :], ytm[:, :], stats[:, 6:7], nw[:, :],
                                                      ALU.mult, ALU.mult), inc=True)
            ynv = b.ms[V_].n
            b.waitm(T_, V_, ynv)
            for g in range(32):
                bk = 2 + g % 2
                bank = pb[bk]
                ysl = yn_cnt % 4
                yh = rgb(YNST + ysl * 128, 256)
                yl = rgb(YNST + ysl * 128 + 64, 256)
                b.wait(T_, b.ms[bank_eng[bk]], bank_ev[bk])
                b.em(T_, lambda e, g=g, bank=bank: e.transpose(
                    bank[:, 0:128], ytm[:, g * 128:(g + 1) * 128], ident[:, :]), inc=True)
                b.waitm(S_, T_)
                b.wait(S_, s_ysp[ysl])
                b.em(S_, lambda e, yh=yh, bank=bank: e.copy(yh[:, 0:128], bank[:, 0:128]), inc=True)
                b.waitm(V_, S_)
                b.em(V_, lambda e, yh=yh, yl=yl, bank=bank: e.tensor_tensor(
                    yl[:, 0:128], bank[:, 0:128], yh[:, 0:128], ALU.subtract), inc=True)
                bank_ev[bk] = b.ms[V_].n; bank_eng[bk] = V_
                b.waitm(SY, V_)
                b.dma(SY, yn_sh.ap()[g * 128:(g + 1) * 128, oci * Q:(oci + 1) * Q],
                      yh[:, 0:128], s_ysp[ysl])
                b.dma(SY, yn_sl.ap()[g * 128:(g + 1) * 128, oci * Q:(oci + 1) * Q],
                      yl[:, 0:128], s_ysp[ysl])
                yn_cnt += 1

    s_wo = [b.sem("wo0"), b.sem("wo1")]
    s_yl = [b.sem(f"yl{s}") for s in range(3)]
    s_asp = [b.sem("asp0"), b.sem("asp1")]
    yst_ev = [0, 0, 0]
    for i in range(4):
        b.wait(SY, s_ysp[i])
    ecnt = 0
    for oq in range(4):
        if oq >= 1:
            b.waitm(SY, T_)
        wst_cv = [0, 0]
        for k in range(32):
            wsl = k % 2
            b.wait(SY, b.ms[V_], wst_cv[wsl])
            b.dma(SY, wst_r[wsl],
                  w_out.ap()[k * 128:(k + 1) * 128, oq * 512:(oq + 1) * 512], s_wo[wsl])
            b.wait(S_, s_wo[wsl])
            if oq >= 1:
                b.waitm(S_, T_)
            b.em(S_, lambda e, k=k, wsl=wsl: e.copy(
                wouth_hi[:, k * 512:(k + 1) * 512], wst_r[wsl]), inc=True)
            b.waitm(V_, S_)
            b.em(V_, lambda e, k=k, wsl=wsl: e.tensor_tensor(
                wouth_lo[:, k * 512:(k + 1) * 512], wst_r[wsl],
                wouth_hi[:, k * 512:(k + 1) * 512], ALU.subtract), inc=True)
            wst_cv[wsl] = b.ms[V_].n
        wo_conv = b.ms[V_].n
        for i in range(n_own):
            sl = i % 3
            if oq > 0 or i >= 3:
                b.wait(SY, b.ms[T_], yst_ev[sl])
            for gsl4 in range(8):
                b.dma(SY, yst_hi[sl].rearrange("p (a t) -> p a t", a=32)[:, gsl4 * 4:(gsl4 + 1) * 4],
                      yn_sh.ap().rearrange("(a p) t -> p a t", p=128)[:, gsl4 * 4:(gsl4 + 1) * 4,
                                                                     i * Q:(i + 1) * Q], s_yl[sl])
                b.dma(SY, yst_lo[sl].rearrange("p (a t) -> p a t", a=32)[:, gsl4 * 4:(gsl4 + 1) * 4],
                      yn_sl.ap().rearrange("(a p) t -> p a t", p=128)[:, gsl4 * 4:(gsl4 + 1) * 4,
                                                                     i * Q:(i + 1) * Q], s_yl[sl])
            b.wait(T_, s_yl[sl])
            if i == 0:
                b.waitm(T_, V_, wo_conv)
            bk = ecnt % 2
            b.wait(T_, b.ms[bank_eng[bk]], bank_ev[bk])
            for k in range(32):
                for yop, wop, st, sp in (
                    (yst_hi[sl], wouth_hi, k == 0, False),
                    (yst_hi[sl], wouth_lo, False, False),
                    (yst_lo[sl], wouth_hi, False, k == 31),
                ):
                    b.em(T_, lambda e, k=k, bk=bk, yop=yop, wop=wop, st=st, sp=sp: e.matmul(
                        pb[bk][:, 0:512], yop[:, k * 128:(k + 1) * 128],
                        wop[:, k * 512:(k + 1) * 512],
                        start=st, stop=sp), inc=sp)
            yst_ev[sl] = b.ms[T_].n
            asl = ecnt % 2
            b.waitm(V_, T_)
            b.wait(V_, s_asp[asl])
            b.em(V_, lambda e, asl=asl, bk=bk: e.tensor_copy(
                ast_r[asl][:, 0:512], pb[bk][:, 0:512]), inc=True)
            bank_ev[bk] = b.ms[V_].n; bank_eng[bk] = V_
            b.waitm(SY, V_)
            b.dma(SY, attn_s.ap()[i * Q:(i + 1) * Q, oq * 512:(oq + 1) * 512],
                  ast_r[asl][:, 0:512], s_asp[asl])
            ecnt += 1

    s_fa = b.sem("fa")
    s_fx = b.sem("fx")
    s_x2p = b.sem("x2p")

    def ln_tile(src, dst, scr):
        b.em(V_, lambda e: e.reduce_sum(stats[:, 0:1], src, axis=AX.X), inc=True)
        b.waitm(S_, V_)
        b.em(S_, lambda e: e.activation(scr, src, AF.Square, accum_out=stats[:, 1:2]), inc=True)
        b.waitm(V_, S_)
        b.em(V_, lambda e: e.tensor_scalar(stats[:, 2:3], stats[:, 0:1], 1.0 / D, None, ALU.mult))
        b.em(V_, lambda e: e.tensor_tensor(stats[:, 3:4], stats[:, 2:3], stats[:, 2:3], ALU.mult))
        b.em(V_, lambda e: e.scalar_tensor_tensor(stats[:, 4:5], stats[:, 1:2], 1.0 / D,
                                                  stats[:, 3:4], ALU.mult, ALU.subtract), inc=True)
        b.waitm(S_, V_)
        b.em(S_, lambda e: e.activation(stats[:, 5:6], stats[:, 4:5], AF.Ln, bias=cvt[:, 0:1], scale=1.0))
        b.em(S_, lambda e: e.activation(stats[:, 6:7], stats[:, 5:6], AF.Exp, scale=-0.5), inc=True)
        b.waitm(V_, S_)
        b.em(V_, lambda e: e.tensor_scalar(dst, src, stats[:, 2:3], stats[:, 6:7],
                                           ALU.subtract, ALU.mult), inc=True)

    b.wait(SY, s_asp[0]); b.wait(SY, s_asp[1])
    fa_ev = 0
    for i in range(n_own):
        b.waitm(SY, V_, fa_ev)
        b.dma(SY, f_attn, attn_s.ap()[i * Q:(i + 1) * Q, :], s_fa)
        b.dma(SY, f_x, x_in.ap()[(halo_tiles + i) * Q:(halo_tiles + i + 1) * Q, :], s_fx)
        b.wait(V_, s_fa)
        b.wait(V_, s_fx)
        b.wait(S_, s_fa)
        ln_tile(f_attn, f_attn, f_tmp)
        if i >= 1:
            b.wait(V_, s_x2p, 16 * i)
        b.em(V_, lambda e: e.tensor_tensor(f_x2, f_x, f_attn, ALU.add), inc=True)
        fa_ev = b.ms[V_].n
        b.waitm(SY, V_)
        b.dma(SY, x2_s.ap()[i * Q:(i + 1) * Q, :], f_x2, s_x2p)
        ln_tile(f_x2, f_x3, f_tmp)
        b.waitm(T_, V_)
        for g in range(4):
            bk = g % 2
            bank = pb[bk]
            b.wait(T_, b.ms[bank_eng[bk]], bank_ev[bk])
            for u in range(4):
                k = g * 4 + u
                b.em(T_, lambda e, k=k, bank=bank, u=u: e.transpose(
                    bank[:, u * 128:(u + 1) * 128], f_x3[:, k * 128:(k + 1) * 128], ident[:, :]),
                    inc=(u == 3))
            b.waitm(S_, T_)
            b.em(S_, lambda e, i=i, g=g, bank=bank: e.copy(
                x3hi[:, (i * 16 + g * 4) * 128:(i * 16 + g * 4 + 4) * 128], bank[:, :]), inc=True)
            b.waitm(V_, S_)
            b.em(V_, lambda e, i=i, g=g, bank=bank: e.tensor_tensor(
                x3lo[:, (i * 16 + g * 4) * 128:(i * 16 + g * 4 + 4) * 128], bank[:, :],
                x3hi[:, (i * 16 + g * 4) * 128:(i * 16 + g * 4 + 4) * 128], ALU.subtract), inc=True)
            bank_ev[bk] = b.ms[V_].n; bank_eng[bk] = V_

    s_wg = [b.sem("wg0"), b.sem("wg1")]
    s_hsp = b.sem("hsp")
    b.waitm(T_, V_)
    njo = len(jo)
    wg_ev = [0, 0]
    wg_cv = [0, 0]
    for c in range(NH):
        for part in range(2):
            slot = part
            col0 = part * INTER + c * 128
            b.wait(SY, b.ms[V_], wg_cv[slot])
            for k in range(16):
                b.dma(SY, wgu_r[slot][:, k * 128:(k + 1) * 128],
                      w_gu_t.ap()[k * 128:(k + 1) * 128, col0:col0 + 128], s_wg[slot])
            b.wait(S_, s_wg[slot])
            b.wait(S_, b.ms[T_], wg_ev[slot])
            b.em(S_, lambda e, slot=slot: e.copy(wguhi_r[slot][:, :], wgu_r[slot][:, :]), inc=True)
            b.waitm(V_, S_)
            b.em(V_, lambda e, slot=slot: e.tensor_tensor(
                wgulo_r[slot][:, :], wgu_r[slot][:, :], wguhi_r[slot][:, :], ALU.subtract), inc=True)
            wg_cv[slot] = b.ms[V_].n
            b.waitm(T_, V_, wg_cv[slot])
            for j, (t0, tw) in enumerate(jo):
                bk = part * 4 + j
                bank = pb[bk]
                b.wait(T_, b.ms[bank_eng[bk]], bank_ev[bk])
                ntl = (tw + 127) // 128
                i0 = t0 // 128
                for k in range(16):
                    for wop, xop, st, sp in (
                        (wguhi_r[slot], x3hi_r, k == 0, False),
                        (wguhi_r[slot], x3lo_r, False, False),
                        (wgulo_r[slot], x3hi_r, False, k == 15),
                    ):
                        b.em(T_, lambda e, k=k, bank=bank, i0=i0, ntl=ntl, tw=tw,
                             wop=wop, xop=xop, st=st, sp=sp: e.matmul(
                            bank[:, 0:tw], wop[:, k * 128:(k + 1) * 128],
                            xop[:, i0:i0 + ntl, k:k + 1, :], start=st, stop=sp),
                            inc=sp)
            wg_ev[slot] = b.ms[T_].n
            if part == 0:
                b.waitm(S_, T_)
                b.waitm(S_, V_)
                for j, (t0, tw) in enumerate(jo):
                    b.em(S_, lambda e, j=j, t0=t0, tw=tw: e.activation(
                        gs_b[:, t0:t0 + tw], pb[j][:, 0:tw], AF.Silu), inc=(j == njo - 1))
                for j in range(njo):
                    bank_ev[j] = b.ms[S_].n; bank_eng[j] = S_
            else:
                b.waitm(V_, S_)
                b.waitm(V_, T_)
                if c >= 1:
                    b.wait(V_, s_hsp, 32 * c)
                for j, (t0, tw) in enumerate(jo):
                    b.em(V_, lambda e, j=j, t0=t0, tw=tw: e.tensor_tensor(
                        hb[:, t0:t0 + tw], pb[4 + j][:, 0:tw], gs_b[:, t0:t0 + tw], ALU.mult),
                        inc=(j == njo - 1))
                for j in range(njo):
                    bank_ev[4 + j] = b.ms[V_].n; bank_eng[4 + j] = V_
                b.waitm(S_, V_)
                b.em(S_, lambda e: e.copy(hb_hi[:, :], hb[:, 0:Town]), inc=True)
                b.waitm(V_, S_)
                b.em(V_, lambda e: e.tensor_tensor(hb_lo[:, :], hb[:, 0:Town], hb_hi[:, :],
                                                   ALU.subtract), inc=True)
                b.waitm(SY, V_)
                b.dma(SY, h_sh.ap()[c * 128:(c + 1) * 128, :], hb_hi[:, :], s_hsp)
                b.dma(SY, h_sl.ap()[c * 128:(c + 1) * 128, :], hb_lo[:, :], s_hsp)

    s_hq = b.sem("hq")
    s_wd = [b.sem("wd0"), b.sem("wd1")]
    s_x2l = b.sem("x2l")
    s_out = b.sem("outd")
    b.wait(SY, s_hsp)
    b.wait(SY, s_x2p)
    b.wait(SY, s_wpc)
    n8 = Town // 256
    wd_ev = [0, 0]
    for q in range(n8):
        b.waitm(SY, T_)
        for g4 in range(16):
            b.dma(SY, hq_hi.rearrange("p (a t) -> p a t", a=64)[:, g4 * 4:(g4 + 1) * 4],
                  h_sh.ap().rearrange("(a p) t -> p a t", p=128)[:, g4 * 4:(g4 + 1) * 4,
                                                                q * 256:(q + 1) * 256], s_hq)
            b.dma(SY, hq_lo.rearrange("p (a t) -> p a t", a=64)[:, g4 * 4:(g4 + 1) * 4],
                  h_sl.ap().rearrange("(a p) t -> p a t", p=128)[:, g4 * 4:(g4 + 1) * 4,
                                                                q * 256:(q + 1) * 256], s_hq)
        b.wait(T_, s_hq)
        for k in range(64):
            slot = k % 2
            b.wait(SY, b.ms[T_], wd_ev[slot])
            b.dma(SY, wdhi_r[slot][:, :], wd_sh.ap()[k * 128:(k + 1) * 128, :], s_wd[slot])
            b.dma(SY, wdlo_r[slot][:, :], wd_sl.ap()[k * 128:(k + 1) * 128, :], s_wd[slot])
            b.wait(T_, s_wd[slot])
            if k == 0:
                for bk in range(8):
                    b.wait(T_, b.ms[bank_eng[bk]], bank_ev[bk])
            for tt in range(2):
                for ob in range(4):
                    bk = tt * 4 + ob
                    for pi, (hop, wop, st, sp) in enumerate((
                        (hq_hi, wdhi_r[slot], k == 0, False),
                        (hq_hi, wdlo_r[slot], False, False),
                        (hq_lo, wdhi_r[slot], False, k == 63),
                    )):
                        b.em(T_, lambda e, k=k, tt=tt, ob=ob, bk=bk, hop=hop, wop=wop,
                             st=st, sp=sp: e.matmul(
                            pb[bk][:, 0:512], hop[:, k * 256 + tt * 128:k * 256 + tt * 128 + 128],
                            wop[:, ob * 512:(ob + 1) * 512],
                            start=st, stop=sp), inc=(pi == 2 and tt == 1 and ob == 3))
            wd_ev[slot] = b.ms[T_].n
        dn_mm = b.ms[T_].n
        for tt in range(2):
            gi = q * 2 + tt
            dsl = tt % 2
            b.waitm(V_, T_, dn_mm)
            for ob in range(4):
                b.em(V_, lambda e, dsl=dsl, tt=tt, ob=ob: e.tensor_copy(
                    dtm_r[dsl][:, ob * 512:(ob + 1) * 512], pb[tt * 4 + ob][:, 0:512]),
                    inc=(ob == 3))
            for ob in range(4):
                bank_ev[tt * 4 + ob] = b.ms[V_].n; bank_eng[tt * 4 + ob] = V_
            b.waitm(SY, V_)
            b.dma(SY, h_x2, x2_s.ap()[gi * Q:(gi + 1) * Q, :], s_x2l)
            b.wait(V_, s_x2l)
            b.wait(S_, s_x2l)
            ln_tile(dtm_r[dsl], h_out, h_sq)
            b.em(V_, lambda e: e.tensor_tensor(h_out, h_out, h_x2, ALU.add), inc=True)
            b.waitm(SY, V_)
            b.dma(SY, out_t.ap()[gi * Q:(gi + 1) * Q, :], h_out, s_out)
            b.wait(SY, s_out)

    b.wait(SY, s_out)

    with nc.Block() as block:
        @block.tensor
        def _(e):
            for fn in b.cmds["tensor"]:
                fn(e)

        @block.vector
        def _(e):
            for fn in b.cmds["vector"]:
                fn(e)

        @block.scalar
        def _(e):
            for fn in b.cmds["scalar"]:
                fn(e)

        @block.sync
        def _(e):
            for fn in b.cmds["sync"]:
                fn(e)

        @block.gpsimd
        def _(e):
            for fn in b.cmds["gpsimd"]:
                fn(e)

    nc.finalize()
    return nc


N_CORES = 8
N_TILES = 18
HALO_TILES = 2


def make_inmaps(inputs, n_tiles=N_TILES, halo_tiles=HALO_TILES, n_cores=N_CORES):
    T = n_tiles * Q
    Town = T - halo_tiles * Q
    halo = halo_tiles * Q
    x = np.ascontiguousarray(np.asarray(inputs["L0_attn_hidden_states"], dtype=np.float32)
                             ).reshape(-1, D)
    total = x.shape[0]
    assert total == n_cores * Town, (total, n_cores, Town)
    seq = total // 2
    w_in_a = np.ascontiguousarray(np.asarray(inputs["L0_attn_W_in"], np.float32))
    w_out_a = np.ascontiguousarray(np.asarray(inputs["L0_attn_W_out"], np.float32))
    w_gu_t = np.ascontiguousarray(np.asarray(inputs["L0_ffn_w_gate_up"], np.float32).T)
    w_down_t = np.ascontiguousarray(np.asarray(inputs["L0_ffn_w_down"], np.float32).T)
    cw_raw = np.asarray(inputs["L0_attn_conv_w"], np.float32)
    cb_raw = np.asarray(inputs["L0_attn_conv_b"], np.float32)
    convw = np.zeros((128, 33 * 4), np.float32)
    convb = np.zeros((128, 33), np.float32)
    for a in range(33):
        nch = min(128, NCONV - a * 128)
        convw[0:nch, a * 4:(a + 1) * 4] = cw_raw[a * 128:a * 128 + nch, :]
        convb[0:nch, a] = cb_raw[a * 128:a * 128 + nch]
    alog_a = np.asarray(inputs["L0_attn_A_log"], np.float32).reshape(NH, 1).copy()
    dtbias_a = np.asarray(inputs["L0_attn_dt_bias"], np.float32).reshape(NH, 1).copy()
    dskip_a = np.ascontiguousarray(np.broadcast_to(
        np.asarray(inputs["L0_attn_D_skip"], np.float32)[None, :], (128, NH)))
    normw_a = np.ascontiguousarray(np.broadcast_to(
        np.asarray(inputs["L0_attn_norm_w"], np.float32)[None, :], (128, DIN)))

    in_maps = []
    for core in range(n_cores):
        r0 = core * Town
        start = r0 - halo
        seq_start = (r0 // seq) * seq
        if start < seq_start:
            pad = seq_start - start
            xs = np.concatenate([np.zeros((pad, D), np.float32), x[seq_start:r0 + Town]], axis=0)
            hm = np.zeros((128, max(halo, 1)), np.float32)
            if halo > pad:
                hm[:, pad:] = 1.0
        else:
            xs = np.ascontiguousarray(x[start:r0 + Town])
            hm = np.ones((128, max(halo, 1)), np.float32)
        in_maps.append({
            "x_in": np.ascontiguousarray(xs), "w_in": w_in_a, "w_out": w_out_a,
            "w_gu_t": w_gu_t, "w_down_t": w_down_t, "convw": convw, "convb": convb,
            "alog": alog_a, "dtbias": dtbias_a, "dskip": dskip_a, "normw": normw_a,
            "halomask": np.ascontiguousarray(hm),
        })
    return in_maps


def kernel(**inputs):
    in_maps = make_inmaps(inputs)
    nc = build_program(N_TILES, HALO_TILES)
    res = run_bass_kernel_spmd(nc, in_maps, core_ids=list(range(N_CORES)))
    out = np.concatenate([np.asarray(r["out"]) for r in res.results], axis=0)
    return out.reshape(2, 8192, D).astype(np.float32)


# version 06
# speedup vs baseline: 1.1444x; 1.0151x over previous
import numpy as np
import concourse.bass as bass
import concourse.mybir as mybir
from concourse.bass_utils import run_bass_kernel_spmd

F32 = mybir.dt.float32
BF16 = mybir.dt.bfloat16
AF = mybir.ActivationFunctionType
ALU = mybir.AluOpType
AX = mybir.AxisListType

D = 2048
DIN = 4096
NH = 64
HP = 64
NCONV = 4128
INDIM = 8288
INTER = 8192
EPS = 1e-5
Q = 128
CLAMP = 40.0


class Sem:
    __slots__ = ("h", "n")

    def __init__(self, nc, name):
        self.h = nc.alloc_semaphore(name)
        self.n = 0


class B:
    ENGINES = ("tensor", "vector", "scalar", "sync", "gpsimd")

    def __init__(self, nc):
        self.nc = nc
        self.cmds = {e: [] for e in self.ENGINES}
        self._i = 0
        self.ms = {e: Sem(nc, "ms_" + e) for e in self.ENGINES}

    def sem(self, name=None):
        self._i += 1
        return Sem(self.nc, name or f"s{self._i}")

    def em(self, eng, make, inc=False, sem=None, n=1):
        caps = []
        if inc:
            m = self.ms[eng]
            m.n += 1
            caps.append((m.h, 1))
        if sem is not None:
            sem.n += n
            caps.append((sem.h, n))

        def fn(e, make=make, caps=tuple(caps)):
            ins = make(e)
            for h, k in caps:
                ins = ins.then_inc(h, k)
        self.cmds[eng].append(fn)

    def dma(self, eng, out, in_, sem=None):
        self.em(eng, lambda e, o=out, i=in_: e.dma_start(out=o, in_=i), sem=sem, n=16)

    def wait(self, eng, sem, val=None):
        v = sem.n if val is None else val
        if v <= 0:
            return
        h = sem.h
        self.cmds[eng].append(lambda e, h=h, v=v: e.wait_ge(h, v))

    def waitm(self, eng, src_eng, val=None):
        self.wait(eng, self.ms[src_eng], val)


def build_program(n_tiles, halo_tiles, dbg=False):
    T = n_tiles * Q
    Thalf = T // 2
    nth = n_tiles // 2
    assert n_tiles % 2 == 0
    Town = T - halo_tiles * Q
    n_own = n_tiles - halo_tiles
    assert Town % 256 == 0

    def mov_tiles(total):
        r, t0 = [], 0
        while t0 < total:
            tw = min(512, total - t0)
            r.append((t0, tw))
            t0 += tw
        return r

    jth = mov_tiles(Thalf)
    jo = mov_tiles(Town)

    nc = bass.Bass("TRN2", target_bir_lowering=False, detect_race_conditions=False)
    b = B(nc)
    T_, V_, S_, SY, G_ = "tensor", "vector", "scalar", "sync", "gpsimd"

    x_in = nc.dram_tensor("x_in", [T, D], F32, kind="ExternalInput")
    w_in = nc.dram_tensor("w_in", [D, INDIM], F32, kind="ExternalInput")
    w_out = nc.dram_tensor("w_out", [DIN, D], F32, kind="ExternalInput")
    w_gu_t = nc.dram_tensor("w_gu_t", [D, 2 * INTER], F32, kind="ExternalInput")
    w_down_t = nc.dram_tensor("w_down_t", [INTER, D], F32, kind="ExternalInput")
    convw = nc.dram_tensor("convw", [128, 33 * 4], F32, kind="ExternalInput")
    convb = nc.dram_tensor("convb", [128, 33], F32, kind="ExternalInput")
    alog = nc.dram_tensor("alog", [NH, 1], F32, kind="ExternalInput")
    dtbias = nc.dram_tensor("dtbias", [NH, 1], F32, kind="ExternalInput")
    dskip = nc.dram_tensor("dskip", [128, NH], F32, kind="ExternalInput")
    normw = nc.dram_tensor("normw", [128, DIN], F32, kind="ExternalInput")
    hm_w = max(halo_tiles * Q, 1)
    halomask = nc.dram_tensor("halomask", [128, hm_w], F32, kind="ExternalInput")
    out_t = nc.dram_tensor("out", [Town, D], F32, kind="ExternalOutput")

    kd = "ExternalOutput" if dbg else "Internal"
    gsilu_s = nc.dram_tensor("gsilu_s", [DIN, T], F32, kind=kd)
    xbc_s = nc.dram_tensor("xbc_s", [NCONV, T], F32, kind=kd)
    yn_sh = nc.dram_tensor("yn_sh", [DIN, Town], BF16, kind=kd)
    yn_sl = nc.dram_tensor("yn_sl", [DIN, Town], BF16, kind=kd)
    attn_s = nc.dram_tensor("attn_s", [Town, D], F32, kind=kd)
    x2_s = nc.dram_tensor("x2_s", [Town, D], F32, kind=kd)
    h_sh = nc.dram_tensor("h_sh", [INTER, Town], BF16, kind=kd)
    h_sl = nc.dram_tensor("h_sl", [INTER, Town], BF16, kind=kd)
    wd_sh = nc.dram_tensor("wd_sh", [INTER, D], BF16)
    wd_sl = nc.dram_tensor("wd_sl", [INTER, D], BF16)

    ident_h = nc.inline_tensor(np.eye(128, dtype=np.float32), "c_ident")
    bdneg = np.zeros((2, 64, 4096), np.float32)
    for j in range(32):
        bdneg[0, j, j * 128:(j + 1) * 128] = -1.0
        bdneg[1, 32 + j, j * 128:(j + 1) * 128] = -1.0
    bdneg_h = [nc.inline_tensor(bdneg[0], "c_bdneg0"), nc.inline_tensor(bdneg[1], "c_bdneg1")]
    triu_h = nc.inline_tensor(np.triu(np.ones((128, 128), np.float32)), "c_triu")
    ones1_h = nc.inline_tensor(np.ones((1, 128), np.float32), "c_ones1")
    eye16_h = nc.inline_tensor(np.eye(16, dtype=np.float32), "c_eye16")

    A = nc.alloc_sbuf_tensor
    ident = A("ident", [128, 128], F32)
    triu = A("triu", [128, 128], F32)
    ones1 = A("ones1", [1, 128], F32)
    eye16 = A("eye16", [16, 16], F32)
    cwsb = A("cwsb", [128, 33 * 4], F32)
    cbsb = A("cbsb", [128, 33], F32)
    aneg = A("aneg", [128, 1], F32)
    dtb = A("dtb", [128, 1], F32)
    dsk = A("dsk", [128, NH], F32)
    hmask = A("hmask", [128, hm_w], F32)
    states = A("states", [16, NH * HP], F32)
    stats = A("stats", [128, 8], F32)
    rowst = A("rowst", [1, 64], F32)
    dax16 = A("dax16", [16, NH], F32)
    dAI = A("dAI", [16, 16], F32)
    bw16 = A("bw16", [128, 16], F32)
    sv3 = A("sv3", [128, 33 * 3], F32)
    cvt = A("cvt", [128, 2], F32)

    rem = nc.sbuf_bytes_remaining // 4
    AR = (rem - 512) // 4 * 4
    arena = A("arena", [128, AR], F32)
    ar = arena.ap()

    def rg(off, n):
        assert off + n <= AR, (off, n, AR)
        return ar[:, off:off + n]

    def rgb(off, nb):
        assert nb % 2 == 0 and off + nb // 2 <= AR, (off, nb, AR)
        return ar[:, off:off + nb // 2].bitcast(BF16)

    x1fm_n = nth * 16 * 128
    o = x1fm_n
    XT = o; o += 2 * D
    X1T = o; o += D
    SQB = o; o += D
    WBUF = o; o += 2 * 2048
    WHI = o; o += 2 * 1024
    WLO = o; o += 2 * 1024
    GBUF = o; o += 2 * 512
    XPRE = o; o += Thalf + 4
    XBCB = o; o += 2 * Thalf
    DTRAW = o; o += T
    WDPP = o; o += 4096
    AB_END = o
    assert AB_END <= AR, (AB_END, AR)

    x1hi = rgb(0, x1fm_n)
    x1lo = rgb(x1fm_n // 2, x1fm_n)
    x1hi_r = x1hi.rearrange("p (i k t) -> p i k t", k=16, t=128)
    x1lo_r = x1lo.rearrange("p (i k t) -> p i k t", k=16, t=128)
    xt_r = [rg(XT, D), rg(XT + D, D)]
    x1t = rg(X1T, D)
    sqb = rg(SQB, D)
    wbuf_r = [rg(WBUF, 2048), rg(WBUF + 2048, 2048)]
    whi_r = [rgb(WHI, 2048), rgb(WHI + 1024, 2048)]
    wlo_r = [rgb(WLO, 2048), rgb(WLO + 1024, 2048)]
    gbuf_r = [rg(GBUF, 512), rg(GBUF + 512, 512)]
    xpre = rg(XPRE, Thalf + 4)
    xbc_r = [rg(XBCB, Thalf), rg(XBCB + Thalf, Thalf)]
    dtraw = rg(DTRAW, T)
    wpp_st = rg(WDPP, 2048)
    wpp_hi = rgb(WDPP + 2048, 2048)
    wpp_lo = rgb(WDPP + 3072, 2048)

    o = 0
    BD0 = o; o += 4096
    BD1 = o; o += 4096
    BIGD = o; o += 4096
    XBCC = o; o += 33 * 128
    XTM = o; o += DIN
    YTM = o; o += DIN
    GFM = o; o += DIN
    CST = o; o += 5 * 128
    BCS = o; o += 3 * 128
    YNST = o; o += 4 * 128
    SSD_B = o
    if max(SSD_B, AB_END) + 5 * T + 128 + DIN <= AR:
        CUMO = max(SSD_B, AB_END)
    else:
        CUMO = SSD_B
        assert CUMO + 3 * T <= DTRAW, (CUMO + 3 * T, DTRAW)
    CUMPRO = CUMO + T
    DTTO = CUMPRO + T
    OCPO = DTTO + T
    CUMLOO = OCPO + T
    DTAO = CUMLOO + T
    NWO = DTAO + 128
    SSD_END = NWO + DIN
    assert SSD_END <= AR, (SSD_END, AR)

    bd_r = [rg(BD0, 4096), rg(BD1, 4096)]
    bigD = rg(BIGD, 4096)
    xbcc = rg(XBCC, 33 * 128)
    xtm = rg(XTM, DIN)
    ytm = rg(YTM, DIN)
    gfm = rg(GFM, DIN)
    cumT_sb = rg(CST, 64)
    ecumT = rg(CST + 64, 64)
    wexT = rg(CST + 128, 64)
    wexs = rg(CST + 192, 128)
    bcs = rg(BCS, 384)
    ynst = rg(YNST, 512)
    cum = rg(CUMO, T)
    cumpr = rg(CUMPRO, T)
    ocp = rg(OCPO, T)
    cumlo = rg(CUMLOO, T)
    dt_t = rg(DTTO, T)
    dtA = rg(DTAO, 128)
    nw = rg(NWO, DIN)
    BW8 = NWO + DIN
    bw8_r = [rg(BW8, 128), rg(BW8 + 128, 128)]
    dai8_r = [rg(BW8 + 256, 128), rg(BW8 + 384, 128)]
    assert BW8 + 512 <= AR, (BW8 + 512, AR)

    WOH = 0
    wouth_hi = [rgb(WOH, 32 * 512), rgb(WOH + 16384, 32 * 512)]
    wouth_lo = [rgb(WOH + 8192, 32 * 512), rgb(WOH + 24576, 32 * 512)]
    WST = WOH + 32768
    wst_r = [rg(WST, 512), rg(WST + 512, 512)]
    YST = WST + 1024
    yst_hi = [rgb(YST + s * 4096, 4096) for s in range(3)]
    yst_lo = [rgb(YST + s * 4096 + 2048, 4096) for s in range(3)]
    AST = YST + 3 * 4096
    ast_r = [rg(AST, 512), rg(AST + 512, 512)]
    assert AST + 1024 <= AR, (AST + 1024, AR)

    x3fm_n = n_own * 16 * 128
    x3hi = rgb(0, x3fm_n)
    x3lo = rgb(x3fm_n // 2, x3fm_n)
    x3hi_r = x3hi.rearrange("p (i k t) -> p i k t", k=16, t=128)
    x3lo_r = x3lo.rearrange("p (i k t) -> p i k t", k=16, t=128)
    fo = x3fm_n
    f_attn = rg(fo, D)
    f_x = rg(fo + D, D)
    f_tmp = rg(fo + 2 * D, D)
    f_x2 = rg(fo + 3 * D, D)
    f_x3 = rg(fo + 4 * D, D)
    assert fo + 5 * D <= AR
    wgu_r = [rg(fo, 2048), rg(fo + 2048, 2048)]
    wguhi_r = [rgb(fo + 4096, 2048), rgb(fo + 5120, 2048)]
    wgulo_r = [rgb(fo + 6144, 2048), rgb(fo + 7168, 2048)]
    gs_b = rg(fo + 8192, Town)
    hb = rg(fo + 8192 + Town, Town)
    hb_hi = rgb(fo + 8192 + 2 * Town, Town)
    hb_lo = rgb(fo + 8192 + 2 * Town + Town // 2, Town)
    assert fo + 8192 + 3 * Town <= AR, (fo + 8192 + 3 * Town, AR)

    hq_hi = rgb(0, 64 * 256)
    hq_lo = rgb(64 * 128, 64 * 256)
    HWD = 64 * 256
    wdhi_r = [rgb(HWD, 2048), rgb(HWD + 1024, 2048)]
    wdlo_r = [rgb(HWD + 2048, 2048), rgb(HWD + 3072, 2048)]
    ho = HWD + 4096
    dtm_r = [rg(ho, D), rg(ho + D, D)]
    h_x2 = rg(ho + 2 * D, D)
    h_out = rg(ho + 3 * D, D)
    h_sq = rg(ho + 4 * D, D)
    WCV = ho + 5 * D
    wcv_r = [rg(WCV, 2048), rg(WCV + 2048, 2048)]
    assert WCV + 4096 <= AR, (WCV + 4096, AR)

    pb = [nc.alloc_psum_tensor(f"pb{i}", [128, 512], F32) for i in range(8)]

    s_ld = b.sem("ld")
    b.dma(SY, ident[:, :], ident_h.ap(), s_ld)
    b.dma(SY, triu[:, :], triu_h.ap(), s_ld)
    b.dma(SY, ones1[:, :], ones1_h.ap(), s_ld)
    b.dma(SY, eye16[:, :], eye16_h.ap(), s_ld)
    b.dma(SY, cwsb[:, :], convw.ap()[:, :], s_ld)
    b.dma(SY, cbsb[:, :], convb.ap()[:, :], s_ld)
    b.dma(SY, aneg[0:64, 0:1], alog.ap(), s_ld)
    b.dma(SY, dtb[0:64, 0:1], dtbias.ap(), s_ld)
    b.dma(SY, dsk[:, :], dskip.ap(), s_ld)
    b.dma(SY, hmask[:, :], halomask.ap(), s_ld)

    b.wait(S_, s_ld)
    b.em(S_, lambda e: e.activation(aneg[0:64, 0:1], aneg[0:64, 0:1], AF.Exp, scale=1.0))
    b.em(S_, lambda e: e.mul(aneg[0:64, 0:1], aneg[0:64, 0:1], -1.0), inc=True)
    b.wait(V_, s_ld)
    b.em(V_, lambda e: e.memset(cvt[:, 0:1], float(EPS)))
    b.em(V_, lambda e: e.memset(cvt[:, 1:2], 1.0))
    b.em(V_, lambda e: e.memset(xpre[:, 0:4], 0.0))
    b.em(V_, lambda e: e.memset(wbuf_r[0][:, :], 0.0))
    b.em(V_, lambda e: e.memset(wbuf_r[1][:, :], 0.0))
    b.em(V_, lambda e: e.memset(stats[:, :], 0.0), inc=True)
    a_init_v = b.ms[V_].n
    a_init_s = b.ms[S_].n

    s_ldx = [b.sem("ldx0"), b.sem("ldx1")]
    s_ldw = [b.sem("ldw0"), b.sem("ldw1")]
    s_gsp = [b.sem("gsp0"), b.sem("gsp1")]
    s_xsp = [b.sem("xsp0"), b.sem("xsp1")]
    NCC = (INDIM + 127) // 128

    bank_ev = [0] * 8
    bank_eng = [V_] * 8
    b.wait(T_, s_ld)
    b.waitm(T_, V_, a_init_v)

    xt_ev = [0, 0]
    gate_cnt = 0
    wconv_v = [0, 0]
    s_wpl = b.sem("wpl")
    s_wpc = b.sem("wpc")
    wpp_k = [0]

    def wd_prepass_step():
        k = wpp_k[0]
        if k >= 64:
            return
        wpp_k[0] += 1
        if k > 0:
            b.wait(G_, s_wpc)
        b.dma(G_, wpp_st, w_down_t.ap()[k * 128:(k + 1) * 128, :], s_wpl)
        b.wait(S_, s_wpl)
        b.em(S_, lambda e: e.copy(wpp_hi[:, :], wpp_st), inc=True)
        b.waitm(V_, S_)
        b.em(V_, lambda e: e.tensor_tensor(wpp_lo[:, :], wpp_st, wpp_hi[:, :], ALU.subtract),
             inc=True)
        b.waitm(S_, V_)
        b.em(S_, lambda e, k=k: e.dma_start(out=wd_sh.ap()[k * 128:(k + 1) * 128, :],
                                            in_=wpp_hi[:, :]), sem=s_wpc, n=16)
        b.em(S_, lambda e, k=k: e.dma_start(out=wd_sl.ap()[k * 128:(k + 1) * 128, :],
                                            in_=wpp_lo[:, :]), sem=s_wpc, n=16)
    for half in range(2):
        tb = half * nth
        for li in range(nth):
            gi = tb + li
            sl = li % 2
            b.wait(SY, b.ms[V_], xt_ev[sl])
            b.dma(SY, xt_r[sl], x_in.ap()[gi * Q:(gi + 1) * Q, :], s_ldx[sl])
            b.wait(V_, s_ldx[sl])
            b.em(V_, lambda e, sl=sl: e.reduce_sum(stats[:, 0:1], xt_r[sl], axis=AX.X), inc=True)
            b.wait(S_, s_ldx[sl])
            b.waitm(S_, V_)
            b.em(S_, lambda e, sl=sl: e.activation(sqb, xt_r[sl], AF.Square, accum_out=stats[:, 1:2]),
                 inc=True)
            b.waitm(V_, S_)
            b.em(V_, lambda e: e.tensor_scalar(stats[:, 2:3], stats[:, 0:1], 1.0 / D, None, ALU.mult))
            b.em(V_, lambda e: e.tensor_tensor(stats[:, 3:4], stats[:, 2:3], stats[:, 2:3], ALU.mult))
            b.em(V_, lambda e: e.scalar_tensor_tensor(stats[:, 4:5], stats[:, 1:2], 1.0 / D,
                                                      stats[:, 3:4], ALU.mult, ALU.subtract), inc=True)
            b.waitm(S_, V_)
            b.em(S_, lambda e: e.activation(stats[:, 5:6], stats[:, 4:5], AF.Ln, bias=cvt[:, 0:1], scale=1.0))
            b.em(S_, lambda e: e.activation(stats[:, 6:7], stats[:, 5:6], AF.Exp, scale=-0.5), inc=True)
            b.waitm(V_, S_)
            b.em(V_, lambda e, sl=sl: e.tensor_scalar(x1t, xt_r[sl], stats[:, 2:3], stats[:, 6:7],
                                                      ALU.subtract, ALU.mult), inc=True)
            xt_ev[sl] = b.ms[V_].n
            b.waitm(T_, V_)
            for g in range(4):
                bank = pb[g]
                b.wait(T_, b.ms[bank_eng[g]], bank_ev[g])
                for u in range(4):
                    k = g * 4 + u
                    b.em(T_, lambda e, k=k, bank=bank, u=u: e.transpose(
                        bank[:, u * 128:(u + 1) * 128], x1t[:, k * 128:(k + 1) * 128], ident[:, :]),
                        inc=(u == 3))
                b.waitm(S_, T_)
                b.em(S_, lambda e, li=li, g=g, bank=bank: e.copy(
                    x1hi[:, (li * 16 + g * 4) * 128:(li * 16 + g * 4 + 4) * 128], bank[:, :]), inc=True)
                b.waitm(V_, S_)
                b.em(V_, lambda e, li=li, g=g, bank=bank: e.tensor_tensor(
                    x1lo[:, (li * 16 + g * 4) * 128:(li * 16 + g * 4 + 4) * 128], bank[:, :],
                    x1hi[:, (li * 16 + g * 4) * 128:(li * 16 + g * 4 + 4) * 128], ALU.subtract), inc=True)
                bank_ev[g] = b.ms[V_].n
                bank_eng[g] = V_

        while wpp_k[0] < 64 and half == 1:
            wd_prepass_step()
        b.waitm(T_, V_)
        for c in range(NCC):
            cw = min(128, INDIM - c * 128)
            sl = c % 2
            if c == 0:
                if half == 1:
                    b.waitm(SY, T_)
                for k in range(16):
                    b.dma(SY, wbuf_r[0][:, k * 128:k * 128 + cw],
                          w_in.ap()[k * 128:(k + 1) * 128, 0:cw], s_ldw[0])
                b.wait(S_, s_ldw[0])
                b.em(S_, lambda e, cw=cw: e.copy(whi_r[0][:, 0:16 * 128], wbuf_r[0][:, :]), inc=True)
                b.waitm(V_, S_)
                b.em(V_, lambda e, cw=cw: e.tensor_tensor(
                    wlo_r[0][:, 0:16 * 128], wbuf_r[0][:, :], whi_r[0][:, 0:16 * 128],
                    ALU.subtract), inc=True)
                wconv_v[0] = b.ms[V_].n
            if c + 1 < NCC:
                cw2 = min(128, INDIM - (c + 1) * 128)
                if c >= 1:
                    b.waitm(SY, T_)
                sl2 = (c + 1) % 2
                for k in range(16):
                    b.dma(SY, wbuf_r[sl2][:, k * 128:k * 128 + cw2],
                          w_in.ap()[k * 128:(k + 1) * 128, (c + 1) * 128:(c + 1) * 128 + cw2],
                          s_ldw[sl2])
                b.wait(S_, s_ldw[sl2])
                if c >= 1:
                    b.waitm(S_, T_)
                b.em(S_, lambda e, sl2=sl2: e.copy(whi_r[sl2][:, 0:16 * 128], wbuf_r[sl2][:, :]),
                     inc=True)
                b.waitm(V_, S_)
                b.em(V_, lambda e, sl2=sl2: e.tensor_tensor(
                    wlo_r[sl2][:, 0:16 * 128], wbuf_r[sl2][:, :], whi_r[sl2][:, 0:16 * 128],
                    ALU.subtract), inc=True)
                wconv_v[sl2] = b.ms[V_].n
            wd_prepass_step()
            b.waitm(T_, V_, wconv_v[sl])
            for j, (t0, tw) in enumerate(jth):
                bank = pb[j]
                b.wait(T_, b.ms[bank_eng[j]], bank_ev[j])
                ntl = (tw + 127) // 128
                i0 = t0 // 128
                if c < 64:
                    for k in range(16):
                        for wop, xop, st, sp in (
                            (whi_r[sl], x1hi_r, k == 0, False),
                            (whi_r[sl], x1lo_r, False, False),
                            (wlo_r[sl], x1hi_r, False, k == 15),
                        ):
                            b.em(T_, lambda e, k=k, cw=cw, bank=bank, i0=i0, ntl=ntl, tw=tw,
                                 wop=wop, xop=xop, st=st, sp=sp:
                                 e.matmul(bank[0:cw, 0:tw], wop[:, k * 128:k * 128 + cw],
                                          xop[:, i0:i0 + ntl, k:k + 1, :],
                                          start=st, stop=sp),
                                 inc=sp)
                else:
                    bank2 = pb[3 + j]
                    b.wait(T_, b.ms[bank_eng[3 + j]], bank_ev[3 + j])
                    for k in range(16):
                        for wop, xop, st, sp in (
                            (whi_r[sl], x1hi_r, k == 0, False),
                            (whi_r[sl], x1lo_r, False, False),
                            (wlo_r[sl], x1hi_r, False, k == 15),
                        ):
                            b.em(T_, lambda e, k=k, bank=bank, i0=i0, ntl=ntl, tw=tw,
                                 wop=wop, xop=xop, st=st, sp=sp:
                                 e.matmul(bank[0:32, 0:tw], wop[:, k * 128:k * 128 + 32],
                                          xop[:, i0:i0 + ntl, k:k + 1, :],
                                          start=st, stop=sp))
                            b.em(T_, lambda e, k=k, bank2=bank2, i0=i0, ntl=ntl, tw=tw,
                                 wop=wop, xop=xop, st=st, sp=sp:
                                 e.matmul(bank2[0:64, 0:tw], wop[:, k * 128 + 32:k * 128 + 96],
                                          xop[:, i0:i0 + ntl, k:k + 1, :],
                                          start=st, stop=sp),
                                 inc=sp)
                mmv = b.ms[T_].n
                if c < 32:
                    gsl = gate_cnt % 2
                    b.waitm(S_, T_, mmv)
                    b.wait(S_, s_gsp[gsl])
                    b.em(S_, lambda e, bank=bank, tw=tw, gsl=gsl:
                         e.activation(gbuf_r[gsl][:, 0:tw], bank[0:128, 0:tw], AF.Silu), inc=True)
                    bank_ev[j] = b.ms[S_].n
                    bank_eng[j] = S_
                    b.waitm(SY, S_)
                    b.dma(SY, gsilu_s.ap()[c * 128:(c + 1) * 128,
                                           half * Thalf + t0:half * Thalf + t0 + tw],
                          gbuf_r[gsl][:, 0:tw], s_gsp[gsl])
                    gate_cnt += 1
                elif c < 64:
                    b.waitm(V_, T_, mmv)
                    b.em(V_, lambda e, bank=bank, t0=t0, tw=tw: e.tensor_copy(
                        xpre[:, 4 + t0:4 + t0 + tw], bank[0:128, 0:tw]), inc=True)
                    bank_ev[j] = b.ms[V_].n
                    bank_eng[j] = V_
                else:
                    b.waitm(V_, T_, mmv)
                    b.em(V_, lambda e, bank=bank, t0=t0, tw=tw: e.tensor_copy(
                        xpre[0:32, 4 + t0:4 + t0 + tw], bank[0:32, 0:tw]))
                    b.em(V_, lambda e, j=j, t0=t0, tw=tw, half=half: e.tensor_copy(
                        dtraw[0:64, half * Thalf + t0:half * Thalf + t0 + tw], pb[3 + j][0:64, 0:tw]),
                        inc=True)
                    bank_ev[j] = b.ms[V_].n
                    bank_eng[j] = V_
                    bank_ev[3 + j] = b.ms[V_].n
                    bank_eng[3 + j] = V_
            if c >= 32:
                cc = c - 32
                nch = 128 if c < 64 else 32
                xb = xbc_r[cc % 2]
                b.wait(V_, s_xsp[cc % 2])
                if half == 1:
                    b.em(V_, lambda e, cc=cc, nch=nch: e.tensor_copy(
                        xpre[0:nch, 1:4], sv3[0:nch, cc * 3:cc * 3 + 3]))
                if dbg == 2:
                    b.em(V_, lambda e, nch=nch, xb=xb: e.tensor_copy(
                        xb[0:nch, 0:Thalf], xpre[0:nch, 4:4 + Thalf]))
                b.em(V_, lambda e, cc=cc, nch=nch, xb=xb: (None if dbg == 2 else e.tensor_scalar(
                    xb[0:nch, 0:Thalf], xpre[0:nch, 1:1 + Thalf], cwsb[0:nch, cc * 4:cc * 4 + 1],
                    cbsb[0:nch, cc:cc + 1], ALU.mult, ALU.add)) or e.nop())
                for k in range(1, 4):
                    if dbg == 2:
                        break
                    b.em(V_, lambda e, cc=cc, nch=nch, xb=xb, k=k: e.scalar_tensor_tensor(
                        xb[0:nch, 0:Thalf], xpre[0:nch, 1 + k:1 + k + Thalf],
                        cwsb[0:nch, cc * 4 + k:cc * 4 + k + 1], xb[0:nch, 0:Thalf],
                        ALU.mult, ALU.add))
                if half == 0:
                    b.em(V_, lambda e, cc=cc, nch=nch: e.tensor_copy(
                        sv3[0:nch, cc * 3:cc * 3 + 3], xpre[0:nch, 1 + Thalf:4 + Thalf]))
                    if halo_tiles > 0:
                        b.em(V_, lambda e, nch=nch, xb=xb: e.tensor_tensor(
                            xb[0:nch, 0:halo_tiles * Q], xb[0:nch, 0:halo_tiles * Q],
                            hmask[0:nch, :], ALU.mult))
                b.em(V_, lambda e, nch=nch, xb=xb: e.tensor_copy(xb[0:nch, 0:1], xb[0:nch, 0:1]),
                     inc=True)
                b.waitm(S_, V_)
                b.em(S_, lambda e, nch=nch, xb=xb: (e.nop() if dbg in (2, 3) else e.activation(
                    xb[0:nch, 0:Thalf], xb[0:nch, 0:Thalf], AF.Silu)), inc=True)
                b.waitm(SY, S_)
                b.dma(SY, xbc_s.ap()[cc * 128:cc * 128 + nch, half * Thalf:(half + 1) * Thalf],
                      xb[0:nch, 0:Thalf], s_xsp[cc % 2])

    b.waitm(S_, V_)
    b.em(S_, lambda e: e.activation(dt_t[0:64, :], dtraw[0:64, :], AF.Exp, bias=dtb[0:64, 0:1],
                                    scale=1.0))
    b.em(S_, lambda e: e.activation(dt_t[0:64, :], dt_t[0:64, :], AF.Ln, bias=cvt[0:64, 1:2], scale=1.0))
    b.em(S_, lambda e: e.activation(dtraw[0:64, :], dt_t[0:64, :], AF.Ln, scale=1.0), inc=True)
    b.waitm(V_, S_)
    b.em(V_, lambda e: e.memset(sqb[0:64, 0:Q], 0.0))
    for ci in range(n_tiles):
        b.em(V_, lambda e, ci=ci: e.tensor_scalar(
            dtA[0:64, 0:Q], dt_t[0:64, ci * Q:(ci + 1) * Q], aneg[0:64, 0:1], None, ALU.mult))
        b.em(V_, lambda e, ci=ci: e.tensor_tensor_scan(
            cum[0:64, ci * Q:(ci + 1) * Q], dtA[0:64, 0:Q], sqb[0:64, 0:Q],
            0.0, ALU.add, ALU.add))
    b.em(V_, lambda e: e.tensor_tensor(cumpr[0:64, :], cum[0:64, :], dtraw[0:64, :], ALU.subtract))
    b.em(V_, lambda e: e.memset(ocp[0:64, :], 1.0))
    b.em(V_, lambda e: e.memset(cum[64:128, :], 0.0))
    b.em(V_, lambda e: e.memset(cumpr[64:128, :], 0.0))
    b.em(V_, lambda e: e.memset(wexs[64:128, :], 0.0))
    b.em(V_, lambda e: e.memset(states[:, :], 0.0))
    b.em(V_, lambda e: e.memset(bd_r[0][32:64, :], 0.0))
    b.em(V_, lambda e: e.memset(bd_r[1][0:32, :], 0.0))
    s_bdl = b.sem("bdl")
    b.waitm(SY, V_)
    for hf in range(2):
        b.dma(SY, bd_r[hf][64:128, :], bdneg_h[hf].ap(), s_bdl)
    b.wait(G_, s_bdl)
    b.wait(T_, s_bdl)
    b.em(V_, lambda e: e.tensor_copy(cumlo[0:64, :], cum[0:64, :]), inc=True)
    s_Cd = b.sem("Cd")
    b.waitm(SY, V_)
    b.dma(SY, ocp[64:128, :], cumpr[0:64, :], s_Cd)

    s_xc = b.sem("xc")
    s_gf = b.sem("gf")
    s_rs = b.sem("rs")
    s_ysp = [b.sem(f"ysp{i}") for i in range(4)]
    s_nw = b.sem("nw")
    b.waitm(SY, V_)
    b.dma(SY, nw[:, :], normw.ap(), s_nw)
    b.wait(G_, s_Cd)
    b.wait(T_, s_Cd)
    b.waitm(G_, V_)

    gfm_free_v = 0
    gfm_free_s = 0
    xbcc_free = 0
    yn_cnt = 0

    for ci in range(n_tiles):
        own = ci >= halo_tiles
        oci = ci - halo_tiles
        tsl = slice(ci * Q, (ci + 1) * Q)
        b.waitm(SY, T_, xbcc_free)
        if ci == 0:
            b.wait(SY, s_xsp[0]); b.wait(SY, s_xsp[1])
        for g4 in range(8):
            b.dma(SY, xbcc[:, g4 * 512:(g4 + 1) * 512].rearrange("p (a t) -> p a t", a=4),
                  xbc_s.ap()[0:4096, :].rearrange("(a p) t -> p a t", p=128)[:, g4 * 4:(g4 + 1) * 4, tsl],
                  s_xc)
        b.dma(SY, bcs[0:16, 0:128], xbc_s.ap()[4096:4112, tsl], s_xc)
        b.dma(SY, bcs[0:16, 128:256], xbc_s.ap()[4112:4128, tsl], s_xc)
        if own:
            b.waitm(SY, V_, gfm_free_v)
            b.waitm(SY, S_, gfm_free_s)
            if oci == 0:
                b.wait(SY, s_gsp[0]); b.wait(SY, s_gsp[1])
            for g4 in range(8):
                b.dma(SY, gfm[:, g4 * 512:(g4 + 1) * 512].rearrange("p (a t) -> p a t", a=4),
                      gsilu_s.ap().rearrange("(a p) t -> p a t", p=128)[:, g4 * 4:(g4 + 1) * 4, tsl],
                      s_gf)

        b.waitm(G_, T_)
        for hf in range(2):
            h0 = hf * 32
            srcap = cumlo[h0:h0 + 32, tsl].unsqueeze(1).to_broadcast([32, 32, 128])
            dstap = bd_r[hf][h0:h0 + 32, :].rearrange("p (h t) -> p h t", h=32)
            b.em(G_, lambda e, srcap=srcap, dstap=dstap: e.affine_select(
                dstap, srcap, pattern=[[1, 32], [0, 128]], compare_op=ALU.is_equal,
                fill=0.0, base=0, channel_multiplier=-1), inc=(hf == 1))
        bd_ready = b.ms[G_].n

        b.wait(T_, b.ms[bank_eng[2]], bank_ev[2])
        b.em(T_, lambda e, tsl=tsl: e.transpose(pb[2][:, 0:128], cum[:, tsl], ident[:, :]), inc=True)
        b.waitm(V_, T_)
        b.em(V_, lambda e: e.tensor_copy(cumT_sb[:, :], pb[2][:, 0:64]), inc=True)
        bank_ev[2] = b.ms[V_].n; bank_eng[2] = V_
        b.waitm(S_, V_)
        b.em(S_, lambda e: e.activation(ecumT[:, :], cumT_sb[:, :], AF.Exp, scale=1.0), inc=True)
        b.em(V_, lambda e, ci=ci: e.tensor_scalar(
            wexs[0:64, :], cumpr[0:64, ci * Q:(ci + 1) * Q],
            cum[0:64, ci * Q + Q - 1:ci * Q + Q], -1.0, ALU.subtract, ALU.mult), inc=True)
        b.waitm(S_, V_)
        b.em(S_, lambda e: e.activation(wexs[0:64, :], wexs[0:64, :], AF.Exp, scale=1.0), inc=True)
        b.waitm(T_, S_)
        b.wait(T_, b.ms[bank_eng[3]], bank_ev[3])
        b.em(T_, lambda e: e.transpose(pb[3][:, 0:128], wexs[:, :], ident[:, :]), inc=True)
        b.waitm(V_, T_)
        b.em(V_, lambda e: e.tensor_copy(wexT[:, :], pb[3][:, 0:64]), inc=True)
        bank_ev[3] = b.ms[V_].n; bank_eng[3] = V_
        b.em(S_, lambda e: e.dma_start(out=rowst[0:1, 0:64], in_=ecumT[127:128, :]), sem=s_rs, n=16)
        b.wait(T_, s_rs)
        b.wait(T_, b.ms[bank_eng[3]], bank_ev[3])
        b.em(T_, lambda e: e.matmul(pb[3][0:16, 0:64], ones1[0:1, 0:16], rowst[0:1, 0:64],
                                    start=True, stop=True), inc=True)
        b.waitm(V_, T_)
        b.em(V_, lambda e: e.tensor_copy(dax16[:, :], pb[3][0:16, 0:64]), inc=True)
        bank_ev[3] = b.ms[V_].n; bank_eng[3] = V_

        b.wait(T_, s_xc)
        for g in range(8):
            bk = 2 + g % 2
            bank = pb[bk]
            b.wait(T_, b.ms[bank_eng[bk]], bank_ev[bk])
            for u in range(4):
                a = g * 4 + u
                b.em(T_, lambda e, a=a, bank=bank, u=u: e.transpose(
                    bank[:, u * 128:(u + 1) * 128], xbcc[:, a * 128:(a + 1) * 128], ident[:, :]),
                    inc=(u == 3))
            b.waitm(V_, T_)
            b.em(V_, lambda e, g=g, bank=bank: e.tensor_copy(
                xtm[:, g * 512:(g + 1) * 512], bank[:, :]), inc=True)
            bank_ev[bk] = b.ms[V_].n; bank_eng[bk] = V_
        b.wait(T_, b.ms[bank_eng[2]], bank_ev[2])
        b.em(T_, lambda e: e.transpose(pb[2][:, 0:16], bcs[0:16, 0:128], ident[0:16, 0:16]), inc=True)
        xbcc_free = b.ms[T_].n
        b.waitm(V_, T_)
        b.em(V_, lambda e: e.tensor_copy(bcs[:, 256:272], pb[2][:, 0:16]), inc=True)
        bank_ev[2] = b.ms[V_].n; bank_eng[2] = V_

        b.wait(T_, b.ms[bank_eng[4]], bank_ev[4])
        b.em(T_, lambda e: e.matmul(pb[4][:, 0:128], bcs[0:16, 0:128], bcs[0:16, 128:256],
                                    start=True, stop=True), inc=True)
        b.waitm(V_, T_)
        b.em(V_, lambda e: e.tensor_tensor(wexs[:, :], pb[4][:, 0:128], triu[:, :], ALU.mult), inc=True)
        bank_ev[4] = b.ms[V_].n; bank_eng[4] = V_
        mcb_v = b.ms[V_].n

        for hf in range(2):
            h0 = hf * 32
            b.wait(T_, b.ms[G_], bd_ready)
            for jj in range(8):
                bk = jj % 2
                bank = pb[bk]
                b.wait(T_, b.ms[bank_eng[bk]], bank_ev[bk])
                b.em(T_, lambda e, hf=hf, jj=jj, bank=bank, tsl=tsl: e.matmul(
                    bank[:, 0:512], ocp[:, tsl], bd_r[hf][:, jj * 512:(jj + 1) * 512],
                    start=True, stop=True), inc=True)
                b.waitm(V_, T_)
                b.em(V_, lambda e, jj=jj, bank=bank: e.tensor_scalar(
                    bigD[:, jj * 512:(jj + 1) * 512], bank[:, 0:512], CLAMP, None, ALU.min), inc=True)
                bank_ev[bk] = b.ms[V_].n; bank_eng[bk] = V_
            b.waitm(S_, V_)
            b.em(S_, lambda e: e.activation(bigD[:, :], bigD[:, :], AF.Exp, scale=1.0), inc=True)
            b.waitm(V_, S_)
            b.em(V_, lambda e: e.tensor_tensor(
                bigD.rearrange("p (h t) -> p h t", h=32), bigD.rearrange("p (h t) -> p h t", h=32),
                wexs.unsqueeze(1).to_broadcast([128, 32, 128]), ALU.mult), inc=True)
            s_ready = b.ms[V_].n
            b.waitm(T_, V_, s_ready)
            for g8 in range(4):
                hg = [h0 + g8 * 8 + j for j in range(8)]
                if own:
                    bi = 4 + g8 % 2
                    bs = 6 + g8 % 2
                    b.wait(T_, b.ms[bank_eng[bi]], bank_ev[bi])
                    b.wait(T_, b.ms[bank_eng[bs]], bank_ev[bs])
                    for j, h in enumerate(hg):
                        hh = h - h0
                        b.em(T_, lambda e, hh=hh, h=h, bi=bi, j=j: e.matmul(
                            pb[bi][:, j * 64:(j + 1) * 64], bigD[:, hh * 128:(hh + 1) * 128],
                            xtm[:, h * 64:(h + 1) * 64], start=True, stop=True))
                        b.em(T_, lambda e, h=h, bs=bs, j=j: e.matmul(
                            pb[bs][:, j * 64:(j + 1) * 64], bcs[0:16, 128:256],
                            states[0:16, h * 64:(h + 1) * 64], start=True, stop=True),
                            inc=(j == 7))
                    ymm = b.ms[T_].n
                    b.waitm(V_, T_, ymm)
                    for j, h in enumerate(hg):
                        b.em(V_, lambda e, h=h, bi=bi, j=j: e.scalar_tensor_tensor(
                            ytm[:, h * 64:(h + 1) * 64], xtm[:, h * 64:(h + 1) * 64],
                            dsk[:, h:h + 1], pb[bi][:, j * 64:(j + 1) * 64], ALU.mult, ALU.add))
                        b.em(V_, lambda e, h=h, bs=bs, j=j: e.scalar_tensor_tensor(
                            ytm[:, h * 64:(h + 1) * 64], pb[bs][:, j * 64:(j + 1) * 64],
                            ecumT[:, h:h + 1], ytm[:, h * 64:(h + 1) * 64], ALU.mult, ALU.add),
                            inc=(j == 7))
                    bank_ev[bi] = b.ms[V_].n; bank_eng[bi] = V_
                    bank_ev[bs] = b.ms[V_].n; bank_eng[bs] = V_
                wsl8 = g8 % 2
                b.waitm(V_, T_)
                for j, h in enumerate(hg):
                    b.em(V_, lambda e, h=h, j=j, wsl8=wsl8: e.tensor_scalar(
                        bw8_r[wsl8][:, j * 16:(j + 1) * 16], bcs[:, 256:272],
                        wexT[:, h:h + 1], None, ALU.mult))
                    b.em(V_, lambda e, h=h, j=j, wsl8=wsl8: e.tensor_scalar(
                        dai8_r[wsl8][0:16, j * 16:(j + 1) * 16], eye16[:, :],
                        dax16[:, h:h + 1], None, ALU.mult), inc=(j == 7))
                bwv = b.ms[V_].n
                bu = 2 + g8 % 2
                b.waitm(T_, V_, bwv)
                b.wait(T_, b.ms[bank_eng[bu]], bank_ev[bu])
                for j, h in enumerate(hg):
                    b.em(T_, lambda e, h=h, bu=bu, j=j, wsl8=wsl8: e.matmul(
                        pb[bu][0:16, j * 64:(j + 1) * 64], bw8_r[wsl8][:, j * 16:(j + 1) * 16],
                        xtm[:, h * 64:(h + 1) * 64], start=True, stop=False))
                    b.em(T_, lambda e, h=h, bu=bu, j=j, wsl8=wsl8: e.matmul(
                        pb[bu][0:16, j * 64:(j + 1) * 64], dai8_r[wsl8][0:16, j * 16:(j + 1) * 16],
                        states[0:16, h * 64:(h + 1) * 64], start=False, stop=True),
                        inc=(j == 7))
                b.waitm(V_, T_)
                b.em(V_, lambda e, h0=h0, g8=g8, bu=bu: e.tensor_copy(
                    states[0:16, (h0 + g8 * 8) * 64:(h0 + g8 * 8 + 8) * 64],
                    pb[bu][0:16, 0:512]), inc=True)
                bank_ev[bu] = b.ms[V_].n; bank_eng[bu] = V_

        if own:
            b.wait(T_, s_gf)
            for g in range(32):
                bk = 2 + g % 2
                bank = pb[bk]
                b.wait(T_, b.ms[bank_eng[bk]], bank_ev[bk])
                b.em(T_, lambda e, g=g, bank=bank: e.transpose(
                    bank[:, 0:128], gfm[:, g * 128:(g + 1) * 128], ident[:, :]), inc=True)
                b.waitm(V_, T_)
                b.em(V_, lambda e, g=g, bank=bank: e.tensor_tensor(
                    ytm[:, g * 128:(g + 1) * 128], bank[:, 0:128],
                    ytm[:, g * 128:(g + 1) * 128], ALU.mult), inc=True)
                bank_ev[bk] = b.ms[V_].n; bank_eng[bk] = V_
            gfm_free_v = b.ms[V_].n
            b.waitm(S_, V_)
            b.em(S_, lambda e: e.activation(gfm[:, :], ytm[:, :], AF.Square, accum_out=stats[:, 1:2]))
            b.em(S_, lambda e: e.activation(stats[:, 5:6], stats[:, 1:2], AF.Ln, bias=cvt[:, 0:1],
                                            scale=1.0 / DIN))
            b.em(S_, lambda e: e.activation(stats[:, 6:7], stats[:, 5:6], AF.Exp, scale=-0.5), inc=True)
            gfm_free_s = b.ms[S_].n
            b.waitm(V_, S_)
            b.wait(V_, s_nw)
            b.em(V_, lambda e: e.scalar_tensor_tensor(ytm[:, :], ytm[:, :], stats[:, 6:7], nw[:, :],
                                                      ALU.mult, ALU.mult), inc=True)
            ynv = b.ms[V_].n
            b.waitm(T_, V_, ynv)
            for g in range(32):
                bk = 2 + g % 2
                bank = pb[bk]
                ysl = yn_cnt % 4
                yh = rgb(YNST + ysl * 128, 256)
                yl = rgb(YNST + ysl * 128 + 64, 256)
                b.wait(T_, b.ms[bank_eng[bk]], bank_ev[bk])
                b.em(T_, lambda e, g=g, bank=bank: e.transpose(
                    bank[:, 0:128], ytm[:, g * 128:(g + 1) * 128], ident[:, :]), inc=True)
                b.waitm(S_, T_)
                b.wait(S_, s_ysp[ysl])
                b.em(S_, lambda e, yh=yh, bank=bank: e.copy(yh[:, 0:128], bank[:, 0:128]), inc=True)
                b.waitm(V_, S_)
                b.em(V_, lambda e, yh=yh, yl=yl, bank=bank: e.tensor_tensor(
                    yl[:, 0:128], bank[:, 0:128], yh[:, 0:128], ALU.subtract), inc=True)
                bank_ev[bk] = b.ms[V_].n; bank_eng[bk] = V_
                b.waitm(SY, V_)
                b.dma(SY, yn_sh.ap()[g * 128:(g + 1) * 128, oci * Q:(oci + 1) * Q],
                      yh[:, 0:128], s_ysp[ysl])
                b.dma(SY, yn_sl.ap()[g * 128:(g + 1) * 128, oci * Q:(oci + 1) * Q],
                      yl[:, 0:128], s_ysp[ysl])
                yn_cnt += 1

    s_wo = [b.sem("wo0"), b.sem("wo1")]
    s_yl = [b.sem(f"yl{s}") for s in range(3)]
    s_asp = [b.sem("asp0"), b.sem("asp1")]
    yst_ev = [0, 0, 0]
    for i in range(4):
        b.wait(SY, s_ysp[i])
    ecnt = 0
    wst_cv = [0, 0]
    quarter_mm_done = [0, 0]
    wo_conv_q = [0, 0]

    def conv_wouth(oq):
        wq = oq % 2
        for k in range(32):
            wsl = k % 2
            b.wait(SY, b.ms[V_], wst_cv[wsl])
            b.dma(SY, wst_r[wsl],
                  w_out.ap()[k * 128:(k + 1) * 128, oq * 512:(oq + 1) * 512], s_wo[wsl])
            b.wait(S_, s_wo[wsl])
            if oq >= 2:
                b.wait(S_, b.ms[T_], quarter_mm_done[wq])
            b.em(S_, lambda e, k=k, wsl=wsl, wq=wq: e.copy(
                wouth_hi[wq][:, k * 512:(k + 1) * 512], wst_r[wsl]), inc=True)
            b.waitm(V_, S_)
            b.em(V_, lambda e, k=k, wsl=wsl, wq=wq: e.tensor_tensor(
                wouth_lo[wq][:, k * 512:(k + 1) * 512], wst_r[wsl],
                wouth_hi[wq][:, k * 512:(k + 1) * 512], ALU.subtract), inc=True)
            wst_cv[wsl] = b.ms[V_].n
        wo_conv_q[wq] = b.ms[V_].n

    conv_wouth(0)
    for oq in range(4):
        wq = oq % 2
        wo_conv = wo_conv_q[wq]
        for i in range(n_own):
            if i == 1 and oq < 3:
                conv_wouth(oq + 1)
            sl = i % 3
            if oq > 0 or i >= 3:
                b.wait(SY, b.ms[T_], yst_ev[sl])
            for gsl4 in range(8):
                b.dma(SY, yst_hi[sl].rearrange("p (a t) -> p a t", a=32)[:, gsl4 * 4:(gsl4 + 1) * 4],
                      yn_sh.ap().rearrange("(a p) t -> p a t", p=128)[:, gsl4 * 4:(gsl4 + 1) * 4,
                                                                     i * Q:(i + 1) * Q], s_yl[sl])
                b.dma(SY, yst_lo[sl].rearrange("p (a t) -> p a t", a=32)[:, gsl4 * 4:(gsl4 + 1) * 4],
                      yn_sl.ap().rearrange("(a p) t -> p a t", p=128)[:, gsl4 * 4:(gsl4 + 1) * 4,
                                                                     i * Q:(i + 1) * Q], s_yl[sl])
            b.wait(T_, s_yl[sl])
            if i == 0:
                b.waitm(T_, V_, wo_conv)
            bk = ecnt % 2
            b.wait(T_, b.ms[bank_eng[bk]], bank_ev[bk])
            for k in range(32):
                for yop, wop, st, sp in (
                    (yst_hi[sl], wouth_hi[wq], k == 0, False),
                    (yst_hi[sl], wouth_lo[wq], False, False),
                    (yst_lo[sl], wouth_hi[wq], False, k == 31),
                ):
                    b.em(T_, lambda e, k=k, bk=bk, yop=yop, wop=wop, st=st, sp=sp: e.matmul(
                        pb[bk][:, 0:512], yop[:, k * 128:(k + 1) * 128],
                        wop[:, k * 512:(k + 1) * 512],
                        start=st, stop=sp), inc=sp)
            yst_ev[sl] = b.ms[T_].n
            asl = ecnt % 2
            b.waitm(V_, T_)
            b.wait(V_, s_asp[asl])
            b.em(V_, lambda e, asl=asl, bk=bk: e.tensor_copy(
                ast_r[asl][:, 0:512], pb[bk][:, 0:512]), inc=True)
            bank_ev[bk] = b.ms[V_].n; bank_eng[bk] = V_
            b.waitm(SY, V_)
            b.dma(SY, attn_s.ap()[i * Q:(i + 1) * Q, oq * 512:(oq + 1) * 512],
                  ast_r[asl][:, 0:512], s_asp[asl])
            ecnt += 1
        quarter_mm_done[wq] = b.ms[T_].n

    s_fa = b.sem("fa")
    s_fx = b.sem("fx")
    s_x2p = b.sem("x2p")

    def ln_tile(src, dst, scr):
        b.em(V_, lambda e: e.reduce_sum(stats[:, 0:1], src, axis=AX.X), inc=True)
        b.waitm(S_, V_)
        b.em(S_, lambda e: e.activation(scr, src, AF.Square, accum_out=stats[:, 1:2]), inc=True)
        b.waitm(V_, S_)
        b.em(V_, lambda e: e.tensor_scalar(stats[:, 2:3], stats[:, 0:1], 1.0 / D, None, ALU.mult))
        b.em(V_, lambda e: e.tensor_tensor(stats[:, 3:4], stats[:, 2:3], stats[:, 2:3], ALU.mult))
        b.em(V_, lambda e: e.scalar_tensor_tensor(stats[:, 4:5], stats[:, 1:2], 1.0 / D,
                                                  stats[:, 3:4], ALU.mult, ALU.subtract), inc=True)
        b.waitm(S_, V_)
        b.em(S_, lambda e: e.activation(stats[:, 5:6], stats[:, 4:5], AF.Ln, bias=cvt[:, 0:1], scale=1.0))
        b.em(S_, lambda e: e.activation(stats[:, 6:7], stats[:, 5:6], AF.Exp, scale=-0.5), inc=True)
        b.waitm(V_, S_)
        b.em(V_, lambda e: e.tensor_scalar(dst, src, stats[:, 2:3], stats[:, 6:7],
                                           ALU.subtract, ALU.mult), inc=True)

    b.wait(SY, s_asp[0]); b.wait(SY, s_asp[1])
    fa_ev = 0
    for i in range(n_own):
        b.waitm(SY, V_, fa_ev)
        b.dma(SY, f_attn, attn_s.ap()[i * Q:(i + 1) * Q, :], s_fa)
        b.dma(SY, f_x, x_in.ap()[(halo_tiles + i) * Q:(halo_tiles + i + 1) * Q, :], s_fx)
        b.wait(V_, s_fa)
        b.wait(V_, s_fx)
        b.wait(S_, s_fa)
        ln_tile(f_attn, f_attn, f_tmp)
        if i >= 1:
            b.wait(V_, s_x2p, 16 * i)
        b.em(V_, lambda e: e.tensor_tensor(f_x2, f_x, f_attn, ALU.add), inc=True)
        fa_ev = b.ms[V_].n
        b.waitm(SY, V_)
        b.dma(SY, x2_s.ap()[i * Q:(i + 1) * Q, :], f_x2, s_x2p)
        ln_tile(f_x2, f_x3, f_tmp)
        b.waitm(T_, V_)
        for g in range(4):
            bk = g % 2
            bank = pb[bk]
            b.wait(T_, b.ms[bank_eng[bk]], bank_ev[bk])
            for u in range(4):
                k = g * 4 + u
                b.em(T_, lambda e, k=k, bank=bank, u=u: e.transpose(
                    bank[:, u * 128:(u + 1) * 128], f_x3[:, k * 128:(k + 1) * 128], ident[:, :]),
                    inc=(u == 3))
            b.waitm(S_, T_)
            b.em(S_, lambda e, i=i, g=g, bank=bank: e.copy(
                x3hi[:, (i * 16 + g * 4) * 128:(i * 16 + g * 4 + 4) * 128], bank[:, :]), inc=True)
            b.waitm(V_, S_)
            b.em(V_, lambda e, i=i, g=g, bank=bank: e.tensor_tensor(
                x3lo[:, (i * 16 + g * 4) * 128:(i * 16 + g * 4 + 4) * 128], bank[:, :],
                x3hi[:, (i * 16 + g * 4) * 128:(i * 16 + g * 4 + 4) * 128], ALU.subtract), inc=True)
            bank_ev[bk] = b.ms[V_].n; bank_eng[bk] = V_

    s_wg = [b.sem("wg0"), b.sem("wg1")]
    s_hsp = b.sem("hsp")
    b.waitm(T_, V_)
    njo = len(jo)
    wg_ev = [0, 0]
    wg_cv = [0, 0]
    for c in range(NH):
        for part in range(2):
            slot = part
            col0 = part * INTER + c * 128
            b.wait(SY, b.ms[V_], wg_cv[slot])
            for k in range(16):
                b.dma(SY, wgu_r[slot][:, k * 128:(k + 1) * 128],
                      w_gu_t.ap()[k * 128:(k + 1) * 128, col0:col0 + 128], s_wg[slot])
            b.wait(S_, s_wg[slot])
            b.wait(S_, b.ms[T_], wg_ev[slot])
            b.em(S_, lambda e, slot=slot: e.copy(wguhi_r[slot][:, :], wgu_r[slot][:, :]), inc=True)
            b.waitm(V_, S_)
            b.em(V_, lambda e, slot=slot: e.tensor_tensor(
                wgulo_r[slot][:, :], wgu_r[slot][:, :], wguhi_r[slot][:, :], ALU.subtract), inc=True)
            wg_cv[slot] = b.ms[V_].n
            b.waitm(T_, V_, wg_cv[slot])
            for j, (t0, tw) in enumerate(jo):
                bk = part * 4 + j
                bank = pb[bk]
                b.wait(T_, b.ms[bank_eng[bk]], bank_ev[bk])
                ntl = (tw + 127) // 128
                i0 = t0 // 128
                for k in range(16):
                    for wop, xop, st, sp in (
                        (wguhi_r[slot], x3hi_r, k == 0, False),
                        (wguhi_r[slot], x3lo_r, False, False),
                        (wgulo_r[slot], x3hi_r, False, k == 15),
                    ):
                        b.em(T_, lambda e, k=k, bank=bank, i0=i0, ntl=ntl, tw=tw,
                             wop=wop, xop=xop, st=st, sp=sp: e.matmul(
                            bank[:, 0:tw], wop[:, k * 128:(k + 1) * 128],
                            xop[:, i0:i0 + ntl, k:k + 1, :], start=st, stop=sp),
                            inc=sp)
            wg_ev[slot] = b.ms[T_].n
            if part == 0:
                b.waitm(S_, T_)
                b.waitm(S_, V_)
                for j, (t0, tw) in enumerate(jo):
                    b.em(S_, lambda e, j=j, t0=t0, tw=tw: e.activation(
                        gs_b[:, t0:t0 + tw], pb[j][:, 0:tw], AF.Silu), inc=(j == njo - 1))
                for j in range(njo):
                    bank_ev[j] = b.ms[S_].n; bank_eng[j] = S_
            else:
                b.waitm(V_, S_)
                b.waitm(V_, T_)
                if c >= 1:
                    b.wait(V_, s_hsp, 32 * c)
                for j, (t0, tw) in enumerate(jo):
                    b.em(V_, lambda e, j=j, t0=t0, tw=tw: e.tensor_tensor(
                        hb[:, t0:t0 + tw], pb[4 + j][:, 0:tw], gs_b[:, t0:t0 + tw], ALU.mult),
                        inc=(j == njo - 1))
                for j in range(njo):
                    bank_ev[4 + j] = b.ms[V_].n; bank_eng[4 + j] = V_
                b.waitm(S_, V_)
                b.em(S_, lambda e: e.copy(hb_hi[:, :], hb[:, 0:Town]), inc=True)
                b.waitm(V_, S_)
                b.em(V_, lambda e: e.tensor_tensor(hb_lo[:, :], hb[:, 0:Town], hb_hi[:, :],
                                                   ALU.subtract), inc=True)
                b.waitm(SY, V_)
                b.dma(SY, h_sh.ap()[c * 128:(c + 1) * 128, :], hb_hi[:, :], s_hsp)
                b.dma(SY, h_sl.ap()[c * 128:(c + 1) * 128, :], hb_lo[:, :], s_hsp)

    s_hq = b.sem("hq")
    s_wd = [b.sem("wd0"), b.sem("wd1")]
    s_x2l = b.sem("x2l")
    s_out = b.sem("outd")
    b.wait(SY, s_hsp)
    b.wait(SY, s_x2p)
    b.wait(SY, s_wpc)
    n8 = Town // 256
    wd_ev = [0, 0]
    hq_half_mm = [0, 0]
    s_hqh = [b.sem("hqh0"), b.sem("hqh1")]

    def load_hq_half(q, hh):
        for g4 in range(8):
            a0 = hh * 32 + g4 * 4
            b.dma(SY, hq_hi.rearrange("p (a t) -> p a t", a=64)[:, a0:a0 + 4],
                  h_sh.ap().rearrange("(a p) t -> p a t", p=128)[:, a0:a0 + 4,
                                                                q * 256:(q + 1) * 256], s_hqh[hh])
            b.dma(SY, hq_lo.rearrange("p (a t) -> p a t", a=64)[:, a0:a0 + 4],
                  h_sl.ap().rearrange("(a p) t -> p a t", p=128)[:, a0:a0 + 4,
                                                                q * 256:(q + 1) * 256], s_hqh[hh])

    load_hq_half(0, 0)
    load_hq_half(0, 1)
    for q in range(n8):
        for k in range(64):
            if k == 0:
                b.wait(T_, s_hqh[0])
            if k == 32:
                b.wait(T_, s_hqh[1])
            if k == 32 and q + 1 < n8:
                b.wait(SY, b.ms[T_], hq_half_mm[0])
                load_hq_half(q + 1, 0)
            slot = k % 2
            b.wait(SY, b.ms[T_], wd_ev[slot])
            b.dma(SY, wdhi_r[slot][:, :], wd_sh.ap()[k * 128:(k + 1) * 128, :], s_wd[slot])
            b.dma(SY, wdlo_r[slot][:, :], wd_sl.ap()[k * 128:(k + 1) * 128, :], s_wd[slot])
            b.wait(T_, s_wd[slot])
            if k == 0:
                for bk in range(8):
                    b.wait(T_, b.ms[bank_eng[bk]], bank_ev[bk])
            for tt in range(2):
                for ob in range(4):
                    bk = tt * 4 + ob
                    for pi, (hop, wop, st, sp) in enumerate((
                        (hq_hi, wdhi_r[slot], k == 0, False),
                        (hq_hi, wdlo_r[slot], False, False),
                        (hq_lo, wdhi_r[slot], False, k == 63),
                    )):
                        b.em(T_, lambda e, k=k, tt=tt, ob=ob, bk=bk, hop=hop, wop=wop,
                             st=st, sp=sp: e.matmul(
                            pb[bk][:, 0:512], hop[:, k * 256 + tt * 128:k * 256 + tt * 128 + 128],
                            wop[:, ob * 512:(ob + 1) * 512],
                            start=st, stop=sp), inc=(pi == 2 and tt == 1 and ob == 3))
            wd_ev[slot] = b.ms[T_].n
            if k == 31:
                hq_half_mm[0] = b.ms[T_].n
        hq_half_mm[1] = b.ms[T_].n
        if q + 1 < n8:
            b.waitm(SY, T_)
            load_hq_half(q + 1, 1)
        dn_mm = b.ms[T_].n
        for tt in range(2):
            gi = q * 2 + tt
            dsl = tt % 2
            b.waitm(V_, T_, dn_mm)
            for ob in range(4):
                b.em(V_, lambda e, dsl=dsl, tt=tt, ob=ob: e.tensor_copy(
                    dtm_r[dsl][:, ob * 512:(ob + 1) * 512], pb[tt * 4 + ob][:, 0:512]),
                    inc=(ob == 3))
            for ob in range(4):
                bank_ev[tt * 4 + ob] = b.ms[V_].n; bank_eng[tt * 4 + ob] = V_
            b.waitm(SY, V_)
            b.dma(SY, h_x2, x2_s.ap()[gi * Q:(gi + 1) * Q, :], s_x2l)
            b.wait(V_, s_x2l)
            b.wait(S_, s_x2l)
            ln_tile(dtm_r[dsl], h_out, h_sq)
            b.em(V_, lambda e: e.tensor_tensor(h_out, h_out, h_x2, ALU.add), inc=True)
            b.waitm(SY, V_)
            b.dma(SY, out_t.ap()[gi * Q:(gi + 1) * Q, :], h_out, s_out)
            b.wait(SY, s_out)

    b.wait(SY, s_out)

    with nc.Block() as block:
        @block.tensor
        def _(e):
            for fn in b.cmds["tensor"]:
                fn(e)

        @block.vector
        def _(e):
            for fn in b.cmds["vector"]:
                fn(e)

        @block.scalar
        def _(e):
            for fn in b.cmds["scalar"]:
                fn(e)

        @block.sync
        def _(e):
            for fn in b.cmds["sync"]:
                fn(e)

        @block.gpsimd
        def _(e):
            for fn in b.cmds["gpsimd"]:
                fn(e)

    nc.finalize()
    return nc


N_CORES = 8
N_TILES = 18
HALO_TILES = 2


def make_inmaps(inputs, n_tiles=N_TILES, halo_tiles=HALO_TILES, n_cores=N_CORES):
    T = n_tiles * Q
    Town = T - halo_tiles * Q
    halo = halo_tiles * Q
    x = np.ascontiguousarray(np.asarray(inputs["L0_attn_hidden_states"], dtype=np.float32)
                             ).reshape(-1, D)
    total = x.shape[0]
    assert total == n_cores * Town, (total, n_cores, Town)
    seq = total // 2
    w_in_a = np.ascontiguousarray(np.asarray(inputs["L0_attn_W_in"], np.float32))
    w_out_a = np.ascontiguousarray(np.asarray(inputs["L0_attn_W_out"], np.float32))
    w_gu_t = np.ascontiguousarray(np.asarray(inputs["L0_ffn_w_gate_up"], np.float32).T)
    w_down_t = np.ascontiguousarray(np.asarray(inputs["L0_ffn_w_down"], np.float32).T)
    cw_raw = np.asarray(inputs["L0_attn_conv_w"], np.float32)
    cb_raw = np.asarray(inputs["L0_attn_conv_b"], np.float32)
    convw = np.zeros((128, 33 * 4), np.float32)
    convb = np.zeros((128, 33), np.float32)
    for a in range(33):
        nch = min(128, NCONV - a * 128)
        convw[0:nch, a * 4:(a + 1) * 4] = cw_raw[a * 128:a * 128 + nch, :]
        convb[0:nch, a] = cb_raw[a * 128:a * 128 + nch]
    alog_a = np.asarray(inputs["L0_attn_A_log"], np.float32).reshape(NH, 1).copy()
    dtbias_a = np.asarray(inputs["L0_attn_dt_bias"], np.float32).reshape(NH, 1).copy()
    dskip_a = np.ascontiguousarray(np.broadcast_to(
        np.asarray(inputs["L0_attn_D_skip"], np.float32)[None, :], (128, NH)))
    normw_a = np.ascontiguousarray(np.broadcast_to(
        np.asarray(inputs["L0_attn_norm_w"], np.float32)[None, :], (128, DIN)))

    in_maps = []
    for core in range(n_cores):
        r0 = core * Town
        start = r0 - halo
        seq_start = (r0 // seq) * seq
        if start < seq_start:
            pad = seq_start - start
            xs = np.concatenate([np.zeros((pad, D), np.float32), x[seq_start:r0 + Town]], axis=0)
            hm = np.zeros((128, max(halo, 1)), np.float32)
            if halo > pad:
                hm[:, pad:] = 1.0
        else:
            xs = np.ascontiguousarray(x[start:r0 + Town])
            hm = np.ones((128, max(halo, 1)), np.float32)
        in_maps.append({
            "x_in": np.ascontiguousarray(xs), "w_in": w_in_a, "w_out": w_out_a,
            "w_gu_t": w_gu_t, "w_down_t": w_down_t, "convw": convw, "convb": convb,
            "alog": alog_a, "dtbias": dtbias_a, "dskip": dskip_a, "normw": normw_a,
            "halomask": np.ascontiguousarray(hm),
        })
    return in_maps


def kernel(**inputs):
    in_maps = make_inmaps(inputs)
    nc = build_program(N_TILES, HALO_TILES)
    res = run_bass_kernel_spmd(nc, in_maps, core_ids=list(range(N_CORES)))
    out = np.concatenate([np.asarray(r["out"]) for r in res.results], axis=0)
    return out.reshape(2, 8192, D).astype(np.float32)


# version 07
# speedup vs baseline: 1.2320x; 1.0766x over previous
import numpy as np
import concourse.bass as bass
import concourse.mybir as mybir
from concourse.bass_utils import run_bass_kernel_spmd

F32 = mybir.dt.float32
BF16 = mybir.dt.bfloat16
AF = mybir.ActivationFunctionType
ALU = mybir.AluOpType
AX = mybir.AxisListType

D = 2048
DIN = 4096
NH = 64
HP = 64
NCONV = 4128
INDIM = 8288
INTER = 8192
EPS = 1e-5
Q = 128
CLAMP = 40.0


class Sem:
    __slots__ = ("h", "n")

    def __init__(self, nc, name):
        self.h = nc.alloc_semaphore(name)
        self.n = 0


class B:
    ENGINES = ("tensor", "vector", "scalar", "sync", "gpsimd")

    def __init__(self, nc):
        self.nc = nc
        self.cmds = {e: [] for e in self.ENGINES}
        self._i = 0
        self.ms = {e: Sem(nc, "ms_" + e) for e in self.ENGINES}

    def sem(self, name=None):
        self._i += 1
        return Sem(self.nc, name or f"s{self._i}")

    def em(self, eng, make, inc=False, sem=None, n=1):
        caps = []
        if inc:
            m = self.ms[eng]
            m.n += 1
            caps.append((m.h, 1))
        if sem is not None:
            sem.n += n
            caps.append((sem.h, n))

        def fn(e, make=make, caps=tuple(caps)):
            ins = make(e)
            for h, k in caps:
                ins = ins.then_inc(h, k)
        self.cmds[eng].append(fn)

    def dma(self, eng, out, in_, sem=None):
        self.em(eng, lambda e, o=out, i=in_: e.dma_start(out=o, in_=i), sem=sem, n=16)

    def wait(self, eng, sem, val=None):
        v = sem.n if val is None else val
        if v <= 0:
            return
        h = sem.h
        self.cmds[eng].append(lambda e, h=h, v=v: e.wait_ge(h, v))

    def waitm(self, eng, src_eng, val=None):
        self.wait(eng, self.ms[src_eng], val)


def build_program(n_tiles, halo_tiles, dbg=False):
    T = n_tiles * Q
    Thalf = T // 2
    nth = n_tiles // 2
    assert n_tiles % 2 == 0
    Town = T - halo_tiles * Q
    n_own = n_tiles - halo_tiles
    assert Town % 256 == 0

    def mov_tiles(total):
        r, t0 = [], 0
        while t0 < total:
            tw = min(512, total - t0)
            r.append((t0, tw))
            t0 += tw
        return r

    jth = mov_tiles(Thalf)
    jo = mov_tiles(Town)

    nc = bass.Bass("TRN2", target_bir_lowering=False, detect_race_conditions=False)
    b = B(nc)
    T_, V_, S_, SY, G_ = "tensor", "vector", "scalar", "sync", "gpsimd"

    x_in = nc.dram_tensor("x_in", [T, D], F32, kind="ExternalInput")
    w_in = nc.dram_tensor("w_in", [D, INDIM], F32, kind="ExternalInput")
    w_out = nc.dram_tensor("w_out", [DIN, D], F32, kind="ExternalInput")
    w_gu_t = nc.dram_tensor("w_gu_t", [D, 2 * INTER], F32, kind="ExternalInput")
    w_down_t = nc.dram_tensor("w_down_t", [INTER, D], F32, kind="ExternalInput")
    convw = nc.dram_tensor("convw", [128, 33 * 4], F32, kind="ExternalInput")
    convb = nc.dram_tensor("convb", [128, 33], F32, kind="ExternalInput")
    alog = nc.dram_tensor("alog", [NH, 1], F32, kind="ExternalInput")
    dtbias = nc.dram_tensor("dtbias", [NH, 1], F32, kind="ExternalInput")
    dskip = nc.dram_tensor("dskip", [128, NH], F32, kind="ExternalInput")
    normw = nc.dram_tensor("normw", [128, DIN], F32, kind="ExternalInput")
    hm_w = max(halo_tiles * Q, 1)
    halomask = nc.dram_tensor("halomask", [128, hm_w], F32, kind="ExternalInput")
    out_t = nc.dram_tensor("out", [Town, D], F32, kind="ExternalOutput")

    kd = "ExternalOutput" if dbg else "Internal"
    gsilu_s = nc.dram_tensor("gsilu_s", [DIN, T], F32, kind=kd)
    xbc_s = nc.dram_tensor("xbc_s", [NCONV, T], F32, kind=kd)
    yn_sh = nc.dram_tensor("yn_sh", [DIN, Town], BF16, kind=kd)
    yn_sl = nc.dram_tensor("yn_sl", [DIN, Town], BF16, kind=kd)
    attn_s = nc.dram_tensor("attn_s", [Town, D], F32, kind=kd)
    x2_s = nc.dram_tensor("x2_s", [Town, D], F32, kind=kd)
    h_sh = nc.dram_tensor("h_sh", [INTER, Town], BF16, kind=kd)
    h_sl = nc.dram_tensor("h_sl", [INTER, Town], BF16, kind=kd)
    wd_sh = nc.dram_tensor("wd_sh", [INTER, D], BF16)
    wd_sl = nc.dram_tensor("wd_sl", [INTER, D], BF16)

    ident_h = nc.inline_tensor(np.eye(128, dtype=np.float32), "c_ident")
    bdneg = np.zeros((2, 64, 4096), np.float32)
    for j in range(32):
        bdneg[0, j, j * 128:(j + 1) * 128] = -1.0
        bdneg[1, 32 + j, j * 128:(j + 1) * 128] = -1.0
    bdneg_h = [nc.inline_tensor(bdneg[0], "c_bdneg0"), nc.inline_tensor(bdneg[1], "c_bdneg1")]
    triu_h = nc.inline_tensor(np.triu(np.ones((128, 128), np.float32)), "c_triu")
    ones1_h = nc.inline_tensor(np.ones((1, 128), np.float32), "c_ones1")
    eye16_h = nc.inline_tensor(np.eye(16, dtype=np.float32), "c_eye16")

    A = nc.alloc_sbuf_tensor
    ident = A("ident", [128, 128], F32)
    triu = A("triu", [128, 128], F32)
    ones1 = A("ones1", [1, 128], F32)
    eye16 = A("eye16", [16, 16], F32)
    cwsb = A("cwsb", [128, 33 * 4], F32)
    cbsb = A("cbsb", [128, 33], F32)
    aneg = A("aneg", [128, 1], F32)
    dtb = A("dtb", [128, 1], F32)
    dsk = A("dsk", [128, NH], F32)
    hmask = A("hmask", [128, hm_w], F32)
    states = A("states", [16, NH * HP], F32)
    stats = A("stats", [128, 8], F32)
    rowst = A("rowst", [1, 64], F32)
    dax16 = A("dax16", [16, NH], F32)
    dAI = A("dAI", [16, 16], F32)
    bw16 = A("bw16", [128, 16], F32)
    sv3 = A("sv3", [128, 33 * 3], F32)
    cvt = A("cvt", [128, 2], F32)

    rem = nc.sbuf_bytes_remaining // 4
    AR = (rem - 512) // 4 * 4
    arena = A("arena", [128, AR], F32)
    ar = arena.ap()

    def rg(off, n):
        assert off + n <= AR, (off, n, AR)
        return ar[:, off:off + n]

    def rgb(off, nb):
        assert nb % 2 == 0 and off + nb // 2 <= AR, (off, nb, AR)
        return ar[:, off:off + nb // 2].bitcast(BF16)

    x1fm_n = nth * 16 * 128
    o = x1fm_n
    XT = o; o += 2 * D
    X1T = o; o += D
    SQB = o; o += D
    WBUF = o; o += 2 * 2048
    WHI = o; o += 2 * 1024
    WLO = o; o += 2 * 1024
    GBUF = o; o += 2 * 512
    XPRE = o; o += Thalf + 4
    XBCB = o; o += 2 * Thalf
    DTRAW = o; o += T
    WDPP = o; o += 4096
    AB_END = o
    assert AB_END <= AR, (AB_END, AR)

    x1hi = rgb(0, x1fm_n)
    x1lo = rgb(x1fm_n // 2, x1fm_n)
    x1hi_r = x1hi.rearrange("p (i k t) -> p i k t", k=16, t=128)
    x1lo_r = x1lo.rearrange("p (i k t) -> p i k t", k=16, t=128)
    xt_r = [rg(XT, D), rg(XT + D, D)]
    x1t = rg(X1T, D)
    sqb = rg(SQB, D)
    wbuf_r = [rg(WBUF, 2048), rg(WBUF + 2048, 2048)]
    whi_r = [rgb(WHI, 2048), rgb(WHI + 1024, 2048)]
    wlo_r = [rgb(WLO, 2048), rgb(WLO + 1024, 2048)]
    gbuf_r = [rg(GBUF, 512), rg(GBUF + 512, 512)]
    xpre = rg(XPRE, Thalf + 4)
    xbc_r = [rg(XBCB, Thalf), rg(XBCB + Thalf, Thalf)]
    dtraw = rg(DTRAW, T)
    wpp_st = rg(WDPP, 2048)
    wpp_hi = rgb(WDPP + 2048, 2048)
    wpp_lo = rgb(WDPP + 3072, 2048)

    o = 0
    BD0 = o; o += 4096
    BD1 = o; o += 4096
    BIGD = o; o += 4096
    XBCC = o; o += 33 * 128
    XTM = o; o += DIN
    YTM = o; o += DIN
    GFM = o; o += DIN
    CST = o; o += 5 * 128
    BCS = o; o += 3 * 128
    YNST = o; o += 4 * 128
    SSD_B = o
    if max(SSD_B, AB_END) + 5 * T + 128 + DIN <= AR:
        CUMO = max(SSD_B, AB_END)
    else:
        CUMO = SSD_B
        assert CUMO + 3 * T <= DTRAW, (CUMO + 3 * T, DTRAW)
    CUMPRO = CUMO + T
    DTTO = CUMPRO + T
    OCPO = DTTO + T
    CUMLOO = OCPO + T
    DTAO = CUMLOO + T
    NWO = DTAO + 128
    SSD_END = NWO + DIN
    assert SSD_END <= AR, (SSD_END, AR)

    bd_r = [rg(BD0, 4096), rg(BD1, 4096)]
    bigD = rg(BIGD, 4096)
    xbcc = rg(XBCC, 33 * 128)
    xtm = rg(XTM, DIN)
    ytm = rg(YTM, DIN)
    gfm = rg(GFM, DIN)
    cumT_sb = rg(CST, 64)
    ecumT = rg(CST + 64, 64)
    wexT = rg(CST + 128, 64)
    wexs = rg(CST + 192, 128)
    bcs = rg(BCS, 384)
    ynst = rg(YNST, 512)
    cum = rg(CUMO, T)
    cumpr = rg(CUMPRO, T)
    ocp = rg(OCPO, T)
    cumlo = rg(CUMLOO, T)
    dt_t = rg(DTTO, T)
    dtA = rg(DTAO, 128)
    nw = rg(NWO, DIN)
    BW8 = NWO + DIN
    bw8_r = [rg(BW8, 128), rg(BW8 + 128, 128)]
    dai8_r = [rg(BW8 + 256, 128), rg(BW8 + 384, 128)]
    assert BW8 + 512 <= AR, (BW8 + 512, AR)

    WOH = 0
    wouth_hi = [rgb(WOH, 32 * 512), rgb(WOH + 16384, 32 * 512)]
    wouth_lo = [rgb(WOH + 8192, 32 * 512), rgb(WOH + 24576, 32 * 512)]
    WST = WOH + 32768
    wst_r = [rg(WST, 512), rg(WST + 512, 512)]
    YST = WST + 1024
    yst_hi = [rgb(YST + s * 4096, 4096) for s in range(2)]
    yst_lo = [rgb(YST + s * 4096 + 2048, 4096) for s in range(2)]
    AST = YST + 2 * 4096
    ast_r = [rg(AST, 1024), rg(AST + 1024, 1024)]
    assert AST + 2048 <= AR, (AST + 2048, AR)

    x3fm_n = n_own * 16 * 128
    x3hi = rgb(0, x3fm_n)
    x3lo = rgb(x3fm_n // 2, x3fm_n)
    x3hi_r = x3hi.rearrange("p (i k t) -> p i k t", k=16, t=128)
    x3lo_r = x3lo.rearrange("p (i k t) -> p i k t", k=16, t=128)
    fo = x3fm_n
    f_attn = rg(fo, D)
    f_x = rg(fo + D, D)
    f_tmp = rg(fo + 2 * D, D)
    f_x2 = rg(fo + 3 * D, D)
    f_x3 = rg(fo + 4 * D, D)
    assert fo + 5 * D <= AR
    wgu_r = [rg(fo, 2048), rg(fo + 2048, 2048)]
    wguhi_r = [rgb(fo + 4096, 2048), rgb(fo + 5120, 2048)]
    wgulo_r = [rgb(fo + 6144, 2048), rgb(fo + 7168, 2048)]
    gs_b = rg(fo + 8192, Town)
    hb = rg(fo + 8192 + Town, Town)
    hb_hi = rgb(fo + 8192 + 2 * Town, Town)
    hb_lo = rgb(fo + 8192 + 2 * Town + Town // 2, Town)
    assert fo + 8192 + 3 * Town <= AR, (fo + 8192 + 3 * Town, AR)

    hq_hi = rgb(0, 64 * 256)
    hq_lo = rgb(64 * 128, 64 * 256)
    HWD = 64 * 256
    wdhi_r = [rgb(HWD, 2048), rgb(HWD + 1024, 2048)]
    wdlo_r = [rgb(HWD + 2048, 2048), rgb(HWD + 3072, 2048)]
    ho = HWD + 4096
    dtm_r = [rg(ho, D), rg(ho + D, D)]
    h_x2 = [rg(ho + 2 * D, D), rg(ho + 5 * D, D)]
    h_out = [rg(ho + 3 * D, D), rg(ho + 6 * D, D)]
    h_sq = rg(ho + 4 * D, D)
    assert ho + 7 * D <= AR, (ho + 7 * D, AR)

    pb = [nc.alloc_psum_tensor(f"pb{i}", [128, 512], F32) for i in range(8)]

    s_ld = b.sem("ld")
    b.dma(SY, ident[:, :], ident_h.ap(), s_ld)
    b.dma(SY, triu[:, :], triu_h.ap(), s_ld)
    b.dma(SY, ones1[:, :], ones1_h.ap(), s_ld)
    b.dma(SY, eye16[:, :], eye16_h.ap(), s_ld)
    b.dma(SY, cwsb[:, :], convw.ap()[:, :], s_ld)
    b.dma(SY, cbsb[:, :], convb.ap()[:, :], s_ld)
    b.dma(SY, aneg[0:64, 0:1], alog.ap(), s_ld)
    b.dma(SY, dtb[0:64, 0:1], dtbias.ap(), s_ld)
    b.dma(SY, dsk[:, :], dskip.ap(), s_ld)
    b.dma(SY, hmask[:, :], halomask.ap(), s_ld)

    b.wait(S_, s_ld)
    b.em(S_, lambda e: e.activation(aneg[0:64, 0:1], aneg[0:64, 0:1], AF.Exp, scale=1.0))
    b.em(S_, lambda e: e.mul(aneg[0:64, 0:1], aneg[0:64, 0:1], -1.0), inc=True)
    b.wait(V_, s_ld)
    b.em(V_, lambda e: e.memset(cvt[:, 0:1], float(EPS)))
    b.em(V_, lambda e: e.memset(cvt[:, 1:2], 1.0))
    b.em(V_, lambda e: e.memset(xpre[:, 0:4], 0.0))
    b.em(V_, lambda e: e.memset(wbuf_r[0][:, :], 0.0))
    b.em(V_, lambda e: e.memset(wbuf_r[1][:, :], 0.0))
    b.em(V_, lambda e: e.memset(stats[:, :], 0.0), inc=True)
    a_init_v = b.ms[V_].n
    a_init_s = b.ms[S_].n

    s_ldx = [b.sem("ldx0"), b.sem("ldx1")]
    s_ldw = [b.sem("ldw0"), b.sem("ldw1")]
    s_gsp = [b.sem("gsp0"), b.sem("gsp1")]
    s_xsp = [b.sem("xsp0"), b.sem("xsp1")]
    NCC = (INDIM + 127) // 128

    bank_ev = [0] * 8
    bank_eng = [V_] * 8
    b.wait(T_, s_ld)
    b.waitm(T_, V_, a_init_v)

    xt_ev = [0, 0]
    gate_cnt = 0
    wconv_v = [0, 0]
    s_wpl = b.sem("wpl")
    s_wpc = b.sem("wpc")
    wpp_k = [0]

    def wd_prepass_step():
        k = wpp_k[0]
        if k >= 64:
            return
        wpp_k[0] += 1
        if k > 0:
            b.wait(G_, s_wpc)
        b.dma(G_, wpp_st, w_down_t.ap()[k * 128:(k + 1) * 128, :], s_wpl)
        b.wait(S_, s_wpl)
        b.em(S_, lambda e: e.copy(wpp_hi[:, :], wpp_st), inc=True)
        b.waitm(V_, S_)
        b.em(V_, lambda e: e.tensor_tensor(wpp_lo[:, :], wpp_st, wpp_hi[:, :], ALU.subtract),
             inc=True)
        b.waitm(S_, V_)
        b.em(S_, lambda e, k=k: e.dma_start(out=wd_sh.ap()[k * 128:(k + 1) * 128, :],
                                            in_=wpp_hi[:, :]), sem=s_wpc, n=16)
        b.em(S_, lambda e, k=k: e.dma_start(out=wd_sl.ap()[k * 128:(k + 1) * 128, :],
                                            in_=wpp_lo[:, :]), sem=s_wpc, n=16)
    for half in range(2):
        tb = half * nth
        for li in range(nth):
            gi = tb + li
            sl = li % 2
            b.wait(SY, b.ms[V_], xt_ev[sl])
            b.dma(SY, xt_r[sl], x_in.ap()[gi * Q:(gi + 1) * Q, :], s_ldx[sl])
            b.wait(V_, s_ldx[sl])
            b.em(V_, lambda e, sl=sl: e.reduce_sum(stats[:, 0:1], xt_r[sl], axis=AX.X), inc=True)
            b.wait(S_, s_ldx[sl])
            b.waitm(S_, V_)
            b.em(S_, lambda e, sl=sl: e.activation(sqb, xt_r[sl], AF.Square, accum_out=stats[:, 1:2]),
                 inc=True)
            b.waitm(V_, S_)
            b.em(V_, lambda e: e.tensor_scalar(stats[:, 2:3], stats[:, 0:1], 1.0 / D, None, ALU.mult))
            b.em(V_, lambda e: e.tensor_tensor(stats[:, 3:4], stats[:, 2:3], stats[:, 2:3], ALU.mult))
            b.em(V_, lambda e: e.scalar_tensor_tensor(stats[:, 4:5], stats[:, 1:2], 1.0 / D,
                                                      stats[:, 3:4], ALU.mult, ALU.subtract), inc=True)
            b.waitm(S_, V_)
            b.em(S_, lambda e: e.activation(stats[:, 5:6], stats[:, 4:5], AF.Ln, bias=cvt[:, 0:1], scale=1.0))
            b.em(S_, lambda e: e.activation(stats[:, 6:7], stats[:, 5:6], AF.Exp, scale=-0.5), inc=True)
            b.waitm(V_, S_)
            b.em(V_, lambda e, sl=sl: e.tensor_scalar(x1t, xt_r[sl], stats[:, 2:3], stats[:, 6:7],
                                                      ALU.subtract, ALU.mult), inc=True)
            xt_ev[sl] = b.ms[V_].n
            b.waitm(T_, V_)
            for g in range(4):
                bank = pb[g]
                b.wait(T_, b.ms[bank_eng[g]], bank_ev[g])
                for u in range(4):
                    k = g * 4 + u
                    b.em(T_, lambda e, k=k, bank=bank, u=u: e.transpose(
                        bank[:, u * 128:(u + 1) * 128], x1t[:, k * 128:(k + 1) * 128], ident[:, :]),
                        inc=(u == 3))
                b.waitm(S_, T_)
                b.em(S_, lambda e, li=li, g=g, bank=bank: e.copy(
                    x1hi[:, (li * 16 + g * 4) * 128:(li * 16 + g * 4 + 4) * 128], bank[:, :]), inc=True)
                b.waitm(V_, S_)
                b.em(V_, lambda e, li=li, g=g, bank=bank: e.tensor_tensor(
                    x1lo[:, (li * 16 + g * 4) * 128:(li * 16 + g * 4 + 4) * 128], bank[:, :],
                    x1hi[:, (li * 16 + g * 4) * 128:(li * 16 + g * 4 + 4) * 128], ALU.subtract), inc=True)
                bank_ev[g] = b.ms[V_].n
                bank_eng[g] = V_

        while wpp_k[0] < 64 and half == 1:
            wd_prepass_step()
        b.waitm(T_, V_)
        for c in range(NCC):
            cw = min(128, INDIM - c * 128)
            sl = c % 2
            if c == 0:
                if half == 1:
                    b.waitm(SY, T_)
                for k in range(16):
                    b.dma(SY, wbuf_r[0][:, k * 128:k * 128 + cw],
                          w_in.ap()[k * 128:(k + 1) * 128, 0:cw], s_ldw[0])
                b.wait(S_, s_ldw[0])
                b.em(S_, lambda e, cw=cw: e.copy(whi_r[0][:, 0:16 * 128], wbuf_r[0][:, :]), inc=True)
                b.waitm(V_, S_)
                b.em(V_, lambda e, cw=cw: e.tensor_tensor(
                    wlo_r[0][:, 0:16 * 128], wbuf_r[0][:, :], whi_r[0][:, 0:16 * 128],
                    ALU.subtract), inc=True)
                wconv_v[0] = b.ms[V_].n
            if c + 1 < NCC:
                cw2 = min(128, INDIM - (c + 1) * 128)
                if c >= 1:
                    b.waitm(SY, T_)
                sl2 = (c + 1) % 2
                for k in range(16):
                    b.dma(SY, wbuf_r[sl2][:, k * 128:k * 128 + cw2],
                          w_in.ap()[k * 128:(k + 1) * 128, (c + 1) * 128:(c + 1) * 128 + cw2],
                          s_ldw[sl2])
                b.wait(S_, s_ldw[sl2])
                if c >= 1:
                    b.waitm(S_, T_)
                b.em(S_, lambda e, sl2=sl2: e.copy(whi_r[sl2][:, 0:16 * 128], wbuf_r[sl2][:, :]),
                     inc=True)
                b.waitm(V_, S_)
                b.em(V_, lambda e, sl2=sl2: e.tensor_tensor(
                    wlo_r[sl2][:, 0:16 * 128], wbuf_r[sl2][:, :], whi_r[sl2][:, 0:16 * 128],
                    ALU.subtract), inc=True)
                wconv_v[sl2] = b.ms[V_].n
            wd_prepass_step()
            b.waitm(T_, V_, wconv_v[sl])
            for j, (t0, tw) in enumerate(jth):
                bank = pb[j]
                b.wait(T_, b.ms[bank_eng[j]], bank_ev[j])
                ntl = (tw + 127) // 128
                i0 = t0 // 128
                if c < 64:
                    for k in range(16):
                        for wop, xop, st, sp in (
                            (whi_r[sl], x1hi_r, k == 0, False),
                            (whi_r[sl], x1lo_r, False, False),
                            (wlo_r[sl], x1hi_r, False, k == 15),
                        ):
                            b.em(T_, lambda e, k=k, cw=cw, bank=bank, i0=i0, ntl=ntl, tw=tw,
                                 wop=wop, xop=xop, st=st, sp=sp:
                                 e.matmul(bank[0:cw, 0:tw], wop[:, k * 128:k * 128 + cw],
                                          xop[:, i0:i0 + ntl, k:k + 1, :],
                                          start=st, stop=sp),
                                 inc=sp)
                else:
                    bank2 = pb[3 + j]
                    b.wait(T_, b.ms[bank_eng[3 + j]], bank_ev[3 + j])
                    for k in range(16):
                        for wop, xop, st, sp in (
                            (whi_r[sl], x1hi_r, k == 0, False),
                            (whi_r[sl], x1lo_r, False, False),
                            (wlo_r[sl], x1hi_r, False, k == 15),
                        ):
                            b.em(T_, lambda e, k=k, bank=bank, i0=i0, ntl=ntl, tw=tw,
                                 wop=wop, xop=xop, st=st, sp=sp:
                                 e.matmul(bank[0:32, 0:tw], wop[:, k * 128:k * 128 + 32],
                                          xop[:, i0:i0 + ntl, k:k + 1, :],
                                          start=st, stop=sp))
                            b.em(T_, lambda e, k=k, bank2=bank2, i0=i0, ntl=ntl, tw=tw,
                                 wop=wop, xop=xop, st=st, sp=sp:
                                 e.matmul(bank2[0:64, 0:tw], wop[:, k * 128 + 32:k * 128 + 96],
                                          xop[:, i0:i0 + ntl, k:k + 1, :],
                                          start=st, stop=sp),
                                 inc=sp)
                mmv = b.ms[T_].n
                if c < 32:
                    gsl = gate_cnt % 2
                    b.waitm(S_, T_, mmv)
                    b.wait(S_, s_gsp[gsl])
                    b.em(S_, lambda e, bank=bank, tw=tw, gsl=gsl:
                         e.activation(gbuf_r[gsl][:, 0:tw], bank[0:128, 0:tw], AF.Silu), inc=True)
                    bank_ev[j] = b.ms[S_].n
                    bank_eng[j] = S_
                    b.waitm(SY, S_)
                    b.dma(SY, gsilu_s.ap()[c * 128:(c + 1) * 128,
                                           half * Thalf + t0:half * Thalf + t0 + tw],
                          gbuf_r[gsl][:, 0:tw], s_gsp[gsl])
                    gate_cnt += 1
                elif c < 64:
                    b.waitm(V_, T_, mmv)
                    b.em(V_, lambda e, bank=bank, t0=t0, tw=tw: e.tensor_copy(
                        xpre[:, 4 + t0:4 + t0 + tw], bank[0:128, 0:tw]), inc=True)
                    bank_ev[j] = b.ms[V_].n
                    bank_eng[j] = V_
                else:
                    b.waitm(V_, T_, mmv)
                    b.em(V_, lambda e, bank=bank, t0=t0, tw=tw: e.tensor_copy(
                        xpre[0:32, 4 + t0:4 + t0 + tw], bank[0:32, 0:tw]))
                    b.em(V_, lambda e, j=j, t0=t0, tw=tw, half=half: e.tensor_copy(
                        dtraw[0:64, half * Thalf + t0:half * Thalf + t0 + tw], pb[3 + j][0:64, 0:tw]),
                        inc=True)
                    bank_ev[j] = b.ms[V_].n
                    bank_eng[j] = V_
                    bank_ev[3 + j] = b.ms[V_].n
                    bank_eng[3 + j] = V_
            if c >= 32:
                cc = c - 32
                nch = 128 if c < 64 else 32
                xb = xbc_r[cc % 2]
                b.wait(V_, s_xsp[cc % 2])
                if half == 1:
                    b.em(V_, lambda e, cc=cc, nch=nch: e.tensor_copy(
                        xpre[0:nch, 1:4], sv3[0:nch, cc * 3:cc * 3 + 3]))
                if dbg == 2:
                    b.em(V_, lambda e, nch=nch, xb=xb: e.tensor_copy(
                        xb[0:nch, 0:Thalf], xpre[0:nch, 4:4 + Thalf]))
                b.em(V_, lambda e, cc=cc, nch=nch, xb=xb: (None if dbg == 2 else e.tensor_scalar(
                    xb[0:nch, 0:Thalf], xpre[0:nch, 1:1 + Thalf], cwsb[0:nch, cc * 4:cc * 4 + 1],
                    cbsb[0:nch, cc:cc + 1], ALU.mult, ALU.add)) or e.nop())
                for k in range(1, 4):
                    if dbg == 2:
                        break
                    b.em(V_, lambda e, cc=cc, nch=nch, xb=xb, k=k: e.scalar_tensor_tensor(
                        xb[0:nch, 0:Thalf], xpre[0:nch, 1 + k:1 + k + Thalf],
                        cwsb[0:nch, cc * 4 + k:cc * 4 + k + 1], xb[0:nch, 0:Thalf],
                        ALU.mult, ALU.add))
                if half == 0:
                    b.em(V_, lambda e, cc=cc, nch=nch: e.tensor_copy(
                        sv3[0:nch, cc * 3:cc * 3 + 3], xpre[0:nch, 1 + Thalf:4 + Thalf]))
                    if halo_tiles > 0:
                        b.em(V_, lambda e, nch=nch, xb=xb: e.tensor_tensor(
                            xb[0:nch, 0:halo_tiles * Q], xb[0:nch, 0:halo_tiles * Q],
                            hmask[0:nch, :], ALU.mult))
                b.em(V_, lambda e, nch=nch, xb=xb: e.tensor_copy(xb[0:nch, 0:1], xb[0:nch, 0:1]),
                     inc=True)
                b.waitm(S_, V_)
                b.em(S_, lambda e, nch=nch, xb=xb: (e.nop() if dbg in (2, 3) else e.activation(
                    xb[0:nch, 0:Thalf], xb[0:nch, 0:Thalf], AF.Silu)), inc=True)
                b.waitm(SY, S_)
                b.dma(SY, xbc_s.ap()[cc * 128:cc * 128 + nch, half * Thalf:(half + 1) * Thalf],
                      xb[0:nch, 0:Thalf], s_xsp[cc % 2])

    b.waitm(S_, V_)
    b.em(S_, lambda e: e.activation(dt_t[0:64, :], dtraw[0:64, :], AF.Exp, bias=dtb[0:64, 0:1],
                                    scale=1.0))
    b.em(S_, lambda e: e.activation(dt_t[0:64, :], dt_t[0:64, :], AF.Ln, bias=cvt[0:64, 1:2], scale=1.0))
    b.em(S_, lambda e: e.activation(dtraw[0:64, :], dt_t[0:64, :], AF.Ln, scale=1.0), inc=True)
    b.waitm(V_, S_)
    b.em(V_, lambda e: e.memset(sqb[0:64, 0:Q], 0.0))
    for ci in range(n_tiles):
        b.em(V_, lambda e, ci=ci: e.tensor_scalar(
            dtA[0:64, 0:Q], dt_t[0:64, ci * Q:(ci + 1) * Q], aneg[0:64, 0:1], None, ALU.mult))
        b.em(V_, lambda e, ci=ci: e.tensor_tensor_scan(
            cum[0:64, ci * Q:(ci + 1) * Q], dtA[0:64, 0:Q], sqb[0:64, 0:Q],
            0.0, ALU.add, ALU.add))
    b.em(V_, lambda e: e.tensor_tensor(cumpr[0:64, :], cum[0:64, :], dtraw[0:64, :], ALU.subtract))
    b.em(V_, lambda e: e.memset(ocp[0:64, :], 1.0))
    b.em(V_, lambda e: e.memset(cum[64:128, :], 0.0))
    b.em(V_, lambda e: e.memset(cumpr[64:128, :], 0.0))
    b.em(V_, lambda e: e.memset(wexs[64:128, :], 0.0))
    b.em(V_, lambda e: e.memset(states[:, :], 0.0))
    b.em(V_, lambda e: e.memset(bd_r[0][32:64, :], 0.0))
    b.em(V_, lambda e: e.memset(bd_r[1][0:32, :], 0.0))
    s_bdl = b.sem("bdl")
    b.waitm(SY, V_)
    for hf in range(2):
        b.dma(SY, bd_r[hf][64:128, :], bdneg_h[hf].ap(), s_bdl)
    b.wait(G_, s_bdl)
    b.wait(T_, s_bdl)
    b.em(V_, lambda e: e.tensor_copy(cumlo[0:64, :], cum[0:64, :]), inc=True)
    s_Cd = b.sem("Cd")
    b.waitm(SY, V_)
    b.dma(SY, ocp[64:128, :], cumpr[0:64, :], s_Cd)

    s_xc = b.sem("xc")
    s_gf = b.sem("gf")
    s_rs = b.sem("rs")
    s_ysp = [b.sem(f"ysp{i}") for i in range(4)]
    s_nw = b.sem("nw")
    b.waitm(SY, V_)
    b.dma(SY, nw[:, :], normw.ap(), s_nw)
    b.wait(G_, s_Cd)
    b.wait(T_, s_Cd)
    b.waitm(G_, V_)

    gfm_free_v = 0
    gfm_free_s = 0
    xbcc_free = 0
    yn_cnt = 0

    for ci in range(n_tiles):
        own = ci >= halo_tiles
        oci = ci - halo_tiles
        tsl = slice(ci * Q, (ci + 1) * Q)
        b.waitm(SY, T_, xbcc_free)
        if ci == 0:
            b.wait(SY, s_xsp[0]); b.wait(SY, s_xsp[1])
        for g4 in range(8):
            b.dma(SY, xbcc[:, g4 * 512:(g4 + 1) * 512].rearrange("p (a t) -> p a t", a=4),
                  xbc_s.ap()[0:4096, :].rearrange("(a p) t -> p a t", p=128)[:, g4 * 4:(g4 + 1) * 4, tsl],
                  s_xc)
        b.dma(SY, bcs[0:16, 0:128], xbc_s.ap()[4096:4112, tsl], s_xc)
        b.dma(SY, bcs[0:16, 128:256], xbc_s.ap()[4112:4128, tsl], s_xc)
        if own:
            b.waitm(SY, V_, gfm_free_v)
            b.waitm(SY, S_, gfm_free_s)
            if oci == 0:
                b.wait(SY, s_gsp[0]); b.wait(SY, s_gsp[1])
            for g4 in range(8):
                b.dma(SY, gfm[:, g4 * 512:(g4 + 1) * 512].rearrange("p (a t) -> p a t", a=4),
                      gsilu_s.ap().rearrange("(a p) t -> p a t", p=128)[:, g4 * 4:(g4 + 1) * 4, tsl],
                      s_gf)

        b.waitm(G_, T_)
        for hf in range(2):
            h0 = hf * 32
            srcap = cumlo[h0:h0 + 32, tsl].unsqueeze(1).to_broadcast([32, 32, 128])
            dstap = bd_r[hf][h0:h0 + 32, :].rearrange("p (h t) -> p h t", h=32)
            b.em(G_, lambda e, srcap=srcap, dstap=dstap: e.affine_select(
                dstap, srcap, pattern=[[1, 32], [0, 128]], compare_op=ALU.is_equal,
                fill=0.0, base=0, channel_multiplier=-1), inc=(hf == 1))
        bd_ready = b.ms[G_].n

        b.wait(T_, b.ms[bank_eng[2]], bank_ev[2])
        b.em(T_, lambda e, tsl=tsl: e.transpose(pb[2][:, 0:128], cum[:, tsl], ident[:, :]), inc=True)
        b.waitm(V_, T_)
        b.em(V_, lambda e: e.tensor_copy(cumT_sb[:, :], pb[2][:, 0:64]), inc=True)
        bank_ev[2] = b.ms[V_].n; bank_eng[2] = V_
        b.waitm(S_, V_)
        b.em(S_, lambda e: e.activation(ecumT[:, :], cumT_sb[:, :], AF.Exp, scale=1.0), inc=True)
        b.em(V_, lambda e, ci=ci: e.tensor_scalar(
            wexs[0:64, :], cumpr[0:64, ci * Q:(ci + 1) * Q],
            cum[0:64, ci * Q + Q - 1:ci * Q + Q], -1.0, ALU.subtract, ALU.mult), inc=True)
        b.waitm(S_, V_)
        b.em(S_, lambda e: e.activation(wexs[0:64, :], wexs[0:64, :], AF.Exp, scale=1.0), inc=True)
        b.waitm(T_, S_)
        b.wait(T_, b.ms[bank_eng[3]], bank_ev[3])
        b.em(T_, lambda e: e.transpose(pb[3][:, 0:128], wexs[:, :], ident[:, :]), inc=True)
        b.waitm(V_, T_)
        b.em(V_, lambda e: e.tensor_copy(wexT[:, :], pb[3][:, 0:64]), inc=True)
        bank_ev[3] = b.ms[V_].n; bank_eng[3] = V_
        b.em(S_, lambda e: e.dma_start(out=rowst[0:1, 0:64], in_=ecumT[127:128, :]), sem=s_rs, n=16)
        b.wait(T_, s_rs)
        b.wait(T_, b.ms[bank_eng[3]], bank_ev[3])
        b.em(T_, lambda e: e.matmul(pb[3][0:16, 0:64], ones1[0:1, 0:16], rowst[0:1, 0:64],
                                    start=True, stop=True), inc=True)
        b.waitm(V_, T_)
        b.em(V_, lambda e: e.tensor_copy(dax16[:, :], pb[3][0:16, 0:64]), inc=True)
        bank_ev[3] = b.ms[V_].n; bank_eng[3] = V_

        b.wait(T_, s_xc)
        for g in range(8):
            bk = 2 + g % 2
            bank = pb[bk]
            b.wait(T_, b.ms[bank_eng[bk]], bank_ev[bk])
            for u in range(4):
                a = g * 4 + u
                b.em(T_, lambda e, a=a, bank=bank, u=u: e.transpose(
                    bank[:, u * 128:(u + 1) * 128], xbcc[:, a * 128:(a + 1) * 128], ident[:, :]),
                    inc=(u == 3))
            b.waitm(V_, T_)
            b.em(V_, lambda e, g=g, bank=bank: e.tensor_copy(
                xtm[:, g * 512:(g + 1) * 512], bank[:, :]), inc=True)
            bank_ev[bk] = b.ms[V_].n; bank_eng[bk] = V_
        b.wait(T_, b.ms[bank_eng[2]], bank_ev[2])
        b.em(T_, lambda e: e.transpose(pb[2][:, 0:16], bcs[0:16, 0:128], ident[0:16, 0:16]), inc=True)
        xbcc_free = b.ms[T_].n
        b.waitm(V_, T_)
        b.em(V_, lambda e: e.tensor_copy(bcs[:, 256:272], pb[2][:, 0:16]), inc=True)
        bank_ev[2] = b.ms[V_].n; bank_eng[2] = V_

        b.wait(T_, b.ms[bank_eng[4]], bank_ev[4])
        b.em(T_, lambda e: e.matmul(pb[4][:, 0:128], bcs[0:16, 0:128], bcs[0:16, 128:256],
                                    start=True, stop=True), inc=True)
        b.waitm(V_, T_)
        b.em(V_, lambda e: e.tensor_tensor(wexs[:, :], pb[4][:, 0:128], triu[:, :], ALU.mult), inc=True)
        bank_ev[4] = b.ms[V_].n; bank_eng[4] = V_
        mcb_v = b.ms[V_].n

        for hf in range(2):
            h0 = hf * 32
            b.wait(T_, b.ms[G_], bd_ready)
            for jj in range(8):
                bk = jj % 2
                bank = pb[bk]
                b.wait(T_, b.ms[bank_eng[bk]], bank_ev[bk])
                b.em(T_, lambda e, hf=hf, jj=jj, bank=bank, tsl=tsl: e.matmul(
                    bank[:, 0:512], ocp[:, tsl], bd_r[hf][:, jj * 512:(jj + 1) * 512],
                    start=True, stop=True), inc=True)
                b.waitm(V_, T_)
                b.em(V_, lambda e, jj=jj, bank=bank: e.tensor_scalar(
                    bigD[:, jj * 512:(jj + 1) * 512], bank[:, 0:512], CLAMP, None, ALU.min), inc=True)
                bank_ev[bk] = b.ms[V_].n; bank_eng[bk] = V_
            b.waitm(S_, V_)
            b.em(S_, lambda e: e.activation(bigD[:, :], bigD[:, :], AF.Exp, scale=1.0), inc=True)
            b.waitm(V_, S_)
            b.em(V_, lambda e: e.tensor_tensor(
                bigD.rearrange("p (h t) -> p h t", h=32), bigD.rearrange("p (h t) -> p h t", h=32),
                wexs.unsqueeze(1).to_broadcast([128, 32, 128]), ALU.mult), inc=True)
            s_ready = b.ms[V_].n
            b.waitm(T_, V_, s_ready)
            for g8 in range(4):
                hg = [h0 + g8 * 8 + j for j in range(8)]
                if own:
                    bi = 4 + g8 % 2
                    bs = 6 + g8 % 2
                    b.wait(T_, b.ms[bank_eng[bi]], bank_ev[bi])
                    b.wait(T_, b.ms[bank_eng[bs]], bank_ev[bs])
                    for j, h in enumerate(hg):
                        hh = h - h0
                        b.em(T_, lambda e, hh=hh, h=h, bi=bi, j=j: e.matmul(
                            pb[bi][:, j * 64:(j + 1) * 64], bigD[:, hh * 128:(hh + 1) * 128],
                            xtm[:, h * 64:(h + 1) * 64], start=True, stop=True))
                        b.em(T_, lambda e, h=h, bs=bs, j=j: e.matmul(
                            pb[bs][:, j * 64:(j + 1) * 64], bcs[0:16, 128:256],
                            states[0:16, h * 64:(h + 1) * 64], start=True, stop=True),
                            inc=(j == 7))
                    ymm = b.ms[T_].n
                    b.waitm(V_, T_, ymm)
                    for j, h in enumerate(hg):
                        b.em(V_, lambda e, h=h, bi=bi, j=j: e.scalar_tensor_tensor(
                            ytm[:, h * 64:(h + 1) * 64], xtm[:, h * 64:(h + 1) * 64],
                            dsk[:, h:h + 1], pb[bi][:, j * 64:(j + 1) * 64], ALU.mult, ALU.add))
                        b.em(V_, lambda e, h=h, bs=bs, j=j: e.scalar_tensor_tensor(
                            ytm[:, h * 64:(h + 1) * 64], pb[bs][:, j * 64:(j + 1) * 64],
                            ecumT[:, h:h + 1], ytm[:, h * 64:(h + 1) * 64], ALU.mult, ALU.add),
                            inc=(j == 7))
                    bank_ev[bi] = b.ms[V_].n; bank_eng[bi] = V_
                    bank_ev[bs] = b.ms[V_].n; bank_eng[bs] = V_
                wsl8 = g8 % 2
                b.waitm(V_, T_)
                for j, h in enumerate(hg):
                    b.em(V_, lambda e, h=h, j=j, wsl8=wsl8: e.tensor_scalar(
                        bw8_r[wsl8][:, j * 16:(j + 1) * 16], bcs[:, 256:272],
                        wexT[:, h:h + 1], None, ALU.mult))
                    b.em(V_, lambda e, h=h, j=j, wsl8=wsl8: e.tensor_scalar(
                        dai8_r[wsl8][0:16, j * 16:(j + 1) * 16], eye16[:, :],
                        dax16[:, h:h + 1], None, ALU.mult), inc=(j == 7))
                bwv = b.ms[V_].n
                bu = 2 + g8 % 2
                b.waitm(T_, V_, bwv)
                b.wait(T_, b.ms[bank_eng[bu]], bank_ev[bu])
                for j, h in enumerate(hg):
                    b.em(T_, lambda e, h=h, bu=bu, j=j, wsl8=wsl8: e.matmul(
                        pb[bu][0:16, j * 64:(j + 1) * 64], bw8_r[wsl8][:, j * 16:(j + 1) * 16],
                        xtm[:, h * 64:(h + 1) * 64], start=True, stop=False))
                    b.em(T_, lambda e, h=h, bu=bu, j=j, wsl8=wsl8: e.matmul(
                        pb[bu][0:16, j * 64:(j + 1) * 64], dai8_r[wsl8][0:16, j * 16:(j + 1) * 16],
                        states[0:16, h * 64:(h + 1) * 64], start=False, stop=True),
                        inc=(j == 7))
                b.waitm(V_, T_)
                b.em(V_, lambda e, h0=h0, g8=g8, bu=bu: e.tensor_copy(
                    states[0:16, (h0 + g8 * 8) * 64:(h0 + g8 * 8 + 8) * 64],
                    pb[bu][0:16, 0:512]), inc=True)
                bank_ev[bu] = b.ms[V_].n; bank_eng[bu] = V_

        if own:
            b.wait(T_, s_gf)
            for g in range(32):
                bk = 2 + g % 2
                bank = pb[bk]
                b.wait(T_, b.ms[bank_eng[bk]], bank_ev[bk])
                b.em(T_, lambda e, g=g, bank=bank: e.transpose(
                    bank[:, 0:128], gfm[:, g * 128:(g + 1) * 128], ident[:, :]), inc=True)
                b.waitm(V_, T_)
                b.em(V_, lambda e, g=g, bank=bank: e.tensor_tensor(
                    ytm[:, g * 128:(g + 1) * 128], bank[:, 0:128],
                    ytm[:, g * 128:(g + 1) * 128], ALU.mult), inc=True)
                bank_ev[bk] = b.ms[V_].n; bank_eng[bk] = V_
            gfm_free_v = b.ms[V_].n
            b.waitm(S_, V_)
            b.em(S_, lambda e: e.activation(gfm[:, :], ytm[:, :], AF.Square, accum_out=stats[:, 1:2]))
            b.em(S_, lambda e: e.activation(stats[:, 5:6], stats[:, 1:2], AF.Ln, bias=cvt[:, 0:1],
                                            scale=1.0 / DIN))
            b.em(S_, lambda e: e.activation(stats[:, 6:7], stats[:, 5:6], AF.Exp, scale=-0.5), inc=True)
            gfm_free_s = b.ms[S_].n
            b.waitm(V_, S_)
            b.wait(V_, s_nw)
            b.em(V_, lambda e: e.scalar_tensor_tensor(ytm[:, :], ytm[:, :], stats[:, 6:7], nw[:, :],
                                                      ALU.mult, ALU.mult), inc=True)
            ynv = b.ms[V_].n
            b.waitm(T_, V_, ynv)
            for g in range(32):
                bk = 2 + g % 2
                bank = pb[bk]
                ysl = yn_cnt % 4
                yh = rgb(YNST + ysl * 128, 256)
                yl = rgb(YNST + ysl * 128 + 64, 256)
                b.wait(T_, b.ms[bank_eng[bk]], bank_ev[bk])
                b.em(T_, lambda e, g=g, bank=bank: e.transpose(
                    bank[:, 0:128], ytm[:, g * 128:(g + 1) * 128], ident[:, :]), inc=True)
                b.waitm(S_, T_)
                b.wait(S_, s_ysp[ysl])
                b.em(S_, lambda e, yh=yh, bank=bank: e.copy(yh[:, 0:128], bank[:, 0:128]), inc=True)
                b.waitm(V_, S_)
                b.em(V_, lambda e, yh=yh, yl=yl, bank=bank: e.tensor_tensor(
                    yl[:, 0:128], bank[:, 0:128], yh[:, 0:128], ALU.subtract), inc=True)
                bank_ev[bk] = b.ms[V_].n; bank_eng[bk] = V_
                b.waitm(SY, V_)
                b.dma(SY, yn_sh.ap()[g * 128:(g + 1) * 128, oci * Q:(oci + 1) * Q],
                      yh[:, 0:128], s_ysp[ysl])
                b.dma(SY, yn_sl.ap()[g * 128:(g + 1) * 128, oci * Q:(oci + 1) * Q],
                      yl[:, 0:128], s_ysp[ysl])
                yn_cnt += 1

    s_wo = [b.sem("wo0"), b.sem("wo1")]
    s_yl = [b.sem(f"yl{s}") for s in range(3)]
    s_asp = [b.sem("asp0"), b.sem("asp1")]
    yst_ev = [0, 0, 0]
    for i in range(4):
        b.wait(SY, s_ysp[i])
    wst_cv = [0, 0]
    pass_mm_done = 0
    wo_conv_q = [0, 0]

    def conv_wouth(oq):
        wq = oq % 2
        for k in range(32):
            wsl = k % 2
            b.wait(SY, b.ms[V_], wst_cv[wsl])
            b.dma(SY, wst_r[wsl],
                  w_out.ap()[k * 128:(k + 1) * 128, oq * 512:(oq + 1) * 512], s_wo[wsl])
            b.wait(S_, s_wo[wsl])
            if oq >= 2:
                b.wait(S_, b.ms[T_], pass_mm_done)
            b.em(S_, lambda e, k=k, wsl=wsl, wq=wq: e.copy(
                wouth_hi[wq][:, k * 512:(k + 1) * 512], wst_r[wsl]), inc=True)
            b.waitm(V_, S_)
            b.em(V_, lambda e, k=k, wsl=wsl, wq=wq: e.tensor_tensor(
                wouth_lo[wq][:, k * 512:(k + 1) * 512], wst_r[wsl],
                wouth_hi[wq][:, k * 512:(k + 1) * 512], ALU.subtract), inc=True)
            wst_cv[wsl] = b.ms[V_].n
        wo_conv_q[wq] = b.ms[V_].n

    conv_wouth(0)
    conv_wouth(1)
    ecnt = 0
    for ep in range(2):
        for i in range(n_own):
            if ep == 1 and i == 0:
                pass_mm_done_snapshot = pass_mm_done
            if ep == 0 and i == 1:
                pass_mm_done = b.ms[T_].n
            sl = i % 2
            if ep > 0 or i >= 2:
                b.wait(SY, b.ms[T_], yst_ev[sl])
            for gsl4 in range(8):
                b.dma(SY, yst_hi[sl].rearrange("p (a t) -> p a t", a=32)[:, gsl4 * 4:(gsl4 + 1) * 4],
                      yn_sh.ap().rearrange("(a p) t -> p a t", p=128)[:, gsl4 * 4:(gsl4 + 1) * 4,
                                                                     i * Q:(i + 1) * Q], s_yl[sl])
                b.dma(SY, yst_lo[sl].rearrange("p (a t) -> p a t", a=32)[:, gsl4 * 4:(gsl4 + 1) * 4],
                      yn_sl.ap().rearrange("(a p) t -> p a t", p=128)[:, gsl4 * 4:(gsl4 + 1) * 4,
                                                                     i * Q:(i + 1) * Q], s_yl[sl])
            b.wait(T_, s_yl[sl])
            if i == 0:
                b.waitm(T_, V_, max(wo_conv_q))
            for wq in range(2):
                bk = wq
                b.wait(T_, b.ms[bank_eng[bk]], bank_ev[bk])
                for k in range(32):
                    for yop, wop, st, sp in (
                        (yst_hi[sl], wouth_hi[wq], k == 0, False),
                        (yst_hi[sl], wouth_lo[wq], False, False),
                        (yst_lo[sl], wouth_hi[wq], False, k == 31),
                    ):
                        b.em(T_, lambda e, k=k, bk=bk, wq=wq, yop=yop, wop=wop, st=st, sp=sp:
                             e.matmul(pb[bk][:, 0:512], yop[:, k * 128:(k + 1) * 128],
                                      wop[:, k * 512:(k + 1) * 512],
                                      start=st, stop=sp), inc=sp)
            yst_ev[sl] = b.ms[T_].n
            asl = ecnt % 2
            b.waitm(V_, T_)
            b.wait(V_, s_asp[asl])
            for wq in range(2):
                b.em(V_, lambda e, asl=asl, wq=wq: e.tensor_copy(
                    ast_r[asl][:, wq * 512:(wq + 1) * 512], pb[wq][:, 0:512]), inc=(wq == 1))
            for wq in range(2):
                bank_ev[wq] = b.ms[V_].n; bank_eng[wq] = V_
            b.waitm(SY, V_)
            b.dma(SY, attn_s.ap()[i * Q:(i + 1) * Q, ep * 1024:(ep + 1) * 1024],
                  ast_r[asl][:, 0:1024], s_asp[asl])
            ecnt += 1
        pass_mm_done = b.ms[T_].n
        if ep == 0:
            conv_wouth(2)
            conv_wouth(3)

    s_fa = b.sem("fa")
    s_fx = b.sem("fx")
    s_x2p = b.sem("x2p")

    def ln_tile(src, dst, scr):
        b.em(V_, lambda e: e.reduce_sum(stats[:, 0:1], src, axis=AX.X), inc=True)
        b.waitm(S_, V_)
        b.em(S_, lambda e: e.activation(scr, src, AF.Square, accum_out=stats[:, 1:2]), inc=True)
        b.waitm(V_, S_)
        b.em(V_, lambda e: e.tensor_scalar(stats[:, 2:3], stats[:, 0:1], 1.0 / D, None, ALU.mult))
        b.em(V_, lambda e: e.tensor_tensor(stats[:, 3:4], stats[:, 2:3], stats[:, 2:3], ALU.mult))
        b.em(V_, lambda e: e.scalar_tensor_tensor(stats[:, 4:5], stats[:, 1:2], 1.0 / D,
                                                  stats[:, 3:4], ALU.mult, ALU.subtract), inc=True)
        b.waitm(S_, V_)
        b.em(S_, lambda e: e.activation(stats[:, 5:6], stats[:, 4:5], AF.Ln, bias=cvt[:, 0:1], scale=1.0))
        b.em(S_, lambda e: e.activation(stats[:, 6:7], stats[:, 5:6], AF.Exp, scale=-0.5), inc=True)
        b.waitm(V_, S_)
        b.em(V_, lambda e: e.tensor_scalar(dst, src, stats[:, 2:3], stats[:, 6:7],
                                           ALU.subtract, ALU.mult), inc=True)

    b.wait(SY, s_asp[0]); b.wait(SY, s_asp[1])
    fa_ev = 0
    for i in range(n_own):
        b.waitm(SY, V_, fa_ev)
        b.dma(SY, f_attn, attn_s.ap()[i * Q:(i + 1) * Q, :], s_fa)
        b.dma(SY, f_x, x_in.ap()[(halo_tiles + i) * Q:(halo_tiles + i + 1) * Q, :], s_fx)
        b.wait(V_, s_fa)
        b.wait(V_, s_fx)
        b.wait(S_, s_fa)
        ln_tile(f_attn, f_attn, f_tmp)
        if i >= 1:
            b.wait(V_, s_x2p, 16 * i)
        b.em(V_, lambda e: e.tensor_tensor(f_x2, f_x, f_attn, ALU.add), inc=True)
        fa_ev = b.ms[V_].n
        b.waitm(SY, V_)
        b.dma(SY, x2_s.ap()[i * Q:(i + 1) * Q, :], f_x2, s_x2p)
        ln_tile(f_x2, f_x3, f_tmp)
        b.waitm(T_, V_)
        for g in range(4):
            bk = g % 2
            bank = pb[bk]
            b.wait(T_, b.ms[bank_eng[bk]], bank_ev[bk])
            for u in range(4):
                k = g * 4 + u
                b.em(T_, lambda e, k=k, bank=bank, u=u: e.transpose(
                    bank[:, u * 128:(u + 1) * 128], f_x3[:, k * 128:(k + 1) * 128], ident[:, :]),
                    inc=(u == 3))
            b.waitm(S_, T_)
            b.em(S_, lambda e, i=i, g=g, bank=bank: e.copy(
                x3hi[:, (i * 16 + g * 4) * 128:(i * 16 + g * 4 + 4) * 128], bank[:, :]), inc=True)
            b.waitm(V_, S_)
            b.em(V_, lambda e, i=i, g=g, bank=bank: e.tensor_tensor(
                x3lo[:, (i * 16 + g * 4) * 128:(i * 16 + g * 4 + 4) * 128], bank[:, :],
                x3hi[:, (i * 16 + g * 4) * 128:(i * 16 + g * 4 + 4) * 128], ALU.subtract), inc=True)
            bank_ev[bk] = b.ms[V_].n; bank_eng[bk] = V_

    s_wg = [b.sem("wg0"), b.sem("wg1")]
    s_hsp = b.sem("hsp")
    b.waitm(T_, V_)
    njo = len(jo)
    wg_ev = [0, 0]
    wg_cv = [0, 0]
    for c in range(NH):
        for part in range(2):
            slot = part
            col0 = part * INTER + c * 128
            b.wait(SY, b.ms[V_], wg_cv[slot])
            for k in range(16):
                b.dma(SY, wgu_r[slot][:, k * 128:(k + 1) * 128],
                      w_gu_t.ap()[k * 128:(k + 1) * 128, col0:col0 + 128], s_wg[slot])
            b.wait(S_, s_wg[slot])
            b.wait(S_, b.ms[T_], wg_ev[slot])
            b.em(S_, lambda e, slot=slot: e.copy(wguhi_r[slot][:, :], wgu_r[slot][:, :]), inc=True)
            b.waitm(V_, S_)
            b.em(V_, lambda e, slot=slot: e.tensor_tensor(
                wgulo_r[slot][:, :], wgu_r[slot][:, :], wguhi_r[slot][:, :], ALU.subtract), inc=True)
            wg_cv[slot] = b.ms[V_].n
            b.waitm(T_, V_, wg_cv[slot])
            for j, (t0, tw) in enumerate(jo):
                bk = part * 4 + j
                bank = pb[bk]
                b.wait(T_, b.ms[bank_eng[bk]], bank_ev[bk])
                ntl = (tw + 127) // 128
                i0 = t0 // 128
                for k in range(16):
                    for wop, xop, st, sp in (
                        (wguhi_r[slot], x3hi_r, k == 0, False),
                        (wguhi_r[slot], x3lo_r, False, False),
                        (wgulo_r[slot], x3hi_r, False, k == 15),
                    ):
                        b.em(T_, lambda e, k=k, bank=bank, i0=i0, ntl=ntl, tw=tw,
                             wop=wop, xop=xop, st=st, sp=sp: e.matmul(
                            bank[:, 0:tw], wop[:, k * 128:(k + 1) * 128],
                            xop[:, i0:i0 + ntl, k:k + 1, :], start=st, stop=sp),
                            inc=sp)
            wg_ev[slot] = b.ms[T_].n
            if part == 0:
                b.waitm(S_, T_)
                b.waitm(S_, V_)
                for j, (t0, tw) in enumerate(jo):
                    b.em(S_, lambda e, j=j, t0=t0, tw=tw: e.activation(
                        gs_b[:, t0:t0 + tw], pb[j][:, 0:tw], AF.Silu), inc=(j == njo - 1))
                for j in range(njo):
                    bank_ev[j] = b.ms[S_].n; bank_eng[j] = S_
            else:
                b.waitm(V_, S_)
                b.waitm(V_, T_)
                if c >= 1:
                    b.wait(V_, s_hsp, 32 * c)
                for j, (t0, tw) in enumerate(jo):
                    b.em(V_, lambda e, j=j, t0=t0, tw=tw: e.tensor_tensor(
                        hb[:, t0:t0 + tw], pb[4 + j][:, 0:tw], gs_b[:, t0:t0 + tw], ALU.mult),
                        inc=(j == njo - 1))
                for j in range(njo):
                    bank_ev[4 + j] = b.ms[V_].n; bank_eng[4 + j] = V_
                b.waitm(S_, V_)
                b.em(S_, lambda e: e.copy(hb_hi[:, :], hb[:, 0:Town]), inc=True)
                b.waitm(V_, S_)
                b.em(V_, lambda e: e.tensor_tensor(hb_lo[:, :], hb[:, 0:Town], hb_hi[:, :],
                                                   ALU.subtract), inc=True)
                b.waitm(SY, V_)
                b.dma(SY, h_sh.ap()[c * 128:(c + 1) * 128, :], hb_hi[:, :], s_hsp)
                b.dma(SY, h_sl.ap()[c * 128:(c + 1) * 128, :], hb_lo[:, :], s_hsp)

    s_hq = b.sem("hq")
    s_wd = [b.sem("wd0"), b.sem("wd1")]
    s_x2l = b.sem("x2l")
    s_out = b.sem("outd")
    s_out2 = [b.sem("out0"), b.sem("out1")]
    b.wait(SY, s_hsp)
    b.wait(SY, s_x2p)
    b.wait(SY, s_wpc)
    n8 = Town // 256
    wd_ev = [0, 0]
    hq_half_mm = [0, 0]
    s_hqh = [b.sem("hqh0"), b.sem("hqh1")]

    def load_hq_half(q, hh):
        for g4 in range(8):
            a0 = hh * 32 + g4 * 4
            b.dma(SY, hq_hi.rearrange("p (a t) -> p a t", a=64)[:, a0:a0 + 4],
                  h_sh.ap().rearrange("(a p) t -> p a t", p=128)[:, a0:a0 + 4,
                                                                q * 256:(q + 1) * 256], s_hqh[hh])
            b.dma(SY, hq_lo.rearrange("p (a t) -> p a t", a=64)[:, a0:a0 + 4],
                  h_sl.ap().rearrange("(a p) t -> p a t", p=128)[:, a0:a0 + 4,
                                                                q * 256:(q + 1) * 256], s_hqh[hh])

    load_hq_half(0, 0)
    load_hq_half(0, 1)
    for q in range(n8):
        for k in range(64):
            if k == 0:
                b.wait(T_, s_hqh[0])
            if k == 32:
                b.wait(T_, s_hqh[1])
            if k == 32 and q + 1 < n8:
                b.wait(SY, b.ms[T_], hq_half_mm[0])
                load_hq_half(q + 1, 0)
            slot = k % 2
            b.wait(SY, b.ms[T_], wd_ev[slot])
            b.dma(SY, wdhi_r[slot][:, :], wd_sh.ap()[k * 128:(k + 1) * 128, :], s_wd[slot])
            b.dma(SY, wdlo_r[slot][:, :], wd_sl.ap()[k * 128:(k + 1) * 128, :], s_wd[slot])
            b.wait(T_, s_wd[slot])
            if k == 0:
                for bk in range(8):
                    b.wait(T_, b.ms[bank_eng[bk]], bank_ev[bk])
            for tt in range(2):
                for ob in range(4):
                    bk = tt * 4 + ob
                    for pi, (hop, wop, st, sp) in enumerate((
                        (hq_hi, wdhi_r[slot], k == 0, False),
                        (hq_hi, wdlo_r[slot], False, False),
                        (hq_lo, wdhi_r[slot], False, k == 63),
                    )):
                        b.em(T_, lambda e, k=k, tt=tt, ob=ob, bk=bk, hop=hop, wop=wop,
                             st=st, sp=sp: e.matmul(
                            pb[bk][:, 0:512], hop[:, k * 256 + tt * 128:k * 256 + tt * 128 + 128],
                            wop[:, ob * 512:(ob + 1) * 512],
                            start=st, stop=sp), inc=(pi == 2 and tt == 1 and ob == 3))
            wd_ev[slot] = b.ms[T_].n
            if k == 31:
                hq_half_mm[0] = b.ms[T_].n
        hq_half_mm[1] = b.ms[T_].n
        if q + 1 < n8:
            b.waitm(SY, T_)
            load_hq_half(q + 1, 1)
        dn_mm = b.ms[T_].n
        b.waitm(V_, T_, dn_mm)
        for tt in range(2):
            for ob in range(4):
                b.em(V_, lambda e, tt=tt, ob=ob: e.tensor_copy(
                    dtm_r[tt][:, ob * 512:(ob + 1) * 512], pb[tt * 4 + ob][:, 0:512]),
                    inc=(tt == 1 and ob == 3))
        for bk2 in range(8):
            bank_ev[bk2] = b.ms[V_].n; bank_eng[bk2] = V_
        for tt in range(2):
            gi = q * 2 + tt
            b.waitm(SY, V_)
            b.wait(SY, s_out2[tt])
            b.dma(SY, h_x2[tt], x2_s.ap()[gi * Q:(gi + 1) * Q, :], s_x2l)
            b.wait(V_, s_x2l)
            b.wait(S_, s_x2l)
            ln_tile(dtm_r[tt], h_out[tt], h_sq)
            b.em(V_, lambda e, tt=tt: e.tensor_tensor(h_out[tt], h_out[tt], h_x2[tt], ALU.add),
                 inc=True)
            b.waitm(SY, V_)
            b.dma(SY, out_t.ap()[gi * Q:(gi + 1) * Q, :], h_out[tt], s_out2[tt])
            s_out.n += 0

    b.wait(SY, s_out2[0])
    b.wait(SY, s_out2[1])

    with nc.Block() as block:
        @block.tensor
        def _(e):
            for fn in b.cmds["tensor"]:
                fn(e)

        @block.vector
        def _(e):
            for fn in b.cmds["vector"]:
                fn(e)

        @block.scalar
        def _(e):
            for fn in b.cmds["scalar"]:
                fn(e)

        @block.sync
        def _(e):
            for fn in b.cmds["sync"]:
                fn(e)

        @block.gpsimd
        def _(e):
            for fn in b.cmds["gpsimd"]:
                fn(e)

    nc.finalize()
    return nc


N_CORES = 8
N_TILES = 18
HALO_TILES = 2


def make_inmaps(inputs, n_tiles=N_TILES, halo_tiles=HALO_TILES, n_cores=N_CORES):
    T = n_tiles * Q
    Town = T - halo_tiles * Q
    halo = halo_tiles * Q
    x = np.ascontiguousarray(np.asarray(inputs["L0_attn_hidden_states"], dtype=np.float32)
                             ).reshape(-1, D)
    total = x.shape[0]
    assert total == n_cores * Town, (total, n_cores, Town)
    seq = total // 2
    w_in_a = np.ascontiguousarray(np.asarray(inputs["L0_attn_W_in"], np.float32))
    w_out_a = np.ascontiguousarray(np.asarray(inputs["L0_attn_W_out"], np.float32))
    w_gu_t = np.ascontiguousarray(np.asarray(inputs["L0_ffn_w_gate_up"], np.float32).T)
    w_down_t = np.ascontiguousarray(np.asarray(inputs["L0_ffn_w_down"], np.float32).T)
    cw_raw = np.asarray(inputs["L0_attn_conv_w"], np.float32)
    cb_raw = np.asarray(inputs["L0_attn_conv_b"], np.float32)
    convw = np.zeros((128, 33 * 4), np.float32)
    convb = np.zeros((128, 33), np.float32)
    for a in range(33):
        nch = min(128, NCONV - a * 128)
        convw[0:nch, a * 4:(a + 1) * 4] = cw_raw[a * 128:a * 128 + nch, :]
        convb[0:nch, a] = cb_raw[a * 128:a * 128 + nch]
    alog_a = np.asarray(inputs["L0_attn_A_log"], np.float32).reshape(NH, 1).copy()
    dtbias_a = np.asarray(inputs["L0_attn_dt_bias"], np.float32).reshape(NH, 1).copy()
    dskip_a = np.ascontiguousarray(np.broadcast_to(
        np.asarray(inputs["L0_attn_D_skip"], np.float32)[None, :], (128, NH)))
    normw_a = np.ascontiguousarray(np.broadcast_to(
        np.asarray(inputs["L0_attn_norm_w"], np.float32)[None, :], (128, DIN)))

    in_maps = []
    for core in range(n_cores):
        r0 = core * Town
        start = r0 - halo
        seq_start = (r0 // seq) * seq
        if start < seq_start:
            pad = seq_start - start
            xs = np.concatenate([np.zeros((pad, D), np.float32), x[seq_start:r0 + Town]], axis=0)
            hm = np.zeros((128, max(halo, 1)), np.float32)
            if halo > pad:
                hm[:, pad:] = 1.0
        else:
            xs = np.ascontiguousarray(x[start:r0 + Town])
            hm = np.ones((128, max(halo, 1)), np.float32)
        in_maps.append({
            "x_in": np.ascontiguousarray(xs), "w_in": w_in_a, "w_out": w_out_a,
            "w_gu_t": w_gu_t, "w_down_t": w_down_t, "convw": convw, "convb": convb,
            "alog": alog_a, "dtbias": dtbias_a, "dskip": dskip_a, "normw": normw_a,
            "halomask": np.ascontiguousarray(hm),
        })
    return in_maps


def kernel(**inputs):
    in_maps = make_inmaps(inputs)
    nc = build_program(N_TILES, HALO_TILES)
    res = run_bass_kernel_spmd(nc, in_maps, core_ids=list(range(N_CORES)))
    out = np.concatenate([np.asarray(r["out"]) for r in res.results], axis=0)
    return out.reshape(2, 8192, D).astype(np.float32)


# version 09
# speedup vs baseline: 1.2722x; 1.0327x over previous
import numpy as np
import concourse.bass as bass
import concourse.mybir as mybir
from concourse.bass_utils import run_bass_kernel_spmd

F32 = mybir.dt.float32
BF16 = mybir.dt.bfloat16
AF = mybir.ActivationFunctionType
ALU = mybir.AluOpType
AX = mybir.AxisListType

D = 2048
DIN = 4096
NH = 64
HP = 64
NCONV = 4128
INDIM = 8288
INTER = 8192
EPS = 1e-5
Q = 128
CLAMP = 40.0


class Sem:
    __slots__ = ("h", "n")

    def __init__(self, nc, name):
        self.h = nc.alloc_semaphore(name)
        self.n = 0


class B:
    ENGINES = ("tensor", "vector", "scalar", "sync", "gpsimd")

    def __init__(self, nc):
        self.nc = nc
        self.cmds = {e: [] for e in self.ENGINES}
        self._i = 0
        self.ms = {e: Sem(nc, "ms_" + e) for e in self.ENGINES}

    def sem(self, name=None):
        self._i += 1
        return Sem(self.nc, name or f"s{self._i}")

    def em(self, eng, make, inc=False, sem=None, n=1):
        caps = []
        if inc:
            m = self.ms[eng]
            m.n += 1
            caps.append((m.h, 1))
        if sem is not None:
            sem.n += n
            caps.append((sem.h, n))

        def fn(e, make=make, caps=tuple(caps)):
            ins = make(e)
            for h, k in caps:
                ins = ins.then_inc(h, k)
        self.cmds[eng].append(fn)

    def dma(self, eng, out, in_, sem=None):
        self.em(eng, lambda e, o=out, i=in_: e.dma_start(out=o, in_=i), sem=sem, n=16)

    def wait(self, eng, sem, val=None):
        v = sem.n if val is None else val
        if v <= 0:
            return
        h = sem.h
        self.cmds[eng].append(lambda e, h=h, v=v: e.wait_ge(h, v))

    def waitm(self, eng, src_eng, val=None):
        self.wait(eng, self.ms[src_eng], val)


def build_program(n_tiles, halo_tiles, dbg=False):
    T = n_tiles * Q
    Thalf = T // 2
    nth = n_tiles // 2
    assert n_tiles % 2 == 0
    Town = T - halo_tiles * Q
    n_own = n_tiles - halo_tiles
    assert Town % 256 == 0

    def mov_tiles(total):
        r, t0 = [], 0
        while t0 < total:
            tw = min(512, total - t0)
            r.append((t0, tw))
            t0 += tw
        return r

    jth = mov_tiles(Thalf)
    jo = mov_tiles(Town)

    nc = bass.Bass("TRN2", target_bir_lowering=False, detect_race_conditions=False)
    b = B(nc)
    T_, V_, S_, SY, G_ = "tensor", "vector", "scalar", "sync", "gpsimd"

    x_in = nc.dram_tensor("x_in", [T, D], F32, kind="ExternalInput")
    w_in = nc.dram_tensor("w_in", [D, INDIM], F32, kind="ExternalInput")
    w_out = nc.dram_tensor("w_out", [DIN, D], F32, kind="ExternalInput")
    w_gu_t = nc.dram_tensor("w_gu_t", [D, 2 * INTER], F32, kind="ExternalInput")
    w_down_t = nc.dram_tensor("w_down_t", [INTER, D], F32, kind="ExternalInput")
    convw = nc.dram_tensor("convw", [128, 33 * 4], F32, kind="ExternalInput")
    convb = nc.dram_tensor("convb", [128, 33], F32, kind="ExternalInput")
    alog = nc.dram_tensor("alog", [NH, 1], F32, kind="ExternalInput")
    dtbias = nc.dram_tensor("dtbias", [NH, 1], F32, kind="ExternalInput")
    dskip = nc.dram_tensor("dskip", [128, NH], F32, kind="ExternalInput")
    normw = nc.dram_tensor("normw", [128, DIN], F32, kind="ExternalInput")
    hm_w = max(halo_tiles * Q, 1)
    halomask = nc.dram_tensor("halomask", [128, hm_w], F32, kind="ExternalInput")
    out_t = nc.dram_tensor("out", [Town, D], F32, kind="ExternalOutput")

    kd = "ExternalOutput" if dbg else "Internal"
    gsilu_s = nc.dram_tensor("gsilu_s", [DIN, T], F32, kind=kd)
    xbc_s = nc.dram_tensor("xbc_s", [NCONV, T], F32, kind=kd)
    yn_sh = nc.dram_tensor("yn_sh", [DIN, Town], BF16, kind=kd)
    yn_sl = nc.dram_tensor("yn_sl", [DIN, Town], BF16, kind=kd)
    attn_s = nc.dram_tensor("attn_s", [Town, D], F32, kind=kd)
    x2_s = nc.dram_tensor("x2_s", [Town, D], F32, kind=kd)
    h_sh = nc.dram_tensor("h_sh", [INTER, Town], BF16, kind=kd)
    h_sl = nc.dram_tensor("h_sl", [INTER, Town], BF16, kind=kd)
    wd_sh = nc.dram_tensor("wd_sh", [INTER, D], BF16)
    wd_sl = nc.dram_tensor("wd_sl", [INTER, D], BF16)

    ident_h = nc.inline_tensor(np.eye(128, dtype=np.float32), "c_ident")
    bdneg = np.zeros((2, 64, 4096), np.float32)
    for j in range(32):
        bdneg[0, j, j * 128:(j + 1) * 128] = -1.0
        bdneg[1, 32 + j, j * 128:(j + 1) * 128] = -1.0
    bdneg_h = [nc.inline_tensor(bdneg[0], "c_bdneg0"), nc.inline_tensor(bdneg[1], "c_bdneg1")]
    triu_h = nc.inline_tensor(np.triu(np.ones((128, 128), np.float32)), "c_triu")
    ones1_h = nc.inline_tensor(np.ones((1, 128), np.float32), "c_ones1")
    eye16_h = nc.inline_tensor(np.eye(16, dtype=np.float32), "c_eye16")

    A = nc.alloc_sbuf_tensor
    ident = A("ident", [128, 128], F32)
    triu = A("triu", [128, 128], F32)
    ones1 = A("ones1", [1, 128], F32)
    eye16 = A("eye16", [16, 16], F32)
    cwsb = A("cwsb", [128, 33 * 4], F32)
    cbsb = A("cbsb", [128, 33], F32)
    aneg = A("aneg", [128, 1], F32)
    dtb = A("dtb", [128, 1], F32)
    dsk = A("dsk", [128, NH], F32)
    hmask = A("hmask", [128, hm_w], F32)
    states = A("states", [16, NH * HP], F32)
    stats = A("stats", [128, 8], F32)
    rowst = A("rowst", [1, 64], F32)
    dax16 = A("dax16", [16, NH], F32)
    dAI = A("dAI", [16, 16], F32)
    bw16 = A("bw16", [128, 16], F32)
    sv3 = A("sv3", [128, 33 * 3], F32)
    cvt = A("cvt", [128, 2], F32)

    rem = nc.sbuf_bytes_remaining // 4
    AR = (rem - 512) // 4 * 4
    arena = A("arena", [128, AR], F32)
    ar = arena.ap()

    def rg(off, n):
        assert off + n <= AR, (off, n, AR)
        return ar[:, off:off + n]

    def rgb(off, nb):
        assert nb % 2 == 0 and off + nb // 2 <= AR, (off, nb, AR)
        return ar[:, off:off + nb // 2].bitcast(BF16)

    x1fm_n = nth * 16 * 128
    o = x1fm_n
    XT = o; o += 2 * D
    X1T = o; o += D
    SQB = o; o += D
    WBUF = o; o += 2 * 2048
    WHI = o; o += 2 * 1024
    WLO = o; o += 2 * 1024
    GBUF = o; o += 2 * 512
    XPRE = o; o += Thalf + 4
    XBCB = o; o += 2 * Thalf
    DTRAW = o; o += T
    WDPP = o; o += 4096
    AB_END = o
    assert AB_END <= AR, (AB_END, AR)

    x1hi = rgb(0, x1fm_n)
    x1lo = rgb(x1fm_n // 2, x1fm_n)
    x1hi_r = x1hi.rearrange("p (i k t) -> p i k t", k=16, t=128)
    x1lo_r = x1lo.rearrange("p (i k t) -> p i k t", k=16, t=128)
    xt_r = [rg(XT, D), rg(XT + D, D)]
    x1t = rg(X1T, D)
    sqb = rg(SQB, D)
    wbuf_r = [rg(WBUF, 2048), rg(WBUF + 2048, 2048)]
    whi_r = [rgb(WHI, 2048), rgb(WHI + 1024, 2048)]
    wlo_r = [rgb(WLO, 2048), rgb(WLO + 1024, 2048)]
    gbuf_r = [rg(GBUF, 512), rg(GBUF + 512, 512)]
    xpre = rg(XPRE, Thalf + 4)
    xbc_r = [rg(XBCB, Thalf), rg(XBCB + Thalf, Thalf)]
    dtraw = rg(DTRAW, T)
    wpp_st = rg(WDPP, 2048)
    wpp_hi = rgb(WDPP + 2048, 2048)
    wpp_lo = rgb(WDPP + 3072, 2048)

    o = 0
    BD0 = o; o += 4096
    BD1 = o; o += 4096
    BIGD = o; o += 4096
    XBCC = o; o += 33 * 128
    XTM = o; o += DIN
    YTM = o; o += DIN
    GFM = o; o += DIN
    CST = o; o += 5 * 128
    BCS = o; o += 3 * 128
    YNST = o; o += 4 * 128
    SSD_B = o
    if max(SSD_B, AB_END) + 5 * T + 128 + DIN <= AR:
        CUMO = max(SSD_B, AB_END)
    else:
        CUMO = SSD_B
        assert CUMO + 3 * T <= DTRAW, (CUMO + 3 * T, DTRAW)
    CUMPRO = CUMO + T
    DTTO = CUMPRO + T
    OCPO = DTTO + T
    CUMLOO = OCPO + T
    DTAO = CUMLOO + T
    NWO = DTAO + 128
    SSD_END = NWO + DIN
    assert SSD_END <= AR, (SSD_END, AR)

    bd_r = [rg(BD0, 4096), rg(BD1, 4096)]
    bigD = rg(BIGD, 4096)
    xbcc = rg(XBCC, 33 * 128)
    xtm = rg(XTM, DIN)
    ytm = rg(YTM, DIN)
    gfm = rg(GFM, DIN)
    cumT_sb = rg(CST, 64)
    ecumT = rg(CST + 64, 64)
    wexT = rg(CST + 128, 64)
    wexs = rg(CST + 192, 128)
    bcs = rg(BCS, 384)
    ynst = rg(YNST, 512)
    cum = rg(CUMO, T)
    cumpr = rg(CUMPRO, T)
    ocp = rg(OCPO, T)
    cumlo = rg(CUMLOO, T)
    dt_t = rg(DTTO, T)
    dtA = rg(DTAO, 128)
    nw = rg(NWO, DIN)
    BW8 = NWO + DIN
    bw8_r = [rg(BW8, 128), rg(BW8 + 128, 128)]
    dai8_r = [rg(BW8 + 256, 128), rg(BW8 + 384, 128)]
    assert BW8 + 512 <= AR, (BW8 + 512, AR)

    WOH = 0
    wouth_hi = [rgb(WOH, 32 * 512), rgb(WOH + 16384, 32 * 512)]
    wouth_lo = [rgb(WOH + 8192, 32 * 512), rgb(WOH + 24576, 32 * 512)]
    WST = WOH + 32768
    wst_r = [rg(WST, 512), rg(WST + 512, 512)]
    YST = WST + 1024
    yst_hi = [rgb(YST + s * 4096, 4096) for s in range(2)]
    yst_lo = [rgb(YST + s * 4096 + 2048, 4096) for s in range(2)]
    AST = YST + 2 * 4096
    ast_r = [rg(AST, 1024), rg(AST + 1024, 1024)]
    assert AST + 2048 <= AR, (AST + 2048, AR)

    x3fm_n = n_own * 16 * 128
    x3hi = rgb(0, x3fm_n)
    x3lo = rgb(x3fm_n // 2, x3fm_n)
    x3hi_r = x3hi.rearrange("p (i k t) -> p i k t", k=16, t=128)
    x3lo_r = x3lo.rearrange("p (i k t) -> p i k t", k=16, t=128)
    fo = x3fm_n
    f_attn = rg(fo, D)
    f_x = rg(fo + D, D)
    f_tmp = rg(fo + 2 * D, D)
    f_x2 = rg(fo + 3 * D, D)
    f_x3 = rg(fo + 4 * D, D)
    assert fo + 5 * D <= AR
    wgu_r = [rg(fo, 2048), rg(fo + 2048, 2048)]
    wguhi_r = [rgb(fo + 4096, 2048), rgb(fo + 5120, 2048)]
    wgulo_r = [rgb(fo + 6144, 2048), rgb(fo + 7168, 2048)]
    gs_b = rg(fo + 8192, Town)
    hb = rg(fo + 8192 + Town, Town)
    hb_hi = rgb(fo + 8192 + 2 * Town, Town)
    hb_lo = rgb(fo + 8192 + 2 * Town + Town // 2, Town)
    assert fo + 8192 + 3 * Town <= AR, (fo + 8192 + 3 * Town, AR)

    hq_hi = rgb(0, 64 * 256)
    hq_lo = rgb(64 * 128, 64 * 256)
    HWD = 64 * 256
    wdhi_r = [rgb(HWD, 2048), rgb(HWD + 1024, 2048)]
    wdlo_r = [rgb(HWD + 2048, 2048), rgb(HWD + 3072, 2048)]
    ho = HWD + 4096
    dtm_r = [rg(ho, D), rg(ho + D, D)]
    h_x2 = [rg(ho + 2 * D, D), rg(ho + 5 * D, D)]
    h_out = [rg(ho + 3 * D, D), rg(ho + 6 * D, D)]
    h_sq = rg(ho + 4 * D, D)
    assert ho + 7 * D <= AR, (ho + 7 * D, AR)

    pb = [nc.alloc_psum_tensor(f"pb{i}", [128, 512], F32) for i in range(8)]

    s_ld = b.sem("ld")
    b.dma(SY, ident[:, :], ident_h.ap(), s_ld)
    b.dma(SY, triu[:, :], triu_h.ap(), s_ld)
    b.dma(SY, ones1[:, :], ones1_h.ap(), s_ld)
    b.dma(SY, eye16[:, :], eye16_h.ap(), s_ld)
    b.dma(SY, cwsb[:, :], convw.ap()[:, :], s_ld)
    b.dma(SY, cbsb[:, :], convb.ap()[:, :], s_ld)
    b.dma(SY, aneg[0:64, 0:1], alog.ap(), s_ld)
    b.dma(SY, dtb[0:64, 0:1], dtbias.ap(), s_ld)
    b.dma(SY, dsk[:, :], dskip.ap(), s_ld)
    b.dma(SY, hmask[:, :], halomask.ap(), s_ld)

    b.wait(S_, s_ld)
    b.em(S_, lambda e: e.activation(aneg[0:64, 0:1], aneg[0:64, 0:1], AF.Exp, scale=1.0))
    b.em(S_, lambda e: e.mul(aneg[0:64, 0:1], aneg[0:64, 0:1], -1.0), inc=True)
    b.wait(V_, s_ld)
    b.em(V_, lambda e: e.memset(cvt[:, 0:1], float(EPS)))
    b.em(V_, lambda e: e.memset(cvt[:, 1:2], 1.0))
    b.em(V_, lambda e: e.memset(xpre[:, 0:4], 0.0))
    b.em(V_, lambda e: e.memset(wbuf_r[0][:, :], 0.0))
    b.em(V_, lambda e: e.memset(wbuf_r[1][:, :], 0.0))
    b.em(V_, lambda e: e.memset(stats[:, :], 0.0), inc=True)
    a_init_v = b.ms[V_].n
    a_init_s = b.ms[S_].n

    s_ldx = [b.sem("ldx0"), b.sem("ldx1")]
    s_ldw = [b.sem("ldw0"), b.sem("ldw1")]
    s_gsp = [b.sem("gsp0"), b.sem("gsp1")]
    s_xsp = [b.sem("xsp0"), b.sem("xsp1")]
    NCC = (INDIM + 127) // 128

    bank_ev = [0] * 8
    bank_eng = [V_] * 8
    b.wait(T_, s_ld)
    b.waitm(T_, V_, a_init_v)

    xt_ev = [0, 0]
    gate_cnt = 0
    wconv_v = [0, 0]
    s_wpl = b.sem("wpl")
    s_wpc = b.sem("wpc")
    wpp_k = [0]

    def wd_prepass_step():
        k = wpp_k[0]
        if k >= 64:
            return
        wpp_k[0] += 1
        if k > 0:
            b.wait(G_, s_wpc)
        b.dma(G_, wpp_st, w_down_t.ap()[k * 128:(k + 1) * 128, :], s_wpl)
        b.wait(S_, s_wpl)
        b.em(S_, lambda e: e.copy(wpp_hi[:, :], wpp_st), inc=True)
        b.waitm(V_, S_)
        b.em(V_, lambda e: e.tensor_tensor(wpp_lo[:, :], wpp_st, wpp_hi[:, :], ALU.subtract),
             inc=True)
        b.waitm(S_, V_)
        b.em(S_, lambda e, k=k: e.dma_start(out=wd_sh.ap()[k * 128:(k + 1) * 128, :],
                                            in_=wpp_hi[:, :]), sem=s_wpc, n=16)
        b.em(S_, lambda e, k=k: e.dma_start(out=wd_sl.ap()[k * 128:(k + 1) * 128, :],
                                            in_=wpp_lo[:, :]), sem=s_wpc, n=16)
    for half in range(2):
        tb = half * nth
        for li in range(nth):
            gi = tb + li
            sl = li % 2
            b.wait(SY, b.ms[V_], xt_ev[sl])
            b.dma(SY, xt_r[sl], x_in.ap()[gi * Q:(gi + 1) * Q, :], s_ldx[sl])
            b.wait(V_, s_ldx[sl])
            b.em(V_, lambda e, sl=sl: e.reduce_sum(stats[:, 0:1], xt_r[sl], axis=AX.X), inc=True)
            b.wait(S_, s_ldx[sl])
            b.waitm(S_, V_)
            b.em(S_, lambda e, sl=sl: e.activation(sqb, xt_r[sl], AF.Square, accum_out=stats[:, 1:2]),
                 inc=True)
            b.waitm(V_, S_)
            b.em(V_, lambda e: e.tensor_scalar(stats[:, 2:3], stats[:, 0:1], 1.0 / D, None, ALU.mult))
            b.em(V_, lambda e: e.tensor_tensor(stats[:, 3:4], stats[:, 2:3], stats[:, 2:3], ALU.mult))
            b.em(V_, lambda e: e.scalar_tensor_tensor(stats[:, 4:5], stats[:, 1:2], 1.0 / D,
                                                      stats[:, 3:4], ALU.mult, ALU.subtract), inc=True)
            b.waitm(S_, V_)
            b.em(S_, lambda e: e.activation(stats[:, 5:6], stats[:, 4:5], AF.Ln, bias=cvt[:, 0:1], scale=1.0))
            b.em(S_, lambda e: e.activation(stats[:, 6:7], stats[:, 5:6], AF.Exp, scale=-0.5), inc=True)
            b.waitm(V_, S_)
            b.em(V_, lambda e, sl=sl: e.tensor_scalar(x1t, xt_r[sl], stats[:, 2:3], stats[:, 6:7],
                                                      ALU.subtract, ALU.mult), inc=True)
            xt_ev[sl] = b.ms[V_].n
            b.waitm(T_, V_)
            for g in range(4):
                bank = pb[g]
                b.wait(T_, b.ms[bank_eng[g]], bank_ev[g])
                for u in range(4):
                    k = g * 4 + u
                    b.em(T_, lambda e, k=k, bank=bank, u=u: e.transpose(
                        bank[:, u * 128:(u + 1) * 128], x1t[:, k * 128:(k + 1) * 128], ident[:, :]),
                        inc=(u == 3))
                b.waitm(S_, T_)
                b.em(S_, lambda e, li=li, g=g, bank=bank: e.copy(
                    x1hi[:, (li * 16 + g * 4) * 128:(li * 16 + g * 4 + 4) * 128], bank[:, :]), inc=True)
                b.waitm(V_, S_)
                b.em(V_, lambda e, li=li, g=g, bank=bank: e.tensor_tensor(
                    x1lo[:, (li * 16 + g * 4) * 128:(li * 16 + g * 4 + 4) * 128], bank[:, :],
                    x1hi[:, (li * 16 + g * 4) * 128:(li * 16 + g * 4 + 4) * 128], ALU.subtract), inc=True)
                bank_ev[g] = b.ms[V_].n
                bank_eng[g] = V_

        while wpp_k[0] < 64 and half == 1:
            wd_prepass_step()
        b.waitm(T_, V_)
        for c in range(NCC):
            cw = min(128, INDIM - c * 128)
            sl = c % 2
            if c == 0:
                if half == 1:
                    b.waitm(SY, T_)
                for k in range(16):
                    b.dma(SY, wbuf_r[0][:, k * 128:k * 128 + cw],
                          w_in.ap()[k * 128:(k + 1) * 128, 0:cw], s_ldw[0])
                b.wait(S_, s_ldw[0])
                b.em(S_, lambda e, cw=cw: e.copy(whi_r[0][:, 0:16 * 128], wbuf_r[0][:, :]), inc=True)
                b.waitm(V_, S_)
                b.em(V_, lambda e, cw=cw: e.tensor_tensor(
                    wlo_r[0][:, 0:16 * 128], wbuf_r[0][:, :], whi_r[0][:, 0:16 * 128],
                    ALU.subtract), inc=True)
                wconv_v[0] = b.ms[V_].n
            if c + 1 < NCC:
                cw2 = min(128, INDIM - (c + 1) * 128)
                if c >= 1:
                    b.waitm(SY, T_)
                sl2 = (c + 1) % 2
                for k in range(16):
                    b.dma(SY, wbuf_r[sl2][:, k * 128:k * 128 + cw2],
                          w_in.ap()[k * 128:(k + 1) * 128, (c + 1) * 128:(c + 1) * 128 + cw2],
                          s_ldw[sl2])
                b.wait(S_, s_ldw[sl2])
                if c >= 1:
                    b.waitm(S_, T_)
                b.em(S_, lambda e, sl2=sl2: e.copy(whi_r[sl2][:, 0:16 * 128], wbuf_r[sl2][:, :]),
                     inc=True)
                b.waitm(V_, S_)
                b.em(V_, lambda e, sl2=sl2: e.tensor_tensor(
                    wlo_r[sl2][:, 0:16 * 128], wbuf_r[sl2][:, :], whi_r[sl2][:, 0:16 * 128],
                    ALU.subtract), inc=True)
                wconv_v[sl2] = b.ms[V_].n
            wd_prepass_step()
            b.waitm(T_, V_, wconv_v[sl])
            for j, (t0, tw) in enumerate(jth):
                bank = pb[j]
                b.wait(T_, b.ms[bank_eng[j]], bank_ev[j])
                ntl = (tw + 127) // 128
                i0 = t0 // 128
                if c < 64:
                    for k in range(16):
                        for wop, xop, st, sp in (
                            (whi_r[sl], x1hi_r, k == 0, False),
                            (whi_r[sl], x1lo_r, False, False),
                            (wlo_r[sl], x1hi_r, False, k == 15),
                        ):
                            b.em(T_, lambda e, k=k, cw=cw, bank=bank, i0=i0, ntl=ntl, tw=tw,
                                 wop=wop, xop=xop, st=st, sp=sp:
                                 e.matmul(bank[0:cw, 0:tw], wop[:, k * 128:k * 128 + cw],
                                          xop[:, i0:i0 + ntl, k:k + 1, :],
                                          start=st, stop=sp),
                                 inc=sp)
                else:
                    bank2 = pb[3 + j]
                    b.wait(T_, b.ms[bank_eng[3 + j]], bank_ev[3 + j])
                    for k in range(16):
                        for wop, xop, st, sp in (
                            (whi_r[sl], x1hi_r, k == 0, False),
                            (whi_r[sl], x1lo_r, False, False),
                            (wlo_r[sl], x1hi_r, False, k == 15),
                        ):
                            b.em(T_, lambda e, k=k, bank=bank, i0=i0, ntl=ntl, tw=tw,
                                 wop=wop, xop=xop, st=st, sp=sp:
                                 e.matmul(bank[0:32, 0:tw], wop[:, k * 128:k * 128 + 32],
                                          xop[:, i0:i0 + ntl, k:k + 1, :],
                                          start=st, stop=sp))
                            b.em(T_, lambda e, k=k, bank2=bank2, i0=i0, ntl=ntl, tw=tw,
                                 wop=wop, xop=xop, st=st, sp=sp:
                                 e.matmul(bank2[0:64, 0:tw], wop[:, k * 128 + 32:k * 128 + 96],
                                          xop[:, i0:i0 + ntl, k:k + 1, :],
                                          start=st, stop=sp),
                                 inc=sp)
                mmv = b.ms[T_].n
                if c < 32:
                    gsl = gate_cnt % 2
                    b.waitm(S_, T_, mmv)
                    b.wait(S_, s_gsp[gsl])
                    b.em(S_, lambda e, bank=bank, tw=tw, gsl=gsl:
                         e.activation(gbuf_r[gsl][:, 0:tw], bank[0:128, 0:tw], AF.Silu), inc=True)
                    bank_ev[j] = b.ms[S_].n
                    bank_eng[j] = S_
                    b.waitm(SY, S_)
                    b.dma(SY, gsilu_s.ap()[c * 128:(c + 1) * 128,
                                           half * Thalf + t0:half * Thalf + t0 + tw],
                          gbuf_r[gsl][:, 0:tw], s_gsp[gsl])
                    gate_cnt += 1
                elif c < 64:
                    b.waitm(V_, T_, mmv)
                    b.em(V_, lambda e, bank=bank, t0=t0, tw=tw: e.tensor_copy(
                        xpre[:, 4 + t0:4 + t0 + tw], bank[0:128, 0:tw]), inc=True)
                    bank_ev[j] = b.ms[V_].n
                    bank_eng[j] = V_
                else:
                    b.waitm(V_, T_, mmv)
                    b.em(V_, lambda e, bank=bank, t0=t0, tw=tw: e.tensor_copy(
                        xpre[0:32, 4 + t0:4 + t0 + tw], bank[0:32, 0:tw]))
                    b.em(V_, lambda e, j=j, t0=t0, tw=tw, half=half: e.tensor_copy(
                        dtraw[0:64, half * Thalf + t0:half * Thalf + t0 + tw], pb[3 + j][0:64, 0:tw]),
                        inc=True)
                    bank_ev[j] = b.ms[V_].n
                    bank_eng[j] = V_
                    bank_ev[3 + j] = b.ms[V_].n
                    bank_eng[3 + j] = V_
            if c >= 32:
                cc = c - 32
                nch = 128 if c < 64 else 32
                xb = xbc_r[cc % 2]
                b.wait(V_, s_xsp[cc % 2])
                if half == 1:
                    b.em(V_, lambda e, cc=cc, nch=nch: e.tensor_copy(
                        xpre[0:nch, 1:4], sv3[0:nch, cc * 3:cc * 3 + 3]))
                if dbg == 2:
                    b.em(V_, lambda e, nch=nch, xb=xb: e.tensor_copy(
                        xb[0:nch, 0:Thalf], xpre[0:nch, 4:4 + Thalf]))
                b.em(V_, lambda e, cc=cc, nch=nch, xb=xb: (None if dbg == 2 else e.tensor_scalar(
                    xb[0:nch, 0:Thalf], xpre[0:nch, 1:1 + Thalf], cwsb[0:nch, cc * 4:cc * 4 + 1],
                    cbsb[0:nch, cc:cc + 1], ALU.mult, ALU.add)) or e.nop())
                for k in range(1, 4):
                    if dbg == 2:
                        break
                    b.em(V_, lambda e, cc=cc, nch=nch, xb=xb, k=k: e.scalar_tensor_tensor(
                        xb[0:nch, 0:Thalf], xpre[0:nch, 1 + k:1 + k + Thalf],
                        cwsb[0:nch, cc * 4 + k:cc * 4 + k + 1], xb[0:nch, 0:Thalf],
                        ALU.mult, ALU.add))
                if half == 0:
                    b.em(V_, lambda e, cc=cc, nch=nch: e.tensor_copy(
                        sv3[0:nch, cc * 3:cc * 3 + 3], xpre[0:nch, 1 + Thalf:4 + Thalf]))
                    if halo_tiles > 0:
                        b.em(V_, lambda e, nch=nch, xb=xb: e.tensor_tensor(
                            xb[0:nch, 0:halo_tiles * Q], xb[0:nch, 0:halo_tiles * Q],
                            hmask[0:nch, :], ALU.mult))
                b.em(V_, lambda e, nch=nch, xb=xb: e.tensor_copy(xb[0:nch, 0:1], xb[0:nch, 0:1]),
                     inc=True)
                b.waitm(S_, V_)
                b.em(S_, lambda e, nch=nch, xb=xb: (e.nop() if dbg in (2, 3) else e.activation(
                    xb[0:nch, 0:Thalf], xb[0:nch, 0:Thalf], AF.Silu)), inc=True)
                b.waitm(SY, S_)
                b.dma(SY, xbc_s.ap()[cc * 128:cc * 128 + nch, half * Thalf:(half + 1) * Thalf],
                      xb[0:nch, 0:Thalf], s_xsp[cc % 2])

    b.waitm(S_, V_)
    b.em(S_, lambda e: e.activation(dt_t[0:64, :], dtraw[0:64, :], AF.Exp, bias=dtb[0:64, 0:1],
                                    scale=1.0))
    b.em(S_, lambda e: e.activation(dt_t[0:64, :], dt_t[0:64, :], AF.Ln, bias=cvt[0:64, 1:2], scale=1.0))
    b.em(S_, lambda e: e.activation(dtraw[0:64, :], dt_t[0:64, :], AF.Ln, scale=1.0), inc=True)
    b.waitm(V_, S_)
    b.em(V_, lambda e: e.memset(sqb[0:64, 0:Q], 0.0))
    for ci in range(n_tiles):
        b.em(V_, lambda e, ci=ci: e.tensor_scalar(
            dtA[0:64, 0:Q], dt_t[0:64, ci * Q:(ci + 1) * Q], aneg[0:64, 0:1], None, ALU.mult))
        b.em(V_, lambda e, ci=ci: e.tensor_tensor_scan(
            cum[0:64, ci * Q:(ci + 1) * Q], dtA[0:64, 0:Q], sqb[0:64, 0:Q],
            0.0, ALU.add, ALU.add))
    b.em(V_, lambda e: e.tensor_tensor(cumpr[0:64, :], cum[0:64, :], dtraw[0:64, :], ALU.subtract))
    b.em(V_, lambda e: e.memset(ocp[0:64, :], 1.0))
    b.em(V_, lambda e: e.memset(cum[64:128, :], 0.0))
    b.em(V_, lambda e: e.memset(cumpr[64:128, :], 0.0))
    b.em(V_, lambda e: e.memset(wexs[64:128, :], 0.0))
    b.em(V_, lambda e: e.memset(states[:, :], 0.0))
    b.em(V_, lambda e: e.memset(bd_r[0][32:64, :], 0.0))
    b.em(V_, lambda e: e.memset(bd_r[1][0:32, :], 0.0))
    s_bdl = b.sem("bdl")
    b.waitm(SY, V_)
    for hf in range(2):
        b.dma(SY, bd_r[hf][64:128, :], bdneg_h[hf].ap(), s_bdl)
    b.wait(G_, s_bdl)
    b.wait(T_, s_bdl)
    b.em(V_, lambda e: e.tensor_copy(cumlo[0:64, :], cum[0:64, :]), inc=True)
    s_Cd = b.sem("Cd")
    b.waitm(SY, V_)
    b.dma(SY, ocp[64:128, :], cumpr[0:64, :], s_Cd)

    s_xc = b.sem("xc")
    s_gf = b.sem("gf")
    s_rs = b.sem("rs")
    s_ysp = [b.sem(f"ysp{i}") for i in range(4)]
    s_nw = b.sem("nw")
    b.waitm(SY, V_)
    b.dma(SY, nw[:, :], normw.ap(), s_nw)
    b.wait(G_, s_Cd)
    b.wait(T_, s_Cd)
    b.waitm(G_, V_)

    gfm_free_v = 0
    gfm_free_s = 0
    xbcc_free = 0
    yn_cnt = 0

    for ci in range(n_tiles):
        own = ci >= halo_tiles
        oci = ci - halo_tiles
        tsl = slice(ci * Q, (ci + 1) * Q)
        b.waitm(SY, T_, xbcc_free)
        if ci == 0:
            b.wait(SY, s_xsp[0]); b.wait(SY, s_xsp[1])
        for g4 in range(8):
            b.dma(SY, xbcc[:, g4 * 512:(g4 + 1) * 512].rearrange("p (a t) -> p a t", a=4),
                  xbc_s.ap()[0:4096, :].rearrange("(a p) t -> p a t", p=128)[:, g4 * 4:(g4 + 1) * 4, tsl],
                  s_xc)
        b.dma(SY, bcs[0:16, 0:128], xbc_s.ap()[4096:4112, tsl], s_xc)
        b.dma(SY, bcs[0:16, 128:256], xbc_s.ap()[4112:4128, tsl], s_xc)
        if own:
            b.waitm(SY, V_, gfm_free_v)
            b.waitm(SY, S_, gfm_free_s)
            if oci == 0:
                b.wait(SY, s_gsp[0]); b.wait(SY, s_gsp[1])
            for g4 in range(8):
                b.dma(SY, gfm[:, g4 * 512:(g4 + 1) * 512].rearrange("p (a t) -> p a t", a=4),
                      gsilu_s.ap().rearrange("(a p) t -> p a t", p=128)[:, g4 * 4:(g4 + 1) * 4, tsl],
                      s_gf)

        b.waitm(G_, T_)
        for hf in range(2):
            h0 = hf * 32
            srcap = cumlo[h0:h0 + 32, tsl].unsqueeze(1).to_broadcast([32, 32, 128])
            dstap = bd_r[hf][h0:h0 + 32, :].rearrange("p (h t) -> p h t", h=32)
            b.em(G_, lambda e, srcap=srcap, dstap=dstap: e.affine_select(
                dstap, srcap, pattern=[[1, 32], [0, 128]], compare_op=ALU.is_equal,
                fill=0.0, base=0, channel_multiplier=-1), inc=(hf == 1))
        bd_ready = b.ms[G_].n

        b.wait(T_, b.ms[bank_eng[2]], bank_ev[2])
        b.em(T_, lambda e, tsl=tsl: e.transpose(pb[2][:, 0:128], cum[:, tsl], ident[:, :]), inc=True)
        b.waitm(V_, T_)
        b.em(V_, lambda e: e.tensor_copy(cumT_sb[:, :], pb[2][:, 0:64]), inc=True)
        bank_ev[2] = b.ms[V_].n; bank_eng[2] = V_
        b.waitm(S_, V_)
        b.em(S_, lambda e: e.activation(ecumT[:, :], cumT_sb[:, :], AF.Exp, scale=1.0), inc=True)
        b.em(V_, lambda e, ci=ci: e.tensor_scalar(
            wexs[0:64, :], cumpr[0:64, ci * Q:(ci + 1) * Q],
            cum[0:64, ci * Q + Q - 1:ci * Q + Q], -1.0, ALU.subtract, ALU.mult), inc=True)
        b.waitm(S_, V_)
        b.em(S_, lambda e: e.activation(wexs[0:64, :], wexs[0:64, :], AF.Exp, scale=1.0), inc=True)
        b.waitm(T_, S_)
        b.wait(T_, b.ms[bank_eng[3]], bank_ev[3])
        b.em(T_, lambda e: e.transpose(pb[3][:, 0:128], wexs[:, :], ident[:, :]), inc=True)
        b.waitm(V_, T_)
        b.em(V_, lambda e: e.tensor_copy(wexT[:, :], pb[3][:, 0:64]), inc=True)
        bank_ev[3] = b.ms[V_].n; bank_eng[3] = V_
        b.em(S_, lambda e: e.dma_start(out=rowst[0:1, 0:64], in_=ecumT[127:128, :]), sem=s_rs, n=16)
        b.wait(T_, s_rs)
        b.wait(T_, b.ms[bank_eng[3]], bank_ev[3])
        b.em(T_, lambda e: e.matmul(pb[3][0:16, 0:64], ones1[0:1, 0:16], rowst[0:1, 0:64],
                                    start=True, stop=True), inc=True)
        b.waitm(V_, T_)
        b.em(V_, lambda e: e.tensor_copy(dax16[:, :], pb[3][0:16, 0:64]), inc=True)
        bank_ev[3] = b.ms[V_].n; bank_eng[3] = V_

        b.wait(T_, s_xc)
        for g in range(8):
            bk = 2 + g % 2
            bank = pb[bk]
            b.wait(T_, b.ms[bank_eng[bk]], bank_ev[bk])
            for u in range(4):
                a = g * 4 + u
                b.em(T_, lambda e, a=a, bank=bank, u=u: e.transpose(
                    bank[:, u * 128:(u + 1) * 128], xbcc[:, a * 128:(a + 1) * 128], ident[:, :]),
                    inc=(u == 3))
            b.waitm(V_, T_)
            b.em(V_, lambda e, g=g, bank=bank: e.tensor_copy(
                xtm[:, g * 512:(g + 1) * 512], bank[:, :]), inc=True)
            bank_ev[bk] = b.ms[V_].n; bank_eng[bk] = V_
        b.wait(T_, b.ms[bank_eng[2]], bank_ev[2])
        b.em(T_, lambda e: e.transpose(pb[2][:, 0:16], bcs[0:16, 0:128], ident[0:16, 0:16]), inc=True)
        xbcc_free = b.ms[T_].n
        b.waitm(V_, T_)
        b.em(V_, lambda e: e.tensor_copy(bcs[:, 256:272], pb[2][:, 0:16]), inc=True)
        bank_ev[2] = b.ms[V_].n; bank_eng[2] = V_

        b.wait(T_, b.ms[bank_eng[4]], bank_ev[4])
        b.em(T_, lambda e: e.matmul(pb[4][:, 0:128], bcs[0:16, 0:128], bcs[0:16, 128:256],
                                    start=True, stop=True), inc=True)
        b.waitm(V_, T_)
        b.em(V_, lambda e: e.tensor_tensor(wexs[:, :], pb[4][:, 0:128], triu[:, :], ALU.mult), inc=True)
        bank_ev[4] = b.ms[V_].n; bank_eng[4] = V_
        mcb_v = b.ms[V_].n

        for hf in range(2):
            h0 = hf * 32
            b.wait(T_, b.ms[G_], bd_ready)
            for jj in range(8):
                bk = jj % 2
                bank = pb[bk]
                b.wait(T_, b.ms[bank_eng[bk]], bank_ev[bk])
                b.em(T_, lambda e, hf=hf, jj=jj, bank=bank, tsl=tsl: e.matmul(
                    bank[:, 0:512], ocp[:, tsl], bd_r[hf][:, jj * 512:(jj + 1) * 512],
                    start=True, stop=True), inc=True)
                b.waitm(V_, T_)
                b.em(V_, lambda e, jj=jj, bank=bank: e.tensor_scalar(
                    bigD[:, jj * 512:(jj + 1) * 512], bank[:, 0:512], CLAMP, None, ALU.min), inc=True)
                bank_ev[bk] = b.ms[V_].n; bank_eng[bk] = V_
            b.waitm(S_, V_)
            b.em(S_, lambda e: e.activation(bigD[:, :], bigD[:, :], AF.Exp, scale=1.0), inc=True)
            b.waitm(V_, S_)
            b.em(V_, lambda e: e.tensor_tensor(
                bigD.rearrange("p (h t) -> p h t", h=32), bigD.rearrange("p (h t) -> p h t", h=32),
                wexs.unsqueeze(1).to_broadcast([128, 32, 128]), ALU.mult), inc=True)
            s_ready = b.ms[V_].n
            b.waitm(T_, V_, s_ready)

            def p1(g8):
                hg = [h0 + g8 * 8 + j for j in range(8)]
                bi = 4 + g8 % 2
                bs = 6 + g8 % 2
                b.wait(T_, b.ms[bank_eng[bi]], bank_ev[bi])
                b.wait(T_, b.ms[bank_eng[bs]], bank_ev[bs])
                for j, h in enumerate(hg):
                    hh = h - h0
                    b.em(T_, lambda e, hh=hh, h=h, bi=bi, j=j: e.matmul(
                        pb[bi][:, j * 64:(j + 1) * 64], bigD[:, hh * 128:(hh + 1) * 128],
                        xtm[:, h * 64:(h + 1) * 64], start=True, stop=True))
                    b.em(T_, lambda e, h=h, bs=bs, j=j: e.matmul(
                        pb[bs][:, j * 64:(j + 1) * 64], bcs[0:16, 128:256],
                        states[0:16, h * 64:(h + 1) * 64], start=True, stop=True),
                        inc=(j == 7))
                return b.ms[T_].n

            def p2(g8, ymm):
                hg = [h0 + g8 * 8 + j for j in range(8)]
                bi = 4 + g8 % 2
                bs = 6 + g8 % 2
                b.waitm(V_, T_, ymm)
                for j, h in enumerate(hg):
                    b.em(V_, lambda e, h=h, bi=bi, j=j: e.scalar_tensor_tensor(
                        ytm[:, h * 64:(h + 1) * 64], xtm[:, h * 64:(h + 1) * 64],
                        dsk[:, h:h + 1], pb[bi][:, j * 64:(j + 1) * 64], ALU.mult, ALU.add))
                    b.em(V_, lambda e, h=h, bs=bs, j=j: e.scalar_tensor_tensor(
                        ytm[:, h * 64:(h + 1) * 64], pb[bs][:, j * 64:(j + 1) * 64],
                        ecumT[:, h:h + 1], ytm[:, h * 64:(h + 1) * 64], ALU.mult, ALU.add),
                        inc=(j == 7))
                bank_ev[bi] = b.ms[V_].n; bank_eng[bi] = V_
                bank_ev[bs] = b.ms[V_].n; bank_eng[bs] = V_

            def p3v(g8):
                hg = [h0 + g8 * 8 + j for j in range(8)]
                wsl8 = g8 % 2
                b.waitm(V_, T_)
                for j, h in enumerate(hg):
                    b.em(V_, lambda e, h=h, j=j, wsl8=wsl8: e.tensor_scalar(
                        bw8_r[wsl8][:, j * 16:(j + 1) * 16], bcs[:, 256:272],
                        wexT[:, h:h + 1], None, ALU.mult))
                    b.em(V_, lambda e, h=h, j=j, wsl8=wsl8: e.tensor_scalar(
                        dai8_r[wsl8][0:16, j * 16:(j + 1) * 16], eye16[:, :],
                        dax16[:, h:h + 1], None, ALU.mult), inc=(j == 7))
                return b.ms[V_].n

            def p3t(g8, bwv):
                hg = [h0 + g8 * 8 + j for j in range(8)]
                wsl8 = g8 % 2
                bu = 2 + g8 % 2
                b.waitm(T_, V_, bwv)
                b.wait(T_, b.ms[bank_eng[bu]], bank_ev[bu])
                for j, h in enumerate(hg):
                    b.em(T_, lambda e, h=h, bu=bu, j=j, wsl8=wsl8: e.matmul(
                        pb[bu][0:16, j * 64:(j + 1) * 64], bw8_r[wsl8][:, j * 16:(j + 1) * 16],
                        xtm[:, h * 64:(h + 1) * 64], start=True, stop=False))
                    b.em(T_, lambda e, h=h, bu=bu, j=j, wsl8=wsl8: e.matmul(
                        pb[bu][0:16, j * 64:(j + 1) * 64], dai8_r[wsl8][0:16, j * 16:(j + 1) * 16],
                        states[0:16, h * 64:(h + 1) * 64], start=False, stop=True),
                        inc=(j == 7))
                return b.ms[T_].n

            def pev(g8):
                bu = 2 + g8 % 2
                b.waitm(V_, T_)
                b.em(V_, lambda e, g8=g8, bu=bu: e.tensor_copy(
                    states[0:16, (h0 + g8 * 8) * 64:(h0 + g8 * 8 + 8) * 64],
                    pb[bu][0:16, 0:512]), inc=True)
                bank_ev[bu] = b.ms[V_].n; bank_eng[bu] = V_

            if own:
                ymms = {0: p1(0)}
                for g8 in range(4):
                    if g8 + 1 < 4:
                        ymms[g8 + 1] = p1(g8 + 1)
                    p2(g8, ymms[g8])
                    bwv = p3v(g8)
                    p3t(g8, bwv)
                    pev(g8)
            else:
                bwvs = {0: p3v(0)}
                for g8 in range(4):
                    if g8 + 1 < 4:
                        bwvs[g8 + 1] = p3v(g8 + 1)
                    p3t(g8, bwvs[g8])
                    pev(g8)

        if own:
            b.wait(T_, s_gf)
            for g in range(32):
                bk = 2 + g % 2
                bank = pb[bk]
                b.wait(T_, b.ms[bank_eng[bk]], bank_ev[bk])
                b.em(T_, lambda e, g=g, bank=bank: e.transpose(
                    bank[:, 0:128], gfm[:, g * 128:(g + 1) * 128], ident[:, :]), inc=True)
                b.waitm(V_, T_)
                b.em(V_, lambda e, g=g, bank=bank: e.tensor_tensor(
                    ytm[:, g * 128:(g + 1) * 128], bank[:, 0:128],
                    ytm[:, g * 128:(g + 1) * 128], ALU.mult), inc=True)
                bank_ev[bk] = b.ms[V_].n; bank_eng[bk] = V_
            gfm_free_v = b.ms[V_].n
            b.waitm(S_, V_)
            b.em(S_, lambda e: e.activation(gfm[:, :], ytm[:, :], AF.Square, accum_out=stats[:, 1:2]))
            b.em(S_, lambda e: e.activation(stats[:, 5:6], stats[:, 1:2], AF.Ln, bias=cvt[:, 0:1],
                                            scale=1.0 / DIN))
            b.em(S_, lambda e: e.activation(stats[:, 6:7], stats[:, 5:6], AF.Exp, scale=-0.5), inc=True)
            gfm_free_s = b.ms[S_].n
            b.waitm(V_, S_)
            b.wait(V_, s_nw)
            b.em(V_, lambda e: e.scalar_tensor_tensor(ytm[:, :], ytm[:, :], stats[:, 6:7], nw[:, :],
                                                      ALU.mult, ALU.mult), inc=True)
            ynv = b.ms[V_].n
            b.waitm(T_, V_, ynv)
            for g in range(32):
                bk = 2 + g % 2
                bank = pb[bk]
                ysl = yn_cnt % 4
                yh = rgb(YNST + ysl * 128, 256)
                yl = rgb(YNST + ysl * 128 + 64, 256)
                b.wait(T_, b.ms[bank_eng[bk]], bank_ev[bk])
                b.em(T_, lambda e, g=g, bank=bank: e.transpose(
                    bank[:, 0:128], ytm[:, g * 128:(g + 1) * 128], ident[:, :]), inc=True)
                b.waitm(S_, T_)
                b.wait(S_, s_ysp[ysl])
                b.em(S_, lambda e, yh=yh, bank=bank: e.copy(yh[:, 0:128], bank[:, 0:128]), inc=True)
                b.waitm(V_, S_)
                b.em(V_, lambda e, yh=yh, yl=yl, bank=bank: e.tensor_tensor(
                    yl[:, 0:128], bank[:, 0:128], yh[:, 0:128], ALU.subtract), inc=True)
                bank_ev[bk] = b.ms[V_].n; bank_eng[bk] = V_
                b.waitm(SY, V_)
                b.dma(SY, yn_sh.ap()[g * 128:(g + 1) * 128, oci * Q:(oci + 1) * Q],
                      yh[:, 0:128], s_ysp[ysl])
                b.dma(SY, yn_sl.ap()[g * 128:(g + 1) * 128, oci * Q:(oci + 1) * Q],
                      yl[:, 0:128], s_ysp[ysl])
                yn_cnt += 1

    s_wo = [b.sem("wo0"), b.sem("wo1")]
    s_yl = [b.sem(f"yl{s}") for s in range(3)]
    s_asp = [b.sem("asp0"), b.sem("asp1")]
    yst_ev = [0, 0, 0]
    for i in range(4):
        b.wait(SY, s_ysp[i])
    wst_cv = [0, 0]
    pass_mm_done = 0
    wo_conv_q = [0, 0]

    def conv_wouth(oq):
        wq = oq % 2
        for k in range(32):
            wsl = k % 2
            b.wait(SY, b.ms[V_], wst_cv[wsl])
            b.dma(SY, wst_r[wsl],
                  w_out.ap()[k * 128:(k + 1) * 128, oq * 512:(oq + 1) * 512], s_wo[wsl])
            b.wait(S_, s_wo[wsl])
            if oq >= 2:
                b.wait(S_, b.ms[T_], pass_mm_done)
            b.em(S_, lambda e, k=k, wsl=wsl, wq=wq: e.copy(
                wouth_hi[wq][:, k * 512:(k + 1) * 512], wst_r[wsl]), inc=True)
            b.waitm(V_, S_)
            b.em(V_, lambda e, k=k, wsl=wsl, wq=wq: e.tensor_tensor(
                wouth_lo[wq][:, k * 512:(k + 1) * 512], wst_r[wsl],
                wouth_hi[wq][:, k * 512:(k + 1) * 512], ALU.subtract), inc=True)
            wst_cv[wsl] = b.ms[V_].n
        wo_conv_q[wq] = b.ms[V_].n

    conv_wouth(0)
    conv_wouth(1)
    ecnt = 0
    for ep in range(2):
        for i in range(n_own):
            if ep == 1 and i == 0:
                pass_mm_done_snapshot = pass_mm_done
            if ep == 0 and i == 1:
                pass_mm_done = b.ms[T_].n
            sl = i % 2
            if ep > 0 or i >= 2:
                b.wait(SY, b.ms[T_], yst_ev[sl])
            for gsl4 in range(8):
                b.dma(SY, yst_hi[sl].rearrange("p (a t) -> p a t", a=32)[:, gsl4 * 4:(gsl4 + 1) * 4],
                      yn_sh.ap().rearrange("(a p) t -> p a t", p=128)[:, gsl4 * 4:(gsl4 + 1) * 4,
                                                                     i * Q:(i + 1) * Q], s_yl[sl])
                b.dma(SY, yst_lo[sl].rearrange("p (a t) -> p a t", a=32)[:, gsl4 * 4:(gsl4 + 1) * 4],
                      yn_sl.ap().rearrange("(a p) t -> p a t", p=128)[:, gsl4 * 4:(gsl4 + 1) * 4,
                                                                     i * Q:(i + 1) * Q], s_yl[sl])
            b.wait(T_, s_yl[sl])
            if i == 0:
                b.waitm(T_, V_, max(wo_conv_q))
            for wq in range(2):
                bk = wq
                b.wait(T_, b.ms[bank_eng[bk]], bank_ev[bk])
                for k in range(32):
                    for yop, wop, st, sp in (
                        (yst_hi[sl], wouth_hi[wq], k == 0, False),
                        (yst_hi[sl], wouth_lo[wq], False, False),
                        (yst_lo[sl], wouth_hi[wq], False, k == 31),
                    ):
                        b.em(T_, lambda e, k=k, bk=bk, wq=wq, yop=yop, wop=wop, st=st, sp=sp:
                             e.matmul(pb[bk][:, 0:512], yop[:, k * 128:(k + 1) * 128],
                                      wop[:, k * 512:(k + 1) * 512],
                                      start=st, stop=sp), inc=sp)
            yst_ev[sl] = b.ms[T_].n
            asl = ecnt % 2
            b.waitm(V_, T_)
            b.wait(V_, s_asp[asl])
            for wq in range(2):
                b.em(V_, lambda e, asl=asl, wq=wq: e.tensor_copy(
                    ast_r[asl][:, wq * 512:(wq + 1) * 512], pb[wq][:, 0:512]), inc=(wq == 1))
            for wq in range(2):
                bank_ev[wq] = b.ms[V_].n; bank_eng[wq] = V_
            b.waitm(SY, V_)
            b.dma(SY, attn_s.ap()[i * Q:(i + 1) * Q, ep * 1024:(ep + 1) * 1024],
                  ast_r[asl][:, 0:1024], s_asp[asl])
            ecnt += 1
        pass_mm_done = b.ms[T_].n
        if ep == 0:
            conv_wouth(2)
            conv_wouth(3)

    s_fa = b.sem("fa")
    s_fx = b.sem("fx")
    s_x2p = b.sem("x2p")

    def ln_tile(src, dst, scr):
        b.em(V_, lambda e: e.reduce_sum(stats[:, 0:1], src, axis=AX.X), inc=True)
        b.waitm(S_, V_)
        b.em(S_, lambda e: e.activation(scr, src, AF.Square, accum_out=stats[:, 1:2]), inc=True)
        b.waitm(V_, S_)
        b.em(V_, lambda e: e.tensor_scalar(stats[:, 2:3], stats[:, 0:1], 1.0 / D, None, ALU.mult))
        b.em(V_, lambda e: e.tensor_tensor(stats[:, 3:4], stats[:, 2:3], stats[:, 2:3], ALU.mult))
        b.em(V_, lambda e: e.scalar_tensor_tensor(stats[:, 4:5], stats[:, 1:2], 1.0 / D,
                                                  stats[:, 3:4], ALU.mult, ALU.subtract), inc=True)
        b.waitm(S_, V_)
        b.em(S_, lambda e: e.activation(stats[:, 5:6], stats[:, 4:5], AF.Ln, bias=cvt[:, 0:1], scale=1.0))
        b.em(S_, lambda e: e.activation(stats[:, 6:7], stats[:, 5:6], AF.Exp, scale=-0.5), inc=True)
        b.waitm(V_, S_)
        b.em(V_, lambda e: e.tensor_scalar(dst, src, stats[:, 2:3], stats[:, 6:7],
                                           ALU.subtract, ALU.mult), inc=True)

    b.wait(SY, s_asp[0]); b.wait(SY, s_asp[1])
    fa_ev = 0
    for i in range(n_own):
        b.waitm(SY, V_, fa_ev)
        b.dma(SY, f_attn, attn_s.ap()[i * Q:(i + 1) * Q, :], s_fa)
        b.dma(SY, f_x, x_in.ap()[(halo_tiles + i) * Q:(halo_tiles + i + 1) * Q, :], s_fx)
        b.wait(V_, s_fa)
        b.wait(V_, s_fx)
        b.wait(S_, s_fa)
        ln_tile(f_attn, f_attn, f_tmp)
        if i >= 1:
            b.wait(V_, s_x2p, 16 * i)
        b.em(V_, lambda e: e.tensor_tensor(f_x2, f_x, f_attn, ALU.add), inc=True)
        fa_ev = b.ms[V_].n
        b.waitm(SY, V_)
        b.dma(SY, x2_s.ap()[i * Q:(i + 1) * Q, :], f_x2, s_x2p)
        ln_tile(f_x2, f_x3, f_tmp)
        b.waitm(T_, V_)
        for g in range(4):
            bk = g % 2
            bank = pb[bk]
            b.wait(T_, b.ms[bank_eng[bk]], bank_ev[bk])
            for u in range(4):
                k = g * 4 + u
                b.em(T_, lambda e, k=k, bank=bank, u=u: e.transpose(
                    bank[:, u * 128:(u + 1) * 128], f_x3[:, k * 128:(k + 1) * 128], ident[:, :]),
                    inc=(u == 3))
            b.waitm(S_, T_)
            b.em(S_, lambda e, i=i, g=g, bank=bank: e.copy(
                x3hi[:, (i * 16 + g * 4) * 128:(i * 16 + g * 4 + 4) * 128], bank[:, :]), inc=True)
            b.waitm(V_, S_)
            b.em(V_, lambda e, i=i, g=g, bank=bank: e.tensor_tensor(
                x3lo[:, (i * 16 + g * 4) * 128:(i * 16 + g * 4 + 4) * 128], bank[:, :],
                x3hi[:, (i * 16 + g * 4) * 128:(i * 16 + g * 4 + 4) * 128], ALU.subtract), inc=True)
            bank_ev[bk] = b.ms[V_].n; bank_eng[bk] = V_

    s_wg = [b.sem("wg0"), b.sem("wg1")]
    s_hsp = b.sem("hsp")
    b.waitm(T_, V_)
    njo = len(jo)
    wg_ev = [0, 0]
    wg_cv = [0, 0]
    for c in range(NH):
        for part in range(2):
            slot = part
            col0 = part * INTER + c * 128
            b.wait(SY, b.ms[V_], wg_cv[slot])
            for k in range(16):
                b.dma(SY, wgu_r[slot][:, k * 128:(k + 1) * 128],
                      w_gu_t.ap()[k * 128:(k + 1) * 128, col0:col0 + 128], s_wg[slot])
            b.wait(S_, s_wg[slot])
            b.wait(S_, b.ms[T_], wg_ev[slot])
            b.em(S_, lambda e, slot=slot: e.copy(wguhi_r[slot][:, :], wgu_r[slot][:, :]), inc=True)
            b.waitm(V_, S_)
            b.em(V_, lambda e, slot=slot: e.tensor_tensor(
                wgulo_r[slot][:, :], wgu_r[slot][:, :], wguhi_r[slot][:, :], ALU.subtract), inc=True)
            wg_cv[slot] = b.ms[V_].n
        for part in range(2):
            slot = part
            b.waitm(T_, V_, wg_cv[slot])
            for j, (t0, tw) in enumerate(jo):
                bk = part * 4 + j
                bank = pb[bk]
                b.wait(T_, b.ms[bank_eng[bk]], bank_ev[bk])
                ntl = (tw + 127) // 128
                i0 = t0 // 128
                for k in range(16):
                    for wop, xop, st, sp in (
                        (wguhi_r[slot], x3hi_r, k == 0, False),
                        (wguhi_r[slot], x3lo_r, False, False),
                        (wgulo_r[slot], x3hi_r, False, k == 15),
                    ):
                        b.em(T_, lambda e, k=k, bank=bank, i0=i0, ntl=ntl, tw=tw,
                             wop=wop, xop=xop, st=st, sp=sp: e.matmul(
                            bank[:, 0:tw], wop[:, k * 128:(k + 1) * 128],
                            xop[:, i0:i0 + ntl, k:k + 1, :], start=st, stop=sp),
                            inc=sp)
            wg_ev[slot] = b.ms[T_].n
            if part == 0:
                b.waitm(S_, T_)
                b.waitm(S_, V_)
                for j, (t0, tw) in enumerate(jo):
                    b.em(S_, lambda e, j=j, t0=t0, tw=tw: e.activation(
                        gs_b[:, t0:t0 + tw], pb[j][:, 0:tw], AF.Silu), inc=(j == njo - 1))
                for j in range(njo):
                    bank_ev[j] = b.ms[S_].n; bank_eng[j] = S_
            else:
                b.waitm(V_, S_)
                b.waitm(V_, T_)
                if c >= 1:
                    b.wait(V_, s_hsp, 32 * c)
                for j, (t0, tw) in enumerate(jo):
                    b.em(V_, lambda e, j=j, t0=t0, tw=tw: e.tensor_tensor(
                        hb[:, t0:t0 + tw], pb[4 + j][:, 0:tw], gs_b[:, t0:t0 + tw], ALU.mult),
                        inc=(j == njo - 1))
                for j in range(njo):
                    bank_ev[4 + j] = b.ms[V_].n; bank_eng[4 + j] = V_
                b.waitm(S_, V_)
                b.em(S_, lambda e: e.copy(hb_hi[:, :], hb[:, 0:Town]), inc=True)
                b.waitm(V_, S_)
                b.em(V_, lambda e: e.tensor_tensor(hb_lo[:, :], hb[:, 0:Town], hb_hi[:, :],
                                                   ALU.subtract), inc=True)
                b.waitm(SY, V_)
                b.dma(SY, h_sh.ap()[c * 128:(c + 1) * 128, :], hb_hi[:, :], s_hsp)
                b.dma(SY, h_sl.ap()[c * 128:(c + 1) * 128, :], hb_lo[:, :], s_hsp)

    s_hq = b.sem("hq")
    s_wd = [b.sem("wd0"), b.sem("wd1")]
    s_x2l = b.sem("x2l")
    s_out = b.sem("outd")
    s_out2 = [b.sem("out0"), b.sem("out1")]
    b.wait(SY, s_hsp)
    b.wait(SY, s_x2p)
    b.wait(SY, s_wpc)
    n8 = Town // 256
    wd_ev = [0, 0]
    hq_half_mm = [0, 0]
    s_hqh = [b.sem("hqh0"), b.sem("hqh1")]

    def load_hq_half(q, hh):
        for g4 in range(8):
            a0 = hh * 32 + g4 * 4
            b.dma(SY, hq_hi.rearrange("p (a t) -> p a t", a=64)[:, a0:a0 + 4],
                  h_sh.ap().rearrange("(a p) t -> p a t", p=128)[:, a0:a0 + 4,
                                                                q * 256:(q + 1) * 256], s_hqh[hh])
            b.dma(SY, hq_lo.rearrange("p (a t) -> p a t", a=64)[:, a0:a0 + 4],
                  h_sl.ap().rearrange("(a p) t -> p a t", p=128)[:, a0:a0 + 4,
                                                                q * 256:(q + 1) * 256], s_hqh[hh])

    load_hq_half(0, 0)
    load_hq_half(0, 1)
    for q in range(n8):
        for k in range(64):
            if k == 0:
                b.wait(T_, s_hqh[0])
            if k == 32:
                b.wait(T_, s_hqh[1])
            if k == 32 and q + 1 < n8:
                b.wait(SY, b.ms[T_], hq_half_mm[0])
                load_hq_half(q + 1, 0)
            slot = k % 2
            b.wait(SY, b.ms[T_], wd_ev[slot])
            b.dma(SY, wdhi_r[slot][:, :], wd_sh.ap()[k * 128:(k + 1) * 128, :], s_wd[slot])
            b.dma(SY, wdlo_r[slot][:, :], wd_sl.ap()[k * 128:(k + 1) * 128, :], s_wd[slot])
            b.wait(T_, s_wd[slot])
            if k == 0:
                for bk in range(8):
                    b.wait(T_, b.ms[bank_eng[bk]], bank_ev[bk])
            for tt in range(2):
                for ob in range(4):
                    bk = tt * 4 + ob
                    for pi, (hop, wop, st, sp) in enumerate((
                        (hq_hi, wdhi_r[slot], k == 0, False),
                        (hq_hi, wdlo_r[slot], False, False),
                        (hq_lo, wdhi_r[slot], False, k == 63),
                    )):
                        b.em(T_, lambda e, k=k, tt=tt, ob=ob, bk=bk, hop=hop, wop=wop,
                             st=st, sp=sp: e.matmul(
                            pb[bk][:, 0:512], hop[:, k * 256 + tt * 128:k * 256 + tt * 128 + 128],
                            wop[:, ob * 512:(ob + 1) * 512],
                            start=st, stop=sp), inc=(pi == 2 and tt == 1 and ob == 3))
            wd_ev[slot] = b.ms[T_].n
            if k == 31:
                hq_half_mm[0] = b.ms[T_].n
        hq_half_mm[1] = b.ms[T_].n
        if q + 1 < n8:
            b.waitm(SY, T_)
            load_hq_half(q + 1, 1)
        dn_mm = b.ms[T_].n
        b.waitm(V_, T_, dn_mm)
        for tt in range(2):
            for ob in range(4):
                b.em(V_, lambda e, tt=tt, ob=ob: e.tensor_copy(
                    dtm_r[tt][:, ob * 512:(ob + 1) * 512], pb[tt * 4 + ob][:, 0:512]),
                    inc=(tt == 1 and ob == 3))
        for bk2 in range(8):
            bank_ev[bk2] = b.ms[V_].n; bank_eng[bk2] = V_
        for tt in range(2):
            gi = q * 2 + tt
            b.waitm(SY, V_)
            b.wait(SY, s_out2[tt])
            b.dma(SY, h_x2[tt], x2_s.ap()[gi * Q:(gi + 1) * Q, :], s_x2l)
            b.wait(V_, s_x2l)
            b.wait(S_, s_x2l)
            ln_tile(dtm_r[tt], h_out[tt], h_sq)
            b.em(V_, lambda e, tt=tt: e.tensor_tensor(h_out[tt], h_out[tt], h_x2[tt], ALU.add),
                 inc=True)
            b.waitm(SY, V_)
            b.dma(SY, out_t.ap()[gi * Q:(gi + 1) * Q, :], h_out[tt], s_out2[tt])
            s_out.n += 0

    b.wait(SY, s_out2[0])
    b.wait(SY, s_out2[1])

    with nc.Block() as block:
        @block.tensor
        def _(e):
            for fn in b.cmds["tensor"]:
                fn(e)

        @block.vector
        def _(e):
            for fn in b.cmds["vector"]:
                fn(e)

        @block.scalar
        def _(e):
            for fn in b.cmds["scalar"]:
                fn(e)

        @block.sync
        def _(e):
            for fn in b.cmds["sync"]:
                fn(e)

        @block.gpsimd
        def _(e):
            for fn in b.cmds["gpsimd"]:
                fn(e)

    nc.finalize()
    return nc


N_CORES = 8
N_TILES = 18
HALO_TILES = 2


def make_inmaps(inputs, n_tiles=N_TILES, halo_tiles=HALO_TILES, n_cores=N_CORES):
    T = n_tiles * Q
    Town = T - halo_tiles * Q
    halo = halo_tiles * Q
    x = np.ascontiguousarray(np.asarray(inputs["L0_attn_hidden_states"], dtype=np.float32)
                             ).reshape(-1, D)
    total = x.shape[0]
    assert total == n_cores * Town, (total, n_cores, Town)
    seq = total // 2
    w_in_a = np.ascontiguousarray(np.asarray(inputs["L0_attn_W_in"], np.float32))
    w_out_a = np.ascontiguousarray(np.asarray(inputs["L0_attn_W_out"], np.float32))
    w_gu_t = np.ascontiguousarray(np.asarray(inputs["L0_ffn_w_gate_up"], np.float32).T)
    w_down_t = np.ascontiguousarray(np.asarray(inputs["L0_ffn_w_down"], np.float32).T)
    cw_raw = np.asarray(inputs["L0_attn_conv_w"], np.float32)
    cb_raw = np.asarray(inputs["L0_attn_conv_b"], np.float32)
    convw = np.zeros((128, 33 * 4), np.float32)
    convb = np.zeros((128, 33), np.float32)
    for a in range(33):
        nch = min(128, NCONV - a * 128)
        convw[0:nch, a * 4:(a + 1) * 4] = cw_raw[a * 128:a * 128 + nch, :]
        convb[0:nch, a] = cb_raw[a * 128:a * 128 + nch]
    alog_a = np.asarray(inputs["L0_attn_A_log"], np.float32).reshape(NH, 1).copy()
    dtbias_a = np.asarray(inputs["L0_attn_dt_bias"], np.float32).reshape(NH, 1).copy()
    dskip_a = np.ascontiguousarray(np.broadcast_to(
        np.asarray(inputs["L0_attn_D_skip"], np.float32)[None, :], (128, NH)))
    normw_a = np.ascontiguousarray(np.broadcast_to(
        np.asarray(inputs["L0_attn_norm_w"], np.float32)[None, :], (128, DIN)))

    in_maps = []
    for core in range(n_cores):
        r0 = core * Town
        start = r0 - halo
        seq_start = (r0 // seq) * seq
        if start < seq_start:
            pad = seq_start - start
            xs = np.concatenate([np.zeros((pad, D), np.float32), x[seq_start:r0 + Town]], axis=0)
            hm = np.zeros((128, max(halo, 1)), np.float32)
            if halo > pad:
                hm[:, pad:] = 1.0
        else:
            xs = np.ascontiguousarray(x[start:r0 + Town])
            hm = np.ones((128, max(halo, 1)), np.float32)
        in_maps.append({
            "x_in": np.ascontiguousarray(xs), "w_in": w_in_a, "w_out": w_out_a,
            "w_gu_t": w_gu_t, "w_down_t": w_down_t, "convw": convw, "convb": convb,
            "alog": alog_a, "dtbias": dtbias_a, "dskip": dskip_a, "normw": normw_a,
            "halomask": np.ascontiguousarray(hm),
        })
    return in_maps


def kernel(**inputs):
    in_maps = make_inmaps(inputs)
    nc = build_program(N_TILES, HALO_TILES)
    res = run_bass_kernel_spmd(nc, in_maps, core_ids=list(range(N_CORES)))
    out = np.concatenate([np.asarray(r["out"]) for r in res.results], axis=0)
    return out.reshape(2, 8192, D).astype(np.float32)


# version 11
# speedup vs baseline: 1.2878x; 1.0123x over previous
import numpy as np
import concourse.bass as bass
import concourse.mybir as mybir
from concourse.bass_utils import run_bass_kernel_spmd

F32 = mybir.dt.float32
BF16 = mybir.dt.bfloat16
AF = mybir.ActivationFunctionType
ALU = mybir.AluOpType
AX = mybir.AxisListType

D = 2048
DIN = 4096
NH = 64
HP = 64
NCONV = 4128
INDIM = 8288
INTER = 8192
EPS = 1e-5
Q = 128
CLAMP = 40.0


class Sem:
    __slots__ = ("h", "n")

    def __init__(self, nc, name):
        self.h = nc.alloc_semaphore(name)
        self.n = 0


class B:
    ENGINES = ("tensor", "vector", "scalar", "sync", "gpsimd")

    def __init__(self, nc):
        self.nc = nc
        self.cmds = {e: [] for e in self.ENGINES}
        self._i = 0
        self.ms = {e: Sem(nc, "ms_" + e) for e in self.ENGINES}

    def sem(self, name=None):
        self._i += 1
        return Sem(self.nc, name or f"s{self._i}")

    def em(self, eng, make, inc=False, sem=None, n=1):
        caps = []
        if inc:
            m = self.ms[eng]
            m.n += 1
            caps.append((m.h, 1))
        if sem is not None:
            sem.n += n
            caps.append((sem.h, n))

        def fn(e, make=make, caps=tuple(caps)):
            ins = make(e)
            for h, k in caps:
                ins = ins.then_inc(h, k)
        self.cmds[eng].append(fn)

    def dma(self, eng, out, in_, sem=None):
        self.em(eng, lambda e, o=out, i=in_: e.dma_start(out=o, in_=i), sem=sem, n=16)

    def wait(self, eng, sem, val=None):
        v = sem.n if val is None else val
        if v <= 0:
            return
        h = sem.h
        self.cmds[eng].append(lambda e, h=h, v=v: e.wait_ge(h, v))

    def waitm(self, eng, src_eng, val=None):
        self.wait(eng, self.ms[src_eng], val)


def build_program(n_tiles, halo_tiles, dbg=False):
    T = n_tiles * Q
    Thalf = T // 2
    nth = n_tiles // 2
    assert n_tiles % 2 == 0
    Town = T - halo_tiles * Q
    n_own = n_tiles - halo_tiles
    assert Town % 256 == 0

    def mov_tiles(total):
        r, t0 = [], 0
        while t0 < total:
            tw = min(512, total - t0)
            r.append((t0, tw))
            t0 += tw
        return r

    jth = mov_tiles(Thalf)
    jo = mov_tiles(Town)

    nc = bass.Bass("TRN2", target_bir_lowering=False, detect_race_conditions=False)
    b = B(nc)
    T_, V_, S_, SY, G_ = "tensor", "vector", "scalar", "sync", "gpsimd"

    x_in = nc.dram_tensor("x_in", [T, D], F32, kind="ExternalInput")
    w_in = nc.dram_tensor("w_in", [D, INDIM], F32, kind="ExternalInput")
    w_out = nc.dram_tensor("w_out", [DIN, D], F32, kind="ExternalInput")
    w_gu_t = nc.dram_tensor("w_gu_t", [D, 2 * INTER], F32, kind="ExternalInput")
    w_down_t = nc.dram_tensor("w_down_t", [INTER, D], F32, kind="ExternalInput")
    convw = nc.dram_tensor("convw", [128, 33 * 4], F32, kind="ExternalInput")
    convb = nc.dram_tensor("convb", [128, 33], F32, kind="ExternalInput")
    alog = nc.dram_tensor("alog", [NH, 1], F32, kind="ExternalInput")
    dtbias = nc.dram_tensor("dtbias", [NH, 1], F32, kind="ExternalInput")
    dskip = nc.dram_tensor("dskip", [128, NH], F32, kind="ExternalInput")
    normw = nc.dram_tensor("normw", [128, DIN], F32, kind="ExternalInput")
    hm_w = max(halo_tiles * Q, 1)
    halomask = nc.dram_tensor("halomask", [128, hm_w], F32, kind="ExternalInput")
    out_t = nc.dram_tensor("out", [Town, D], F32, kind="ExternalOutput")

    kd = "ExternalOutput" if dbg else "Internal"
    gsilu_s = nc.dram_tensor("gsilu_s", [DIN, T], F32, kind=kd)
    xbc_s = nc.dram_tensor("xbc_s", [NCONV, T], F32, kind=kd)
    yn_sh = nc.dram_tensor("yn_sh", [DIN, Town], BF16, kind=kd)
    yn_sl = nc.dram_tensor("yn_sl", [DIN, Town], BF16, kind=kd)
    attn_s = nc.dram_tensor("attn_s", [Town, D], F32, kind=kd)
    x2_s = nc.dram_tensor("x2_s", [Town, D], F32, kind=kd)
    h_sh = nc.dram_tensor("h_sh", [INTER, Town], BF16, kind=kd)
    h_sl = nc.dram_tensor("h_sl", [INTER, Town], BF16, kind=kd)
    wd_sh = nc.dram_tensor("wd_sh", [INTER, D], BF16)
    wd_sl = nc.dram_tensor("wd_sl", [INTER, D], BF16)
    wo_sh = nc.dram_tensor("wo_sh", [DIN, D], BF16)
    wo_sl = nc.dram_tensor("wo_sl", [DIN, D], BF16)

    ident_h = nc.inline_tensor(np.eye(128, dtype=np.float32), "c_ident")
    bdneg = np.zeros((2, 64, 4096), np.float32)
    for j in range(32):
        bdneg[0, j, j * 128:(j + 1) * 128] = -1.0
        bdneg[1, 32 + j, j * 128:(j + 1) * 128] = -1.0
    bdneg_h = [nc.inline_tensor(bdneg[0], "c_bdneg0"), nc.inline_tensor(bdneg[1], "c_bdneg1")]
    triu_h = nc.inline_tensor(np.triu(np.ones((128, 128), np.float32)), "c_triu")
    ones1_h = nc.inline_tensor(np.ones((1, 128), np.float32), "c_ones1")
    eye16_h = nc.inline_tensor(np.eye(16, dtype=np.float32), "c_eye16")

    A = nc.alloc_sbuf_tensor
    ident = A("ident", [128, 128], F32)
    triu = A("triu", [128, 128], F32)
    ones1 = A("ones1", [1, 128], F32)
    eye16 = A("eye16", [16, 16], F32)
    cwsb = A("cwsb", [128, 33 * 4], F32)
    cbsb = A("cbsb", [128, 33], F32)
    aneg = A("aneg", [128, 1], F32)
    dtb = A("dtb", [128, 1], F32)
    dsk = A("dsk", [128, NH], F32)
    hmask = A("hmask", [128, hm_w], F32)
    states = A("states", [16, NH * HP], F32)
    stats = A("stats", [128, 8], F32)
    rowst = A("rowst", [1, 64], F32)
    dax16 = A("dax16", [16, NH], F32)
    dAI = A("dAI", [16, 16], F32)
    bw16 = A("bw16", [128, 16], F32)
    sv3 = A("sv3", [128, 33 * 3], F32)
    cvt = A("cvt", [128, 2], F32)

    rem = nc.sbuf_bytes_remaining // 4
    AR = (rem - 512) // 4 * 4
    arena = A("arena", [128, AR], F32)
    ar = arena.ap()

    def rg(off, n):
        assert off + n <= AR, (off, n, AR)
        return ar[:, off:off + n]

    def rgb(off, nb):
        assert nb % 2 == 0 and off + nb // 2 <= AR, (off, nb, AR)
        return ar[:, off:off + nb // 2].bitcast(BF16)

    x1fm_n = nth * 16 * 128
    o = x1fm_n
    XT = o; o += 2 * D
    X1T = o; o += D
    SQB = o; o += D
    WBUF = o; o += 2 * 2048
    WHI = o; o += 2 * 1024
    WLO = o; o += 2 * 1024
    GBUF = o; o += 2 * 512
    XPRE = o; o += Thalf + 4
    XBCB = o; o += 2 * Thalf
    DTRAW = o; o += T
    WDPP = o; o += 4096
    AB_END = o
    assert AB_END <= AR, (AB_END, AR)

    x1hi = rgb(0, x1fm_n)
    x1lo = rgb(x1fm_n // 2, x1fm_n)
    x1hi_r = x1hi.rearrange("p (i k t) -> p i k t", k=16, t=128)
    x1lo_r = x1lo.rearrange("p (i k t) -> p i k t", k=16, t=128)
    xt_r = [rg(XT, D), rg(XT + D, D)]
    x1t = rg(X1T, D)
    sqb = rg(SQB, D)
    wbuf_r = [rg(WBUF, 2048), rg(WBUF + 2048, 2048)]
    whi_r = [rgb(WHI, 2048), rgb(WHI + 1024, 2048)]
    wlo_r = [rgb(WLO, 2048), rgb(WLO + 1024, 2048)]
    gbuf_r = [rg(GBUF, 512), rg(GBUF + 512, 512)]
    xpre = rg(XPRE, Thalf + 4)
    xbc_r = [rg(XBCB, Thalf), rg(XBCB + Thalf, Thalf)]
    dtraw = rg(DTRAW, T)
    wpp_st = rg(WDPP, 2048)
    wpp_hi = rgb(WDPP + 2048, 2048)
    wpp_lo = rgb(WDPP + 3072, 2048)

    o = 0
    BD0 = o; o += 4096
    BD1 = o; o += 4096
    BIGD = o; o += 4096
    XBCC = o; o += 33 * 128
    XTM = o; o += DIN
    YTM = o; o += DIN
    GFM = o; o += DIN
    CST = o; o += 5 * 128
    BCS = o; o += 3 * 128
    YNST = o; o += 4 * 128
    SSD_B = o
    if max(SSD_B, AB_END) + 5 * T + 128 + DIN <= AR:
        CUMO = max(SSD_B, AB_END)
    else:
        CUMO = SSD_B
        assert CUMO + 3 * T <= DTRAW, (CUMO + 3 * T, DTRAW)
    CUMPRO = CUMO + T
    DTTO = CUMPRO + T
    OCPO = DTTO + T
    CUMLOO = OCPO + T
    DTAO = CUMLOO + T
    NWO = DTAO + 128
    SSD_END = NWO + DIN
    assert SSD_END <= AR, (SSD_END, AR)

    bd_r = [rg(BD0, 4096), rg(BD1, 4096)]
    bigDq = [rg(BIGD, 2048), rg(BIGD + 2048, 2048)]
    xbcc = rg(XBCC, 33 * 128)
    xtm = rg(XTM, DIN)
    ytm = rg(YTM, DIN)
    gfm = rg(GFM, DIN)
    cumT_sb = rg(CST, 64)
    ecumT = rg(CST + 64, 64)
    wexT = rg(CST + 128, 64)
    wexs = rg(CST + 192, 128)
    bcs = rg(BCS, 384)
    ynst = rg(YNST, 512)
    cum = rg(CUMO, T)
    cumpr = rg(CUMPRO, T)
    ocp = rg(OCPO, T)
    cumlo = rg(CUMLOO, T)
    dt_t = rg(DTTO, T)
    dtA = rg(DTAO, 128)
    nw = rg(NWO, DIN)
    BW8 = NWO + DIN
    bw8_r = [rg(BW8, 128), rg(BW8 + 128, 128)]
    dai8_r = [rg(BW8 + 256, 128), rg(BW8 + 384, 128)]
    assert BW8 + 512 <= AR, (BW8 + 512, AR)

    WOH = 0
    wouth_hi = [rgb(WOH, 32 * 512), rgb(WOH + 16384, 32 * 512)]
    wouth_lo = [rgb(WOH + 8192, 32 * 512), rgb(WOH + 24576, 32 * 512)]
    WST = WOH + 32768
    wst_r = [rg(WST, 512), rg(WST + 512, 512)]
    YST = WST + 1024
    yst_hi = [rgb(YST + s * 4096, 4096) for s in range(2)]
    yst_lo = [rgb(YST + s * 4096 + 2048, 4096) for s in range(2)]
    AST = YST + 2 * 4096
    ast_r = [rg(AST, 1024), rg(AST + 1024, 1024)]
    assert AST + 2048 <= AR, (AST + 2048, AR)

    x3fm_n = n_own * 16 * 128
    x3hi = rgb(0, x3fm_n)
    x3lo = rgb(x3fm_n // 2, x3fm_n)
    x3hi_r = x3hi.rearrange("p (i k t) -> p i k t", k=16, t=128)
    x3lo_r = x3lo.rearrange("p (i k t) -> p i k t", k=16, t=128)
    fo = x3fm_n
    f_attn = rg(fo, D)
    f_x = rg(fo + D, D)
    f_tmp = rg(fo + 2 * D, D)
    f_x2 = rg(fo + 3 * D, D)
    f_x3 = rg(fo + 4 * D, D)
    assert fo + 5 * D <= AR
    wgu_r = [rg(fo, 2048), rg(fo + 2048, 2048)]
    wguhi_r = [rgb(fo + 4096, 2048), rgb(fo + 5120, 2048)]
    wgulo_r = [rgb(fo + 6144, 2048), rgb(fo + 7168, 2048)]
    gs_b = rg(fo + 8192, Town)
    hb = rg(fo + 8192 + Town, Town)
    hb_hi = rgb(fo + 8192 + 2 * Town, Town)
    hb_lo = rgb(fo + 8192 + 2 * Town + Town // 2, Town)
    assert fo + 8192 + 3 * Town <= AR, (fo + 8192 + 3 * Town, AR)

    hq_hi = rgb(0, 64 * 256)
    hq_lo = rgb(64 * 128, 64 * 256)
    HWD = 64 * 256
    wdhi_r = [rgb(HWD + s * 2048, 2048) for s in range(4)]
    wdlo_r = [rgb(HWD + s * 2048 + 1024, 2048) for s in range(4)]
    ho = HWD + 8192
    dtm_r = [rg(ho, D), rg(ho + D, D)]
    h_x2 = [rg(ho + 2 * D, D), rg(ho + 5 * D, D)]
    h_out = [rg(ho + 3 * D, D), rg(ho + 6 * D, D)]
    h_sq = rg(ho + 4 * D, D)
    assert ho + 7 * D <= AR, (ho + 7 * D, AR)

    pb = [nc.alloc_psum_tensor(f"pb{i}", [128, 512], F32) for i in range(8)]

    s_ld = b.sem("ld")
    b.dma(SY, ident[:, :], ident_h.ap(), s_ld)
    b.dma(SY, triu[:, :], triu_h.ap(), s_ld)
    b.dma(SY, ones1[:, :], ones1_h.ap(), s_ld)
    b.dma(SY, eye16[:, :], eye16_h.ap(), s_ld)
    b.dma(SY, cwsb[:, :], convw.ap()[:, :], s_ld)
    b.dma(SY, cbsb[:, :], convb.ap()[:, :], s_ld)
    b.dma(SY, aneg[0:64, 0:1], alog.ap(), s_ld)
    b.dma(SY, dtb[0:64, 0:1], dtbias.ap(), s_ld)
    b.dma(SY, dsk[:, :], dskip.ap(), s_ld)
    b.dma(SY, hmask[:, :], halomask.ap(), s_ld)

    b.wait(S_, s_ld)
    b.em(S_, lambda e: e.activation(aneg[0:64, 0:1], aneg[0:64, 0:1], AF.Exp, scale=1.0))
    b.em(S_, lambda e: e.mul(aneg[0:64, 0:1], aneg[0:64, 0:1], -1.0), inc=True)
    b.wait(V_, s_ld)
    b.em(V_, lambda e: e.memset(cvt[:, 0:1], float(EPS)))
    b.em(V_, lambda e: e.memset(cvt[:, 1:2], 1.0))
    b.em(V_, lambda e: e.memset(xpre[:, 0:4], 0.0))
    b.em(V_, lambda e: e.memset(wbuf_r[0][:, :], 0.0))
    b.em(V_, lambda e: e.memset(wbuf_r[1][:, :], 0.0))
    b.em(V_, lambda e: e.memset(stats[:, :], 0.0), inc=True)
    a_init_v = b.ms[V_].n
    a_init_s = b.ms[S_].n

    s_ldx = [b.sem("ldx0"), b.sem("ldx1")]
    s_ldw = [b.sem("ldw0"), b.sem("ldw1")]
    s_gsp = [b.sem("gsp0"), b.sem("gsp1")]
    s_xsp = [b.sem("xsp0"), b.sem("xsp1")]
    NCC = (INDIM + 127) // 128

    bank_ev = [0] * 8
    bank_eng = [V_] * 8
    b.wait(T_, s_ld)
    b.waitm(T_, V_, a_init_v)

    xt_ev = [0, 0]
    gate_cnt = 0
    wconv_v = [0, 0]
    s_wpl = b.sem("wpl")
    s_wpc = b.sem("wpc")
    wpp_k = [0]

    def wd_prepass_step():
        k = wpp_k[0]
        if k >= 96:
            return
        wpp_k[0] += 1
        if k < 64:
            src_ap = w_down_t.ap()[k * 128:(k + 1) * 128, :]
            dst_h = wd_sh.ap()[k * 128:(k + 1) * 128, :]
            dst_l = wd_sl.ap()[k * 128:(k + 1) * 128, :]
        else:
            k2 = k - 64
            src_ap = w_out.ap()[k2 * 128:(k2 + 1) * 128, :]
            dst_h = wo_sh.ap()[k2 * 128:(k2 + 1) * 128, :]
            dst_l = wo_sl.ap()[k2 * 128:(k2 + 1) * 128, :]
        if k > 0:
            b.wait(G_, s_wpc)
        b.dma(G_, wpp_st, src_ap, s_wpl)
        b.wait(S_, s_wpl)
        b.em(S_, lambda e: e.copy(wpp_hi[:, :], wpp_st), inc=True)
        b.waitm(V_, S_)
        b.em(V_, lambda e: e.tensor_tensor(wpp_lo[:, :], wpp_st, wpp_hi[:, :], ALU.subtract),
             inc=True)
        b.waitm(S_, V_)
        b.em(S_, lambda e, dst_h=dst_h: e.dma_start(out=dst_h, in_=wpp_hi[:, :]), sem=s_wpc, n=16)
        b.em(S_, lambda e, dst_l=dst_l: e.dma_start(out=dst_l, in_=wpp_lo[:, :]), sem=s_wpc, n=16)
    for half in range(2):
        tb = half * nth
        for li in range(nth):
            gi = tb + li
            sl = li % 2
            b.wait(SY, b.ms[V_], xt_ev[sl])
            b.dma(SY, xt_r[sl], x_in.ap()[gi * Q:(gi + 1) * Q, :], s_ldx[sl])
            b.wait(V_, s_ldx[sl])
            b.em(V_, lambda e, sl=sl: e.reduce_sum(stats[:, 0:1], xt_r[sl], axis=AX.X), inc=True)
            b.wait(S_, s_ldx[sl])
            b.waitm(S_, V_)
            b.em(S_, lambda e, sl=sl: e.activation(sqb, xt_r[sl], AF.Square, accum_out=stats[:, 1:2]),
                 inc=True)
            b.waitm(V_, S_)
            b.em(V_, lambda e: e.tensor_scalar(stats[:, 2:3], stats[:, 0:1], 1.0 / D, None, ALU.mult))
            b.em(V_, lambda e: e.tensor_tensor(stats[:, 3:4], stats[:, 2:3], stats[:, 2:3], ALU.mult))
            b.em(V_, lambda e: e.scalar_tensor_tensor(stats[:, 4:5], stats[:, 1:2], 1.0 / D,
                                                      stats[:, 3:4], ALU.mult, ALU.subtract), inc=True)
            b.waitm(S_, V_)
            b.em(S_, lambda e: e.activation(stats[:, 5:6], stats[:, 4:5], AF.Ln, bias=cvt[:, 0:1], scale=1.0))
            b.em(S_, lambda e: e.activation(stats[:, 6:7], stats[:, 5:6], AF.Exp, scale=-0.5), inc=True)
            b.waitm(V_, S_)
            b.em(V_, lambda e, sl=sl: e.tensor_scalar(x1t, xt_r[sl], stats[:, 2:3], stats[:, 6:7],
                                                      ALU.subtract, ALU.mult), inc=True)
            xt_ev[sl] = b.ms[V_].n
            b.waitm(T_, V_)
            for g in range(4):
                bank = pb[g]
                b.wait(T_, b.ms[bank_eng[g]], bank_ev[g])
                for u in range(4):
                    k = g * 4 + u
                    b.em(T_, lambda e, k=k, bank=bank, u=u: e.transpose(
                        bank[:, u * 128:(u + 1) * 128], x1t[:, k * 128:(k + 1) * 128], ident[:, :]),
                        inc=(u == 3))
                b.waitm(S_, T_)
                b.em(S_, lambda e, li=li, g=g, bank=bank: e.copy(
                    x1hi[:, (li * 16 + g * 4) * 128:(li * 16 + g * 4 + 4) * 128], bank[:, :]), inc=True)
                b.waitm(V_, S_)
                b.em(V_, lambda e, li=li, g=g, bank=bank: e.tensor_tensor(
                    x1lo[:, (li * 16 + g * 4) * 128:(li * 16 + g * 4 + 4) * 128], bank[:, :],
                    x1hi[:, (li * 16 + g * 4) * 128:(li * 16 + g * 4 + 4) * 128], ALU.subtract), inc=True)
                bank_ev[g] = b.ms[V_].n
                bank_eng[g] = V_

        while wpp_k[0] < 96 and half == 1:
            wd_prepass_step()
        b.waitm(T_, V_)
        for c in range(NCC):
            cw = min(128, INDIM - c * 128)
            sl = c % 2
            if c == 0:
                if half == 1:
                    b.waitm(SY, T_)
                for k in range(16):
                    b.dma(SY, wbuf_r[0][:, k * 128:k * 128 + cw],
                          w_in.ap()[k * 128:(k + 1) * 128, 0:cw], s_ldw[0])
                b.wait(S_, s_ldw[0])
                b.em(S_, lambda e, cw=cw: e.copy(whi_r[0][:, 0:16 * 128], wbuf_r[0][:, :]), inc=True)
                b.waitm(V_, S_)
                b.em(V_, lambda e, cw=cw: e.tensor_tensor(
                    wlo_r[0][:, 0:16 * 128], wbuf_r[0][:, :], whi_r[0][:, 0:16 * 128],
                    ALU.subtract), inc=True)
                wconv_v[0] = b.ms[V_].n
            if c + 1 < NCC:
                cw2 = min(128, INDIM - (c + 1) * 128)
                if c >= 1:
                    b.waitm(SY, T_)
                sl2 = (c + 1) % 2
                for k in range(16):
                    b.dma(SY, wbuf_r[sl2][:, k * 128:k * 128 + cw2],
                          w_in.ap()[k * 128:(k + 1) * 128, (c + 1) * 128:(c + 1) * 128 + cw2],
                          s_ldw[sl2])
                b.wait(S_, s_ldw[sl2])
                if c >= 1:
                    b.waitm(S_, T_)
                b.em(S_, lambda e, sl2=sl2: e.copy(whi_r[sl2][:, 0:16 * 128], wbuf_r[sl2][:, :]),
                     inc=True)
                b.waitm(V_, S_)
                b.em(V_, lambda e, sl2=sl2: e.tensor_tensor(
                    wlo_r[sl2][:, 0:16 * 128], wbuf_r[sl2][:, :], whi_r[sl2][:, 0:16 * 128],
                    ALU.subtract), inc=True)
                wconv_v[sl2] = b.ms[V_].n
            wd_prepass_step()
            b.waitm(T_, V_, wconv_v[sl])
            for j, (t0, tw) in enumerate(jth):
                bank = pb[j]
                b.wait(T_, b.ms[bank_eng[j]], bank_ev[j])
                ntl = (tw + 127) // 128
                i0 = t0 // 128
                if c < 64:
                    for k in range(16):
                        for wop, xop, st, sp in (
                            (whi_r[sl], x1hi_r, k == 0, False),
                            (whi_r[sl], x1lo_r, False, False),
                            (wlo_r[sl], x1hi_r, False, k == 15),
                        ):
                            b.em(T_, lambda e, k=k, cw=cw, bank=bank, i0=i0, ntl=ntl, tw=tw,
                                 wop=wop, xop=xop, st=st, sp=sp:
                                 e.matmul(bank[0:cw, 0:tw], wop[:, k * 128:k * 128 + cw],
                                          xop[:, i0:i0 + ntl, k:k + 1, :],
                                          start=st, stop=sp),
                                 inc=sp)
                else:
                    bank2 = pb[3 + j]
                    b.wait(T_, b.ms[bank_eng[3 + j]], bank_ev[3 + j])
                    for k in range(16):
                        for wop, xop, st, sp in (
                            (whi_r[sl], x1hi_r, k == 0, False),
                            (whi_r[sl], x1lo_r, False, False),
                            (wlo_r[sl], x1hi_r, False, k == 15),
                        ):
                            b.em(T_, lambda e, k=k, bank=bank, i0=i0, ntl=ntl, tw=tw,
                                 wop=wop, xop=xop, st=st, sp=sp:
                                 e.matmul(bank[0:32, 0:tw], wop[:, k * 128:k * 128 + 32],
                                          xop[:, i0:i0 + ntl, k:k + 1, :],
                                          start=st, stop=sp))
                            b.em(T_, lambda e, k=k, bank2=bank2, i0=i0, ntl=ntl, tw=tw,
                                 wop=wop, xop=xop, st=st, sp=sp:
                                 e.matmul(bank2[0:64, 0:tw], wop[:, k * 128 + 32:k * 128 + 96],
                                          xop[:, i0:i0 + ntl, k:k + 1, :],
                                          start=st, stop=sp),
                                 inc=sp)
                mmv = b.ms[T_].n
                if c < 32:
                    gsl = gate_cnt % 2
                    b.waitm(S_, T_, mmv)
                    b.wait(S_, s_gsp[gsl])
                    b.em(S_, lambda e, bank=bank, tw=tw, gsl=gsl:
                         e.activation(gbuf_r[gsl][:, 0:tw], bank[0:128, 0:tw], AF.Silu), inc=True)
                    bank_ev[j] = b.ms[S_].n
                    bank_eng[j] = S_
                    b.waitm(SY, S_)
                    b.dma(SY, gsilu_s.ap()[c * 128:(c + 1) * 128,
                                           half * Thalf + t0:half * Thalf + t0 + tw],
                          gbuf_r[gsl][:, 0:tw], s_gsp[gsl])
                    gate_cnt += 1
                elif c < 64:
                    b.waitm(V_, T_, mmv)
                    b.em(V_, lambda e, bank=bank, t0=t0, tw=tw: e.tensor_copy(
                        xpre[:, 4 + t0:4 + t0 + tw], bank[0:128, 0:tw]), inc=True)
                    bank_ev[j] = b.ms[V_].n
                    bank_eng[j] = V_
                else:
                    b.waitm(V_, T_, mmv)
                    b.em(V_, lambda e, bank=bank, t0=t0, tw=tw: e.tensor_copy(
                        xpre[0:32, 4 + t0:4 + t0 + tw], bank[0:32, 0:tw]))
                    b.em(V_, lambda e, j=j, t0=t0, tw=tw, half=half: e.tensor_copy(
                        dtraw[0:64, half * Thalf + t0:half * Thalf + t0 + tw], pb[3 + j][0:64, 0:tw]),
                        inc=True)
                    bank_ev[j] = b.ms[V_].n
                    bank_eng[j] = V_
                    bank_ev[3 + j] = b.ms[V_].n
                    bank_eng[3 + j] = V_
            if c >= 32:
                cc = c - 32
                nch = 128 if c < 64 else 32
                xb = xbc_r[cc % 2]
                b.wait(V_, s_xsp[cc % 2])
                if half == 1:
                    b.em(V_, lambda e, cc=cc, nch=nch: e.tensor_copy(
                        xpre[0:nch, 1:4], sv3[0:nch, cc * 3:cc * 3 + 3]))
                if dbg == 2:
                    b.em(V_, lambda e, nch=nch, xb=xb: e.tensor_copy(
                        xb[0:nch, 0:Thalf], xpre[0:nch, 4:4 + Thalf]))
                b.em(V_, lambda e, cc=cc, nch=nch, xb=xb: (None if dbg == 2 else e.tensor_scalar(
                    xb[0:nch, 0:Thalf], xpre[0:nch, 1:1 + Thalf], cwsb[0:nch, cc * 4:cc * 4 + 1],
                    cbsb[0:nch, cc:cc + 1], ALU.mult, ALU.add)) or e.nop())
                for k in range(1, 4):
                    if dbg == 2:
                        break
                    b.em(V_, lambda e, cc=cc, nch=nch, xb=xb, k=k: e.scalar_tensor_tensor(
                        xb[0:nch, 0:Thalf], xpre[0:nch, 1 + k:1 + k + Thalf],
                        cwsb[0:nch, cc * 4 + k:cc * 4 + k + 1], xb[0:nch, 0:Thalf],
                        ALU.mult, ALU.add))
                if half == 0:
                    b.em(V_, lambda e, cc=cc, nch=nch: e.tensor_copy(
                        sv3[0:nch, cc * 3:cc * 3 + 3], xpre[0:nch, 1 + Thalf:4 + Thalf]))
                    if halo_tiles > 0:
                        b.em(V_, lambda e, nch=nch, xb=xb: e.tensor_tensor(
                            xb[0:nch, 0:halo_tiles * Q], xb[0:nch, 0:halo_tiles * Q],
                            hmask[0:nch, :], ALU.mult))
                b.em(V_, lambda e, nch=nch, xb=xb: e.tensor_copy(xb[0:nch, 0:1], xb[0:nch, 0:1]),
                     inc=True)
                b.waitm(S_, V_)
                b.em(S_, lambda e, nch=nch, xb=xb: (e.nop() if dbg in (2, 3) else e.activation(
                    xb[0:nch, 0:Thalf], xb[0:nch, 0:Thalf], AF.Silu)), inc=True)
                b.waitm(SY, S_)
                b.dma(SY, xbc_s.ap()[cc * 128:cc * 128 + nch, half * Thalf:(half + 1) * Thalf],
                      xb[0:nch, 0:Thalf], s_xsp[cc % 2])

    b.waitm(S_, V_)
    b.em(S_, lambda e: e.activation(dt_t[0:64, :], dtraw[0:64, :], AF.Exp, bias=dtb[0:64, 0:1],
                                    scale=1.0))
    b.em(S_, lambda e: e.activation(dt_t[0:64, :], dt_t[0:64, :], AF.Ln, bias=cvt[0:64, 1:2], scale=1.0))
    b.em(S_, lambda e: e.activation(dtraw[0:64, :], dt_t[0:64, :], AF.Ln, scale=1.0), inc=True)
    b.waitm(V_, S_)
    b.em(V_, lambda e: e.memset(sqb[0:64, 0:Q], 0.0))
    for ci in range(n_tiles):
        b.em(V_, lambda e, ci=ci: e.tensor_scalar(
            dtA[0:64, 0:Q], dt_t[0:64, ci * Q:(ci + 1) * Q], aneg[0:64, 0:1], None, ALU.mult))
        b.em(V_, lambda e, ci=ci: e.tensor_tensor_scan(
            cum[0:64, ci * Q:(ci + 1) * Q], dtA[0:64, 0:Q], sqb[0:64, 0:Q],
            0.0, ALU.add, ALU.add))
    b.em(V_, lambda e: e.tensor_tensor(cumpr[0:64, :], cum[0:64, :], dtraw[0:64, :], ALU.subtract))
    b.em(V_, lambda e: e.memset(ocp[0:64, :], 1.0))
    b.em(V_, lambda e: e.memset(cum[64:128, :], 0.0))
    b.em(V_, lambda e: e.memset(cumpr[64:128, :], 0.0))
    b.em(V_, lambda e: e.memset(wexs[64:128, :], 0.0))
    b.em(V_, lambda e: e.memset(states[:, :], 0.0))
    b.em(V_, lambda e: e.memset(bd_r[0][32:64, :], 0.0))
    b.em(V_, lambda e: e.memset(bd_r[1][0:32, :], 0.0))
    s_bdl = b.sem("bdl")
    b.waitm(SY, V_)
    for hf in range(2):
        b.dma(SY, bd_r[hf][64:128, :], bdneg_h[hf].ap(), s_bdl)
    b.wait(G_, s_bdl)
    b.wait(T_, s_bdl)
    b.em(V_, lambda e: e.tensor_copy(cumlo[0:64, :], cum[0:64, :]), inc=True)
    s_Cd = b.sem("Cd")
    b.waitm(SY, V_)
    b.dma(SY, ocp[64:128, :], cumpr[0:64, :], s_Cd)

    s_xc = b.sem("xc")
    s_gf = b.sem("gf")
    s_rs = b.sem("rs")
    s_ysp = [b.sem(f"ysp{i}") for i in range(4)]
    s_nw = b.sem("nw")
    b.waitm(SY, V_)
    b.dma(SY, nw[:, :], normw.ap(), s_nw)
    b.wait(G_, s_Cd)
    b.wait(T_, s_Cd)
    b.waitm(G_, V_)

    gfm_free_v = 0
    gfm_free_s = 0
    xbcc_free = 0
    yn_cnt = 0

    for ci in range(n_tiles):
        own = ci >= halo_tiles
        oci = ci - halo_tiles
        tsl = slice(ci * Q, (ci + 1) * Q)
        b.waitm(SY, T_, xbcc_free)
        if ci == 0:
            b.wait(SY, s_xsp[0]); b.wait(SY, s_xsp[1])
        for g4 in range(8):
            b.dma(SY, xbcc[:, g4 * 512:(g4 + 1) * 512].rearrange("p (a t) -> p a t", a=4),
                  xbc_s.ap()[0:4096, :].rearrange("(a p) t -> p a t", p=128)[:, g4 * 4:(g4 + 1) * 4, tsl],
                  s_xc)
        b.dma(SY, bcs[0:16, 0:128], xbc_s.ap()[4096:4112, tsl], s_xc)
        b.dma(SY, bcs[0:16, 128:256], xbc_s.ap()[4112:4128, tsl], s_xc)
        if own:
            b.waitm(SY, V_, gfm_free_v)
            b.waitm(SY, S_, gfm_free_s)
            if oci == 0:
                b.wait(SY, s_gsp[0]); b.wait(SY, s_gsp[1])
            for g4 in range(8):
                b.dma(SY, gfm[:, g4 * 512:(g4 + 1) * 512].rearrange("p (a t) -> p a t", a=4),
                      gsilu_s.ap().rearrange("(a p) t -> p a t", p=128)[:, g4 * 4:(g4 + 1) * 4, tsl],
                      s_gf)

        b.waitm(G_, T_)
        for hf in range(2):
            h0 = hf * 32
            srcap = cumlo[h0:h0 + 32, tsl].unsqueeze(1).to_broadcast([32, 32, 128])
            dstap = bd_r[hf][h0:h0 + 32, :].rearrange("p (h t) -> p h t", h=32)
            b.em(G_, lambda e, srcap=srcap, dstap=dstap: e.affine_select(
                dstap, srcap, pattern=[[1, 32], [0, 128]], compare_op=ALU.is_equal,
                fill=0.0, base=0, channel_multiplier=-1), inc=(hf == 1))
        bd_ready = b.ms[G_].n

        b.wait(T_, b.ms[bank_eng[2]], bank_ev[2])
        b.em(T_, lambda e, tsl=tsl: e.transpose(pb[2][:, 0:128], cum[:, tsl], ident[:, :]), inc=True)
        b.waitm(V_, T_)
        b.em(V_, lambda e: e.tensor_copy(cumT_sb[:, :], pb[2][:, 0:64]), inc=True)
        bank_ev[2] = b.ms[V_].n; bank_eng[2] = V_
        b.waitm(S_, V_)
        b.em(S_, lambda e: e.activation(ecumT[:, :], cumT_sb[:, :], AF.Exp, scale=1.0), inc=True)
        b.em(V_, lambda e, ci=ci: e.tensor_scalar(
            wexs[0:64, :], cumpr[0:64, ci * Q:(ci + 1) * Q],
            cum[0:64, ci * Q + Q - 1:ci * Q + Q], -1.0, ALU.subtract, ALU.mult), inc=True)
        b.waitm(S_, V_)
        b.em(S_, lambda e: e.activation(wexs[0:64, :], wexs[0:64, :], AF.Exp, scale=1.0), inc=True)
        b.waitm(T_, S_)
        b.wait(T_, b.ms[bank_eng[3]], bank_ev[3])
        b.em(T_, lambda e: e.transpose(pb[3][:, 0:128], wexs[:, :], ident[:, :]), inc=True)
        b.waitm(V_, T_)
        b.em(V_, lambda e: e.tensor_copy(wexT[:, :], pb[3][:, 0:64]), inc=True)
        bank_ev[3] = b.ms[V_].n; bank_eng[3] = V_
        b.em(S_, lambda e: e.dma_start(out=rowst[0:1, 0:64], in_=ecumT[127:128, :]), sem=s_rs, n=16)
        b.wait(T_, s_rs)
        b.wait(T_, b.ms[bank_eng[3]], bank_ev[3])
        b.em(T_, lambda e: e.matmul(pb[3][0:16, 0:64], ones1[0:1, 0:16], rowst[0:1, 0:64],
                                    start=True, stop=True), inc=True)
        b.waitm(V_, T_)
        b.em(V_, lambda e: e.tensor_copy(dax16[:, :], pb[3][0:16, 0:64]), inc=True)
        bank_ev[3] = b.ms[V_].n; bank_eng[3] = V_

        b.wait(T_, s_xc)
        for g in range(8):
            bk = 2 + g % 2
            bank = pb[bk]
            b.wait(T_, b.ms[bank_eng[bk]], bank_ev[bk])
            for u in range(4):
                a = g * 4 + u
                b.em(T_, lambda e, a=a, bank=bank, u=u: e.transpose(
                    bank[:, u * 128:(u + 1) * 128], xbcc[:, a * 128:(a + 1) * 128], ident[:, :]),
                    inc=(u == 3))
            b.waitm(V_, T_)
            b.em(V_, lambda e, g=g, bank=bank: e.tensor_copy(
                xtm[:, g * 512:(g + 1) * 512], bank[:, :]), inc=True)
            bank_ev[bk] = b.ms[V_].n; bank_eng[bk] = V_
        b.wait(T_, b.ms[bank_eng[2]], bank_ev[2])
        b.em(T_, lambda e: e.transpose(pb[2][:, 0:16], bcs[0:16, 0:128], ident[0:16, 0:16]), inc=True)
        xbcc_free = b.ms[T_].n
        b.waitm(V_, T_)
        b.em(V_, lambda e: e.tensor_copy(bcs[:, 256:272], pb[2][:, 0:16]), inc=True)
        bank_ev[2] = b.ms[V_].n; bank_eng[2] = V_

        b.wait(T_, b.ms[bank_eng[4]], bank_ev[4])
        b.em(T_, lambda e: e.matmul(pb[4][:, 0:128], bcs[0:16, 0:128], bcs[0:16, 128:256],
                                    start=True, stop=True), inc=True)
        b.waitm(V_, T_)
        b.em(V_, lambda e: e.tensor_tensor(wexs[:, :], pb[4][:, 0:128], triu[:, :], ALU.mult), inc=True)
        bank_ev[4] = b.ms[V_].n; bank_eng[4] = V_
        mcb_v = b.ms[V_].n

        def d_phase(qt):
            hf = qt // 2
            qb = bigDq[qt % 2]
            b.wait(T_, b.ms[G_], bd_ready)
            for jj in range(4):
                bk = jj % 2
                bank = pb[bk]
                b.wait(T_, b.ms[bank_eng[bk]], bank_ev[bk])
                col = (qt % 2) * 2048 + jj * 512
                b.em(T_, lambda e, hf=hf, col=col, bank=bank, tsl=tsl: e.matmul(
                    bank[:, 0:512], ocp[:, tsl], bd_r[hf][:, col:col + 512],
                    start=True, stop=True), inc=True)
                b.waitm(V_, T_)
                b.em(V_, lambda e, jj=jj, bank=bank, qb=qb: e.tensor_scalar(
                    qb[:, jj * 512:(jj + 1) * 512], bank[:, 0:512], CLAMP, None, ALU.min),
                    inc=True)
                bank_ev[bk] = b.ms[V_].n; bank_eng[bk] = V_
            b.waitm(S_, V_)
            b.em(S_, lambda e, qb=qb: e.activation(qb[:, :], qb[:, :], AF.Exp, scale=1.0), inc=True)
            b.waitm(V_, S_)
            b.em(V_, lambda e, qb=qb: e.tensor_tensor(
                qb.rearrange("p (h t) -> p h t", h=16), qb.rearrange("p (h t) -> p h t", h=16),
                wexs.unsqueeze(1).to_broadcast([128, 16, 128]), ALU.mult), inc=True)
            return b.ms[V_].n

        def p1(qt, g2, sv):
            qh0 = qt * 16 + g2 * 8
            qb = bigDq[qt % 2]
            hg = [qh0 + j for j in range(8)]
            gid = qt * 2 + g2
            bi = 4 + gid % 2
            bs = 6 + gid % 2
            b.waitm(T_, V_, sv)
            b.wait(T_, b.ms[bank_eng[bi]], bank_ev[bi])
            b.wait(T_, b.ms[bank_eng[bs]], bank_ev[bs])
            for j, h in enumerate(hg):
                hh = h - qt * 16
                b.em(T_, lambda e, hh=hh, h=h, bi=bi, j=j, qb=qb: e.matmul(
                    pb[bi][:, j * 64:(j + 1) * 64], qb[:, hh * 128:(hh + 1) * 128],
                    xtm[:, h * 64:(h + 1) * 64], start=True, stop=True))
                b.em(T_, lambda e, h=h, bs=bs, j=j: e.matmul(
                    pb[bs][:, j * 64:(j + 1) * 64], bcs[0:16, 128:256],
                    states[0:16, h * 64:(h + 1) * 64], start=True, stop=True),
                    inc=(j == 7))
            return b.ms[T_].n

        def p2(qt, g2, ymm):
            qh0 = qt * 16 + g2 * 8
            hg = [qh0 + j for j in range(8)]
            gid = qt * 2 + g2
            bi = 4 + gid % 2
            bs = 6 + gid % 2
            b.waitm(V_, T_, ymm)
            for j, h in enumerate(hg):
                b.em(V_, lambda e, h=h, bi=bi, j=j: e.scalar_tensor_tensor(
                    ytm[:, h * 64:(h + 1) * 64], xtm[:, h * 64:(h + 1) * 64],
                    dsk[:, h:h + 1], pb[bi][:, j * 64:(j + 1) * 64], ALU.mult, ALU.add))
                b.em(V_, lambda e, h=h, bs=bs, j=j: e.scalar_tensor_tensor(
                    ytm[:, h * 64:(h + 1) * 64], pb[bs][:, j * 64:(j + 1) * 64],
                    ecumT[:, h:h + 1], ytm[:, h * 64:(h + 1) * 64], ALU.mult, ALU.add),
                    inc=(j == 7))
            bank_ev[bi] = b.ms[V_].n; bank_eng[bi] = V_
            bank_ev[bs] = b.ms[V_].n; bank_eng[bs] = V_

        def p3v(gid):
            hg = [gid * 8 + j for j in range(8)]
            wsl8 = gid % 2
            b.waitm(V_, T_)
            for j, h in enumerate(hg):
                b.em(V_, lambda e, h=h, j=j, wsl8=wsl8: e.tensor_scalar(
                    bw8_r[wsl8][:, j * 16:(j + 1) * 16], bcs[:, 256:272],
                    wexT[:, h:h + 1], None, ALU.mult))
                b.em(V_, lambda e, h=h, j=j, wsl8=wsl8: e.tensor_scalar(
                    dai8_r[wsl8][0:16, j * 16:(j + 1) * 16], eye16[:, :],
                    dax16[:, h:h + 1], None, ALU.mult), inc=(j == 7))
            return b.ms[V_].n

        def p3t(gid, bwv):
            hg = [gid * 8 + j for j in range(8)]
            wsl8 = gid % 2
            bu = 2 + gid % 2
            b.waitm(T_, V_, bwv)
            b.wait(T_, b.ms[bank_eng[bu]], bank_ev[bu])
            for j, h in enumerate(hg):
                b.em(T_, lambda e, h=h, bu=bu, j=j, wsl8=wsl8: e.matmul(
                    pb[bu][0:16, j * 64:(j + 1) * 64], bw8_r[wsl8][:, j * 16:(j + 1) * 16],
                    xtm[:, h * 64:(h + 1) * 64], start=True, stop=False))
                b.em(T_, lambda e, h=h, bu=bu, j=j, wsl8=wsl8: e.matmul(
                    pb[bu][0:16, j * 64:(j + 1) * 64], dai8_r[wsl8][0:16, j * 16:(j + 1) * 16],
                    states[0:16, h * 64:(h + 1) * 64], start=False, stop=True),
                    inc=(j == 7))
            return b.ms[T_].n

        def pev(gid):
            bu = 2 + gid % 2
            b.waitm(V_, T_)
            b.em(V_, lambda e, gid=gid, bu=bu: e.tensor_copy(
                states[0:16, gid * 8 * 64:(gid * 8 + 8) * 64], pb[bu][0:16, 0:512]), inc=True)
            bank_ev[bu] = b.ms[V_].n; bank_eng[bu] = V_

        if own:
            sv = {0: d_phase(0), 1: d_phase(1)}
            ymms = {}
            for qt in range(4):
                for g2 in range(2):
                    gid = qt * 2 + g2
                    if gid == 0:
                        ymms[0] = p1(0, 0, sv[0])
                    ngid = gid + 1
                    if ngid < 8:
                        nqt, ng2 = divmod(ngid, 2)
                        if ng2 == 0 and nqt + 1 < 4:
                            sv[nqt + 1] = d_phase(nqt + 1)
                        ymms[ngid] = p1(nqt, ng2, sv[nqt])
                    p2(qt, g2, ymms[gid])
                    bwv = p3v(gid)
                    p3t(gid, bwv)
                    pev(gid)
        else:
            bwvs = {0: p3v(0)}
            for gid in range(8):
                if gid + 1 < 8:
                    bwvs[gid + 1] = p3v(gid + 1)
                p3t(gid, bwvs[gid])
                pev(gid)

        if own:
            b.wait(T_, s_gf)
            for g in range(32):
                bk = 2 + g % 2
                bank = pb[bk]
                b.wait(T_, b.ms[bank_eng[bk]], bank_ev[bk])
                b.em(T_, lambda e, g=g, bank=bank: e.transpose(
                    bank[:, 0:128], gfm[:, g * 128:(g + 1) * 128], ident[:, :]), inc=True)
                b.waitm(V_, T_)
                b.em(V_, lambda e, g=g, bank=bank: e.tensor_tensor(
                    ytm[:, g * 128:(g + 1) * 128], bank[:, 0:128],
                    ytm[:, g * 128:(g + 1) * 128], ALU.mult), inc=True)
                bank_ev[bk] = b.ms[V_].n; bank_eng[bk] = V_
            gfm_free_v = b.ms[V_].n
            b.waitm(S_, V_)
            b.em(S_, lambda e: e.activation(gfm[:, :], ytm[:, :], AF.Square, accum_out=stats[:, 1:2]))
            b.em(S_, lambda e: e.activation(stats[:, 5:6], stats[:, 1:2], AF.Ln, bias=cvt[:, 0:1],
                                            scale=1.0 / DIN))
            b.em(S_, lambda e: e.activation(stats[:, 6:7], stats[:, 5:6], AF.Exp, scale=-0.5), inc=True)
            gfm_free_s = b.ms[S_].n
            b.waitm(V_, S_)
            b.wait(V_, s_nw)
            b.em(V_, lambda e: e.scalar_tensor_tensor(ytm[:, :], ytm[:, :], stats[:, 6:7], nw[:, :],
                                                      ALU.mult, ALU.mult), inc=True)
            ynv = b.ms[V_].n
            b.waitm(T_, V_, ynv)
            for g in range(32):
                bk = 2 + g % 2
                bank = pb[bk]
                ysl = yn_cnt % 4
                yh = rgb(YNST + ysl * 128, 256)
                yl = rgb(YNST + ysl * 128 + 64, 256)
                b.wait(T_, b.ms[bank_eng[bk]], bank_ev[bk])
                b.em(T_, lambda e, g=g, bank=bank: e.transpose(
                    bank[:, 0:128], ytm[:, g * 128:(g + 1) * 128], ident[:, :]), inc=True)
                b.waitm(S_, T_)
                b.wait(S_, s_ysp[ysl])
                b.em(S_, lambda e, yh=yh, bank=bank: e.copy(yh[:, 0:128], bank[:, 0:128]), inc=True)
                b.waitm(V_, S_)
                b.em(V_, lambda e, yh=yh, yl=yl, bank=bank: e.tensor_tensor(
                    yl[:, 0:128], bank[:, 0:128], yh[:, 0:128], ALU.subtract), inc=True)
                bank_ev[bk] = b.ms[V_].n; bank_eng[bk] = V_
                b.waitm(SY, V_)
                b.dma(SY, yn_sh.ap()[g * 128:(g + 1) * 128, oci * Q:(oci + 1) * Q],
                      yh[:, 0:128], s_ysp[ysl])
                b.dma(SY, yn_sl.ap()[g * 128:(g + 1) * 128, oci * Q:(oci + 1) * Q],
                      yl[:, 0:128], s_ysp[ysl])
                yn_cnt += 1

    s_wo = [b.sem("wo0"), b.sem("wo1")]
    s_yl = [b.sem(f"yl{s}") for s in range(2)]
    s_asp = [b.sem("asp0"), b.sem("asp1")]
    yst_ev = [0, 0]
    for i in range(4):
        b.wait(SY, s_ysp[i])
    b.wait(SY, s_wpc)
    quarter_mm_done = [0, 0]

    def load_wouth(oq):
        wq = oq % 2
        b.wait(SY, b.ms[T_], quarter_mm_done[wq])
        for g4 in range(8):
            b.dma(SY, wouth_hi[wq].rearrange("p (a t) -> p a t", a=32)[:, g4 * 4:(g4 + 1) * 4],
                  wo_sh.ap().rearrange("(a p) t -> p a t", p=128)[:, g4 * 4:(g4 + 1) * 4,
                                                                 oq * 512:(oq + 1) * 512],
                  s_wo[wq])
            b.dma(SY, wouth_lo[wq].rearrange("p (a t) -> p a t", a=32)[:, g4 * 4:(g4 + 1) * 4],
                  wo_sl.ap().rearrange("(a p) t -> p a t", p=128)[:, g4 * 4:(g4 + 1) * 4,
                                                                 oq * 512:(oq + 1) * 512],
                  s_wo[wq])

    load_wouth(0)
    load_wouth(1)
    ecnt = 0
    for ep in range(2):
        for i in range(n_own):
            sl = i % 2
            if ep > 0 or i >= 2:
                b.wait(SY, b.ms[T_], yst_ev[sl])
            for gsl4 in range(8):
                b.dma(SY, yst_hi[sl].rearrange("p (a t) -> p a t", a=32)[:, gsl4 * 4:(gsl4 + 1) * 4],
                      yn_sh.ap().rearrange("(a p) t -> p a t", p=128)[:, gsl4 * 4:(gsl4 + 1) * 4,
                                                                     i * Q:(i + 1) * Q], s_yl[sl])
                b.dma(SY, yst_lo[sl].rearrange("p (a t) -> p a t", a=32)[:, gsl4 * 4:(gsl4 + 1) * 4],
                      yn_sl.ap().rearrange("(a p) t -> p a t", p=128)[:, gsl4 * 4:(gsl4 + 1) * 4,
                                                                     i * Q:(i + 1) * Q], s_yl[sl])
            b.wait(T_, s_yl[sl])
            if i == 0:
                b.wait(T_, s_wo[0])
                b.wait(T_, s_wo[1])
            for wq in range(2):
                bk = wq
                b.wait(T_, b.ms[bank_eng[bk]], bank_ev[bk])
                for k in range(32):
                    for yop, wop, st, sp in (
                        (yst_hi[sl], wouth_hi[wq], k == 0, False),
                        (yst_hi[sl], wouth_lo[wq], False, False),
                        (yst_lo[sl], wouth_hi[wq], False, k == 31),
                    ):
                        b.em(T_, lambda e, k=k, bk=bk, wq=wq, yop=yop, wop=wop, st=st, sp=sp:
                             e.matmul(pb[bk][:, 0:512], yop[:, k * 128:(k + 1) * 128],
                                      wop[:, k * 512:(k + 1) * 512],
                                      start=st, stop=sp), inc=sp)
            yst_ev[sl] = b.ms[T_].n
            asl = ecnt % 2
            b.waitm(V_, T_)
            b.wait(V_, s_asp[asl])
            for wq in range(2):
                b.em(V_, lambda e, asl=asl, wq=wq: e.tensor_copy(
                    ast_r[asl][:, wq * 512:(wq + 1) * 512], pb[wq][:, 0:512]), inc=(wq == 1))
            for wq in range(2):
                bank_ev[wq] = b.ms[V_].n; bank_eng[wq] = V_
            b.waitm(SY, V_)
            b.dma(SY, attn_s.ap()[i * Q:(i + 1) * Q, ep * 1024:(ep + 1) * 1024],
                  ast_r[asl][:, 0:1024], s_asp[asl])
            ecnt += 1
        quarter_mm_done[0] = b.ms[T_].n
        quarter_mm_done[1] = b.ms[T_].n
        if ep == 0:
            load_wouth(2)
            load_wouth(3)

    s_fa = b.sem("fa")
    s_fx = b.sem("fx")
    s_x2p = b.sem("x2p")

    def ln_tile(src, dst, scr):
        b.em(V_, lambda e: e.reduce_sum(stats[:, 0:1], src, axis=AX.X), inc=True)
        b.waitm(S_, V_)
        b.em(S_, lambda e: e.activation(scr, src, AF.Square, accum_out=stats[:, 1:2]), inc=True)
        b.waitm(V_, S_)
        b.em(V_, lambda e: e.tensor_scalar(stats[:, 2:3], stats[:, 0:1], 1.0 / D, None, ALU.mult))
        b.em(V_, lambda e: e.tensor_tensor(stats[:, 3:4], stats[:, 2:3], stats[:, 2:3], ALU.mult))
        b.em(V_, lambda e: e.scalar_tensor_tensor(stats[:, 4:5], stats[:, 1:2], 1.0 / D,
                                                  stats[:, 3:4], ALU.mult, ALU.subtract), inc=True)
        b.waitm(S_, V_)
        b.em(S_, lambda e: e.activation(stats[:, 5:6], stats[:, 4:5], AF.Ln, bias=cvt[:, 0:1], scale=1.0))
        b.em(S_, lambda e: e.activation(stats[:, 6:7], stats[:, 5:6], AF.Exp, scale=-0.5), inc=True)
        b.waitm(V_, S_)
        b.em(V_, lambda e: e.tensor_scalar(dst, src, stats[:, 2:3], stats[:, 6:7],
                                           ALU.subtract, ALU.mult), inc=True)

    b.wait(SY, s_asp[0]); b.wait(SY, s_asp[1])
    fa_ev = 0
    for i in range(n_own):
        b.waitm(SY, V_, fa_ev)
        b.dma(SY, f_attn, attn_s.ap()[i * Q:(i + 1) * Q, :], s_fa)
        b.dma(SY, f_x, x_in.ap()[(halo_tiles + i) * Q:(halo_tiles + i + 1) * Q, :], s_fx)
        b.wait(V_, s_fa)
        b.wait(V_, s_fx)
        b.wait(S_, s_fa)
        ln_tile(f_attn, f_attn, f_tmp)
        if i >= 1:
            b.wait(V_, s_x2p, 16 * i)
        b.em(V_, lambda e: e.tensor_tensor(f_x2, f_x, f_attn, ALU.add), inc=True)
        fa_ev = b.ms[V_].n
        b.waitm(SY, V_)
        b.dma(SY, x2_s.ap()[i * Q:(i + 1) * Q, :], f_x2, s_x2p)
        ln_tile(f_x2, f_x3, f_tmp)
        b.waitm(T_, V_)
        for g in range(4):
            bk = g % 2
            bank = pb[bk]
            b.wait(T_, b.ms[bank_eng[bk]], bank_ev[bk])
            for u in range(4):
                k = g * 4 + u
                b.em(T_, lambda e, k=k, bank=bank, u=u: e.transpose(
                    bank[:, u * 128:(u + 1) * 128], f_x3[:, k * 128:(k + 1) * 128], ident[:, :]),
                    inc=(u == 3))
            b.waitm(S_, T_)
            b.em(S_, lambda e, i=i, g=g, bank=bank: e.copy(
                x3hi[:, (i * 16 + g * 4) * 128:(i * 16 + g * 4 + 4) * 128], bank[:, :]), inc=True)
            b.waitm(V_, S_)
            b.em(V_, lambda e, i=i, g=g, bank=bank: e.tensor_tensor(
                x3lo[:, (i * 16 + g * 4) * 128:(i * 16 + g * 4 + 4) * 128], bank[:, :],
                x3hi[:, (i * 16 + g * 4) * 128:(i * 16 + g * 4 + 4) * 128], ALU.subtract), inc=True)
            bank_ev[bk] = b.ms[V_].n; bank_eng[bk] = V_

    s_wg = [b.sem("wg0"), b.sem("wg1")]
    s_hsp = b.sem("hsp")
    b.waitm(T_, V_)
    njo = len(jo)
    wg_ev = [0, 0]
    wg_cv = [0, 0]
    for c in range(NH):
        for part in range(2):
            slot = part
            col0 = part * INTER + c * 128
            b.wait(SY, b.ms[V_], wg_cv[slot])
            for k in range(16):
                b.dma(SY, wgu_r[slot][:, k * 128:(k + 1) * 128],
                      w_gu_t.ap()[k * 128:(k + 1) * 128, col0:col0 + 128], s_wg[slot])
            b.wait(S_, s_wg[slot])
            b.wait(S_, b.ms[T_], wg_ev[slot])
            b.em(S_, lambda e, slot=slot: e.copy(wguhi_r[slot][:, :], wgu_r[slot][:, :]), inc=True)
            b.waitm(V_, S_)
            b.em(V_, lambda e, slot=slot: e.tensor_tensor(
                wgulo_r[slot][:, :], wgu_r[slot][:, :], wguhi_r[slot][:, :], ALU.subtract), inc=True)
            wg_cv[slot] = b.ms[V_].n
        for part in range(2):
            slot = part
            b.waitm(T_, V_, wg_cv[slot])
            for j, (t0, tw) in enumerate(jo):
                bk = part * 4 + j
                bank = pb[bk]
                b.wait(T_, b.ms[bank_eng[bk]], bank_ev[bk])
                ntl = (tw + 127) // 128
                i0 = t0 // 128
                for k in range(16):
                    for wop, xop, st, sp in (
                        (wguhi_r[slot], x3hi_r, k == 0, False),
                        (wguhi_r[slot], x3lo_r, False, False),
                        (wgulo_r[slot], x3hi_r, False, k == 15),
                    ):
                        b.em(T_, lambda e, k=k, bank=bank, i0=i0, ntl=ntl, tw=tw,
                             wop=wop, xop=xop, st=st, sp=sp: e.matmul(
                            bank[:, 0:tw], wop[:, k * 128:(k + 1) * 128],
                            xop[:, i0:i0 + ntl, k:k + 1, :], start=st, stop=sp),
                            inc=sp)
            wg_ev[slot] = b.ms[T_].n
            if part == 0:
                b.waitm(S_, T_)
                b.waitm(S_, V_)
                for j, (t0, tw) in enumerate(jo):
                    b.em(S_, lambda e, j=j, t0=t0, tw=tw: e.activation(
                        gs_b[:, t0:t0 + tw], pb[j][:, 0:tw], AF.Silu), inc=(j == njo - 1))
                for j in range(njo):
                    bank_ev[j] = b.ms[S_].n; bank_eng[j] = S_
            else:
                b.waitm(V_, S_)
                b.waitm(V_, T_)
                if c >= 1:
                    b.wait(V_, s_hsp, 32 * c)
                for j, (t0, tw) in enumerate(jo):
                    b.em(V_, lambda e, j=j, t0=t0, tw=tw: e.tensor_tensor(
                        hb[:, t0:t0 + tw], pb[4 + j][:, 0:tw], gs_b[:, t0:t0 + tw], ALU.mult),
                        inc=(j == njo - 1))
                for j in range(njo):
                    bank_ev[4 + j] = b.ms[V_].n; bank_eng[4 + j] = V_
                b.waitm(S_, V_)
                b.em(S_, lambda e: e.copy(hb_hi[:, :], hb[:, 0:Town]), inc=True)
                b.waitm(V_, S_)
                b.em(V_, lambda e: e.tensor_tensor(hb_lo[:, :], hb[:, 0:Town], hb_hi[:, :],
                                                   ALU.subtract), inc=True)
                b.waitm(SY, V_)
                b.dma(SY, h_sh.ap()[c * 128:(c + 1) * 128, :], hb_hi[:, :], s_hsp)
                b.dma(SY, h_sl.ap()[c * 128:(c + 1) * 128, :], hb_lo[:, :], s_hsp)

    s_hq = b.sem("hq")
    s_wd = [b.sem(f"wd{s}") for s in range(4)]
    s_x2l = b.sem("x2l")
    s_out = b.sem("outd")
    s_out2 = [b.sem("out0"), b.sem("out1")]
    b.wait(SY, s_hsp)
    b.wait(SY, s_x2p)
    b.wait(SY, s_wpc)
    n8 = Town // 256
    wd_ev = [0, 0, 0, 0]
    hq_half_mm = [0, 0]
    s_hqh = [b.sem("hqh0"), b.sem("hqh1")]

    def load_hq_half(q, hh):
        for g4 in range(8):
            a0 = hh * 32 + g4 * 4
            b.dma(SY, hq_hi.rearrange("p (a t) -> p a t", a=64)[:, a0:a0 + 4],
                  h_sh.ap().rearrange("(a p) t -> p a t", p=128)[:, a0:a0 + 4,
                                                                q * 256:(q + 1) * 256], s_hqh[hh])
            b.dma(SY, hq_lo.rearrange("p (a t) -> p a t", a=64)[:, a0:a0 + 4],
                  h_sl.ap().rearrange("(a p) t -> p a t", p=128)[:, a0:a0 + 4,
                                                                q * 256:(q + 1) * 256], s_hqh[hh])

    load_hq_half(0, 0)
    load_hq_half(0, 1)
    for q in range(n8):
        for k in range(64):
            if k == 0:
                b.wait(T_, s_hqh[0])
            if k == 32:
                b.wait(T_, s_hqh[1])
            if k == 32 and q + 1 < n8:
                b.wait(SY, b.ms[T_], hq_half_mm[0])
                load_hq_half(q + 1, 0)
            slot = k % 4
            b.wait(SY, b.ms[T_], wd_ev[slot])
            b.dma(SY, wdhi_r[slot][:, :], wd_sh.ap()[k * 128:(k + 1) * 128, :], s_wd[slot])
            b.dma(SY, wdlo_r[slot][:, :], wd_sl.ap()[k * 128:(k + 1) * 128, :], s_wd[slot])
            b.wait(T_, s_wd[slot])
            if k == 0:
                for bk in range(8):
                    b.wait(T_, b.ms[bank_eng[bk]], bank_ev[bk])
            for tt in range(2):
                for ob in range(4):
                    bk = tt * 4 + ob
                    for pi, (hop, wop, st, sp) in enumerate((
                        (hq_hi, wdhi_r[slot], k == 0, False),
                        (hq_hi, wdlo_r[slot], False, False),
                        (hq_lo, wdhi_r[slot], False, k == 63),
                    )):
                        b.em(T_, lambda e, k=k, tt=tt, ob=ob, bk=bk, hop=hop, wop=wop,
                             st=st, sp=sp: e.matmul(
                            pb[bk][:, 0:512], hop[:, k * 256 + tt * 128:k * 256 + tt * 128 + 128],
                            wop[:, ob * 512:(ob + 1) * 512],
                            start=st, stop=sp), inc=(pi == 2 and tt == 1 and ob == 3))
            wd_ev[slot] = b.ms[T_].n
            if k == 31:
                hq_half_mm[0] = b.ms[T_].n
        hq_half_mm[1] = b.ms[T_].n
        if q + 1 < n8:
            b.waitm(SY, T_)
            load_hq_half(q + 1, 1)
        dn_mm = b.ms[T_].n
        b.waitm(V_, T_, dn_mm)
        for tt in range(2):
            for ob in range(4):
                b.em(V_, lambda e, tt=tt, ob=ob: e.tensor_copy(
                    dtm_r[tt][:, ob * 512:(ob + 1) * 512], pb[tt * 4 + ob][:, 0:512]),
                    inc=(tt == 1 and ob == 3))
        for bk2 in range(8):
            bank_ev[bk2] = b.ms[V_].n; bank_eng[bk2] = V_
        for tt in range(2):
            gi = q * 2 + tt
            b.waitm(SY, V_)
            b.wait(SY, s_out2[tt])
            b.dma(SY, h_x2[tt], x2_s.ap()[gi * Q:(gi + 1) * Q, :], s_x2l)
            b.wait(V_, s_x2l)
            b.wait(S_, s_x2l)
            ln_tile(dtm_r[tt], h_out[tt], h_sq)
            b.em(V_, lambda e, tt=tt: e.tensor_tensor(h_out[tt], h_out[tt], h_x2[tt], ALU.add),
                 inc=True)
            b.waitm(SY, V_)
            b.dma(SY, out_t.ap()[gi * Q:(gi + 1) * Q, :], h_out[tt], s_out2[tt])
            s_out.n += 0

    b.wait(SY, s_out2[0])
    b.wait(SY, s_out2[1])

    with nc.Block() as block:
        @block.tensor
        def _(e):
            for fn in b.cmds["tensor"]:
                fn(e)

        @block.vector
        def _(e):
            for fn in b.cmds["vector"]:
                fn(e)

        @block.scalar
        def _(e):
            for fn in b.cmds["scalar"]:
                fn(e)

        @block.sync
        def _(e):
            for fn in b.cmds["sync"]:
                fn(e)

        @block.gpsimd
        def _(e):
            for fn in b.cmds["gpsimd"]:
                fn(e)

    nc.finalize()
    return nc


N_CORES = 8
N_TILES = 18
HALO_TILES = 2


def make_inmaps(inputs, n_tiles=N_TILES, halo_tiles=HALO_TILES, n_cores=N_CORES):
    T = n_tiles * Q
    Town = T - halo_tiles * Q
    halo = halo_tiles * Q
    x = np.ascontiguousarray(np.asarray(inputs["L0_attn_hidden_states"], dtype=np.float32)
                             ).reshape(-1, D)
    total = x.shape[0]
    assert total == n_cores * Town, (total, n_cores, Town)
    seq = total // 2
    w_in_a = np.ascontiguousarray(np.asarray(inputs["L0_attn_W_in"], np.float32))
    w_out_a = np.ascontiguousarray(np.asarray(inputs["L0_attn_W_out"], np.float32))
    w_gu_t = np.ascontiguousarray(np.asarray(inputs["L0_ffn_w_gate_up"], np.float32).T)
    w_down_t = np.ascontiguousarray(np.asarray(inputs["L0_ffn_w_down"], np.float32).T)
    cw_raw = np.asarray(inputs["L0_attn_conv_w"], np.float32)
    cb_raw = np.asarray(inputs["L0_attn_conv_b"], np.float32)
    convw = np.zeros((128, 33 * 4), np.float32)
    convb = np.zeros((128, 33), np.float32)
    for a in range(33):
        nch = min(128, NCONV - a * 128)
        convw[0:nch, a * 4:(a + 1) * 4] = cw_raw[a * 128:a * 128 + nch, :]
        convb[0:nch, a] = cb_raw[a * 128:a * 128 + nch]
    alog_a = np.asarray(inputs["L0_attn_A_log"], np.float32).reshape(NH, 1).copy()
    dtbias_a = np.asarray(inputs["L0_attn_dt_bias"], np.float32).reshape(NH, 1).copy()
    dskip_a = np.ascontiguousarray(np.broadcast_to(
        np.asarray(inputs["L0_attn_D_skip"], np.float32)[None, :], (128, NH)))
    normw_a = np.ascontiguousarray(np.broadcast_to(
        np.asarray(inputs["L0_attn_norm_w"], np.float32)[None, :], (128, DIN)))

    in_maps = []
    for core in range(n_cores):
        r0 = core * Town
        start = r0 - halo
        seq_start = (r0 // seq) * seq
        if start < seq_start:
            pad = seq_start - start
            xs = np.concatenate([np.zeros((pad, D), np.float32), x[seq_start:r0 + Town]], axis=0)
            hm = np.zeros((128, max(halo, 1)), np.float32)
            if halo > pad:
                hm[:, pad:] = 1.0
        else:
            xs = np.ascontiguousarray(x[start:r0 + Town])
            hm = np.ones((128, max(halo, 1)), np.float32)
        in_maps.append({
            "x_in": np.ascontiguousarray(xs), "w_in": w_in_a, "w_out": w_out_a,
            "w_gu_t": w_gu_t, "w_down_t": w_down_t, "convw": convw, "convb": convb,
            "alog": alog_a, "dtbias": dtbias_a, "dskip": dskip_a, "normw": normw_a,
            "halomask": np.ascontiguousarray(hm),
        })
    return in_maps


def kernel(**inputs):
    in_maps = make_inmaps(inputs)
    nc = build_program(N_TILES, HALO_TILES)
    res = run_bass_kernel_spmd(nc, in_maps, core_ids=list(range(N_CORES)))
    out = np.concatenate([np.asarray(r["out"]) for r in res.results], axis=0)
    return out.reshape(2, 8192, D).astype(np.float32)
